# version 10
# baseline (speedup 1.0000x reference)
import contextlib
from contextlib import contextmanager
import numpy as np
import concourse.bass as bass
import concourse.mybir as mybir
from concourse.bass_utils import run_bass_kernel_spmd

F32 = mybir.dt.float32
BF16 = mybir.dt.bfloat16
U8 = mybir.dt.uint8
AF = mybir.ActivationFunctionType
ALU = mybir.AluOpType
AX = mybir.AxisListType

T = 4096
D = 1024
NCH = 8
H = 16
HD = 64
CH = 128
NCHUNK = T // CH
SLAB = 512
NSLAB = T // SLAB
HALF = T // 2
EPS_LN = 1e-5
EPS_ATTN = 1e-12
NEG_BIG = -1e30
DBG_SKIP = set()


class Sem:
    __slots__ = ("h", "v")

    def __init__(self, h):
        self.h = h
        self.v = 0

    @property
    def tok(self):
        return (self, self.v)


class Prog:
    ENGINES = ("pe", "act", "dve", "pool", "sp")

    def __init__(self):
        self.nc = bass.Bass("TRN2", target_bir_lowering=False)
        self.stack = contextlib.ExitStack()
        self.sem_stack = contextlib.ExitStack()
        self.sched = {e: [] for e in self.ENGINES}
        self._names = {}
        self.all_sems = []
        self.ext_inputs = set()
        self.sem_free = []
        self._scope_sems = None

    @contextmanager
    def scope(self):
        old = self.stack
        self.stack = contextlib.ExitStack()
        outer_scope = self._scope_sems
        self._scope_sems = []
        try:
            yield
        finally:
            self.stack.close()
            self.stack = old
            self.sem_free.extend(self._scope_sems)
            self._scope_sems = outer_scope

    def name(self, base):
        n = self._names.get(base, 0)
        self._names[base] = n + 1
        return "%s_%d" % (base, n)

    def sem(self, base="s"):
        if self._scope_sems is not None and self.sem_free:
            sm = self.sem_free.pop()
        else:
            sm = Sem(self.sem_stack.enter_context(self.nc.semaphore(self.name(base))))
            self.all_sems.append(sm)
        if self._scope_sems is not None:
            self._scope_sems.append(sm)
        return sm

    def sbuf(self, base, shape, dt):
        return self.stack.enter_context(self.nc.sbuf_tensor(self.name(base), shape, dt))

    def psum(self, base, shape, dt):
        return self.stack.enter_context(self.nc.psum_tensor(self.name(base), shape, dt))

    def dram(self, name, shape, dt, kind=None):
        if kind is None:
            return self.nc.dram_tensor(name, shape, dt).ap()
        return self.nc.dram_tensor(name, shape, dt, kind=kind).ap()

    def _emit(self, eng, thunk, wait, inc, amt):
        tok = None
        if inc is not None:
            inc.v += amt
            tok = (inc, inc.v)
        self.sched[eng].append((tuple(wait), thunk, (inc, amt) if inc is not None else None))
        return tok

    def op(self, eng, thunk, wait=(), inc=None):
        return self._emit(eng, thunk, wait, inc, 1)

    def dma(self, eng, thunk, wait=(), inc=None):
        return self._emit(eng, thunk, wait, inc, 16)

    def finalize(self):
        tail = tuple((sm, sm.v) for sm in self.all_sems if sm.v > 0)
        self.sched["sp"].append((tail, lambda e: None, None))
        sched = self.sched

        def runner(eng_name):
            def go(eng):
                last = {}
                for waits, thunk, inc in sched[eng_name]:
                    for sem, val in waits:
                        if last.get(id(sem), -1) < val:
                            eng.wait_ge(sem.h, val)
                            last[id(sem)] = val
                    inst = thunk(eng)
                    if inc is not None:
                        sem, amt = inc
                        inst.then_inc(sem.h, amt)
            return go

        with self.nc.Block() as block:
            block.tensor(runner("pe"))
            block.scalar(runner("act"))
            block.vector(runner("dve"))
            block.gpsimd(runner("pool"))
            block.sync(runner("sp"))
        self.stack.close()
        self.sem_stack.close()
        return self.nc


def bcast_mid(ap2d, n):
    return bass.AP(ap2d.tensor, ap2d.offset,
                   [list(ap2d.ap[0]), [0, n], list(ap2d.ap[1])])


def fm3(sb_ap, width, nch=NCH):
    return sb_ap.rearrange("p (c t) -> p c t", c=nch)


def hbm_slabT(hbm, t0, w):
    return hbm[:, :, t0:t0 + w].transpose([1, 0, 2])



def make_const_arrays():
    c = {}
    c["c_ident"] = np.eye(128, dtype=np.float32)
    c["c_ones_col"] = np.ones((128, 1), dtype=np.float32)
    c["c_ones_row"] = np.ones((1, 128), dtype=np.float32)
    c["c_triu4"] = np.tile(np.triu(np.ones((128, 128), dtype=np.float32)), (1, 4))
    oh16 = np.zeros((16, 16 * 128), dtype=np.float32)
    for e in range(16):
        oh16[e, e * 128:(e + 1) * 128] = 1.0
    c["c_oh16"] = oh16
    oh8 = np.zeros((8, 8 * 128), dtype=np.float32)
    for e in range(8):
        oh8[e, e * 128:(e + 1) * 128] = 1.0
    c["c_oh8"] = oh8
    kb = np.zeros((128, 8), dtype=np.float32)
    kb[:, 0] = EPS_LN
    kb[:, 1] = EPS_ATTN
    c["c_kb"] = kb
    return c


def emit_consts(p):
    c = {}
    hb = {}
    for k, v in make_const_arrays().items():
        p.ext_inputs.add(k)
        hb[k] = p.dram(k, list(v.shape), F32, kind="ExternalInput")
    c["ident"] = p.sbuf("ident", [128, 128], BF16)
    c["ones_col_bf"] = p.sbuf("ones_col_bf", [128, 1], BF16)
    c["ones_col_f"] = p.sbuf("ones_col_f", [128, 1], F32)
    c["ones_row_f"] = p.sbuf("ones_row_f", [1, 128], F32)
    c["triu4"] = p.sbuf("triu4", [128, 512], F32)
    c["oh16"] = p.sbuf("oh16", [16, 2048], BF16)
    c["oh8"] = p.sbuf("oh8", [8, 1024], BF16)
    c["kb"] = p.sbuf("kb", [128, 8], F32)
    c["identf"] = p.sbuf("identf", [128, 128], F32)
    s1 = p.sem("csemp")
    s2 = p.sem("csems")
    p.dma("pool", lambda e: e.dma_start(out=c["ident"][:], in_=hb["c_ident"]), inc=s1)
    p.dma("pool", lambda e: e.dma_start(out=c["ones_col_bf"][:], in_=hb["c_ones_col"]), inc=s1)
    p.dma("pool", lambda e: e.dma_start(out=c["oh16"][:], in_=hb["c_oh16"]), inc=s1)
    p.dma("pool", lambda e: e.dma_start(out=c["oh8"][:], in_=hb["c_oh8"]), inc=s1)
    p.dma("sp", lambda e: e.dma_start(out=c["ones_col_f"][:], in_=hb["c_ones_col"]), inc=s2)
    p.dma("sp", lambda e: e.dma_start(out=c["ones_row_f"][:], in_=hb["c_ones_row"]), inc=s2)
    p.dma("sp", lambda e: e.dma_start(out=c["triu4"][:], in_=hb["c_triu4"]), inc=s2)
    p.dma("sp", lambda e: e.dma_start(out=c["kb"][:], in_=hb["c_kb"]), inc=s2)
    p.dma("sp", lambda e: e.dma_start(out=c["identf"][:], in_=hb["c_ident"]), inc=s2)
    c["tok"] = (s1, s1.v)
    c["tok2"] = (s2, s2.v)
    return c



def emit_ln_rows(p, toks, pfx, c, xs_chunk, nch, w, ps_mean, ps_sq, ps_A, ps_B,
                 rows, dep_sq_tok, dep_ld_tok, waitAB=(), wait_stats=(),
                 defer_bcast=False):
    spe, sact, sdve = toks["pe"], toks["act"], toks["dve"]
    t = None
    for cc in range(nch):
        w0 = ((dep_ld_tok,) if dep_ld_tok is not None else ()) + tuple(wait_stats)
        t = p.op("pe", lambda e, cc=cc: e.matmul(
            ps_mean[:], c["ones_col_f"][:], xs_chunk(cc),
            start=(cc == 0), stop=(cc == nch - 1)),
            wait=w0 if cc == 0 else (), inc=spe if cc == nch - 1 else None)
    tok_mean = t
    sq_chunk = toks["sq_chunk"]
    for cc in range(nch):
        t = p.op("pe", lambda e, cc=cc: e.matmul(
            ps_sq[:], c["ones_col_f"][:], sq_chunk(cc),
            start=(cc == 0), stop=(cc == nch - 1)),
            wait=(dep_sq_tok,) if cc == 0 else (), inc=spe if cc == nch - 1 else None)
    tok_sq = t
    mu, m2, musq, var, rstd, bb = (rows[0:1, i * w:(i + 1) * w] for i in range(6))
    tmu = p.op("act", lambda e: e.activation(out=mu, in_=ps_mean[:], func=AF.Copy, scale=1.0 / D),
         wait=(tok_mean,), inc=sact)
    p.op("act", lambda e: e.activation(out=m2, in_=ps_sq[:], func=AF.Copy, scale=1.0 / D),
         wait=(tok_sq,))
    tok_stats = p.op("act", lambda e: e.activation(out=musq, in_=mu, func=AF.Square),
                     wait=(tmu,), inc=sact)
    td = p.op("dve", lambda e: e.tensor_sub(var, m2, musq), wait=(tok_stats,), inc=sdve)
    ta = p.op("act", lambda e: e.activation(out=var, in_=var, func=AF.Ln, bias=c["kb"][0:1, 0:1]),
              wait=(td,), inc=sact)
    ta2 = p.op("act", lambda e: e.activation(out=rstd, in_=var, func=AF.Exp, scale=-0.5),
               wait=(ta,), inc=sact)
    tok_rows = p.op("dve", lambda e: e.scalar_tensor_tensor(
        bb, mu, -1.0, rstd, ALU.mult, ALU.mult), wait=(ta2,), inc=sdve)
    if toks.get("defer_bcast") or defer_bcast:
        return tok_rows, None, tok_stats
    p.op("pe", lambda e: e.matmul(ps_A[:], c["ones_row_f"][:], rstd, start=True, stop=True),
         wait=(tok_rows,) + tuple(waitAB))
    tok_bc = p.op("pe", lambda e: e.matmul(ps_B[:], c["ones_row_f"][:], bb, start=True, stop=True),
                  inc=spe)
    return tok_rows, tok_bc, tok_stats


def emit_ln(p, c, src, dst, deps, pfx="ln", gamma_sb=None, beta_sb=None):
    spe, sact, sdve = p.sem(pfx), p.sem(pfx), p.sem(pfx)
    sld = [p.sem(pfx) for _ in range(2)]
    sst = [p.sem(pfx) for _ in range(2)]
    xs = [p.sbuf(pfx + "x", [128, NCH * SLAB], F32) for _ in range(2)]
    xq = [p.sbuf(pfx + "q", [128, NCH * SLAB], F32) for _ in range(2)]
    xo = [p.sbuf(pfx + "o", [128, NCH * SLAB], BF16) for _ in range(2)]
    rows = [p.sbuf(pfx + "r", [1, 6 * SLAB], F32) for _ in range(2)]
    ps_mean = [p.psum(pfx + "pm", [1, SLAB], F32) for _ in range(2)]
    ps_sq = [p.psum(pfx + "ps", [1, SLAB], F32) for _ in range(2)]
    ps_A = [p.psum(pfx + "pa", [128, SLAB], F32) for _ in range(2)]
    ps_B = [p.psum(pfx + "pb", [128, SLAB], F32) for _ in range(2)]
    xs_free = [None, None]
    xo_free = [None, None]
    tok_x2 = [None, None]
    tok_stats = [None, None]
    tok_rows_s = [None, None]
    tok_st = None
    for s in range(NSLAB + 1):
        b = s % 2
        if s < NSLAB:
            t0 = s * SLAB
            w = tuple(deps) if s < 2 else ()
            if xs_free[b] is not None:
                w = w + (xs_free[b],)
            tok_ld = p.dma("sp", lambda e, b=b, t0=t0: e.dma_start(
                out=fm3(xs[b][:], SLAB), in_=hbm_slabT(src, t0, SLAB)), wait=w,
                inc=sld[b])
            tok_sq = p.op("act", lambda e, b=b: e.activation(
                out=xq[b][:], in_=xs[b][:], func=AF.Square), wait=(tok_ld,), inc=sact)
            toks = {"pe": spe, "act": sact, "dve": sdve,
                    "sq_chunk": lambda cc, b=b: xq[b][:, cc * SLAB:(cc + 1) * SLAB]}
            tok_rows, tok_bc_unused, tok_stats2 = emit_ln_rows(
                p, toks, pfx, c,
                lambda cc, b=b: xs[b][:, cc * SLAB:(cc + 1) * SLAB],
                NCH, SLAB, ps_mean[b], ps_sq[b], ps_A[b], ps_B[b], rows[b],
                tok_sq, tok_ld,
                waitAB=(tok_x2[b],) if tok_x2[b] is not None else (),
                wait_stats=(tok_stats[b],) if tok_stats[b] is not None else (),
                defer_bcast=True)
            tok_stats[b] = tok_stats2
            tok_rows_s[b] = tok_rows
        if s >= 1:
            sb_ = (s - 1) % 2
            t0p = (s - 1) * SLAB
            p.op("pe", lambda e, b=sb_: e.matmul(
                ps_A[b][:], c["ones_row_f"][:], rows[b][0:1, 4 * SLAB:5 * SLAB],
                start=True, stop=True),
                wait=(tok_rows_s[sb_],) + ((tok_x2[sb_],) if tok_x2[sb_] is not None else ()))
            tok_bc = p.op("pe", lambda e, b=sb_: e.matmul(
                ps_B[b][:], c["ones_row_f"][:], rows[b][0:1, 5 * SLAB:6 * SLAB],
                start=True, stop=True), inc=spe)
            wx = (tok_bc,)
            if xo_free[sb_] is not None:
                wx = wx + (xo_free[sb_],)
            xs3 = fm3(xs[sb_][:], SLAB)
            xq3 = fm3(xq[sb_][:], SLAB)
            xo3 = fm3(xo[sb_][:], SLAB)
            tmul = p.op("dve", lambda e, b=sb_, xs3=xs3, xq3=xq3: e.tensor_tensor(
                xq3, xs3, bcast_mid(ps_A[b][:], NCH), ALU.mult), wait=wx, inc=sdve)
            if gamma_sb is None:
                tok_x = p.op("dve", lambda e, b=sb_, xq3=xq3, xo3=xo3: e.tensor_tensor(
                    xo3, xq3, bcast_mid(ps_B[b][:], NCH), ALU.add),
                    wait=(tmul,), inc=sdve)
            else:
                tadd = p.op("dve", lambda e, b=sb_, xq3=xq3: e.tensor_tensor(
                    xq3, xq3, bcast_mid(ps_B[b][:], NCH), ALU.add),
                    wait=(tmul,), inc=sdve)
                tok_x = None
                for cc in range(NCH):
                    xq_c = xq[sb_][:, cc * SLAB:(cc + 1) * SLAB]
                    xo_c = xo[sb_][:, cc * SLAB:(cc + 1) * SLAB]
                    tok_x = p.op("dve", lambda e, a=xq_c, o=xo_c, cc=cc: e.tensor_scalar(
                        o, a, gamma_sb[:, cc:cc + 1], beta_sb[:, cc:cc + 1],
                        ALU.mult, ALU.add), wait=(tadd,) if cc == 0 else (),
                        inc=sdve if cc == NCH - 1 else None)
            xs_free[sb_] = tok_x
            tok_x2[sb_] = tok_x
            tok_st = p.dma("sp", lambda e, b=sb_, t0p=t0p: e.dma_start(
                out=hbm_slabT(dst, t0p, SLAB), in_=fm3(xo[b][:], SLAB)),
                wait=(tok_x,), inc=sst[sb_])
            xo_free[sb_] = tok_st
    return [tok_st]




def emit_qkv(p, c, xn, qd, kd, vd, w_hbm, kind, deps, pfx="qkv"):
    spe, sact, sdve, spool = p.sem(pfx), p.sem(pfx), p.sem(pfx), p.sem(pfx)
    sw, sx = p.sem(pfx), p.sem(pfx)
    sst = [p.sem(pfx) for _ in range(4)]
    w_sb = p.sbuf(pfx + "w", [128, NCH * 3072], BF16)
    tok_w = p.dma("pool", lambda e: e.dma_start(
        out=fm3(w_sb[:], 3072), in_=w_hbm.rearrange("(c p) m -> p c m", p=128)),
        wait=tuple(deps), inc=sw)
    xn_sb = p.sbuf(pfx + "x", [128, NCH * T], BF16)
    tok_x = p.dma("sp", lambda e: e.dma_start(
        out=fm3(xn_sb[:], T), in_=xn.transpose([1, 0, 2])), wait=tuple(deps), inc=sx)
    nps = 4
    pss = [p.psum(pfx + "p", [128, SLAB], F32) for _ in range(nps)]
    ps_free = [None] * nps
    sc1 = [p.sbuf(pfx + "s1", [128, SLAB], F32) for _ in range(2)]
    sc2 = [p.sbuf(pfx + "s2", [128, SLAB], F32) for _ in range(2)]
    ob = [p.sbuf(pfx + "o", [128, SLAB], BF16) for _ in range(4)]
    ob_free = [None] * 4
    sc_free = [None, None]
    tok_st = None
    i = 0
    for mc in range(24):
        dest = (qd, kd, vd)[mc // 8]
        cc_out = mc % 8
        act = kind if mc < 16 else "copy"
        for s in range(NSLAB):
            t0 = s * SLAB
            pb = i % nps
            w0 = (tok_w, tok_x) if i == 0 else ()
            if ps_free[pb] is not None:
                w0 = w0 + (ps_free[pb],)
            tmm = None
            for kc in range(NCH):
                lhs = w_sb[:, kc * 3072 + mc * 128: kc * 3072 + (mc + 1) * 128]
                rhs = xn_sb[:, kc * T + t0: kc * T + t0 + SLAB]
                tmm = p.op("pe", lambda e, l=lhs, r=rhs, pb=pb, kc=kc: e.matmul(
                    pss[pb][:], l, r, start=(kc == 0), stop=(kc == NCH - 1)),
                    wait=w0 if kc == 0 else (), inc=spe if kc == NCH - 1 else None)
            oi = i % 4
            wo = (tmm,)
            if ob_free[oi] is not None:
                wo = wo + (ob_free[oi],)
            if act == "silu":
                tok_ev = p.op("act", lambda e, pb=pb, oi=oi: e.activation(
                    out=ob[oi][:], in_=pss[pb][:], func=AF.Silu), wait=wo, inc=sact)
                ps_free[pb] = tok_ev
            elif act == "copy":
                tok_ev = p.op("act", lambda e, pb=pb, oi=oi: e.copy(
                    out=ob[oi][:], in_=pss[pb][:]), wait=wo, inc=sact)
                ps_free[pb] = tok_ev
            else:
                si = i % 2
                if sc_free[si] is not None:
                    wo = wo + (sc_free[si],)
                t1 = p.op("act", lambda e, pb=pb, si=si: e.activation(
                    out=sc1[si][:], in_=pss[pb][:], func=AF.Exp),
                    wait=wo, inc=sact)
                t2 = p.op("dve", lambda e, si=si: e.tensor_single_scalar(
                    sc1[si][:], sc1[si][:], 1.0, ALU.min), wait=(t1,), inc=sdve)
                t3 = p.op("dve", lambda e, pb=pb, si=si: e.tensor_single_scalar(
                    sc2[si][:], pss[pb][:], 0.0, ALU.max), wait=(t2,), inc=sdve)
                ps_free[pb] = t3
                tok_ev = p.op("pool", lambda e, si=si, oi=oi: e.tensor_add(
                    ob[oi][:], sc1[si][:], sc2[si][:]), wait=(t2, t3), inc=spool)
                sc_free[si] = tok_ev
            tok_st = p.dma("sp", lambda e, d=dest, cc=cc_out, t0=t0, oi=oi: e.dma_start(
                out=d[cc, :, t0:t0 + SLAB], in_=ob[oi][:]), wait=(tok_ev,), inc=sst[oi])
            ob_free[oi] = tok_st
            i += 1
    return [tok_st]



def emit_proj_residual(p, c, a_in, x_in, x_out, w_hbm, deps, pfx="wo"):
    spe, sact, sdve = p.sem(pfx), p.sem(pfx), p.sem(pfx)
    sw = p.sem(pfx)
    sa = [p.sem(pfx) for _ in range(2)]
    sxi = [p.sem(pfx) for _ in range(2)]
    sst = [p.sem(pfx) for _ in range(2)]
    w_sb = p.sbuf(pfx + "w", [128, NCH * D], BF16)
    tok_w = p.dma("pool", lambda e: e.dma_start(
        out=fm3(w_sb[:], D), in_=w_hbm.rearrange("(c p) m -> p c m", p=128)),
        wait=tuple(deps), inc=sw)
    ab = [p.sbuf(pfx + "a", [128, NCH * SLAB], BF16) for _ in range(2)]
    xb = [p.sbuf(pfx + "xi", [128, NCH * SLAB], F32) for _ in range(2)]
    ob = [p.sbuf(pfx + "xo", [128, NCH * SLAB], F32) for _ in range(2)]
    pss = [p.psum(pfx + "p", [128, SLAB], F32) for _ in range(4)]
    ps_free = [None] * 4
    in_free = [None, None]
    ob_free = [None, None]
    tok_st = None
    for s in range(NSLAB):
        b = s % 2
        t0 = s * SLAB
        w0 = tuple(deps) if s < 2 else ()
        if in_free[b] is not None:
            w0 = w0 + (in_free[b],)
        tok_a = p.dma("sp", lambda e, b=b, t0=t0: e.dma_start(
            out=fm3(ab[b][:], SLAB), in_=hbm_slabT(a_in, t0, SLAB)), wait=w0, inc=sa[b])
        tok_xi = p.dma("sp", lambda e, b=b, t0=t0: e.dma_start(
            out=fm3(xb[b][:], SLAB), in_=hbm_slabT(x_in, t0, SLAB)), inc=sxi[b])
        last_ev = None
        for mc in range(NCH):
            pb = mc % 4
            wmm = (tok_a,) if mc == 0 else ()
            wmm = wmm + ((tok_w,) if s == 0 and mc == 0 else ())
            if ps_free[pb] is not None:
                wmm = wmm + (ps_free[pb],)
            tmm = None
            for kc in range(NCH):
                lhs = w_sb[:, kc * D + mc * 128: kc * D + (mc + 1) * 128]
                rhs = ab[b][:, kc * SLAB: (kc + 1) * SLAB]
                tmm = p.op("pe", lambda e, l=lhs, r=rhs, pb=pb, kc=kc: e.matmul(
                    pss[pb][:], l, r, start=(kc == 0), stop=(kc == NCH - 1)),
                    wait=wmm if kc == 0 else (), inc=spe if kc == NCH - 1 else None)
            wev = (tmm, tok_xi)
            if mc == 0 and ob_free[b] is not None:
                wev = wev + (ob_free[b],)
            last_ev = p.op("dve", lambda e, b=b, mc=mc, pb=pb: e.tensor_add(
                ob[b][:, mc * SLAB:(mc + 1) * SLAB], pss[pb][:],
                xb[b][:, mc * SLAB:(mc + 1) * SLAB]), wait=wev, inc=sdve)
            ps_free[pb] = last_ev
        in_free[b] = last_ev
        tok_st = p.dma("sp", lambda e, b=b, t0=t0: e.dma_start(
            out=hbm_slabT(x_out, t0, SLAB), in_=fm3(ob[b][:], SLAB)),
            wait=(last_ev,), inc=sst[b])
        ob_free[b] = tok_st
    return [tok_st]



def emit_attn(p, c, qd, kd, vd, ad, denom, deps, pfx="at"):
    spe, sact, sdve = p.sem(pfx), p.sem(pfx), p.sem(pfx)
    sq = [p.sem(pfx) for _ in range(2)]
    sk = [p.sem(pfx) for _ in range(2)]
    sv = [p.sem(pfx) for _ in range(2)]
    sst = [p.sem(pfx) for _ in range(2)]
    qs = [p.sbuf(pfx + "q", [128, NCH * CH], BF16) for _ in range(2)]
    ks = [p.sbuf(pfx + "k", [128, NCH * CH], BF16) for _ in range(2)]
    vs = [p.sbuf(pfx + "v", [128, NCH * CH], BF16) for _ in range(2)]
    ktm = p.sbuf(pfx + "ktm", [128, NCH * CH], BF16)
    vtm = p.sbuf(pfx + "vtm", [128, NCH * CH], BF16)
    S_sb = p.sbuf(pfx + "S", [128, 512], BF16)
    z_sb = p.sbuf(pfx + "z", [128, 8], BF16)
    sc_sb = [p.sbuf(pfx + "sc", [128, 512], BF16) for _ in range(2)]
    nm_sb = [p.sbuf(pfx + "nm", [128, 256], F32) for _ in range(2)]
    rden = p.sbuf(pfx + "rd", [128, 256], F32)
    at_sb = [p.sbuf(pfx + "o", [128, NCH * CH], BF16) for _ in range(2)]
    ps_S = p.psum(pfx + "pS", [128, 512], F32)
    ps_sc = [p.psum(pfx + "psc", [128, 512], F32) for _ in range(2)]
    ps_nm = [p.psum(pfx + "pnm", [128, 512], F32) for _ in range(2)]
    ps_tr = p.psum(pfx + "ptr", [128, 1024], BF16)
    if denom:
        ps_z = p.psum(pfx + "pz", [128, 512], F32)

    def hslice(sb, h):
        pb = (h % 2) * 64
        return sb[pb:pb + 64, (h // 2) * CH:(h // 2) * CH + CH]

    SCCOL = [0, 256, 128, 384]

    t0s = tuple(deps)
    tok_S0 = p.op("dve", lambda e: e.memset(S_sb[:], 0.0), wait=t0s, inc=sdve)
    tok_psS0 = p.op("dve", lambda e: e.memset(ps_S[:], 0.0), inc=sdve)
    tok_psz0 = None
    if denom:
        tok_S0 = p.op("dve", lambda e: e.memset(z_sb[:], 0.0), inc=sdve)
        tok_psz0 = p.op("dve", lambda e: e.memset(ps_z[:, 0:8], 0.0), inc=sdve)
    qkv_free = [None, None]
    scfree = None
    scb_free = [None, None]
    nmfree = [None, None]
    at_free = [None, None]
    tok_trfree = None
    tok_Ssb = tok_S0
    tok_zsb = tok_S0
    tok_Supd_prev = None
    tok_zupd_prev = None
    tok_st = None
    for ci in range(NCHUNK):
        b = ci % 2
        c0 = ci * CH
        w0 = t0s if ci < 2 else ()
        if qkv_free[b] is not None:
            w0 = w0 + (qkv_free[b],)
        tok_q = p.dma("sp", lambda e, b=b, c0=c0: e.dma_start(
            out=fm3(qs[b][:], CH), in_=hbm_slabT(qd, c0, CH)), wait=w0, inc=sq[b])
        tok_k = p.dma("sp", lambda e, b=b, c0=c0: e.dma_start(
            out=fm3(ks[b][:], CH), in_=hbm_slabT(kd, c0, CH)), inc=sk[b])
        tok_v = p.dma("sp", lambda e, b=b, c0=c0: e.dma_start(
            out=fm3(vs[b][:], CH), in_=hbm_slabT(vd, c0, CH)), inc=sv[b])
        wtr = (tok_k,) + ((tok_trfree,) if tok_trfree is not None else ())
        tk = None
        if "tr" in DBG_SKIP:
            tok_ktm = p.op("dve", lambda e: e.memset(ktm[:], 0.001), wait=wtr, inc=sdve)
            tok_vtm = p.op("dve", lambda e: e.memset(vtm[:], 0.001), wait=(tok_v,), inc=sdve)
            tok_trfree = tok_vtm
            dbg_skip_tr = True
        else:
            dbg_skip_tr = False
        if not dbg_skip_tr:
            for cc in range(NCH):
                tk = p.op("pe", lambda e, b=b, cc=cc: e.transpose(
                    ps_tr[:, cc * CH:(cc + 1) * CH], ks[b][:, cc * CH:(cc + 1) * CH],
                    c["ident"][:]), wait=wtr if cc == 0 else (),
                    inc=spe if cc == NCH - 1 else None)
            tok_ktm = p.op("dve", lambda e: e.tensor_copy(out=ktm[:], in_=ps_tr[:]),
                           wait=(tk,), inc=sdve)
            tv = None
            for cc in range(NCH):
                tv = p.op("pe", lambda e, b=b, cc=cc: e.transpose(
                    ps_tr[:, cc * CH:(cc + 1) * CH], vs[b][:, cc * CH:(cc + 1) * CH],
                    c["ident"][:]), wait=(tok_v, tok_ktm) if cc == 0 else (),
                    inc=spe if cc == NCH - 1 else None)
            tok_vtm = p.op("dve", lambda e: e.tensor_copy(out=vtm[:], in_=ps_tr[:]),
                           wait=(tv,), inc=sdve)
            tok_trfree = tok_vtm
        ep_toks = []
        for g in range(4):
            b2 = g % 2
            heads = [4 * g + j for j in range(4)]
            wsc = (tok_q,)
            if scfree is not None:
                wsc = wsc + (scfree,)
            tsc = None
            if "sc" in DBG_SKIP:
                tsc = p.op("pe", lambda e: e.matmul(
                    ps_sc[0][:], c["ident"][:], qs[b][:, 0:512], start=True,
                    stop=True), wait=wsc, inc=spe)
            else:
              for j, h in enumerate(heads):
                pb = (h % 2) * 64
                tsc = p.op("pe", lambda e, b=b, j=j, h=h, pb=pb: e.matmul(
                    ps_sc[j % 2][:, (j // 2) * CH:(j // 2 + 1) * CH],
                    hslice(ks[b], h), hslice(qs[b], h), start=True, stop=True,
                    tile_position=(pb, 0)), wait=wsc if j == 0 else (),
                    inc=spe if j == 3 else None)
            wev = (tsc,)
            if scb_free[b2] is not None:
                wev = wev + (scb_free[b2],)
            p.op("dve", lambda e, b2=b2: e.tensor_mul(
                sc_sb[b2][:, 0:256], ps_sc[0][:, 0:256], c["triu4"][:, 0:256]),
                wait=wev)
            tok_scev = p.op("dve", lambda e, b2=b2: e.tensor_mul(
                sc_sb[b2][:, 256:512], ps_sc[1][:, 0:256], c["triu4"][:, 0:256]),
                inc=sdve)
            if denom:
                wd = (tok_scev, tok_zsb)
                td = None
                for j, h in enumerate(heads):
                    pb = (h % 2) * 64
                    cb = j // 2
                    dreg = ps_sc[b2][pb:pb + 64, 256 + cb * CH:256 + (cb + 1) * CH]
                    zsl = z_sb[pb:pb + 64, (h // 2):(h // 2) + 1]
                    zrep = bass.AP(zsl.tensor, zsl.offset,
                                   [list(zsl.ap[0]), [0, 64]])
                    ones64 = bass.AP(c["ones_col_bf"][:].tensor, 0,
                                     [[1, 128], [0, 64]])
                    td = p.op("pe", lambda e, b=b, o=dreg, z=zrep, h=h, pb=pb: e.matmul(
                        o, z, hslice(qs[b], h),
                        start=True, stop=False, tile_position=(pb, pb)),
                        wait=wd if j == 0 else (), inc=None)
                    td = p.op("pe", lambda e, o=dreg, n1=ones64, j=j, b2=b2, pb=pb: e.matmul(
                        o, n1, sc_sb[b2][:, SCCOL[j]:SCCOL[j] + CH],
                        start=False, stop=True, tile_position=(0, pb)),
                        inc=spe if j == 3 else None)
                tl = p.op("act", lambda e, b2=b2: e.activation(
                    out=rden[:], in_=ps_sc[b2][:, 256:512], func=AF.Ln),
                    wait=(td,), inc=sact)
                tok_rden = p.op("act", lambda e: e.activation(
                    out=rden[:], in_=rden[:], func=AF.Exp, scale=-1.0),
                    wait=(tl,), inc=sact)
            wnm = (tok_scev, tok_Ssb, tok_vtm)
            if nmfree[b2] is not None:
                wnm = wnm + (nmfree[b2],)
            tnm = None
            if "nm" in DBG_SKIP:
                tnm = p.op("pe", lambda e, b2=b2: e.matmul(
                    ps_nm[b2][:], c["ident"][:], qs[b][:, 0:512], start=True,
                    stop=True), wait=wnm, inc=spe)
            else:
              for j, h in enumerate(heads):
                pb = (h % 2) * 64
                cb = j // 2
                out_ap = ps_nm[b2][pb:pb + 64, cb * CH:(cb + 1) * CH]
                tnm = p.op("pe", lambda e, o=out_ap, h=h, pb=pb, b=b, j=j: e.matmul(
                    o, S_sb[pb:pb + 64, (h // 2) * 64:(h // 2) * 64 + 64],
                    hslice(qs[b], h), start=True, stop=False,
                    tile_position=(pb, pb)), wait=wnm if j == 0 else (), inc=None)
                tnm = p.op("pe", lambda e, o=out_ap, h=h, pb=pb, j=j, b2=b2: e.matmul(
                    o, vtm[:, h * 64:(h + 1) * 64],
                    sc_sb[b2][:, SCCOL[j]:SCCOL[j] + CH], start=False, stop=True,
                    tile_position=(0, pb)), inc=spe if j == 3 else None)
            if denom:
                tok_nmev = p.op("act", lambda e, b2=b2: e.copy(
                    out=nm_sb[b2][:], in_=ps_nm[b2][:, 0:256]), wait=(tnm,), inc=sact)
                nmfree[b2] = tok_nmev
                wep = (tok_nmev, tok_rden)
                if g < 2 and at_free[b] is not None:
                    wep = wep + (at_free[b],)
                tmul = p.op("dve", lambda e, b2=b2: e.tensor_mul(
                    nm_sb[b2][:], nm_sb[b2][:], rden[:]), wait=wep,
                    inc=sdve)
                tep = p.op("dve", lambda e, b=b, g=g, b2=b2: e.tensor_single_scalar(
                    at_sb[b][:, 2 * g * CH:(2 * g + 2) * CH], nm_sb[b2][:],
                    EPS_ATTN, ALU.add), wait=(tmul,), inc=sdve)
                scfree = tok_rden
                scb_free[b2] = tnm
            else:
                wep = (tnm,)
                if g < 2 and at_free[b] is not None:
                    wep = wep + (at_free[b],)
                tep = p.op("act", lambda e, b=b, g=g, b2=b2: e.copy(
                    out=at_sb[b][:, 2 * g * CH:(2 * g + 2) * CH],
                    in_=ps_nm[b2][:, 0:256]), wait=wep, inc=sact)
                nmfree[b2] = tep
                scfree = tok_scev
                scb_free[b2] = tnm
            ep_toks.append(tep)
        wS = (tok_ktm, tok_vtm)
        if ci == 0:
            wS = wS + (tok_psS0,)
        if tok_Ssb is not None:
            wS = wS + (tok_Ssb,)
        tS = None
        for h in range(H):
            pb = (h % 2) * 64
            tS = p.op("pe", lambda e, h=h, pb=pb, ci=ci: e.matmul(
                ps_S[pb:pb + 64, (h // 2) * 64:(h // 2) * 64 + 64],
                ktm[:, h * 64:(h + 1) * 64], vtm[:, h * 64:(h + 1) * 64],
                start=(ci == 0 and h == 0), stop=(ci == NCHUNK - 1 and h == H - 1),
                skip_group_check=True,
                tile_position=(0, pb)), wait=wS if h == 0 else (),
                inc=spe if h == H - 1 else None)
        tok_Supd = tS
        if denom:
            wz = (tok_zsb,)
            if ci == 0:
                wz = wz + (tok_psz0,)
            tz = None
            for h in range(H):
                pb = (h % 2) * 64
                tz = p.op("pe", lambda e, h=h, pb=pb, ci=ci: e.matmul(
                    ps_z[pb:pb + 64, (h // 2):(h // 2) + 1],
                    ktm[:, h * 64:(h + 1) * 64], c["ones_col_bf"][:],
                    start=(ci == 0 and h == 0),
                    stop=(ci == NCHUNK - 1 and h == H - 1),
                    skip_group_check=True,
                    tile_position=(0, pb)), wait=wz if h == 0 else (),
                    inc=spe if h == H - 1 else None)
            tok_zupd = tz
        if ci < NCHUNK - 1:
            tok_Ssb = p.op("act", lambda e: e.copy(out=S_sb[:], in_=ps_S[:]),
                           wait=(tok_Supd,), inc=sact)
            if denom:
                tok_zsb = p.op("act", lambda e: e.copy(
                    out=z_sb[:], in_=ps_z[:, 0:8]), wait=(tok_zupd,), inc=sact)
        qkv_free[b] = tok_Supd
        tok_st = p.dma("sp", lambda e, b=b, c0=c0: e.dma_start(
            out=hbm_slabT(ad, c0, CH), in_=fm3(at_sb[b][:], CH)),
            wait=(ep_toks[-1],), inc=sst[b])
        at_free[b] = tok_st
    return [tok_st]



def emit_router(p, c, xn, gd, w_hbm, mode, deps, bias_sb=None, bias_hbm=None,
                pfx="rt"):
    E = 16 if mode == "top4x16" else 8
    TOPK = 4 if mode == "top4x16" else 2
    spe, sact, sdve = p.sem(pfx), p.sem(pfx), p.sem(pfx)
    sx = [p.sem(pfx) for _ in range(2)]
    sw = p.sem(pfx)
    sst = [p.sem(pfx) for _ in range(2)]
    w_sb = p.sbuf(pfx + "w", [128, NCH * E], BF16)
    tok_w = p.dma("pool", lambda e: e.dma_start(
        out=fm3(w_sb[:], E), in_=w_hbm.rearrange("(c p) m -> p c m", p=128)),
        wait=tuple(deps), inc=sw)
    if bias_hbm is not None:
        bias_sb = p.sbuf(pfx + "bias", [E, 1], F32)
        sbm = p.sem(pfx)
        deps = list(deps) + [p.dma("sp", lambda e: e.dma_start(
            out=bias_sb[:], in_=bias_hbm), wait=tuple(deps), inc=sbm)]
    xs = [p.sbuf(pfx + "x", [128, NCH * SLAB], BF16) for _ in range(2)]
    lg = p.sbuf(pfx + "lg", [E, SLAB], F32)
    tm = p.sbuf(pfx + "tm", [128, 4 * E], F32)
    gt = p.sbuf(pfx + "gt", [128, 4 * E], F32)
    sc1 = p.sbuf(pfx + "s1", [128, 4 * E], F32)
    sc2 = p.sbuf(pfx + "s2", [128, 4 * E], F32)
    rowm = p.sbuf(pfx + "rm", [128, 8], F32)
    go = [p.sbuf(pfx + "go", [E, SLAB], BF16) for _ in range(2)]
    ps_l = p.psum(pfx + "pl", [128, 512], F32)
    ps_f = p.psum(pfx + "pf", [128, 512], F32)
    ps_b = p.psum(pfx + "pb", [128, 512], F32)
    xs_free = [None, None]
    go_free = [None, None]
    tok_lgread = None
    tok_tm = None
    tok_go = None
    tok_st = None
    for s in range(NSLAB):
        b = s % 2
        t0 = s * SLAB
        w0 = tuple(deps) if s < 2 else ()
        if xs_free[b] is not None:
            w0 = w0 + (xs_free[b],)
        tok_x = p.dma("sp", lambda e, b=b, t0=t0: e.dma_start(
            out=fm3(xs[b][:], SLAB), in_=hbm_slabT(xn, t0, SLAB)), wait=w0, inc=sx[b])
        wmm = (tok_x,) + ((tok_w,) if s == 0 else ())
        if tok_lgread is not None:
            wmm = wmm + (tok_lgread,)
        tmm = None
        for kc in range(NCH):
            tmm = p.op("pe", lambda e, kc=kc, b=b: e.matmul(
                ps_l[0:E, :], w_sb[:, kc * E:(kc + 1) * E],
                xs[b][:, kc * SLAB:(kc + 1) * SLAB],
                start=(kc == 0), stop=(kc == NCH - 1)),
                wait=wmm if kc == 0 else (), inc=spe if kc == NCH - 1 else None)
        xs_free[b] = tmm
        if bias_sb is None:
            tok_lg = p.op("dve", lambda e: e.tensor_copy(out=lg[:], in_=ps_l[0:E, :]),
                          wait=(tmm,), inc=sdve)
        else:
            tok_lg = p.op("dve", lambda e: e.tensor_scalar_add(
                lg[:], ps_l[0:E, :], bias_sb[0:E, 0:1]), wait=(tmm,), inc=sdve)
        tok_lgread = tok_lg
        wtr = (tok_lg,)
        if tok_tm is not None:
            wtr = wtr + (tok_tm,)
        ttr = None
        for j in range(4):
            ttr = p.op("pe", lambda e, j=j: e.transpose(
                ps_f[:, j * E:(j + 1) * E], lg[0:E, j * CH:(j + 1) * CH],
                c["identf"][0:E, 0:E]), wait=wtr if j == 0 else (),
                inc=spe if j == 3 else None)
        tok_tm = p.op("dve", lambda e: e.tensor_copy(out=tm[:], in_=ps_f[:, 0:4 * E]),
                      wait=(ttr,), inc=sdve)
        E4 = 4 * E
        tm3 = tm[:, 0:E4].rearrange("p (c t) -> p c t", c=4)
        gt3 = gt[:, 0:E4].rearrange("p (c t) -> p c t", c=4)
        s13 = sc1[:, 0:E4].rearrange("p (c t) -> p c t", c=4)
        s23 = sc2[:, 0:E4].rearrange("p (c t) -> p c t", c=4)
        m3 = rowm[:, 0:4].rearrange("p (c t) -> p c t", c=4)
        sm3 = rowm[:, 4:8].rearrange("p (c t) -> p c t", c=4)
        mb = bass.AP(rowm[:].tensor, rowm[:].offset,
                     [list(rowm[:].ap[0]), [1, 4], [0, E]])
        smb = bass.AP(rowm[:].tensor, rowm[:].offset + 4,
                      [list(rowm[:].ap[0]), [1, 4], [0, E]])
        tprev = [tok_tm]

        def step(thunk, eng="dve", extra=()):
            w = ((tprev[0],) if tprev[0] is not None else ()) + tuple(extra)
            tprev[0] = p.op(eng, thunk, wait=w, inc=(sdve if eng == "dve" else sact))
            return tprev[0]

        if mode == "top4x16":
            step(lambda e: e.reduce_max(m3, tm3, axis=AX.X))
            step(lambda e: e.tensor_tensor(s13, tm3, mb, ALU.subtract))
            step(lambda e: e.activation(out=sc1[:, 0:E4], in_=sc1[:, 0:E4],
                                        func=AF.Exp), eng="act")
            step(lambda e: e.reduce_sum(sm3, s13, axis=AX.X))
            step(lambda e: e.reciprocal(out=rowm[:, 4:8], in_=rowm[:, 4:8]))
            step(lambda e: e.tensor_tensor(tm3, s13, smb, ALU.mult))
            step(lambda e: e.tensor_copy(out=sc1[:, 0:E4], in_=tm[:, 0:E4]))
            for it in range(TOPK - 1):
                step(lambda e: e.reduce_max(m3, s13, axis=AX.X))
                step(lambda e: e.tensor_tensor(s23, s13, mb, ALU.is_ge))
                step(lambda e: e.scalar_tensor_tensor(
                    sc1[:, 0:E4], sc2[:, 0:E4], NEG_BIG, sc1[:, 0:E4],
                    ALU.mult, ALU.add))
            step(lambda e: e.reduce_max(m3, s13, axis=AX.X))
            step(lambda e: e.tensor_tensor(s23, tm3, mb, ALU.is_ge))
            tok_gt = step(lambda e: e.tensor_tensor(gt3, tm3, s23, ALU.mult))
        else:
            step(lambda e: e.reduce_max(m3, tm3, axis=AX.X))
            step(lambda e: e.tensor_tensor(s13, tm3, mb, ALU.subtract))
            step(lambda e: e.tensor_tensor(s23, tm3, mb, ALU.is_ge))
            step(lambda e: e.scalar_tensor_tensor(
                sc2[:, 0:E4], sc2[:, 0:E4], NEG_BIG, tm[:, 0:E4],
                ALU.mult, ALU.add))
            step(lambda e: e.activation(out=sc1[:, 0:E4], in_=sc1[:, 0:E4],
                                        func=AF.Exp), eng="act")
            step(lambda e: e.reduce_max(m3, s23, axis=AX.X))
            step(lambda e: e.tensor_tensor(s23, tm3, mb, ALU.is_ge))
            step(lambda e: e.tensor_tensor(s13, s13, s23, ALU.mult))
            step(lambda e: e.reduce_sum(sm3, s13, axis=AX.X))
            step(lambda e: e.reciprocal(out=rowm[:, 4:8], in_=rowm[:, 4:8]))
            tok_gt = step(lambda e: e.tensor_tensor(gt3, s13, smb, ALU.mult))
        wbk = (tok_gt,)
        if tok_go is not None:
            wbk = wbk + (tok_go,)
        tbk = None
        for j in range(4):
            tbk = p.op("pe", lambda e, j=j: e.transpose(
                ps_b[0:E, j * CH:(j + 1) * CH], gt[:, j * E:(j + 1) * E],
                c["identf"][:]), wait=wbk if j == 0 else (),
                inc=spe if j == 3 else None)
        wgo = (tbk,)
        if go_free[b] is not None:
            wgo = wgo + (go_free[b],)
        tok_go = p.op("dve", lambda e, b=b: e.tensor_copy(
            out=go[b][:], in_=ps_b[0:E, 0:SLAB]), wait=wgo, inc=sdve)
        tok_st = p.dma("sp", lambda e, b=b, t0=t0: e.dma_start(
            out=gd[0:E, t0:t0 + SLAB], in_=go[b][:]), wait=(tok_go,), inc=sst[b])
        go_free[b] = tok_st
    return [tok_st]



def emit_moe0(p, c, xn, gfm, x_in, x_out, up_hbm, dn_hbm, deps, pfx="m0"):
    NE = 18
    SEG = 2048
    NSEG = T // SEG
    SL = SEG // SLAB
    spe, sact, sdve = p.sem(pfx), p.sem(pfx), p.sem(pfx)
    swu = [p.sem(pfx) for _ in range(2)]
    swd = [p.sem(pfx) for _ in range(2)]
    sxh, sg = p.sem(pfx), p.sem(pfx)
    sxi = [p.sem(pfx) for _ in range(2)]
    sst = [p.sem(pfx) for _ in range(2)]
    acc = p.sbuf(pfx + "acc", [128, NCH * SEG], F32)
    xn_h = p.sbuf(pfx + "xn", [128, NCH * SEG], BF16)
    g_h = p.sbuf(pfx + "g", [16, SEG], BF16)
    wu = [p.sbuf(pfx + "wu", [128, NCH * 512], BF16) for _ in range(2)]
    wd = [p.sbuf(pfx + "wd", [128, 4 * D], BF16) for _ in range(2)]
    act_b = [p.sbuf(pfx + "act", [128, 4 * SLAB], BF16) for _ in range(2)]
    gl = p.sbuf(pfx + "gl", [128, SLAB], F32)
    xi_b = [p.sbuf(pfx + "xi", [128, NCH * SLAB], F32) for _ in range(2)]
    ob = xi_b
    ps_u = [p.psum(pfx + "pu", [128, SLAB], F32) for _ in range(3)]
    ps_d = [p.psum(pfx + "pd", [128, SLAB], F32) for _ in range(3)]
    ps_bc = p.psum(pfx + "pbc", [128, SLAB], F32)
    wu_free = [None, None]
    wd_free = [None, None]
    units = [(seg, e) for seg in range(NSEG) for e in range(NE)]
    tok_w = {}

    def load_unit(i, first=False):
        seg, e = units[i]
        par = i % 2
        w = tuple(deps) if i < 2 else ()
        if wu_free[par] is not None:
            w = w + (wu_free[par],)
        tok_wu = p.dma("pool", lambda e_, e=e: e_.dma_start(
            out=fm3(wu[par][:], 512),
            in_=up_hbm[e].rearrange("(c p) m -> p c m", p=128)), wait=w, inc=swu[par])
        w = ()
        if wd_free[par] is not None:
            w = (wd_free[par],)
        tok_wd = p.dma("pool", lambda e_, e=e: e_.dma_start(
            out=fm3(wd[par][:], D, nch=4),
            in_=dn_hbm[e].rearrange("(c p) m -> p c m", p=128)), wait=w, inc=swd[par])
        tok_w[i] = (tok_wu, tok_wd)

    load_unit(0, first=True)
    load_unit(1)
    tok_st = None
    ps_u_free = [None, None, None]
    ps_d_free = [None, None, None]
    bc_free = None
    act_free = [None, None]
    tok_acc = None
    for seg in range(NSEG):
        h0 = seg * SEG
        tok_xh = p.dma("sp", lambda e, h0=h0: e.dma_start(
            out=fm3(xn_h[:], SEG), in_=hbm_slabT(xn, h0, SEG)),
            wait=tuple(deps) + ((tok_acc,) if tok_acc is not None else ()), inc=sxh)
        tok_g = p.dma("sp", lambda e, h0=h0: e.dma_start(
            out=g_h[:], in_=gfm[0:16, h0:h0 + SEG]),
            wait=tuple(deps) if seg == 0 else (), inc=sg)
        tok_accfree = tok_acc
        pend = None

        def emit_down(e, sl, ab, tok_act, tok_wd, first_for_acc):
            nonlocal tok_acc
            t0 = sl * SLAB
            par_ = None
            tmm = None
            for mc in range(NCH):
                pd = mc % 3
                wmm = (tok_act,)
                if mc == 0 and sl == 0:
                    wmm = wmm + (tok_wd,)
                if ps_d_free[pd] is not None:
                    wmm = wmm + (ps_d_free[pd],)
                tmm = None
                for kc in range(4):
                    lhs = wd[e % 2 if False else _wpar[0]][:, kc * D + mc * 128: kc * D + (mc + 1) * 128]
                    rhs = act_b[ab][:, kc * SLAB:(kc + 1) * SLAB]
                    tmm = p.op("pe", lambda e_, l=lhs, r=rhs, pd=pd, kc=kc: e_.matmul(
                        ps_d[pd][:], l, r, start=(kc == 0), stop=(kc == 3)),
                        wait=wmm if kc == 0 else (), inc=spe if kc == 3 else None)
                if first_for_acc:
                    wacc = (tmm,)
                    if tok_accfree is not None:
                        wacc = wacc + (tok_accfree,)
                    tok_acc = p.op("dve", lambda e_, mc=mc, t0=t0, pd=pd: e_.tensor_scalar(
                        acc[:, mc * SEG + t0: mc * SEG + t0 + SLAB], ps_d[pd][:],
                        1.0, None, ALU.mult), wait=wacc, inc=sdve)
                else:
                    tok_acc = p.op("dve", lambda e_, mc=mc, t0=t0, pd=pd: e_.scalar_tensor_tensor(
                        acc[:, mc * SEG + t0: mc * SEG + t0 + SLAB], ps_d[pd][:], 1.0,
                        acc[:, mc * SEG + t0: mc * SEG + t0 + SLAB], ALU.mult, ALU.add),
                        wait=(tmm,), inc=sdve)
                ps_d_free[pd] = tok_acc
            return tmm

        _wpar = [0]
        for e in range(NE):
            ui = seg * NE + e
            par = ui % 2
            routed = e >= 2
            tok_wu, tok_wd = tok_w[ui]
            last_up_mm = None
            last_dn_mm = None
            for s in range(SL):
                t0 = s * SLAB
                ab = s % 2
                if routed:
                    wb = (tok_g,)
                    if bc_free is not None:
                        wb = wb + (bc_free,)
                    tok_bc = p.op("pe", lambda e_, e=e, t0=t0: e_.matmul(
                        ps_bc[:], c["oh16"][:, (e - 2) * 128:(e - 1) * 128],
                        g_h[0:16, t0:t0 + SLAB], start=True, stop=True), wait=wb,
                        inc=spe)
                tok_act = None
                for mc in range(4):
                    pu = mc % 3
                    wmm = ()
                    if mc == 0 and s == 0:
                        wmm = (tok_wu, tok_xh)
                    if ps_u_free[pu] is not None:
                        wmm = wmm + (ps_u_free[pu],)
                    if mc == 0 and act_free[ab] is not None:
                        wmm = wmm + (act_free[ab],)
                    tmm = None
                    for kc in range(NCH):
                        lhs = wu[par][:, kc * 512 + mc * 128: kc * 512 + (mc + 1) * 128]
                        rhs = xn_h[:, kc * SEG + t0: kc * SEG + t0 + SLAB]
                        tmm = p.op("pe", lambda e_, l=lhs, r=rhs, pu=pu, kc=kc: e_.matmul(
                            ps_u[pu][:], l, r, start=(kc == 0), stop=(kc == NCH - 1)),
                            wait=wmm if kc == 0 else (), inc=spe if kc == NCH - 1 else None)
                    last_up_mm = tmm
                    if routed:
                        tge = p.op("act", lambda e_, pu=pu: e_.activation(
                            out=gl[:], in_=ps_u[pu][:], func=AF.Gelu), wait=(tmm,), inc=sact)
                        ps_u_free[pu] = tge
                        tok_act = p.op("dve", lambda e_, mc=mc, ab=ab: e_.tensor_mul(
                            act_b[ab][:, mc * SLAB:(mc + 1) * SLAB], gl[:], ps_bc[:]),
                            wait=(tge, tok_bc), inc=sdve)
                    else:
                        tok_act = p.op("act", lambda e_, pu=pu, mc=mc, ab=ab: e_.activation(
                            out=act_b[ab][:, mc * SLAB:(mc + 1) * SLAB], in_=ps_u[pu][:],
                            func=AF.Gelu), wait=(tmm,), inc=sact)
                        ps_u_free[pu] = tok_act
                if routed:
                    bc_free = tok_act
                if pend is not None:
                    pe_, ps_, pab, pact, pwd, pfirst = pend
                    _wpar[0] = (seg * NE + pe_) % 2
                    last_dn_mm = emit_down(pe_, ps_, pab, pact, pwd, pfirst)
                    act_free[pab] = last_dn_mm
                    if ps_ == SL - 1:
                        wd_free[(seg * NE + pe_) % 2] = last_dn_mm
                        if seg * NE + pe_ + 2 < len(units):
                            load_unit(seg * NE + pe_ + 2)
                pend = (e, s, ab, tok_act, tok_wd, e == 0)
            wu_free[par] = last_up_mm
        if pend is not None:
            pe_, ps_, pab, pact, pwd, pfirst = pend
            _wpar[0] = (seg * NE + pe_) % 2
            last_dn_mm = emit_down(pe_, ps_, pab, pact, pwd, pfirst)
            act_free[pab] = last_dn_mm
            if ps_ == SL - 1:
                wd_free[(seg * NE + pe_) % 2] = last_dn_mm
                if seg * NE + pe_ + 2 < len(units):
                    load_unit(seg * NE + pe_ + 2)
            pend = None
        last = None
        for s in range(SL):
            t0g = h0 + s * SLAB
            t0 = s * SLAB
            b = s % 2
            w0 = (tok_acc,)
            if seg == 0 and s < 2:
                w0 = w0 + tuple(deps)
            if sst[b].v > 0:
                w0 = w0 + ((sst[b], sst[b].v),)
            tok_xi = p.dma("sp", lambda e, b=b, t0g=t0g: e.dma_start(
                out=fm3(xi_b[b][:], SLAB), in_=hbm_slabT(x_in, t0g, SLAB)),
                wait=w0, inc=sxi[b])
            wadd = (tok_xi, tok_acc)
            tok_add = p.op("dve", lambda e, b=b, t0=t0: e.tensor_add(
                xi_b[b][:].rearrange("p (c t) -> p c t", c=NCH),
                xi_b[b][:].rearrange("p (c t) -> p c t", c=NCH),
                acc[:].rearrange("p (c t) -> p c t", c=NCH)[:, :, t0:t0 + SLAB]),
                wait=wadd, inc=sdve)
            last = p.dma("sp", lambda e, b=b, t0g=t0g: e.dma_start(
                out=hbm_slabT(x_out, t0g, SLAB), in_=fm3(ob[b][:], SLAB)),
                wait=(tok_add,), inc=sst[b])
        tok_st = last
        tok_acc = tok_st
    return [tok_st]



def emit_postattn1(p, c, attn, xn2, x_in, x_out, wg1, wg2, wout,
                   bg1h, bg2h, gmh, bth, deps, pfx="pa"):
    spe, sact, sdve = p.sem(pfx), p.sem(pfx), p.sem(pfx)
    swp, sws = p.sem(pfx), p.sem(pfx)
    sa = [p.sem(pfx) for _ in range(2)]
    sx = [p.sem(pfx) for _ in range(2)]
    sxi = [p.sem(pfx) for _ in range(2)]
    sst = [p.sem(pfx) for _ in range(2)]
    wg1_sb = p.sbuf(pfx + "wg1", [128, NCH * 64], BF16)
    wg2_sb = p.sbuf(pfx + "wg2", [64, D], BF16)
    wo_sb = p.sbuf(pfx + "wo", [128, NCH * D], BF16)
    bg1_sb = p.sbuf(pfx + "bg1", [64, 1], F32)
    bg2_sb = p.sbuf(pfx + "bg2", [128, 8], F32)
    gm_sb = p.sbuf(pfx + "gm", [128, 8], F32)
    bt_sb = p.sbuf(pfx + "bt", [128, 8], F32)
    p.dma("pool", lambda e: e.dma_start(
        out=fm3(wg1_sb[:], 64), in_=wg1.rearrange("(c p) m -> p c m", p=128)),
        wait=tuple(deps), inc=swp)
    p.dma("pool", lambda e: e.dma_start(out=wg2_sb[:], in_=wg2), inc=swp)
    tok_wp = p.dma("pool", lambda e: e.dma_start(
        out=fm3(wo_sb[:], D), in_=wout.rearrange("(c p) m -> p c m", p=128)), inc=swp)
    p.dma("sp", lambda e: e.dma_start(out=bg1_sb[:], in_=bg1h),
          wait=tuple(deps), inc=sws)
    p.dma("sp", lambda e: e.dma_start(out=bg2_sb[:], in_=bg2h), inc=sws)
    p.dma("sp", lambda e: e.dma_start(out=gm_sb[:], in_=gmh), inc=sws)
    tok_ws = p.dma("sp", lambda e: e.dma_start(out=bt_sb[:], in_=bth), inc=sws)
    at_b = [p.sbuf(pfx + "at", [128, NCH * SLAB], BF16) for _ in range(2)]
    xn_b = [p.sbuf(pfx + "xn", [128, NCH * SLAB], BF16) for _ in range(2)]
    xi_b = [p.sbuf(pfx + "xi", [128, NCH * SLAB], F32) for _ in range(2)]
    g1_sb = p.sbuf(pfx + "g1", [64, SLAB], BF16)
    gate_sb = p.sbuf(pfx + "gate", [128, NCH * SLAB], BF16)
    ag = p.sbuf(pfx + "ag", [128, NCH * SLAB], F32)
    sq = p.sbuf(pfx + "sq", [128, NCH * SLAB], F32)
    ln_sb = p.sbuf(pfx + "ln", [128, NCH * SLAB], BF16)
    rows = p.sbuf(pfx + "r", [1, 6 * SLAB], F32)
    ps1 = p.psum(pfx + "p1", [128, SLAB], F32)
    ps2 = [p.psum(pfx + "p2", [128, SLAB], F32) for _ in range(2)]
    ps_mean = p.psum(pfx + "pm", [1, SLAB], F32)
    ps_sq = p.psum(pfx + "ps", [1, SLAB], F32)
    ps_A = p.psum(pfx + "pa", [128, SLAB], F32)
    ps_B = p.psum(pfx + "pb", [128, SLAB], F32)
    in_free = [None, None]
    st_free = [None, None]
    xn_free = [None, None]
    ps2_free = [None, None]
    tok_g1rd = None
    tok_gate_rd = None
    tok_stats = None
    tok_x = None
    tok_st = None
    for s in range(NSLAB):
        b = s % 2
        t0 = s * SLAB
        w0 = tuple(deps) if s < 2 else ()
        if in_free[b] is not None:
            w0 = w0 + (in_free[b],)
        tok_at = p.dma("sp", lambda e, b=b, t0=t0: e.dma_start(
            out=fm3(at_b[b][:], SLAB), in_=hbm_slabT(attn, t0, SLAB)), wait=w0, inc=sa[b])
        tok_xn = p.dma("sp", lambda e, b=b, t0=t0: e.dma_start(
            out=fm3(xn_b[b][:], SLAB), in_=hbm_slabT(xn2, t0, SLAB)),
            wait=w0 + ((xn_free[b],) if xn_free[b] is not None else ()), inc=sx[b])
        tok_xi = p.dma("sp", lambda e, b=b, t0=t0: e.dma_start(
            out=fm3(xi_b[b][:], SLAB), in_=hbm_slabT(x_in, t0, SLAB)),
            wait=w0 + ((st_free[b],) if st_free[b] is not None else ()), inc=sxi[b])
        wmm = (tok_xn,)
        if s == 0:
            wmm = wmm + (tok_wp, tok_ws)
        if tok_g1rd is not None:
            wmm = wmm + (tok_g1rd,)
        tmm = None
        for kc in range(NCH):
            tmm = p.op("pe", lambda e, kc=kc, b=b: e.matmul(
                ps1[0:64, :], wg1_sb[:, kc * 64:(kc + 1) * 64],
                xn_b[b][:, kc * SLAB:(kc + 1) * SLAB],
                start=(kc == 0), stop=(kc == NCH - 1)),
                wait=wmm if kc == 0 else (), inc=spe if kc == NCH - 1 else None)
        tok_g1 = p.op("dve", lambda e: e.tensor_scalar_add(
            g1_sb[:], ps1[0:64, :], bg1_sb[:, 0:1]), wait=(tmm,), inc=sdve)
        tok_g1rd = tok_g1
        xn_free[b] = tmm
        last_gate = None
        for mc in range(NCH):
            pp = mc % 2
            wg = (tok_g1,)
            if ps2_free[pp] is not None:
                wg = wg + (ps2_free[pp],)
            if mc == 0 and tok_gate_rd is not None:
                wg = wg + (tok_gate_rd,)
            tg = p.op("pe", lambda e, mc=mc, pp=pp: e.matmul(
                ps2[pp][:], wg2_sb[:, mc * 128:(mc + 1) * 128], g1_sb[:],
                start=True, stop=True), wait=wg, inc=spe)
            last_gate = p.op("act", lambda e, mc=mc, pp=pp: e.activation(
                out=gate_sb[:, mc * SLAB:(mc + 1) * SLAB], in_=ps2[pp][:],
                func=AF.Sigmoid, bias=bg2_sb[:, mc:mc + 1]), wait=(tg,), inc=sact)
            ps2_free[pp] = last_gate
        tok_ag = p.op("dve", lambda e, b=b: e.tensor_mul(
            ag[:], at_b[b][:], gate_sb[:]), wait=(last_gate, tok_at), inc=sdve)
        tok_gate_rd = tok_ag
        tok_sq = p.op("act", lambda e: e.activation(
            out=sq[:], in_=ag[:], func=AF.Square), wait=(tok_ag,), inc=sact)
        toks = {"pe": spe, "act": sact, "dve": sdve,
                "sq_chunk": lambda cc: sq[:, cc * SLAB:(cc + 1) * SLAB]}
        tok_rows, tok_bc, tok_stats = emit_ln_rows(
            p, toks, pfx, c,
            lambda cc: ag[:, cc * SLAB:(cc + 1) * SLAB],
            NCH, SLAB, ps_mean, ps_sq, ps_A, ps_B, rows,
            tok_sq, tok_ag, waitAB=(tok_x,) if tok_x is not None else (),
            wait_stats=(tok_stats,) if tok_stats is not None else ())
        wx = (tok_bc,)
        ag3 = fm3(ag[:], SLAB)
        sq3 = fm3(sq[:], SLAB)
        tmul = p.op("dve", lambda e, ag3=ag3, sq3=sq3: e.tensor_tensor(
            sq3, ag3, bcast_mid(ps_A[:], NCH), ALU.mult), wait=wx, inc=sdve)
        tadd = p.op("dve", lambda e, sq3=sq3: e.tensor_tensor(
            sq3, sq3, bcast_mid(ps_B[:], NCH), ALU.add), wait=(tmul,), inc=sdve)
        for cc in range(NCH):
            sq_c = sq[:, cc * SLAB:(cc + 1) * SLAB]
            ln_c = ln_sb[:, cc * SLAB:(cc + 1) * SLAB]
            tok_x = p.op("dve", lambda e, a=sq_c, o=ln_c, cc=cc: e.tensor_scalar(
                o, a, gm_sb[:, cc:cc + 1], bt_sb[:, cc:cc + 1], ALU.mult, ALU.add),
                wait=(tadd,) if cc == 0 else (), inc=sdve if cc == NCH - 1 else None)
        last_ev = None
        for mc in range(NCH):
            pp = mc % 2
            wmm = (tok_x,) if mc == 0 else ()
            if ps2_free[pp] is not None:
                wmm = wmm + (ps2_free[pp],)
            tmm = None
            for kc in range(NCH):
                lhs = wo_sb[:, kc * D + mc * 128: kc * D + (mc + 1) * 128]
                rhs = ln_sb[:, kc * SLAB:(kc + 1) * SLAB]
                tmm = p.op("pe", lambda e, l=lhs, r=rhs, pp=pp, kc=kc: e.matmul(
                    ps2[pp][:], l, r, start=(kc == 0), stop=(kc == NCH - 1)),
                    wait=wmm if kc == 0 else (), inc=spe if kc == NCH - 1 else None)
            wev = (tmm, tok_xi)
            if mc == 0 and st_free[b] is not None:
                wev = wev + (st_free[b],)
            last_ev = p.op("dve", lambda e, b=b, mc=mc, pp=pp: e.tensor_add(
                xi_b[b][:, mc * SLAB:(mc + 1) * SLAB], ps2[pp][:],
                xi_b[b][:, mc * SLAB:(mc + 1) * SLAB]), wait=wev, inc=sdve)
            ps2_free[pp] = last_ev
        in_free[b] = last_ev
        tok_st = p.dma("sp", lambda e, b=b, t0=t0: e.dma_start(
            out=hbm_slabT(x_out, t0, SLAB), in_=fm3(xi_b[b][:], SLAB)),
            wait=(last_ev,), inc=sst[b])
        st_free[b] = tok_st
    return [tok_st]



def emit_moe1(p, c, xn2, wfm, x_in, y_out, w0h, w1h, w2h, b0h, b1h, b2h,
              deps, pfx="m1"):
    NE = 8
    HB = 2
    SEG = 1024
    NSEG = T // SEG
    SL = SEG // SLAB
    spe, sact, sdve, spool = p.sem(pfx), p.sem(pfx), p.sem(pfx), p.sem(pfx)
    sw0, sw1, sw2 = p.sem(pfx), p.sem(pfx), p.sem(pfx)
    swf, sb = p.sem(pfx), p.sem(pfx)
    sx = [p.sem(pfx) for _ in range(2)]
    sxi = [p.sem(pfx) for _ in range(2)]
    sst = [p.sem(pfx) for _ in range(2)]
    acc = p.sbuf(pfx + "acc", [128, NCH * SEG], F32)
    xn_b = [p.sbuf(pfx + "xn", [128, NCH * SLAB], BF16) for _ in range(2)]
    wf_h = p.sbuf(pfx + "wf", [8, SEG], BF16)
    w0_sb = p.sbuf(pfx + "w0", [128, NCH * 2048], BF16)
    w1_sb = p.sbuf(pfx + "w1", [128, NCH * 2048], BF16)
    w2_sb = p.sbuf(pfx + "w2", [128, 16 * D], BF16)
    b0_sb = p.sbuf(pfx + "b0", [128, 128], F32)
    b1_sb = p.sbuf(pfx + "b1", [128, 128], F32)
    b2_sb = p.sbuf(pfx + "b2", [8, D], BF16)
    act_sb = [p.sbuf(pfx + "act", [128, 16 * SLAB], BF16)]
    silu_sb = p.sbuf(pfx + "sl", [128, 2 * SLAB], F32)
    t_sb = p.sbuf(pfx + "t", [128, 2 * SLAB], F32)
    wbc_sb = p.sbuf(pfx + "wbc", [128, SEG], BF16)
    xi_b = [p.sbuf(pfx + "xi", [128, NCH * SLAB], F32)]
    ps_h = [p.psum(pfx + "ph", [128, SLAB], F32) for _ in range(4)]
    ps_o = [p.psum(pfx + "po", [128, SLAB], F32) for _ in range(4)]
    p.dma("sp", lambda e: e.dma_start(out=b0_sb[:], in_=b0h),
          wait=tuple(deps), inc=sb)
    p.dma("sp", lambda e: e.dma_start(out=b1_sb[:], in_=b1h), inc=sb)
    tok_bs = (sb, sb.v)
    sb2 = p.sem(pfx)
    tok_b2 = p.dma("pool", lambda e: e.dma_start(out=b2_sb[:], in_=b2h),
                   wait=tuple(deps), inc=sb2)

    w01_free = None
    w2_free = None
    tok_w01 = {}
    tok_w2 = {}
    units = [(seg, e) for seg in range(NSEG) for e in range(NE)]

    def load_w01(i, first=False):
        seg, e = units[i]
        w = tuple(deps) if i == 0 else ()
        if w01_free is not None:
            w = w + (w01_free,)
        ta_ = p.dma("pool", lambda e_, e=e: e_.dma_start(
            out=fm3(w0_sb[:], 2048),
            in_=w0h[e].rearrange("(c p) m -> p c m", p=128)), wait=w, inc=sw0)
        tb_ = p.dma("pool", lambda e_, e=e: e_.dma_start(
            out=fm3(w1_sb[:], 2048),
            in_=w1h[e].rearrange("(c p) m -> p c m", p=128)), inc=sw1)
        tok_w01[i] = (ta_, tb_)

    def load_w2(i):
        seg, e = units[i]
        w = tuple(deps) if i == 0 else ()
        if w2_free is not None:
            w = w + (w2_free,)
        tok_w2[i] = p.dma("pool", lambda e_, e=e: e_.dma_start(
            out=fm3(w2_sb[:], D, nch=16),
            in_=w2h[e].rearrange("(c p) m -> p c m", p=128)), wait=w, inc=sw2)

    load_w01(0, first=True)
    load_w2(0)
    ps_h_free = [None] * 4
    ps_o_free = [None] * 4
    wbc_w_free = None
    wbc_r_free = None
    act_free = [None]
    xn_free = [None, None]
    tok_acc = None
    tok_seg_done = None
    tok_st = None
    for seg in range(NSEG):
        h0 = seg * SEG
        wfw = () if tok_seg_done is None else (tok_seg_done,)
        tok_wf = p.dma("sp", lambda e, h0=h0: e.dma_start(
            out=wf_h[:], in_=wfm[0:8, h0:h0 + SEG]),
            wait=wfw + (tuple(deps) if seg == 0 else ()), inc=swf)
        tok_accfree = tok_acc
        for e in range(NE):
            ui = seg * NE + e
            tok_wbc = None
            for sl2 in range(SL):
                wb = (tok_wf,)
                if ps_h_free[0] is not None:
                    wb = wb + (ps_h_free[0],)
                if wbc_r_free is not None:
                    wb = wb + (wbc_r_free,)
                tbc = p.op("pe", lambda e_, e=e, sl2=sl2: e_.matmul(
                    ps_h[0][:], c["oh8"][:, e * 128:(e + 1) * 128],
                    wf_h[0:8, sl2 * SLAB:(sl2 + 1) * SLAB], start=True, stop=True),
                    wait=wb, inc=spe)
                tok_wbc = p.op("dve", lambda e_, sl2=sl2: e_.tensor_copy(
                    out=wbc_sb[:, sl2 * SLAB:(sl2 + 1) * SLAB], in_=ps_h[0][:]),
                    wait=(tbc,), inc=sdve)
                ps_h_free[0] = tok_wbc
            for s in range(SL):
                t0 = s * SLAB
                xb = s % 2
                ab = 0
                wxn = tuple(deps) if (seg == 0 and e == 0 and s < 2) else ()
                if xn_free[xb] is not None:
                    wxn = wxn + (xn_free[xb],)
                tok_xn = p.dma("sp", lambda e_, xb=xb, t0=t0, h0=h0: e_.dma_start(
                    out=fm3(xn_b[xb][:], SLAB), in_=hbm_slabT(xn2, h0 + t0, SLAB)),
                    wait=wxn, inc=sx[xb])
                tok_act = None
                prev_tt = [None, None]
                last_up = None
                for hb in range(HB):
                    for hc in range(8):
                        hq = hb * 8 + hc
                        pq = (hq % 2) * 2
                        wmm = ()
                        if hq == 0 and s == 0:
                            wmm = tok_w01[ui] + (tok_bs,)
                        if hq == 0:
                            wmm = wmm + (tok_xn, tok_wbc)
                            if act_free[ab] is not None:
                                wmm = wmm + (act_free[ab],)
                        t0mm = None
                        t1mm = None
                        for kc in range(NCH):
                            l0 = w0_sb[:, kc * 2048 + hq * 128: kc * 2048 + (hq + 1) * 128]
                            l1 = w1_sb[:, kc * 2048 + hq * 128: kc * 2048 + (hq + 1) * 128]
                            rhs = xn_b[xb][:, kc * SLAB:(kc + 1) * SLAB]
                            wk = wmm if kc == 0 else ()
                            if kc == 0 and ps_h_free[pq] is not None:
                                wk = wk + (ps_h_free[pq],)
                            t0mm = p.op("pe", lambda e_, l=l0, r=rhs, kc=kc, pq=pq: e_.matmul(
                                ps_h[pq][:], l, r, start=(kc == 0), stop=(kc == NCH - 1)),
                                wait=wk, inc=None)
                            wk2 = ()
                            if kc == 0 and ps_h_free[pq + 1] is not None:
                                wk2 = (ps_h_free[pq + 1],)
                            t1mm = p.op("pe", lambda e_, l=l1, r=rhs, kc=kc, pq=pq: e_.matmul(
                                ps_h[pq + 1][:], l, r, start=(kc == 0), stop=(kc == NCH - 1)),
                                wait=wk2, inc=spe if kc == NCH - 1 else None)
                        last_up = t1mm
                        bcol = e * 16 + hq
                        sq2 = hq % 2
                        wsl = (t1mm,)
                        if prev_tt[sq2] is not None:
                            wsl = wsl + (prev_tt[sq2],)
                        ts = p.op("act", lambda e_, bcol=bcol, pq=pq, sq2=sq2: e_.activation(
                            out=silu_sb[:, sq2 * SLAB:(sq2 + 1) * SLAB], in_=ps_h[pq + 1][:],
                            func=AF.Silu, bias=b1_sb[:, bcol:bcol + 1]), wait=wsl, inc=sact)
                        ps_h_free[pq + 1] = ts
                        tt = p.op("dve", lambda e_, bcol=bcol, pq=pq, sq2=sq2: e_.scalar_tensor_tensor(
                            t_sb[:, sq2 * SLAB:(sq2 + 1) * SLAB], ps_h[pq][:],
                            b0_sb[:, bcol:bcol + 1],
                            silu_sb[:, sq2 * SLAB:(sq2 + 1) * SLAB],
                            ALU.add, ALU.mult), wait=(ts,), inc=sdve)
                        prev_tt[sq2] = tt
                        ps_h_free[pq] = tt
                        tok_act = p.op("pool", lambda e_, hq=hq, ab=ab, sq2=sq2, t0=t0: e_.tensor_mul(
                            act_sb[ab][:, hq * SLAB:(hq + 1) * SLAB],
                            t_sb[:, sq2 * SLAB:(sq2 + 1) * SLAB],
                            wbc_sb[:, t0:t0 + SLAB]), wait=(tt,), inc=spool)
                wbc_r_free = tok_act
                for sub in range(2):
                    ts0 = t0 + sub * 256
                    for mc in range(NCH):
                        po = mc % 4
                        reg = ps_o[po][:, (mc // 4) * 256:(mc // 4) * 256 + 256]
                        wmm = (tok_act,)
                        if mc == 0 and s == 0 and sub == 0:
                            wmm = wmm + (tok_w2[ui],)
                        if ps_o_free[po] is not None:
                            wmm = wmm + (ps_o_free[po],)
                        tmm = None
                        for kq in range(16):
                            lhs = w2_sb[:, kq * D + mc * 128: kq * D + (mc + 1) * 128]
                            rhs = act_sb[ab][:, kq * SLAB + sub * 256: kq * SLAB + sub * 256 + 256]
                            tmm = p.op("pe", lambda e_, l=lhs, r=rhs, reg=reg, kq=kq, e2=e: e_.matmul(
                                reg, l, r, start=(kq == 0),
                                stop=(kq == 15 and e2 != 0)),
                                wait=wmm if kq == 0 else (),
                                inc=spe if (kq == 15 and e != 0) else None)
                        if e == 0:
                            tmm = p.op("pe", lambda e_, mc=mc, ts0=ts0, reg=reg: e_.matmul(
                                reg, b2_sb[0:8, mc * 128:(mc + 1) * 128],
                                wf_h[0:8, ts0:ts0 + 256], start=False, stop=True),
                                wait=(tok_b2,), inc=spe)
                        if mc >= 4:
                            av = acc[:].rearrange("p (g cq t) -> p g cq t", g=2, cq=4)[
                                :, :, mc - 4, t0 + sub * 256: t0 + sub * 256 + 256]
                            if e == 0:
                                wacc = (tmm,)
                                if tok_accfree is not None:
                                    wacc = wacc + (tok_accfree,)
                                tok_acc = p.op("dve", lambda e_, po=po, av=av: e_.tensor_scalar(
                                    av, ps_o[po][:].rearrange("p (g t) -> p g t", g=2),
                                    1.0, None, ALU.mult), wait=wacc, inc=sdve)
                            else:
                                tok_acc = p.op("dve", lambda e_, po=po, av=av: e_.scalar_tensor_tensor(
                                    av, ps_o[po][:].rearrange("p (g t) -> p g t", g=2),
                                    1.0, av, ALU.mult, ALU.add), wait=(tmm,), inc=sdve)
                            ps_o_free[po] = tok_acc
                act_free[ab] = tmm
                xn_free[xb] = last_up
            if ui + 1 < len(units):
                w01_free = last_up
                load_w01(ui + 1)
                w2_free = tmm
                load_w2(ui + 1)
        last = None
        for s in range(SL):
            t0g = h0 + s * SLAB
            t0 = s * SLAB
            b = 0
            w0_ = (tok_acc,)
            if sst[b].v > 0:
                w0_ = w0_ + ((sst[b], sst[b].v),)
            tok_xi = p.dma("sp", lambda e, b=b, t0g=t0g: e.dma_start(
                out=fm3(xi_b[b][:], SLAB), in_=hbm_slabT(x_in, t0g, SLAB)),
                wait=w0_, inc=sxi[b])
            tok_add = p.op("dve", lambda e, b=b, t0=t0: e.tensor_add(
                xi_b[b][:].rearrange("p (c t) -> p c t", c=NCH),
                xi_b[b][:].rearrange("p (c t) -> p c t", c=NCH),
                acc[:].rearrange("p (c t) -> p c t", c=NCH)[:, :, t0:t0 + SLAB]),
                wait=(tok_xi, tok_acc), inc=sdve)
            tok_acc = tok_add
            last = p.dma("sp", lambda e, b=b, t0g=t0g: e.dma_start(
                out=hbm_slabT(y_out, t0g, SLAB), in_=fm3(xi_b[b][:], SLAB)),
                wait=(tok_add,), inc=sst[b])
        tok_st = last
        tok_seg_done = tok_st
    return [tok_st]



def _stage(inputs):
    f32 = lambda a: np.asarray(a, np.float32)
    wT = lambda a: np.ascontiguousarray(f32(a).T)
    pp = lambda v: np.ascontiguousarray(f32(v).reshape(8, 128).T)
    bpp = lambda b: np.ascontiguousarray(
        f32(b).reshape(8, 16, 128).transpose(2, 0, 1).reshape(128, 128))
    st = dict(make_const_arrays())
    st["w_qkv0"] = np.ascontiguousarray(np.concatenate(
        [wT(inputs["L0_attn_Wq"]), wT(inputs["L0_attn_Wk"]),
         wT(inputs["L0_attn_Wv"])], axis=1))
    st["w_o0"] = wT(inputs["L0_attn_Wo"])
    st["rtr0_w"] = wT(inputs["L0_ffn_router_w"])
    st["up0"] = np.ascontiguousarray(np.concatenate(
        [f32(inputs["L0_ffn_shared_up"]).transpose(0, 2, 1),
         f32(inputs["L0_ffn_routed_up"]).transpose(0, 2, 1)]))
    st["dn0"] = np.ascontiguousarray(np.concatenate(
        [f32(inputs["L0_ffn_shared_down"]).transpose(0, 2, 1),
         f32(inputs["L0_ffn_routed_down"]).transpose(0, 2, 1)]))
    st["w_qkv1"] = wT(inputs["L1_attn_w_qkv"])
    st["wg1"] = wT(inputs["L1_attn_w_g1"])
    st["wg2"] = wT(inputs["L1_attn_w_g2"])
    st["wout"] = wT(inputs["L1_attn_w_out"])
    st["bg1"] = np.ascontiguousarray(f32(inputs["L1_attn_b_g1"])[:, None])
    st["bg2"] = pp(inputs["L1_attn_b_g2"])
    st["gm"] = pp(inputs["L1_attn_gamma"])
    st["bt"] = pp(inputs["L1_attn_beta"])
    st["rtr1_w"] = wT(inputs["L1_ffn_gate_w"])
    st["rtr1_b"] = np.ascontiguousarray(f32(inputs["L1_ffn_gate_b"])[:, None])
    st["w0h"] = np.ascontiguousarray(f32(inputs["L1_ffn_w0"]).transpose(0, 2, 1))
    st["w1h"] = np.ascontiguousarray(f32(inputs["L1_ffn_w1"]).transpose(0, 2, 1))
    st["w2h"] = np.ascontiguousarray(f32(inputs["L1_ffn_w2"]).transpose(0, 2, 1))
    st["b0h"] = bpp(inputs["L1_ffn_b0"])
    st["b1h"] = bpp(inputs["L1_ffn_b1"])
    st["b2h"] = np.ascontiguousarray(f32(inputs["L1_ffn_b2"]))
    return st


def _run_op(build, st, per_core, out_names):
    p = Prog()
    build(p)
    nc = p.finalize()
    in_maps = []
    for i in range(8):
        m = {k: st[k] for k in st if k in p.ext_inputs}
        m.update(per_core[i])
        in_maps.append(m)
    res = run_bass_kernel_spmd(nc, in_maps, core_ids=list(range(8)))
    return {nm: [res.results[i][nm] for i in range(8)] for nm in out_names}


FUSED = True


def kernel(**inputs):
    if FUSED:
        return _kernel_fused(inputs)
    return _kernel_unfused(inputs)


def _kernel_unfused(inputs):
    st = _stage(inputs)
    x = np.asarray(inputs["L0_attn_x"], np.float32)
    xfm = [np.ascontiguousarray(x[i].T).reshape(NCH, 128, T) for i in range(8)]

    FMF = [NCH, 128, T]

    def ext(p, name, shape, dt):
        p.ext_inputs.add(name)
        return p.dram(name, shape, dt, kind="ExternalInput")

    def fin(p, toks):
        p.op("sp", lambda e: None, wait=tuple(toks))

    def b_ln(src_is, dst, gamma=None):
        def b(p):
            c = emit_consts(p)
            src = ext(p, "src", FMF, F32)
            d = p.dram(dst, FMF, BF16, kind="ExternalOutput")
            fin(p, emit_ln(p, c, src, d, [c["tok"], c["tok2"]]))
        return b

    def b_qkv(kind, wname):
        def b(p):
            c = emit_consts(p)
            xn = ext(p, "xn", FMF, BF16)
            w = ext(p, wname, [D, 3 * D], F32)
            qd = p.dram("qd", FMF, BF16, kind="ExternalOutput")
            kd = p.dram("kd", FMF, BF16, kind="ExternalOutput")
            vd = p.dram("vd", FMF, BF16, kind="ExternalOutput")
            fin(p, emit_qkv(p, c, xn, qd, kd, vd, w, kind, [c["tok"], c["tok2"]]))
        return b

    def b_attn(denom):
        def b(p):
            c = emit_consts(p)
            qd = ext(p, "qd", FMF, BF16)
            kd = ext(p, "kd", FMF, BF16)
            vd = ext(p, "vd", FMF, BF16)
            ad = p.dram("ad", FMF, BF16, kind="ExternalOutput")
            fin(p, emit_attn(p, c, qd, kd, vd, ad, denom, [c["tok"], c["tok2"]]))
        return b

    def b_wo(p):
        c = emit_consts(p)
        a = ext(p, "a", FMF, BF16)
        xi = ext(p, "xi", FMF, F32)
        w = ext(p, "w_o0", [D, D], F32)
        xo = p.dram("xo", FMF, F32, kind="ExternalOutput")
        fin(p, emit_proj_residual(p, c, a, xi, xo, w, [c["tok"], c["tok2"]]))

    def b_rtr(mode, wname, bias=None):
        E = 16 if mode == "top4x16" else 8
        def b(p):
            c = emit_consts(p)
            xn = ext(p, "xn", FMF, BF16)
            w = ext(p, wname, [D, E], F32)
            gd = p.dram("gd", [E, T], BF16, kind="ExternalOutput")
            deps = [c["tok"], c["tok2"]]
            bias_sb = None
            if bias is not None:
                bh = ext(p, bias, [E, 1], F32)
                bias_sb = p.sbuf("bias", [E, 1], F32)
                sbm = p.sem("bs")
                deps.append(p.dma("sp", lambda e: e.dma_start(out=bias_sb[:], in_=bh), inc=sbm))
            fin(p, emit_router(p, c, xn, gd, w, mode, deps, bias_sb=bias_sb))
        return b

    def b_moe0(p):
        c = emit_consts(p)
        xn = ext(p, "xn", FMF, BF16)
        gfm = ext(p, "gfm", [16, T], BF16)
        xi = ext(p, "xi", FMF, F32)
        up = ext(p, "up0", [18, D, 512], F32)
        dn = ext(p, "dn0", [18, 512, D], F32)
        xo = p.dram("xo", FMF, F32, kind="ExternalOutput")
        fin(p, emit_moe0(p, c, xn, gfm, xi, xo, up, dn, [c["tok"], c["tok2"]]))

    def b_pa1(p):
        c = emit_consts(p)
        at = ext(p, "at", FMF, BF16)
        xn = ext(p, "xn", FMF, BF16)
        xi = ext(p, "xi", FMF, F32)
        wg1 = ext(p, "wg1", [D, 64], F32)
        wg2 = ext(p, "wg2", [64, D], F32)
        wout = ext(p, "wout", [D, D], F32)
        bg1 = ext(p, "bg1", [64, 1], F32)
        bg2 = ext(p, "bg2", [128, 8], F32)
        gm = ext(p, "gm", [128, 8], F32)
        bt = ext(p, "bt", [128, 8], F32)
        xo = p.dram("xo", FMF, F32, kind="ExternalOutput")
        fin(p, emit_postattn1(p, c, at, xn, xi, xo, wg1, wg2, wout,
                              bg1, bg2, gm, bt, [c["tok"], c["tok2"]]))

    def b_moe1(p):
        c = emit_consts(p)
        xn = ext(p, "xn", FMF, BF16)
        wfm = ext(p, "wfm", [8, T], BF16)
        xi = ext(p, "xi", FMF, F32)
        w0 = ext(p, "w0h", [8, D, 2048], F32)
        w1 = ext(p, "w1h", [8, D, 2048], F32)
        w2 = ext(p, "w2h", [8, 2048, D], F32)
        b0 = ext(p, "b0h", [128, 128], F32)
        b1 = ext(p, "b1h", [128, 128], F32)
        b2 = ext(p, "b2h", [8, D], F32)
        yo = p.dram("yo", FMF, F32, kind="ExternalOutput")
        fin(p, emit_moe1(p, c, xn, wfm, xi, yo, w0, w1, w2, b0, b1, b2,
                         [c["tok"], c["tok2"]]))

    pc = lambda key, arrs: [{key: arrs[i]} for i in range(8)]

    def pc2(**kw):
        return [{k: v[i] for k, v in kw.items()} for i in range(8)]

    r = _run_op(b_ln("src", "dst"), st, pc("src", xfm), ["dst"])
    xn0 = r["dst"]
    r = _run_op(b_qkv("elu1", "w_qkv0"), st, pc("xn", xn0), ["qd", "kd", "vd"])
    r = _run_op(b_attn(True), st, pc2(qd=r["qd"], kd=r["kd"], vd=r["vd"]), ["ad"])
    a0 = r["ad"]
    r = _run_op(b_wo, st, pc2(a=a0, xi=xfm), ["xo"])
    x1 = r["xo"]
    r = _run_op(b_rtr("top4x16", "rtr0_w"), st, pc("xn", xn0), ["gd"])
    g0 = r["gd"]
    r = _run_op(b_moe0, st, pc2(xn=xn0, gfm=g0, xi=x1), ["xo"])
    x2 = r["xo"]
    r = _run_op(b_ln("src", "dst"), st, pc("src", x2), ["dst"])
    xn1 = r["dst"]
    r = _run_op(b_qkv("silu", "w_qkv1"), st, pc("xn", xn1), ["qd", "kd", "vd"])
    r = _run_op(b_attn(False), st, pc2(qd=r["qd"], kd=r["kd"], vd=r["vd"]), ["ad"])
    a1 = r["ad"]
    r = _run_op(b_rtr("top2x8", "rtr1_w", bias="rtr1_b"), st, pc("xn", xn1), ["gd"])
    w1g = r["gd"]
    r = _run_op(b_pa1, st, pc2(at=a1, xn=xn1, xi=x2), ["xo"])
    x3 = r["xo"]
    r = _run_op(b_moe1, st, pc2(xn=xn1, wfm=w1g, xi=x3), ["yo"])
    y = r["yo"]
    out = np.empty((8, T, D), np.float32)
    for i in range(8):
        out[i] = np.asarray(y[i], np.float32).reshape(D, T).T
    return out


def _kernel_fused(inputs):
    st = _stage(inputs)
    x = np.asarray(inputs["L0_attn_x"], np.float32)
    xfm = [np.ascontiguousarray(x[i].T).reshape(NCH, 128, T) for i in range(8)]
    FMF = [NCH, 128, T]
    p = Prog()

    def ext(name, shape, dt=F32):
        p.ext_inputs.add(name)
        return p.dram(name, shape, dt, kind="ExternalInput")

    c = emit_consts(p)
    x0 = ext("x0", FMF)
    w_qkv0 = ext("w_qkv0", [D, 3 * D])
    w_o0 = ext("w_o0", [D, D])
    rtr0_w = ext("rtr0_w", [D, 16])
    up0 = ext("up0", [18, D, 512])
    dn0 = ext("dn0", [18, 512, D])
    w_qkv1 = ext("w_qkv1", [D, 3 * D])
    wg1 = ext("wg1", [D, 64])
    wg2 = ext("wg2", [64, D])
    wout = ext("wout", [D, D])
    bg1 = ext("bg1", [64, 1])
    bg2 = ext("bg2", [128, 8])
    gm = ext("gm", [128, 8])
    bt = ext("bt", [128, 8])
    rtr1_w = ext("rtr1_w", [D, 8])
    rtr1_b = ext("rtr1_b", [8, 1])
    w0h = ext("w0h", [8, D, 2048])
    w1h = ext("w1h", [8, D, 2048])
    w2h = ext("w2h", [8, 2048, D])
    b0h = ext("b0h", [128, 128])
    b1h = ext("b1h", [128, 128])
    b2h = ext("b2h", [8, D])
    xn0 = p.dram("s_xn0", FMF, BF16)
    qd = p.dram("s_q", FMF, BF16)
    kd = p.dram("s_k", FMF, BF16)
    vd = p.dram("s_v", FMF, BF16)
    a0 = p.dram("s_a0", FMF, BF16)
    x1 = p.dram("s_x1", FMF, F32)
    g0 = p.dram("s_g0", [16, T], BF16)
    x2 = p.dram("s_x2", FMF, F32)
    xn1 = p.dram("s_xn1", FMF, BF16)
    qd1 = p.dram("s_q1", FMF, BF16)
    kd1 = p.dram("s_k1", FMF, BF16)
    vd1 = p.dram("s_v1", FMF, BF16)
    a1 = p.dram("s_a1", FMF, BF16)
    w1g = p.dram("s_w1g", [8, T], BF16)
    x3 = p.dram("s_x3", FMF, F32)
    y = p.dram("y", FMF, F32, kind="ExternalOutput")

    dep0 = [c["tok"], c["tok2"]]
    with p.scope():
        t_ln0 = emit_ln(p, c, x0, xn0, dep0, pfx="ln0")
    with p.scope():
        t_qkv0 = emit_qkv(p, c, xn0, qd, kd, vd, w_qkv0, "elu1", t_ln0, pfx="q0")
    with p.scope():
        t_at0 = emit_attn(p, c, qd, kd, vd, a0, True, t_qkv0, pfx="a0")
    with p.scope():
        t_wo0 = emit_proj_residual(p, c, a0, x0, x1, w_o0, t_at0, pfx="w0")
    with p.scope():
        t_rt0 = emit_router(p, c, xn0, g0, rtr0_w, "top4x16", t_wo0, pfx="r0")
    with p.scope():
        t_m0 = emit_moe0(p, c, xn0, g0, x1, x2, up0, dn0, t_rt0, pfx="m0")
    with p.scope():
        t_ln1 = emit_ln(p, c, x2, xn1, t_m0, pfx="ln1")
    with p.scope():
        t_qkv1 = emit_qkv(p, c, xn1, qd1, kd1, vd1, w_qkv1, "silu", t_ln1, pfx="q1")
    with p.scope():
        t_at1 = emit_attn(p, c, qd1, kd1, vd1, a1, False, t_qkv1, pfx="a1")
    with p.scope():
        t_rt1 = emit_router(p, c, xn1, w1g, rtr1_w, "top2x8", t_at1,
                            bias_sb=None, bias_hbm=rtr1_b, pfx="r1")
    with p.scope():
        t_pa1 = emit_postattn1(p, c, a1, xn1, x2, x3, wg1, wg2, wout,
                               bg1, bg2, gm, bt, t_rt1, pfx="pa")
    with p.scope():
        t_m1 = emit_moe1(p, c, xn1, w1g, x3, y, w0h, w1h, w2h, b0h, b1h, b2h,
                         t_pa1, pfx="m1")
    p.op("sp", lambda e: None, wait=tuple(t_m1))
    nc = p.finalize()
    in_maps = []
    for i in range(8):
        m = {k: st[k] for k in st if k in p.ext_inputs}
        m["x0"] = xfm[i]
        in_maps.append(m)
    res = run_bass_kernel_spmd(nc, in_maps, core_ids=list(range(8)))
    out = np.empty((8, T, D), np.float32)
    for i in range(8):
        out[i] = np.asarray(res.results[i]["y"], np.float32).reshape(D, T).T
    return out


# version 11
# speedup vs baseline: 1.0468x; 1.0468x over previous
import contextlib
from contextlib import contextmanager
import numpy as np
import concourse.bass as bass
import concourse.mybir as mybir
from concourse.bass_utils import run_bass_kernel_spmd

F32 = mybir.dt.float32
BF16 = mybir.dt.bfloat16
U8 = mybir.dt.uint8
AF = mybir.ActivationFunctionType
ALU = mybir.AluOpType
AX = mybir.AxisListType

T = 4096
D = 1024
NCH = 8
H = 16
HD = 64
CH = 128
NCHUNK = T // CH
SLAB = 512
NSLAB = T // SLAB
HALF = T // 2
EPS_LN = 1e-5
EPS_ATTN = 1e-12
NEG_BIG = -1e30
DBG_SKIP = set()


class Sem:
    __slots__ = ("h", "v")

    def __init__(self, h):
        self.h = h
        self.v = 0

    @property
    def tok(self):
        return (self, self.v)


class Prog:
    ENGINES = ("pe", "act", "dve", "pool", "sp")

    def __init__(self):
        self.nc = bass.Bass("TRN2", target_bir_lowering=False)
        self.stack = contextlib.ExitStack()
        self.sem_stack = contextlib.ExitStack()
        self.sched = {e: [] for e in self.ENGINES}
        self._names = {}
        self.all_sems = []
        self.ext_inputs = set()
        self.sem_free = []
        self._scope_sems = None

    @contextmanager
    def scope(self):
        old = self.stack
        self.stack = contextlib.ExitStack()
        outer_scope = self._scope_sems
        self._scope_sems = []
        try:
            yield
        finally:
            self.stack.close()
            self.stack = old
            self.sem_free.extend(self._scope_sems)
            self._scope_sems = outer_scope

    def name(self, base):
        n = self._names.get(base, 0)
        self._names[base] = n + 1
        return "%s_%d" % (base, n)

    def sem(self, base="s"):
        if self._scope_sems is not None and self.sem_free:
            sm = self.sem_free.pop()
        else:
            sm = Sem(self.sem_stack.enter_context(self.nc.semaphore(self.name(base))))
            self.all_sems.append(sm)
        if self._scope_sems is not None:
            self._scope_sems.append(sm)
        return sm

    def sbuf(self, base, shape, dt):
        return self.stack.enter_context(self.nc.sbuf_tensor(self.name(base), shape, dt))

    def psum(self, base, shape, dt):
        return self.stack.enter_context(self.nc.psum_tensor(self.name(base), shape, dt))

    def dram(self, name, shape, dt, kind=None):
        if kind is None:
            return self.nc.dram_tensor(name, shape, dt).ap()
        return self.nc.dram_tensor(name, shape, dt, kind=kind).ap()

    def _emit(self, eng, thunk, wait, inc, amt):
        tok = None
        if inc is not None:
            inc.v += amt
            tok = (inc, inc.v)
        self.sched[eng].append((tuple(wait), thunk, (inc, amt) if inc is not None else None))
        return tok

    def op(self, eng, thunk, wait=(), inc=None):
        return self._emit(eng, thunk, wait, inc, 1)

    def dma(self, eng, thunk, wait=(), inc=None):
        return self._emit(eng, thunk, wait, inc, 16)

    def finalize(self):
        tail = tuple((sm, sm.v) for sm in self.all_sems if sm.v > 0)
        self.sched["sp"].append((tail, lambda e: None, None))
        sched = self.sched

        def runner(eng_name):
            def go(eng):
                last = {}
                for waits, thunk, inc in sched[eng_name]:
                    for sem, val in waits:
                        if last.get(id(sem), -1) < val:
                            eng.wait_ge(sem.h, val)
                            last[id(sem)] = val
                    inst = thunk(eng)
                    if inc is not None:
                        sem, amt = inc
                        inst.then_inc(sem.h, amt)
            return go

        with self.nc.Block() as block:
            block.tensor(runner("pe"))
            block.scalar(runner("act"))
            block.vector(runner("dve"))
            block.gpsimd(runner("pool"))
            block.sync(runner("sp"))
        self.stack.close()
        self.sem_stack.close()
        return self.nc


def bcast_mid(ap2d, n):
    return bass.AP(ap2d.tensor, ap2d.offset,
                   [list(ap2d.ap[0]), [0, n], list(ap2d.ap[1])])


def fm3(sb_ap, width, nch=NCH):
    return sb_ap.rearrange("p (c t) -> p c t", c=nch)


def hbm_slabT(hbm, t0, w):
    return hbm[:, :, t0:t0 + w].transpose([1, 0, 2])



def make_const_arrays():
    c = {}
    c["c_ident"] = np.eye(128, dtype=np.float32)
    c["c_ones_col"] = np.ones((128, 1), dtype=np.float32)
    c["c_ones_row"] = np.ones((1, 128), dtype=np.float32)
    c["c_triu4"] = np.tile(np.triu(np.ones((128, 128), dtype=np.float32)), (1, 4))
    oh16 = np.zeros((16, 16 * 128), dtype=np.float32)
    for e in range(16):
        oh16[e, e * 128:(e + 1) * 128] = 1.0
    c["c_oh16"] = oh16
    oh8 = np.zeros((8, 8 * 128), dtype=np.float32)
    for e in range(8):
        oh8[e, e * 128:(e + 1) * 128] = 1.0
    c["c_oh8"] = oh8
    kb = np.zeros((128, 8), dtype=np.float32)
    kb[:, 0] = EPS_LN
    kb[:, 1] = EPS_ATTN
    c["c_kb"] = kb
    return c


def emit_consts(p):
    c = {}
    hb = {}
    for k, v in make_const_arrays().items():
        p.ext_inputs.add(k)
        hb[k] = p.dram(k, list(v.shape), F32, kind="ExternalInput")
    c["ident"] = p.sbuf("ident", [128, 128], BF16)
    c["ones_col_bf"] = p.sbuf("ones_col_bf", [128, 1], BF16)
    c["ones_col_f"] = p.sbuf("ones_col_f", [128, 1], F32)
    c["ones_row_f"] = p.sbuf("ones_row_f", [1, 128], F32)
    c["triu4"] = p.sbuf("triu4", [128, 512], F32)
    c["oh16"] = p.sbuf("oh16", [16, 2048], BF16)
    c["oh8"] = p.sbuf("oh8", [8, 1024], BF16)
    c["kb"] = p.sbuf("kb", [128, 8], F32)
    c["identf"] = p.sbuf("identf", [128, 128], F32)
    s1 = p.sem("csemp")
    s2 = p.sem("csems")
    p.dma("pool", lambda e: e.dma_start(out=c["ident"][:], in_=hb["c_ident"]), inc=s1)
    p.dma("pool", lambda e: e.dma_start(out=c["ones_col_bf"][:], in_=hb["c_ones_col"]), inc=s1)
    p.dma("pool", lambda e: e.dma_start(out=c["oh16"][:], in_=hb["c_oh16"]), inc=s1)
    p.dma("pool", lambda e: e.dma_start(out=c["oh8"][:], in_=hb["c_oh8"]), inc=s1)
    p.dma("sp", lambda e: e.dma_start(out=c["ones_col_f"][:], in_=hb["c_ones_col"]), inc=s2)
    p.dma("sp", lambda e: e.dma_start(out=c["ones_row_f"][:], in_=hb["c_ones_row"]), inc=s2)
    p.dma("sp", lambda e: e.dma_start(out=c["triu4"][:], in_=hb["c_triu4"]), inc=s2)
    p.dma("sp", lambda e: e.dma_start(out=c["kb"][:], in_=hb["c_kb"]), inc=s2)
    p.dma("sp", lambda e: e.dma_start(out=c["identf"][:], in_=hb["c_ident"]), inc=s2)
    c["tok"] = (s1, s1.v)
    c["tok2"] = (s2, s2.v)
    return c



def emit_ln_rows(p, toks, pfx, c, xs_chunk, nch, w, ps_mean, ps_sq, ps_A, ps_B,
                 rows, dep_sq_tok, dep_ld_tok, waitAB=(), wait_stats=(),
                 defer_bcast=False):
    spe, sact, sdve = toks["pe"], toks["act"], toks["dve"]
    t = None
    for cc in range(nch):
        w0 = ((dep_ld_tok,) if dep_ld_tok is not None else ()) + tuple(wait_stats)
        t = p.op("pe", lambda e, cc=cc: e.matmul(
            ps_mean[:], c["ones_col_f"][:], xs_chunk(cc),
            start=(cc == 0), stop=(cc == nch - 1)),
            wait=w0 if cc == 0 else (), inc=spe if cc == nch - 1 else None)
    tok_mean = t
    sq_chunk = toks["sq_chunk"]
    for cc in range(nch):
        t = p.op("pe", lambda e, cc=cc: e.matmul(
            ps_sq[:], c["ones_col_f"][:], sq_chunk(cc),
            start=(cc == 0), stop=(cc == nch - 1)),
            wait=(dep_sq_tok,) if cc == 0 else (), inc=spe if cc == nch - 1 else None)
    tok_sq = t
    mu, m2, musq, var, rstd, bb = (rows[0:1, i * w:(i + 1) * w] for i in range(6))
    tmu = p.op("act", lambda e: e.activation(out=mu, in_=ps_mean[:], func=AF.Copy, scale=1.0 / D),
         wait=(tok_mean,), inc=sact)
    p.op("act", lambda e: e.activation(out=m2, in_=ps_sq[:], func=AF.Copy, scale=1.0 / D),
         wait=(tok_sq,))
    tok_stats = p.op("act", lambda e: e.activation(out=musq, in_=mu, func=AF.Square),
                     wait=(tmu,), inc=sact)
    td = p.op("dve", lambda e: e.tensor_sub(var, m2, musq), wait=(tok_stats,), inc=sdve)
    ta = p.op("act", lambda e: e.activation(out=var, in_=var, func=AF.Ln, bias=c["kb"][0:1, 0:1]),
              wait=(td,), inc=sact)
    ta2 = p.op("act", lambda e: e.activation(out=rstd, in_=var, func=AF.Exp, scale=-0.5),
               wait=(ta,), inc=sact)
    tok_rows = p.op("dve", lambda e: e.scalar_tensor_tensor(
        bb, mu, -1.0, rstd, ALU.mult, ALU.mult), wait=(ta2,), inc=sdve)
    if toks.get("defer_bcast") or defer_bcast:
        return tok_rows, None, tok_stats
    p.op("pe", lambda e: e.matmul(ps_A[:], c["ones_row_f"][:], rstd, start=True, stop=True),
         wait=(tok_rows,) + tuple(waitAB))
    tok_bc = p.op("pe", lambda e: e.matmul(ps_B[:], c["ones_row_f"][:], bb, start=True, stop=True),
                  inc=spe)
    return tok_rows, tok_bc, tok_stats


def emit_ln(p, c, src, dst, deps, pfx="ln", gamma_sb=None, beta_sb=None):
    spe, sact, sdve = p.sem(pfx), p.sem(pfx), p.sem(pfx)
    sld = [p.sem(pfx) for _ in range(2)]
    sst = [p.sem(pfx) for _ in range(2)]
    xs = [p.sbuf(pfx + "x", [128, NCH * SLAB], F32) for _ in range(2)]
    xq = [p.sbuf(pfx + "q", [128, NCH * SLAB], F32) for _ in range(2)]
    xo = [p.sbuf(pfx + "o", [128, NCH * SLAB], BF16) for _ in range(2)]
    rows = [p.sbuf(pfx + "r", [1, 6 * SLAB], F32) for _ in range(2)]
    ps_mean = [p.psum(pfx + "pm", [1, SLAB], F32) for _ in range(2)]
    ps_sq = [p.psum(pfx + "ps", [1, SLAB], F32) for _ in range(2)]
    ps_A = [p.psum(pfx + "pa", [128, SLAB], F32) for _ in range(2)]
    ps_B = [p.psum(pfx + "pb", [128, SLAB], F32) for _ in range(2)]
    xs_free = [None, None]
    xo_free = [None, None]
    tok_x2 = [None, None]
    tok_stats = [None, None]
    tok_rows_s = [None, None]
    tok_st = None
    for s in range(NSLAB + 1):
        b = s % 2
        if s < NSLAB:
            t0 = s * SLAB
            w = tuple(deps) if s < 2 else ()
            if xs_free[b] is not None:
                w = w + (xs_free[b],)
            tok_ld = p.dma("sp", lambda e, b=b, t0=t0: e.dma_start(
                out=fm3(xs[b][:], SLAB), in_=hbm_slabT(src, t0, SLAB)), wait=w,
                inc=sld[b])
            tok_sq = p.op("act", lambda e, b=b: e.activation(
                out=xq[b][:], in_=xs[b][:], func=AF.Square), wait=(tok_ld,), inc=sact)
            toks = {"pe": spe, "act": sact, "dve": sdve,
                    "sq_chunk": lambda cc, b=b: xq[b][:, cc * SLAB:(cc + 1) * SLAB]}
            tok_rows, tok_bc_unused, tok_stats2 = emit_ln_rows(
                p, toks, pfx, c,
                lambda cc, b=b: xs[b][:, cc * SLAB:(cc + 1) * SLAB],
                NCH, SLAB, ps_mean[b], ps_sq[b], ps_A[b], ps_B[b], rows[b],
                tok_sq, tok_ld,
                waitAB=(tok_x2[b],) if tok_x2[b] is not None else (),
                wait_stats=(tok_stats[b],) if tok_stats[b] is not None else (),
                defer_bcast=True)
            tok_stats[b] = tok_stats2
            tok_rows_s[b] = tok_rows
        if s >= 1:
            sb_ = (s - 1) % 2
            t0p = (s - 1) * SLAB
            p.op("pe", lambda e, b=sb_: e.matmul(
                ps_A[b][:], c["ones_row_f"][:], rows[b][0:1, 4 * SLAB:5 * SLAB],
                start=True, stop=True),
                wait=(tok_rows_s[sb_],) + ((tok_x2[sb_],) if tok_x2[sb_] is not None else ()))
            tok_bc = p.op("pe", lambda e, b=sb_: e.matmul(
                ps_B[b][:], c["ones_row_f"][:], rows[b][0:1, 5 * SLAB:6 * SLAB],
                start=True, stop=True), inc=spe)
            wx = (tok_bc,)
            if xo_free[sb_] is not None:
                wx = wx + (xo_free[sb_],)
            xs3 = fm3(xs[sb_][:], SLAB)
            xq3 = fm3(xq[sb_][:], SLAB)
            xo3 = fm3(xo[sb_][:], SLAB)
            tmul = p.op("dve", lambda e, b=sb_, xs3=xs3, xq3=xq3: e.tensor_tensor(
                xq3, xs3, bcast_mid(ps_A[b][:], NCH), ALU.mult), wait=wx, inc=sdve)
            if gamma_sb is None:
                tok_x = p.op("dve", lambda e, b=sb_, xq3=xq3, xo3=xo3: e.tensor_tensor(
                    xo3, xq3, bcast_mid(ps_B[b][:], NCH), ALU.add),
                    wait=(tmul,), inc=sdve)
            else:
                tadd = p.op("dve", lambda e, b=sb_, xq3=xq3: e.tensor_tensor(
                    xq3, xq3, bcast_mid(ps_B[b][:], NCH), ALU.add),
                    wait=(tmul,), inc=sdve)
                tok_x = None
                for cc in range(NCH):
                    xq_c = xq[sb_][:, cc * SLAB:(cc + 1) * SLAB]
                    xo_c = xo[sb_][:, cc * SLAB:(cc + 1) * SLAB]
                    tok_x = p.op("dve", lambda e, a=xq_c, o=xo_c, cc=cc: e.tensor_scalar(
                        o, a, gamma_sb[:, cc:cc + 1], beta_sb[:, cc:cc + 1],
                        ALU.mult, ALU.add), wait=(tadd,) if cc == 0 else (),
                        inc=sdve if cc == NCH - 1 else None)
            xs_free[sb_] = tok_x
            tok_x2[sb_] = tok_x
            tok_st = p.dma("sp", lambda e, b=sb_, t0p=t0p: e.dma_start(
                out=hbm_slabT(dst, t0p, SLAB), in_=fm3(xo[b][:], SLAB)),
                wait=(tok_x,), inc=sst[sb_])
            xo_free[sb_] = tok_st
    return [tok_st]




def emit_qkv(p, c, xn, qd, kd, vd, w_hbm, kind, deps, pfx="qkv"):
    spe, sact, sdve, spool = p.sem(pfx), p.sem(pfx), p.sem(pfx), p.sem(pfx)
    sw, sx = p.sem(pfx), p.sem(pfx)
    sst = [p.sem(pfx) for _ in range(4)]
    w_sb = p.sbuf(pfx + "w", [128, NCH * 3072], BF16)
    tok_w = p.dma("pool", lambda e: e.dma_start(
        out=fm3(w_sb[:], 3072), in_=w_hbm.rearrange("(c p) m -> p c m", p=128)),
        wait=tuple(deps), inc=sw)
    xn_sb = p.sbuf(pfx + "x", [128, NCH * T], BF16)
    tok_x = p.dma("sp", lambda e: e.dma_start(
        out=fm3(xn_sb[:], T), in_=xn.transpose([1, 0, 2])), wait=tuple(deps), inc=sx)
    nps = 4
    pss = [p.psum(pfx + "p", [128, SLAB], F32) for _ in range(nps)]
    ps_free = [None] * nps
    sc1 = [p.sbuf(pfx + "s1", [128, SLAB], F32) for _ in range(2)]
    sc2 = [p.sbuf(pfx + "s2", [128, SLAB], F32) for _ in range(2)]
    ob = [p.sbuf(pfx + "o", [128, SLAB], BF16) for _ in range(4)]
    ob_free = [None] * 4
    sc_free = [None, None]
    tok_st = None
    i = 0
    for mc in range(24):
        dest = (qd, kd, vd)[mc // 8]
        cc_out = mc % 8
        act = kind if mc < 16 else "copy"
        for s in range(NSLAB):
            t0 = s * SLAB
            pb = i % nps
            w0 = (tok_w, tok_x) if i == 0 else ()
            if ps_free[pb] is not None:
                w0 = w0 + (ps_free[pb],)
            tmm = None
            for kc in range(NCH):
                lhs = w_sb[:, kc * 3072 + mc * 128: kc * 3072 + (mc + 1) * 128]
                rhs = xn_sb[:, kc * T + t0: kc * T + t0 + SLAB]
                tmm = p.op("pe", lambda e, l=lhs, r=rhs, pb=pb, kc=kc: e.matmul(
                    pss[pb][:], l, r, start=(kc == 0), stop=(kc == NCH - 1)),
                    wait=w0 if kc == 0 else (), inc=spe if kc == NCH - 1 else None)
            oi = i % 4
            wo = (tmm,)
            if ob_free[oi] is not None:
                wo = wo + (ob_free[oi],)
            if act == "silu":
                tok_ev = p.op("act", lambda e, pb=pb, oi=oi: e.activation(
                    out=ob[oi][:], in_=pss[pb][:], func=AF.Silu), wait=wo, inc=sact)
                ps_free[pb] = tok_ev
            elif act == "copy":
                tok_ev = p.op("act", lambda e, pb=pb, oi=oi: e.copy(
                    out=ob[oi][:], in_=pss[pb][:]), wait=wo, inc=sact)
                ps_free[pb] = tok_ev
            else:
                si = i % 2
                if sc_free[si] is not None:
                    wo = wo + (sc_free[si],)
                t1 = p.op("act", lambda e, pb=pb, si=si: e.activation(
                    out=sc1[si][:], in_=pss[pb][:], func=AF.Exp),
                    wait=wo, inc=sact)
                t2 = p.op("dve", lambda e, si=si: e.tensor_single_scalar(
                    sc1[si][:], sc1[si][:], 1.0, ALU.min), wait=(t1,), inc=sdve)
                t3 = p.op("dve", lambda e, pb=pb, si=si: e.tensor_single_scalar(
                    sc2[si][:], pss[pb][:], 0.0, ALU.max), wait=(t2,), inc=sdve)
                ps_free[pb] = t3
                tok_ev = p.op("pool", lambda e, si=si, oi=oi: e.tensor_add(
                    ob[oi][:], sc1[si][:], sc2[si][:]), wait=(t2, t3), inc=spool)
                sc_free[si] = tok_ev
            tok_st = p.dma("sp", lambda e, d=dest, cc=cc_out, t0=t0, oi=oi: e.dma_start(
                out=d[cc, :, t0:t0 + SLAB], in_=ob[oi][:]), wait=(tok_ev,), inc=sst[oi])
            ob_free[oi] = tok_st
            i += 1
    return [tok_st]



def emit_proj_residual(p, c, a_in, x_in, x_out, w_hbm, deps, pfx="wo"):
    spe, sact, sdve = p.sem(pfx), p.sem(pfx), p.sem(pfx)
    sw = p.sem(pfx)
    sa = [p.sem(pfx) for _ in range(2)]
    sxi = [p.sem(pfx) for _ in range(2)]
    sst = [p.sem(pfx) for _ in range(2)]
    w_sb = p.sbuf(pfx + "w", [128, NCH * D], BF16)
    tok_w = p.dma("pool", lambda e: e.dma_start(
        out=fm3(w_sb[:], D), in_=w_hbm.rearrange("(c p) m -> p c m", p=128)),
        wait=tuple(deps), inc=sw)
    ab = [p.sbuf(pfx + "a", [128, NCH * SLAB], BF16) for _ in range(2)]
    xb = [p.sbuf(pfx + "xi", [128, NCH * SLAB], F32) for _ in range(2)]
    ob = [p.sbuf(pfx + "xo", [128, NCH * SLAB], F32) for _ in range(2)]
    pss = [p.psum(pfx + "p", [128, SLAB], F32) for _ in range(4)]
    ps_free = [None] * 4
    in_free = [None, None]
    ob_free = [None, None]
    tok_st = None
    for s in range(NSLAB):
        b = s % 2
        t0 = s * SLAB
        w0 = tuple(deps) if s < 2 else ()
        if in_free[b] is not None:
            w0 = w0 + (in_free[b],)
        tok_a = p.dma("sp", lambda e, b=b, t0=t0: e.dma_start(
            out=fm3(ab[b][:], SLAB), in_=hbm_slabT(a_in, t0, SLAB)), wait=w0, inc=sa[b])
        tok_xi = p.dma("sp", lambda e, b=b, t0=t0: e.dma_start(
            out=fm3(xb[b][:], SLAB), in_=hbm_slabT(x_in, t0, SLAB)), inc=sxi[b])
        last_ev = None
        for mc in range(NCH):
            pb = mc % 4
            wmm = (tok_a,) if mc == 0 else ()
            wmm = wmm + ((tok_w,) if s == 0 and mc == 0 else ())
            if ps_free[pb] is not None:
                wmm = wmm + (ps_free[pb],)
            tmm = None
            for kc in range(NCH):
                lhs = w_sb[:, kc * D + mc * 128: kc * D + (mc + 1) * 128]
                rhs = ab[b][:, kc * SLAB: (kc + 1) * SLAB]
                tmm = p.op("pe", lambda e, l=lhs, r=rhs, pb=pb, kc=kc: e.matmul(
                    pss[pb][:], l, r, start=(kc == 0), stop=(kc == NCH - 1)),
                    wait=wmm if kc == 0 else (), inc=spe if kc == NCH - 1 else None)
            wev = (tmm, tok_xi)
            if mc == 0 and ob_free[b] is not None:
                wev = wev + (ob_free[b],)
            last_ev = p.op("dve", lambda e, b=b, mc=mc, pb=pb: e.tensor_add(
                ob[b][:, mc * SLAB:(mc + 1) * SLAB], pss[pb][:],
                xb[b][:, mc * SLAB:(mc + 1) * SLAB]), wait=wev, inc=sdve)
            ps_free[pb] = last_ev
        in_free[b] = last_ev
        tok_st = p.dma("sp", lambda e, b=b, t0=t0: e.dma_start(
            out=hbm_slabT(x_out, t0, SLAB), in_=fm3(ob[b][:], SLAB)),
            wait=(last_ev,), inc=sst[b])
        ob_free[b] = tok_st
    return [tok_st]



def emit_attn(p, c, qd, kd, vd, ad, denom, deps, pfx="at"):
    spe, sact, sdve = p.sem(pfx), p.sem(pfx), p.sem(pfx)
    sq = [p.sem(pfx) for _ in range(2)]
    sk = [p.sem(pfx) for _ in range(2)]
    sv = [p.sem(pfx) for _ in range(2)]
    sst = [p.sem(pfx) for _ in range(2)]
    qs = [p.sbuf(pfx + "q", [128, NCH * CH], BF16) for _ in range(2)]
    ks = [p.sbuf(pfx + "k", [128, NCH * CH], BF16) for _ in range(2)]
    vs = [p.sbuf(pfx + "v", [128, NCH * CH], BF16) for _ in range(2)]
    ktm = p.sbuf(pfx + "ktm", [128, NCH * CH], BF16)
    vtm = p.sbuf(pfx + "vtm", [128, NCH * CH], BF16)
    S_sb = p.sbuf(pfx + "S", [128, 512], BF16)
    z_sb = p.sbuf(pfx + "z", [128, 8], BF16)
    sc_sb = [p.sbuf(pfx + "sc", [128, 512], BF16) for _ in range(2)]
    nm_sb = [p.sbuf(pfx + "nm", [128, 256], F32) for _ in range(2)]
    rden = p.sbuf(pfx + "rd", [128, 256], F32)
    at_sb = [p.sbuf(pfx + "o", [128, NCH * CH], BF16) for _ in range(2)]
    ps_S = p.psum(pfx + "pS", [128, 512], F32)
    ps_sc = [p.psum(pfx + "psc", [128, 512], F32) for _ in range(2)]
    ps_nm = [p.psum(pfx + "pnm", [128, 512], F32) for _ in range(2)]
    ps_tr = p.psum(pfx + "ptr", [128, 1024], BF16)
    if denom:
        ps_z = p.psum(pfx + "pz", [128, 512], F32)

    def hslice(sb, h):
        pb = (h % 2) * 64
        return sb[pb:pb + 64, (h // 2) * CH:(h // 2) * CH + CH]

    SCCOL = [0, 256, 128, 384]

    t0s = tuple(deps)
    tok_S0 = p.op("dve", lambda e: e.memset(S_sb[:], 0.0), wait=t0s, inc=sdve)
    tok_psS0 = p.op("dve", lambda e: e.memset(ps_S[:], 0.0), inc=sdve)
    tok_psz0 = None
    if denom:
        tok_S0 = p.op("dve", lambda e: e.memset(z_sb[:], 0.0), inc=sdve)
        tok_psz0 = p.op("dve", lambda e: e.memset(ps_z[:, 0:8], 0.0), inc=sdve)
    qkv_free = [None, None]
    scfree = None
    scb_free = [None, None]
    nmfree = [None, None]
    at_free = [None, None]
    tok_trfree = None
    tok_Ssb = tok_S0
    tok_zsb = tok_S0
    tok_Supd_prev = None
    tok_zupd_prev = None
    tok_st = None
    for ci in range(NCHUNK):
        b = ci % 2
        c0 = ci * CH
        w0 = t0s if ci < 2 else ()
        if qkv_free[b] is not None:
            w0 = w0 + (qkv_free[b],)
        tok_q = p.dma("sp", lambda e, b=b, c0=c0: e.dma_start(
            out=fm3(qs[b][:], CH), in_=hbm_slabT(qd, c0, CH)), wait=w0, inc=sq[b])
        tok_k = p.dma("sp", lambda e, b=b, c0=c0: e.dma_start(
            out=fm3(ks[b][:], CH), in_=hbm_slabT(kd, c0, CH)), inc=sk[b])
        tok_v = p.dma("sp", lambda e, b=b, c0=c0: e.dma_start(
            out=fm3(vs[b][:], CH), in_=hbm_slabT(vd, c0, CH)), inc=sv[b])
        wtr = (tok_k,) + ((tok_trfree,) if tok_trfree is not None else ())
        tk = None
        if "tr" in DBG_SKIP:
            tok_ktm = p.op("dve", lambda e: e.memset(ktm[:], 0.001), wait=wtr, inc=sdve)
            tok_vtm = p.op("dve", lambda e: e.memset(vtm[:], 0.001), wait=(tok_v,), inc=sdve)
            tok_trfree = tok_vtm
            dbg_skip_tr = True
        else:
            dbg_skip_tr = False
        if not dbg_skip_tr:
            for cc in range(NCH):
                tk = p.op("pe", lambda e, b=b, cc=cc: e.transpose(
                    ps_tr[:, cc * CH:(cc + 1) * CH], ks[b][:, cc * CH:(cc + 1) * CH],
                    c["ident"][:]), wait=wtr if cc == 0 else (),
                    inc=spe if cc == NCH - 1 else None)
            tok_ktm = p.op("dve", lambda e: e.tensor_copy(out=ktm[:], in_=ps_tr[:]),
                           wait=(tk,), inc=sdve)
            tv = None
            for cc in range(NCH):
                tv = p.op("pe", lambda e, b=b, cc=cc: e.transpose(
                    ps_tr[:, cc * CH:(cc + 1) * CH], vs[b][:, cc * CH:(cc + 1) * CH],
                    c["ident"][:]), wait=(tok_v, tok_ktm) if cc == 0 else (),
                    inc=spe if cc == NCH - 1 else None)
            tok_vtm = p.op("dve", lambda e: e.tensor_copy(out=vtm[:], in_=ps_tr[:]),
                           wait=(tv,), inc=sdve)
            tok_trfree = tok_vtm
        ep_toks = []
        for g in range(4):
            b2 = g % 2
            heads = [4 * g + j for j in range(4)]
            wsc = (tok_q,)
            if scfree is not None:
                wsc = wsc + (scfree,)
            tsc = None
            if "sc" in DBG_SKIP:
                tsc = p.op("pe", lambda e: e.matmul(
                    ps_sc[0][:], c["ident"][:], qs[b][:, 0:512], start=True,
                    stop=True), wait=wsc, inc=spe)
            else:
              for j, h in enumerate(heads):
                pb = (h % 2) * 64
                tsc = p.op("pe", lambda e, b=b, j=j, h=h, pb=pb: e.matmul(
                    ps_sc[j % 2][:, (j // 2) * CH:(j // 2 + 1) * CH],
                    hslice(ks[b], h), hslice(qs[b], h), start=True, stop=True,
                    tile_position=(pb, 0)), wait=wsc if j == 0 else (),
                    inc=spe if j == 3 else None)
            wev = (tsc,)
            if scb_free[b2] is not None:
                wev = wev + (scb_free[b2],)
            p.op("dve", lambda e, b2=b2: e.tensor_mul(
                sc_sb[b2][:, 0:256], ps_sc[0][:, 0:256], c["triu4"][:, 0:256]),
                wait=wev)
            tok_scev = p.op("dve", lambda e, b2=b2: e.tensor_mul(
                sc_sb[b2][:, 256:512], ps_sc[1][:, 0:256], c["triu4"][:, 0:256]),
                inc=sdve)
            if denom:
                wd = (tok_scev, tok_zsb)
                td = None
                for j, h in enumerate(heads):
                    pb = (h % 2) * 64
                    cb = j // 2
                    dreg = ps_sc[b2][pb:pb + 64, 256 + cb * CH:256 + (cb + 1) * CH]
                    zsl = z_sb[pb:pb + 64, (h // 2):(h // 2) + 1]
                    zrep = bass.AP(zsl.tensor, zsl.offset,
                                   [list(zsl.ap[0]), [0, 64]])
                    ones64 = bass.AP(c["ones_col_bf"][:].tensor, 0,
                                     [[1, 128], [0, 64]])
                    td = p.op("pe", lambda e, b=b, o=dreg, z=zrep, h=h, pb=pb: e.matmul(
                        o, z, hslice(qs[b], h),
                        start=True, stop=False, tile_position=(pb, pb)),
                        wait=wd if j == 0 else (), inc=None)
                    td = p.op("pe", lambda e, o=dreg, n1=ones64, j=j, b2=b2, pb=pb: e.matmul(
                        o, n1, sc_sb[b2][:, SCCOL[j]:SCCOL[j] + CH],
                        start=False, stop=True, tile_position=(0, pb)),
                        inc=spe if j == 3 else None)
                tl = p.op("act", lambda e, b2=b2: e.activation(
                    out=rden[:], in_=ps_sc[b2][:, 256:512], func=AF.Ln),
                    wait=(td,), inc=sact)
                tok_rden = p.op("act", lambda e: e.activation(
                    out=rden[:], in_=rden[:], func=AF.Exp, scale=-1.0),
                    wait=(tl,), inc=sact)
            wnm = (tok_scev, tok_Ssb, tok_vtm)
            if nmfree[b2] is not None:
                wnm = wnm + (nmfree[b2],)
            tnm = None
            if "nm" in DBG_SKIP:
                tnm = p.op("pe", lambda e, b2=b2: e.matmul(
                    ps_nm[b2][:], c["ident"][:], qs[b][:, 0:512], start=True,
                    stop=True), wait=wnm, inc=spe)
            else:
              for j, h in enumerate(heads):
                pb = (h % 2) * 64
                cb = j // 2
                out_ap = ps_nm[b2][pb:pb + 64, cb * CH:(cb + 1) * CH]
                tnm = p.op("pe", lambda e, o=out_ap, h=h, pb=pb, b=b, j=j: e.matmul(
                    o, S_sb[pb:pb + 64, (h // 2) * 64:(h // 2) * 64 + 64],
                    hslice(qs[b], h), start=True, stop=False,
                    tile_position=(pb, pb)), wait=wnm if j == 0 else (), inc=None)
                tnm = p.op("pe", lambda e, o=out_ap, h=h, pb=pb, j=j, b2=b2: e.matmul(
                    o, vtm[:, h * 64:(h + 1) * 64],
                    sc_sb[b2][:, SCCOL[j]:SCCOL[j] + CH], start=False, stop=True,
                    tile_position=(0, pb)), inc=spe if j == 3 else None)
            if denom:
                tok_nmev = p.op("act", lambda e, b2=b2: e.copy(
                    out=nm_sb[b2][:], in_=ps_nm[b2][:, 0:256]), wait=(tnm,), inc=sact)
                nmfree[b2] = tok_nmev
                wep = (tok_nmev, tok_rden)
                if g < 2 and at_free[b] is not None:
                    wep = wep + (at_free[b],)
                tmul = p.op("dve", lambda e, b2=b2: e.tensor_mul(
                    nm_sb[b2][:], nm_sb[b2][:], rden[:]), wait=wep,
                    inc=sdve)
                tep = p.op("dve", lambda e, b=b, g=g, b2=b2: e.tensor_single_scalar(
                    at_sb[b][:, 2 * g * CH:(2 * g + 2) * CH], nm_sb[b2][:],
                    EPS_ATTN, ALU.add), wait=(tmul,), inc=sdve)
                scfree = tok_rden
                scb_free[b2] = tnm
            else:
                wep = (tnm,)
                if g < 2 and at_free[b] is not None:
                    wep = wep + (at_free[b],)
                tep = p.op("act", lambda e, b=b, g=g, b2=b2: e.copy(
                    out=at_sb[b][:, 2 * g * CH:(2 * g + 2) * CH],
                    in_=ps_nm[b2][:, 0:256]), wait=wep, inc=sact)
                nmfree[b2] = tep
                scfree = tok_scev
                scb_free[b2] = tnm
            ep_toks.append(tep)
        wS = (tok_ktm, tok_vtm)
        if ci == 0:
            wS = wS + (tok_psS0,)
        if tok_Ssb is not None:
            wS = wS + (tok_Ssb,)
        tS = None
        for h in range(H):
            pb = (h % 2) * 64
            tS = p.op("pe", lambda e, h=h, pb=pb, ci=ci: e.matmul(
                ps_S[pb:pb + 64, (h // 2) * 64:(h // 2) * 64 + 64],
                ktm[:, h * 64:(h + 1) * 64], vtm[:, h * 64:(h + 1) * 64],
                start=(ci == 0 and h == 0), stop=(ci == NCHUNK - 1 and h == H - 1),
                skip_group_check=True,
                tile_position=(0, pb)), wait=wS if h == 0 else (),
                inc=spe if h == H - 1 else None)
        tok_Supd = tS
        if denom:
            wz = (tok_zsb,)
            if ci == 0:
                wz = wz + (tok_psz0,)
            tz = None
            for h in range(H):
                pb = (h % 2) * 64
                tz = p.op("pe", lambda e, h=h, pb=pb, ci=ci: e.matmul(
                    ps_z[pb:pb + 64, (h // 2):(h // 2) + 1],
                    ktm[:, h * 64:(h + 1) * 64], c["ones_col_bf"][:],
                    start=(ci == 0 and h == 0),
                    stop=(ci == NCHUNK - 1 and h == H - 1),
                    skip_group_check=True,
                    tile_position=(0, pb)), wait=wz if h == 0 else (),
                    inc=spe if h == H - 1 else None)
            tok_zupd = tz
        if ci < NCHUNK - 1:
            tok_Ssb = p.op("act", lambda e: e.copy(out=S_sb[:], in_=ps_S[:]),
                           wait=(tok_Supd,), inc=sact)
            if denom:
                tok_zsb = p.op("act", lambda e: e.copy(
                    out=z_sb[:], in_=ps_z[:, 0:8]), wait=(tok_zupd,), inc=sact)
        qkv_free[b] = tok_Supd
        tok_st = p.dma("sp", lambda e, b=b, c0=c0: e.dma_start(
            out=hbm_slabT(ad, c0, CH), in_=fm3(at_sb[b][:], CH)),
            wait=(ep_toks[-1],), inc=sst[b])
        at_free[b] = tok_st
    return [tok_st]



def emit_router(p, c, xn, gd, w_hbm, mode, deps, bias_sb=None, bias_hbm=None,
                pfx="rt"):
    E = 16 if mode == "top4x16" else 8
    TOPK = 4 if mode == "top4x16" else 2
    spe, sact, sdve = p.sem(pfx), p.sem(pfx), p.sem(pfx)
    sx = [p.sem(pfx) for _ in range(2)]
    sw = p.sem(pfx)
    sst = [p.sem(pfx) for _ in range(2)]
    w_sb = p.sbuf(pfx + "w", [128, NCH * E], BF16)
    tok_w = p.dma("pool", lambda e: e.dma_start(
        out=fm3(w_sb[:], E), in_=w_hbm.rearrange("(c p) m -> p c m", p=128)),
        wait=tuple(deps), inc=sw)
    if bias_hbm is not None:
        bias_sb = p.sbuf(pfx + "bias", [E, 1], F32)
        sbm = p.sem(pfx)
        deps = list(deps) + [p.dma("sp", lambda e: e.dma_start(
            out=bias_sb[:], in_=bias_hbm), wait=tuple(deps), inc=sbm)]
    xs = [p.sbuf(pfx + "x", [128, NCH * SLAB], BF16) for _ in range(2)]
    lg = p.sbuf(pfx + "lg", [E, SLAB], F32)
    tm = p.sbuf(pfx + "tm", [128, 4 * E], F32)
    gt = p.sbuf(pfx + "gt", [128, 4 * E], F32)
    sc1 = p.sbuf(pfx + "s1", [128, 4 * E], F32)
    sc2 = p.sbuf(pfx + "s2", [128, 4 * E], F32)
    rowm = p.sbuf(pfx + "rm", [128, 8], F32)
    go = [p.sbuf(pfx + "go", [E, SLAB], BF16) for _ in range(2)]
    ps_l = p.psum(pfx + "pl", [128, 512], F32)
    ps_f = p.psum(pfx + "pf", [128, 512], F32)
    ps_b = p.psum(pfx + "pb", [128, 512], F32)
    xs_free = [None, None]
    go_free = [None, None]
    tok_lgread = None
    tok_tm = None
    tok_go = None
    tok_st = None
    for s in range(NSLAB):
        b = s % 2
        t0 = s * SLAB
        w0 = tuple(deps) if s < 2 else ()
        if xs_free[b] is not None:
            w0 = w0 + (xs_free[b],)
        tok_x = p.dma("sp", lambda e, b=b, t0=t0: e.dma_start(
            out=fm3(xs[b][:], SLAB), in_=hbm_slabT(xn, t0, SLAB)), wait=w0, inc=sx[b])
        wmm = (tok_x,) + ((tok_w,) if s == 0 else ())
        if tok_lgread is not None:
            wmm = wmm + (tok_lgread,)
        tmm = None
        for kc in range(NCH):
            tmm = p.op("pe", lambda e, kc=kc, b=b: e.matmul(
                ps_l[0:E, :], w_sb[:, kc * E:(kc + 1) * E],
                xs[b][:, kc * SLAB:(kc + 1) * SLAB],
                start=(kc == 0), stop=(kc == NCH - 1)),
                wait=wmm if kc == 0 else (), inc=spe if kc == NCH - 1 else None)
        xs_free[b] = tmm
        if bias_sb is None:
            tok_lg = p.op("dve", lambda e: e.tensor_copy(out=lg[:], in_=ps_l[0:E, :]),
                          wait=(tmm,), inc=sdve)
        else:
            tok_lg = p.op("dve", lambda e: e.tensor_scalar_add(
                lg[:], ps_l[0:E, :], bias_sb[0:E, 0:1]), wait=(tmm,), inc=sdve)
        tok_lgread = tok_lg
        wtr = (tok_lg,)
        if tok_tm is not None:
            wtr = wtr + (tok_tm,)
        ttr = None
        for j in range(4):
            ttr = p.op("pe", lambda e, j=j: e.transpose(
                ps_f[:, j * E:(j + 1) * E], lg[0:E, j * CH:(j + 1) * CH],
                c["identf"][0:E, 0:E]), wait=wtr if j == 0 else (),
                inc=spe if j == 3 else None)
        tok_tm = p.op("dve", lambda e: e.tensor_copy(out=tm[:], in_=ps_f[:, 0:4 * E]),
                      wait=(ttr,), inc=sdve)
        E4 = 4 * E
        tm3 = tm[:, 0:E4].rearrange("p (c t) -> p c t", c=4)
        gt3 = gt[:, 0:E4].rearrange("p (c t) -> p c t", c=4)
        s13 = sc1[:, 0:E4].rearrange("p (c t) -> p c t", c=4)
        s23 = sc2[:, 0:E4].rearrange("p (c t) -> p c t", c=4)
        m3 = rowm[:, 0:4].rearrange("p (c t) -> p c t", c=4)
        sm3 = rowm[:, 4:8].rearrange("p (c t) -> p c t", c=4)
        mb = bass.AP(rowm[:].tensor, rowm[:].offset,
                     [list(rowm[:].ap[0]), [1, 4], [0, E]])
        smb = bass.AP(rowm[:].tensor, rowm[:].offset + 4,
                      [list(rowm[:].ap[0]), [1, 4], [0, E]])
        tprev = [tok_tm]

        def step(thunk, eng="dve", extra=()):
            w = ((tprev[0],) if tprev[0] is not None else ()) + tuple(extra)
            tprev[0] = p.op(eng, thunk, wait=w, inc=(sdve if eng == "dve" else sact))
            return tprev[0]

        if mode == "top4x16":
            step(lambda e: e.reduce_max(m3, tm3, axis=AX.X))
            step(lambda e: e.tensor_tensor(s13, tm3, mb, ALU.subtract))
            step(lambda e: e.activation(out=sc1[:, 0:E4], in_=sc1[:, 0:E4],
                                        func=AF.Exp), eng="act")
            step(lambda e: e.reduce_sum(sm3, s13, axis=AX.X))
            step(lambda e: e.reciprocal(out=rowm[:, 4:8], in_=rowm[:, 4:8]))
            step(lambda e: e.tensor_tensor(tm3, s13, smb, ALU.mult))
            step(lambda e: e.tensor_copy(out=sc1[:, 0:E4], in_=tm[:, 0:E4]))
            for it in range(TOPK - 1):
                step(lambda e: e.reduce_max(m3, s13, axis=AX.X))
                step(lambda e: e.tensor_tensor(s23, s13, mb, ALU.is_ge))
                step(lambda e: e.scalar_tensor_tensor(
                    sc1[:, 0:E4], sc2[:, 0:E4], NEG_BIG, sc1[:, 0:E4],
                    ALU.mult, ALU.add))
            step(lambda e: e.reduce_max(m3, s13, axis=AX.X))
            step(lambda e: e.tensor_tensor(s23, tm3, mb, ALU.is_ge))
            tok_gt = step(lambda e: e.tensor_tensor(gt3, tm3, s23, ALU.mult))
        else:
            step(lambda e: e.reduce_max(m3, tm3, axis=AX.X))
            step(lambda e: e.tensor_tensor(s13, tm3, mb, ALU.subtract))
            step(lambda e: e.tensor_tensor(s23, tm3, mb, ALU.is_ge))
            step(lambda e: e.scalar_tensor_tensor(
                sc2[:, 0:E4], sc2[:, 0:E4], NEG_BIG, tm[:, 0:E4],
                ALU.mult, ALU.add))
            step(lambda e: e.activation(out=sc1[:, 0:E4], in_=sc1[:, 0:E4],
                                        func=AF.Exp), eng="act")
            step(lambda e: e.reduce_max(m3, s23, axis=AX.X))
            step(lambda e: e.tensor_tensor(s23, tm3, mb, ALU.is_ge))
            step(lambda e: e.tensor_tensor(s13, s13, s23, ALU.mult))
            step(lambda e: e.reduce_sum(sm3, s13, axis=AX.X))
            step(lambda e: e.reciprocal(out=rowm[:, 4:8], in_=rowm[:, 4:8]))
            tok_gt = step(lambda e: e.tensor_tensor(gt3, s13, smb, ALU.mult))
        wbk = (tok_gt,)
        if tok_go is not None:
            wbk = wbk + (tok_go,)
        tbk = None
        for j in range(4):
            tbk = p.op("pe", lambda e, j=j: e.transpose(
                ps_b[0:E, j * CH:(j + 1) * CH], gt[:, j * E:(j + 1) * E],
                c["identf"][:]), wait=wbk if j == 0 else (),
                inc=spe if j == 3 else None)
        wgo = (tbk,)
        if go_free[b] is not None:
            wgo = wgo + (go_free[b],)
        tok_go = p.op("dve", lambda e, b=b: e.tensor_copy(
            out=go[b][:], in_=ps_b[0:E, 0:SLAB]), wait=wgo, inc=sdve)
        tok_st = p.dma("sp", lambda e, b=b, t0=t0: e.dma_start(
            out=gd[0:E, t0:t0 + SLAB], in_=go[b][:]), wait=(tok_go,), inc=sst[b])
        go_free[b] = tok_st
    return [tok_st]



def emit_moe0(p, c, xn, gfm, x_in, x_out, up_hbm, dn_hbm, deps, pfx="m0"):
    NE = 18
    SEG = 2048
    NSEG = T // SEG
    SL = SEG // SLAB
    spe, sact, sdve = p.sem(pfx), p.sem(pfx), p.sem(pfx)
    swu = [p.sem(pfx) for _ in range(2)]
    swd = [p.sem(pfx) for _ in range(2)]
    sxh, sg = p.sem(pfx), p.sem(pfx)
    sxi = [p.sem(pfx) for _ in range(2)]
    sst = [p.sem(pfx) for _ in range(2)]
    acc = p.sbuf(pfx + "acc", [128, NCH * SEG], F32)
    xn_h = p.sbuf(pfx + "xn", [128, NCH * SEG], BF16)
    g_h = p.sbuf(pfx + "g", [16, SEG], BF16)
    wu = [p.sbuf(pfx + "wu", [128, NCH * 512], BF16) for _ in range(2)]
    wd = [p.sbuf(pfx + "wd", [128, 4 * D], BF16) for _ in range(2)]
    act_b = [p.sbuf(pfx + "act", [128, 4 * SLAB], BF16) for _ in range(2)]
    gl = p.sbuf(pfx + "gl", [128, SLAB], F32)
    xi_b = [p.sbuf(pfx + "xi", [128, NCH * SLAB], F32) for _ in range(2)]
    ob = xi_b
    ps_u = [p.psum(pfx + "pu", [128, SLAB], F32) for _ in range(3)]
    ps_d = [p.psum(pfx + "pd", [128, SLAB], F32) for _ in range(3)]
    ps_bc = p.psum(pfx + "pbc", [128, SLAB], F32)
    wu_free = [None, None]
    wd_free = [None, None]
    units = [(seg, e) for seg in range(NSEG) for e in range(NE)]
    tok_w = {}

    def load_unit(i, first=False):
        seg, e = units[i]
        par = i % 2
        w = tuple(deps) if i < 2 else ()
        if wu_free[par] is not None:
            w = w + (wu_free[par],)
        tok_wu = p.dma("pool", lambda e_, e=e: e_.dma_start(
            out=fm3(wu[par][:], 512),
            in_=up_hbm[e].rearrange("(c p) m -> p c m", p=128)), wait=w, inc=swu[par])
        w = ()
        if wd_free[par] is not None:
            w = (wd_free[par],)
        tok_wd = p.dma("pool", lambda e_, e=e: e_.dma_start(
            out=fm3(wd[par][:], D, nch=4),
            in_=dn_hbm[e].rearrange("(c p) m -> p c m", p=128)), wait=w, inc=swd[par])
        tok_w[i] = (tok_wu, tok_wd)

    load_unit(0, first=True)
    load_unit(1)
    tok_st = None
    ps_u_free = [None, None, None]
    ps_d_free = [None, None, None]
    bc_free = None
    act_free = [None, None]
    tok_acc = None
    for seg in range(NSEG):
        h0 = seg * SEG
        tok_xh = p.dma("sp", lambda e, h0=h0: e.dma_start(
            out=fm3(xn_h[:], SEG), in_=hbm_slabT(xn, h0, SEG)),
            wait=tuple(deps) + ((tok_acc,) if tok_acc is not None else ()), inc=sxh)
        tok_g = p.dma("sp", lambda e, h0=h0: e.dma_start(
            out=g_h[:], in_=gfm[0:16, h0:h0 + SEG]),
            wait=tuple(deps) if seg == 0 else (), inc=sg)
        tok_accfree = tok_acc
        pend = None

        def emit_down(e, sl, ab, tok_act, tok_wd, first_for_acc):
            nonlocal tok_acc
            t0 = sl * SLAB
            par_ = None
            tmm = None
            for mc in range(NCH):
                pd = mc % 3
                wmm = (tok_act,)
                if mc == 0 and sl == 0:
                    wmm = wmm + (tok_wd,)
                if ps_d_free[pd] is not None:
                    wmm = wmm + (ps_d_free[pd],)
                tmm = None
                for kc in range(4):
                    lhs = wd[e % 2 if False else _wpar[0]][:, kc * D + mc * 128: kc * D + (mc + 1) * 128]
                    rhs = act_b[ab][:, kc * SLAB:(kc + 1) * SLAB]
                    tmm = p.op("pe", lambda e_, l=lhs, r=rhs, pd=pd, kc=kc: e_.matmul(
                        ps_d[pd][:], l, r, start=(kc == 0), stop=(kc == 3)),
                        wait=wmm if kc == 0 else (), inc=spe if kc == 3 else None)
                if first_for_acc:
                    wacc = (tmm,)
                    if tok_accfree is not None:
                        wacc = wacc + (tok_accfree,)
                    tok_acc = p.op("dve", lambda e_, mc=mc, t0=t0, pd=pd: e_.tensor_scalar(
                        acc[:, mc * SEG + t0: mc * SEG + t0 + SLAB], ps_d[pd][:],
                        1.0, None, ALU.mult), wait=wacc, inc=sdve)
                else:
                    tok_acc = p.op("dve", lambda e_, mc=mc, t0=t0, pd=pd: e_.scalar_tensor_tensor(
                        acc[:, mc * SEG + t0: mc * SEG + t0 + SLAB], ps_d[pd][:], 1.0,
                        acc[:, mc * SEG + t0: mc * SEG + t0 + SLAB], ALU.mult, ALU.add),
                        wait=(tmm,), inc=sdve)
                ps_d_free[pd] = tok_acc
            return tmm

        _wpar = [0]
        for e in range(NE):
            ui = seg * NE + e
            par = ui % 2
            routed = e >= 2
            tok_wu, tok_wd = tok_w[ui]
            last_up_mm = None
            last_dn_mm = None
            for s in range(SL):
                t0 = s * SLAB
                ab = s % 2
                if routed:
                    wb = (tok_g,)
                    if bc_free is not None:
                        wb = wb + (bc_free,)
                    tok_bc = p.op("pe", lambda e_, e=e, t0=t0: e_.matmul(
                        ps_bc[:], c["oh16"][:, (e - 2) * 128:(e - 1) * 128],
                        g_h[0:16, t0:t0 + SLAB], start=True, stop=True), wait=wb,
                        inc=spe)
                tok_act = None
                for mc in range(4):
                    pu = mc % 3
                    wmm = ()
                    if mc == 0 and s == 0:
                        wmm = (tok_wu, tok_xh)
                    if ps_u_free[pu] is not None:
                        wmm = wmm + (ps_u_free[pu],)
                    if mc == 0 and act_free[ab] is not None:
                        wmm = wmm + (act_free[ab],)
                    tmm = None
                    for kc in range(NCH):
                        lhs = wu[par][:, kc * 512 + mc * 128: kc * 512 + (mc + 1) * 128]
                        rhs = xn_h[:, kc * SEG + t0: kc * SEG + t0 + SLAB]
                        tmm = p.op("pe", lambda e_, l=lhs, r=rhs, pu=pu, kc=kc: e_.matmul(
                            ps_u[pu][:], l, r, start=(kc == 0), stop=(kc == NCH - 1)),
                            wait=wmm if kc == 0 else (), inc=spe if kc == NCH - 1 else None)
                    last_up_mm = tmm
                    if routed:
                        tge = p.op("act", lambda e_, pu=pu: e_.activation(
                            out=gl[:], in_=ps_u[pu][:], func=AF.Gelu), wait=(tmm,), inc=sact)
                        ps_u_free[pu] = tge
                        tok_act = p.op("dve", lambda e_, mc=mc, ab=ab: e_.tensor_mul(
                            act_b[ab][:, mc * SLAB:(mc + 1) * SLAB], gl[:], ps_bc[:]),
                            wait=(tge, tok_bc), inc=sdve)
                    else:
                        tok_act = p.op("act", lambda e_, pu=pu, mc=mc, ab=ab: e_.activation(
                            out=act_b[ab][:, mc * SLAB:(mc + 1) * SLAB], in_=ps_u[pu][:],
                            func=AF.Gelu), wait=(tmm,), inc=sact)
                        ps_u_free[pu] = tok_act
                if routed:
                    bc_free = tok_act
                if pend is not None:
                    pe_, ps_, pab, pact, pwd, pfirst = pend
                    _wpar[0] = (seg * NE + pe_) % 2
                    last_dn_mm = emit_down(pe_, ps_, pab, pact, pwd, pfirst)
                    act_free[pab] = last_dn_mm
                    if ps_ == SL - 1:
                        wd_free[(seg * NE + pe_) % 2] = last_dn_mm
                        if seg * NE + pe_ + 2 < len(units):
                            load_unit(seg * NE + pe_ + 2)
                pend = (e, s, ab, tok_act, tok_wd, e == 0)
            wu_free[par] = last_up_mm
        if pend is not None:
            pe_, ps_, pab, pact, pwd, pfirst = pend
            _wpar[0] = (seg * NE + pe_) % 2
            last_dn_mm = emit_down(pe_, ps_, pab, pact, pwd, pfirst)
            act_free[pab] = last_dn_mm
            if ps_ == SL - 1:
                wd_free[(seg * NE + pe_) % 2] = last_dn_mm
                if seg * NE + pe_ + 2 < len(units):
                    load_unit(seg * NE + pe_ + 2)
            pend = None
        last = None
        for s in range(SL):
            t0g = h0 + s * SLAB
            t0 = s * SLAB
            b = s % 2
            w0 = (tok_acc,)
            if seg == 0 and s < 2:
                w0 = w0 + tuple(deps)
            if sst[b].v > 0:
                w0 = w0 + ((sst[b], sst[b].v),)
            tok_xi = p.dma("sp", lambda e, b=b, t0g=t0g: e.dma_start(
                out=fm3(xi_b[b][:], SLAB), in_=hbm_slabT(x_in, t0g, SLAB)),
                wait=w0, inc=sxi[b])
            wadd = (tok_xi, tok_acc)
            tok_add = p.op("dve", lambda e, b=b, t0=t0: e.tensor_add(
                xi_b[b][:].rearrange("p (c t) -> p c t", c=NCH),
                xi_b[b][:].rearrange("p (c t) -> p c t", c=NCH),
                acc[:].rearrange("p (c t) -> p c t", c=NCH)[:, :, t0:t0 + SLAB]),
                wait=wadd, inc=sdve)
            last = p.dma("sp", lambda e, b=b, t0g=t0g: e.dma_start(
                out=hbm_slabT(x_out, t0g, SLAB), in_=fm3(ob[b][:], SLAB)),
                wait=(tok_add,), inc=sst[b])
        tok_st = last
        tok_acc = tok_st
    return [tok_st]



def emit_postattn1(p, c, attn, xn2, x_in, x_out, wg1, wg2, wout,
                   bg1h, bg2h, gmh, bth, deps, pfx="pa"):
    spe, sact, sdve = p.sem(pfx), p.sem(pfx), p.sem(pfx)
    swp, sws = p.sem(pfx), p.sem(pfx)
    sa = [p.sem(pfx) for _ in range(2)]
    sx = [p.sem(pfx) for _ in range(2)]
    sxi = [p.sem(pfx) for _ in range(2)]
    sst = [p.sem(pfx) for _ in range(2)]
    wg1_sb = p.sbuf(pfx + "wg1", [128, NCH * 64], BF16)
    wg2_sb = p.sbuf(pfx + "wg2", [64, D], BF16)
    wo_sb = p.sbuf(pfx + "wo", [128, NCH * D], BF16)
    bg1_sb = p.sbuf(pfx + "bg1", [64, 1], F32)
    bg2_sb = p.sbuf(pfx + "bg2", [128, 8], F32)
    gm_sb = p.sbuf(pfx + "gm", [128, 8], F32)
    bt_sb = p.sbuf(pfx + "bt", [128, 8], F32)
    p.dma("pool", lambda e: e.dma_start(
        out=fm3(wg1_sb[:], 64), in_=wg1.rearrange("(c p) m -> p c m", p=128)),
        wait=tuple(deps), inc=swp)
    p.dma("pool", lambda e: e.dma_start(out=wg2_sb[:], in_=wg2), inc=swp)
    tok_wp = p.dma("pool", lambda e: e.dma_start(
        out=fm3(wo_sb[:], D), in_=wout.rearrange("(c p) m -> p c m", p=128)), inc=swp)
    p.dma("sp", lambda e: e.dma_start(out=bg1_sb[:], in_=bg1h),
          wait=tuple(deps), inc=sws)
    p.dma("sp", lambda e: e.dma_start(out=bg2_sb[:], in_=bg2h), inc=sws)
    p.dma("sp", lambda e: e.dma_start(out=gm_sb[:], in_=gmh), inc=sws)
    tok_ws = p.dma("sp", lambda e: e.dma_start(out=bt_sb[:], in_=bth), inc=sws)
    at_b = [p.sbuf(pfx + "at", [128, NCH * SLAB], BF16) for _ in range(2)]
    xn_b = [p.sbuf(pfx + "xn", [128, NCH * SLAB], BF16) for _ in range(2)]
    xi_b = [p.sbuf(pfx + "xi", [128, NCH * SLAB], F32) for _ in range(2)]
    g1_sb = p.sbuf(pfx + "g1", [64, SLAB], BF16)
    gate_sb = p.sbuf(pfx + "gate", [128, NCH * SLAB], BF16)
    ag = p.sbuf(pfx + "ag", [128, NCH * SLAB], F32)
    sq = p.sbuf(pfx + "sq", [128, NCH * SLAB], F32)
    ln_sb = p.sbuf(pfx + "ln", [128, NCH * SLAB], BF16)
    rows = p.sbuf(pfx + "r", [1, 6 * SLAB], F32)
    ps1 = p.psum(pfx + "p1", [128, SLAB], F32)
    ps2 = [p.psum(pfx + "p2", [128, SLAB], F32) for _ in range(2)]
    ps_mean = p.psum(pfx + "pm", [1, SLAB], F32)
    ps_sq = p.psum(pfx + "ps", [1, SLAB], F32)
    ps_A = p.psum(pfx + "pa", [128, SLAB], F32)
    ps_B = p.psum(pfx + "pb", [128, SLAB], F32)
    in_free = [None, None]
    st_free = [None, None]
    xn_free = [None, None]
    ps2_free = [None, None]
    tok_g1rd = None
    tok_gate_rd = None
    tok_stats = None
    tok_x = None
    tok_st = None
    for s in range(NSLAB):
        b = s % 2
        t0 = s * SLAB
        w0 = tuple(deps) if s < 2 else ()
        if in_free[b] is not None:
            w0 = w0 + (in_free[b],)
        tok_at = p.dma("sp", lambda e, b=b, t0=t0: e.dma_start(
            out=fm3(at_b[b][:], SLAB), in_=hbm_slabT(attn, t0, SLAB)), wait=w0, inc=sa[b])
        tok_xn = p.dma("sp", lambda e, b=b, t0=t0: e.dma_start(
            out=fm3(xn_b[b][:], SLAB), in_=hbm_slabT(xn2, t0, SLAB)),
            wait=w0 + ((xn_free[b],) if xn_free[b] is not None else ()), inc=sx[b])
        tok_xi = p.dma("sp", lambda e, b=b, t0=t0: e.dma_start(
            out=fm3(xi_b[b][:], SLAB), in_=hbm_slabT(x_in, t0, SLAB)),
            wait=w0 + ((st_free[b],) if st_free[b] is not None else ()), inc=sxi[b])
        wmm = (tok_xn,)
        if s == 0:
            wmm = wmm + (tok_wp, tok_ws)
        if tok_g1rd is not None:
            wmm = wmm + (tok_g1rd,)
        tmm = None
        for kc in range(NCH):
            tmm = p.op("pe", lambda e, kc=kc, b=b: e.matmul(
                ps1[0:64, :], wg1_sb[:, kc * 64:(kc + 1) * 64],
                xn_b[b][:, kc * SLAB:(kc + 1) * SLAB],
                start=(kc == 0), stop=(kc == NCH - 1)),
                wait=wmm if kc == 0 else (), inc=spe if kc == NCH - 1 else None)
        tok_g1 = p.op("dve", lambda e: e.tensor_scalar_add(
            g1_sb[:], ps1[0:64, :], bg1_sb[:, 0:1]), wait=(tmm,), inc=sdve)
        tok_g1rd = tok_g1
        xn_free[b] = tmm
        last_gate = None
        for mc in range(NCH):
            pp = mc % 2
            wg = (tok_g1,)
            if ps2_free[pp] is not None:
                wg = wg + (ps2_free[pp],)
            if mc == 0 and tok_gate_rd is not None:
                wg = wg + (tok_gate_rd,)
            tg = p.op("pe", lambda e, mc=mc, pp=pp: e.matmul(
                ps2[pp][:], wg2_sb[:, mc * 128:(mc + 1) * 128], g1_sb[:],
                start=True, stop=True), wait=wg, inc=spe)
            last_gate = p.op("act", lambda e, mc=mc, pp=pp: e.activation(
                out=gate_sb[:, mc * SLAB:(mc + 1) * SLAB], in_=ps2[pp][:],
                func=AF.Sigmoid, bias=bg2_sb[:, mc:mc + 1]), wait=(tg,), inc=sact)
            ps2_free[pp] = last_gate
        tok_ag = p.op("dve", lambda e, b=b: e.tensor_mul(
            ag[:], at_b[b][:], gate_sb[:]), wait=(last_gate, tok_at), inc=sdve)
        tok_gate_rd = tok_ag
        tok_sq = p.op("act", lambda e: e.activation(
            out=sq[:], in_=ag[:], func=AF.Square), wait=(tok_ag,), inc=sact)
        toks = {"pe": spe, "act": sact, "dve": sdve,
                "sq_chunk": lambda cc: sq[:, cc * SLAB:(cc + 1) * SLAB]}
        tok_rows, tok_bc, tok_stats = emit_ln_rows(
            p, toks, pfx, c,
            lambda cc: ag[:, cc * SLAB:(cc + 1) * SLAB],
            NCH, SLAB, ps_mean, ps_sq, ps_A, ps_B, rows,
            tok_sq, tok_ag, waitAB=(tok_x,) if tok_x is not None else (),
            wait_stats=(tok_stats,) if tok_stats is not None else ())
        wx = (tok_bc,)
        ag3 = fm3(ag[:], SLAB)
        sq3 = fm3(sq[:], SLAB)
        tmul = p.op("dve", lambda e, ag3=ag3, sq3=sq3: e.tensor_tensor(
            sq3, ag3, bcast_mid(ps_A[:], NCH), ALU.mult), wait=wx, inc=sdve)
        tadd = p.op("dve", lambda e, sq3=sq3: e.tensor_tensor(
            sq3, sq3, bcast_mid(ps_B[:], NCH), ALU.add), wait=(tmul,), inc=sdve)
        for cc in range(NCH):
            sq_c = sq[:, cc * SLAB:(cc + 1) * SLAB]
            ln_c = ln_sb[:, cc * SLAB:(cc + 1) * SLAB]
            tok_x = p.op("dve", lambda e, a=sq_c, o=ln_c, cc=cc: e.tensor_scalar(
                o, a, gm_sb[:, cc:cc + 1], bt_sb[:, cc:cc + 1], ALU.mult, ALU.add),
                wait=(tadd,) if cc == 0 else (), inc=sdve if cc == NCH - 1 else None)
        last_ev = None
        for mc in range(NCH):
            pp = mc % 2
            wmm = (tok_x,) if mc == 0 else ()
            if ps2_free[pp] is not None:
                wmm = wmm + (ps2_free[pp],)
            tmm = None
            for kc in range(NCH):
                lhs = wo_sb[:, kc * D + mc * 128: kc * D + (mc + 1) * 128]
                rhs = ln_sb[:, kc * SLAB:(kc + 1) * SLAB]
                tmm = p.op("pe", lambda e, l=lhs, r=rhs, pp=pp, kc=kc: e.matmul(
                    ps2[pp][:], l, r, start=(kc == 0), stop=(kc == NCH - 1)),
                    wait=wmm if kc == 0 else (), inc=spe if kc == NCH - 1 else None)
            wev = (tmm, tok_xi)
            if mc == 0 and st_free[b] is not None:
                wev = wev + (st_free[b],)
            last_ev = p.op("dve", lambda e, b=b, mc=mc, pp=pp: e.tensor_add(
                xi_b[b][:, mc * SLAB:(mc + 1) * SLAB], ps2[pp][:],
                xi_b[b][:, mc * SLAB:(mc + 1) * SLAB]), wait=wev, inc=sdve)
            ps2_free[pp] = last_ev
        in_free[b] = last_ev
        tok_st = p.dma("sp", lambda e, b=b, t0=t0: e.dma_start(
            out=hbm_slabT(x_out, t0, SLAB), in_=fm3(xi_b[b][:], SLAB)),
            wait=(last_ev,), inc=sst[b])
        st_free[b] = tok_st
    return [tok_st]



def emit_moe1(p, c, xn2, wfm, x_in, y_out, w0h, w1h, w2h, b0h, b1h, b2h,
              deps, pfx="m1"):
    NE = 8
    HB = 2
    SEG = 1024
    NSEG = T // SEG
    SL = SEG // SLAB
    spe, sact, sdve, spool = p.sem(pfx), p.sem(pfx), p.sem(pfx), p.sem(pfx)
    sw0, sw1, sw2 = p.sem(pfx), p.sem(pfx), p.sem(pfx)
    swf, sb = p.sem(pfx), p.sem(pfx)
    sx = [p.sem(pfx) for _ in range(2)]
    sxi = [p.sem(pfx) for _ in range(2)]
    sst = [p.sem(pfx) for _ in range(2)]
    acc = p.sbuf(pfx + "acc", [128, NCH * SEG], F32)
    xn_b = [p.sbuf(pfx + "xn", [128, NCH * SLAB], BF16) for _ in range(2)]
    wf_h = p.sbuf(pfx + "wf", [8, SEG], BF16)
    w0_sb = p.sbuf(pfx + "w0", [128, NCH * 2048], BF16)
    w1_sb = p.sbuf(pfx + "w1", [128, NCH * 2048], BF16)
    w2_sb = p.sbuf(pfx + "w2", [128, 16 * D], BF16)
    b0_sb = p.sbuf(pfx + "b0", [128, 128], F32)
    b1_sb = p.sbuf(pfx + "b1", [128, 128], F32)
    b2_sb = p.sbuf(pfx + "b2", [8, D], BF16)
    act_sb = [p.sbuf(pfx + "act", [128, 16 * SLAB], BF16) for _ in range(2)]
    silu_sb = p.sbuf(pfx + "sl", [128, 2 * SLAB], BF16)
    t_sb = p.sbuf(pfx + "t", [128, 2 * SLAB], BF16)
    wbc_sb = p.sbuf(pfx + "wbc", [128, SEG], BF16)
    ps_h = [p.psum(pfx + "ph", [128, SLAB], F32) for _ in range(4)]
    ps_o = [p.psum(pfx + "po", [128, SLAB], F32) for _ in range(4)]
    p.dma("sp", lambda e: e.dma_start(out=b0_sb[:], in_=b0h),
          wait=tuple(deps), inc=sb)
    p.dma("sp", lambda e: e.dma_start(out=b1_sb[:], in_=b1h), inc=sb)
    tok_bs = (sb, sb.v)
    sb2 = p.sem(pfx)
    tok_b2 = p.dma("pool", lambda e: e.dma_start(out=b2_sb[:], in_=b2h),
                   wait=tuple(deps), inc=sb2)

    w01_free = None
    w2_free = None
    tok_w01 = {}
    tok_w2 = {}
    units = [(seg, e) for seg in range(NSEG) for e in range(NE)]

    def load_w01(i, first=False):
        seg, e = units[i]
        w = tuple(deps) if i == 0 else ()
        if w01_free is not None:
            w = w + (w01_free,)
        ta_ = p.dma("pool", lambda e_, e=e: e_.dma_start(
            out=fm3(w0_sb[:], 2048),
            in_=w0h[e].rearrange("(c p) m -> p c m", p=128)), wait=w, inc=sw0)
        tb_ = p.dma("pool", lambda e_, e=e: e_.dma_start(
            out=fm3(w1_sb[:], 2048),
            in_=w1h[e].rearrange("(c p) m -> p c m", p=128)), inc=sw1)
        tok_w01[i] = (ta_, tb_)

    def load_w2(i):
        seg, e = units[i]
        w = tuple(deps) if i == 0 else ()
        if w2_free is not None:
            w = w + (w2_free,)
        tok_w2[i] = p.dma("pool", lambda e_, e=e: e_.dma_start(
            out=fm3(w2_sb[:], D, nch=16),
            in_=w2h[e].rearrange("(c p) m -> p c m", p=128)), wait=w, inc=sw2)

    load_w01(0, first=True)
    load_w2(0)
    ps_h_free = [None] * 4
    ps_o_free = [None] * 4
    wbc_w_free = None
    wbc_r_free = None
    act_free = [None, None]
    xn_free = [None, None]
    tok_acc = None
    tok_seg_done = None
    tok_st = None
    for seg in range(NSEG):
        h0 = seg * SEG
        wfw = () if tok_seg_done is None else (tok_seg_done,)
        tok_wf = p.dma("sp", lambda e, h0=h0: e.dma_start(
            out=wf_h[:], in_=wfm[0:8, h0:h0 + SEG]),
            wait=wfw + (tuple(deps) if seg == 0 else ()), inc=swf)
        tok_accfree = tok_acc
        for e in range(NE):
            ui = seg * NE + e
            tok_wbc = None
            for sl2 in range(SL):
                wb = (tok_wf,)
                if ps_h_free[0] is not None:
                    wb = wb + (ps_h_free[0],)
                if wbc_r_free is not None:
                    wb = wb + (wbc_r_free,)
                tbc = p.op("pe", lambda e_, e=e, sl2=sl2: e_.matmul(
                    ps_h[0][:], c["oh8"][:, e * 128:(e + 1) * 128],
                    wf_h[0:8, sl2 * SLAB:(sl2 + 1) * SLAB], start=True, stop=True),
                    wait=wb, inc=spe)
                tok_wbc = p.op("dve", lambda e_, sl2=sl2: e_.tensor_copy(
                    out=wbc_sb[:, sl2 * SLAB:(sl2 + 1) * SLAB], in_=ps_h[0][:]),
                    wait=(tbc,), inc=sdve)
                ps_h_free[0] = tok_wbc
            tok_xns = []
            for s in range(SL):
                xb = s % 2
                wxn = tuple(deps) if (seg == 0 and e == 0) else ()
                if xn_free[xb] is not None:
                    wxn = wxn + (xn_free[xb],)
                tok_xns.append(p.dma("sp", lambda e_, xb=xb, t0=s * SLAB, h0=h0: e_.dma_start(
                    out=fm3(xn_b[xb][:], SLAB), in_=hbm_slabT(xn2, h0 + t0, SLAB)),
                    wait=wxn, inc=sx[xb]))
            tok_act_s = [None, None]
            prev_tt = [None, None]
            last_up = None
            for hq in range(16):
                for kc in range(NCH):
                    for sp2 in range(SL):
                        l0 = w0_sb[:, kc * 2048 + hq * 128: kc * 2048 + (hq + 1) * 128]
                        l1 = w1_sb[:, kc * 2048 + hq * 128: kc * 2048 + (hq + 1) * 128]
                        rhs = xn_b[sp2][:, kc * SLAB:(kc + 1) * SLAB]
                        wk = ()
                        if hq == 0 and kc == 0 and sp2 == 0:
                            wk = tok_w01[ui] + (tok_bs, tok_wbc)
                            if act_free[0] is not None:
                                wk = wk + (act_free[0], act_free[1])
                        if kc == 0 and sp2 == 0:
                            wk = wk + tuple(t for t in tok_xns)
                        if kc == 0 and ps_h_free[sp2 * 2] is not None:
                            wk = wk + (ps_h_free[sp2 * 2],)
                        p.op("pe", lambda e_, l=l0, r=rhs, b_=sp2 * 2, kc=kc: e_.matmul(
                            ps_h[b_][:], l, r, start=(kc == 0), stop=(kc == NCH - 1)),
                            wait=wk, inc=None)
                        wk2 = ()
                        if kc == 0 and ps_h_free[sp2 * 2 + 1] is not None:
                            wk2 = (ps_h_free[sp2 * 2 + 1],)
                        last_up = p.op("pe", lambda e_, l=l1, r=rhs, b_=sp2 * 2 + 1, kc=kc: e_.matmul(
                            ps_h[b_][:], l, r, start=(kc == 0), stop=(kc == NCH - 1)),
                            wait=wk2, inc=spe if kc == NCH - 1 and sp2 == SL - 1 else None)
                bcol = e * 16 + hq
                for sp2 in range(SL):
                    wsl = (last_up,)
                    if prev_tt[sp2] is not None:
                        wsl = wsl + (prev_tt[sp2],)
                    ts = p.op("act", lambda e_, bcol=bcol, b_=sp2 * 2 + 1, sp2=sp2: e_.activation(
                        out=silu_sb[:, sp2 * SLAB:(sp2 + 1) * SLAB], in_=ps_h[b_][:],
                        func=AF.Silu, bias=b1_sb[:, bcol:bcol + 1]), wait=wsl, inc=sact)
                    ps_h_free[sp2 * 2 + 1] = ts
                    tt = p.op("dve", lambda e_, bcol=bcol, b_=sp2 * 2, sp2=sp2: e_.scalar_tensor_tensor(
                        t_sb[:, sp2 * SLAB:(sp2 + 1) * SLAB], ps_h[b_][:],
                        b0_sb[:, bcol:bcol + 1],
                        silu_sb[:, sp2 * SLAB:(sp2 + 1) * SLAB],
                        ALU.add, ALU.mult), wait=(ts,), inc=sdve)
                    prev_tt[sp2] = tt
                    ps_h_free[sp2 * 2] = tt
                    tok_act_s[sp2] = p.op("pool", lambda e_, hq=hq, sp2=sp2, t0=sp2 * SLAB: e_.tensor_mul(
                        act_sb[sp2][:, hq * SLAB:(hq + 1) * SLAB],
                        t_sb[:, sp2 * SLAB:(sp2 + 1) * SLAB],
                        wbc_sb[:, t0:t0 + SLAB]), wait=(tt,), inc=spool)
            for sp2 in range(SL):
                xn_free[sp2] = last_up
            wbc_r_free = tok_act_s[SL - 1]
            for s in range(SL):
                t0 = s * SLAB
                ab = s
                tok_act = tok_act_s[s]
                for sub in range(2):
                    ts0 = t0 + sub * 256
                    for mc in range(NCH):
                        po = mc % 4
                        reg = ps_o[po][:, (mc // 4) * 256:(mc // 4) * 256 + 256]
                        wmm = (tok_act,)
                        if mc == 0 and s == 0 and sub == 0:
                            wmm = wmm + (tok_w2[ui],)
                        if ps_o_free[po] is not None:
                            wmm = wmm + (ps_o_free[po],)
                        tmm = None
                        for kq in range(16):
                            lhs = w2_sb[:, kq * D + mc * 128: kq * D + (mc + 1) * 128]
                            rhs = act_sb[ab][:, kq * SLAB + sub * 256: kq * SLAB + sub * 256 + 256]
                            tmm = p.op("pe", lambda e_, l=lhs, r=rhs, reg=reg, kq=kq, e2=e: e_.matmul(
                                reg, l, r, start=(kq == 0),
                                stop=(kq == 15 and e2 != 0)),
                                wait=wmm if kq == 0 else (),
                                inc=spe if (kq == 15 and e != 0) else None)
                        if e == 0:
                            tmm = p.op("pe", lambda e_, mc=mc, ts0=ts0, reg=reg: e_.matmul(
                                reg, b2_sb[0:8, mc * 128:(mc + 1) * 128],
                                wf_h[0:8, ts0:ts0 + 256], start=False, stop=True),
                                wait=(tok_b2,), inc=spe)
                        if mc >= 4:
                            av = acc[:].rearrange("p (g cq t) -> p g cq t", g=2, cq=4)[
                                :, :, mc - 4, t0 + sub * 256: t0 + sub * 256 + 256]
                            if e == 0:
                                wacc = (tmm,)
                                if tok_accfree is not None:
                                    wacc = wacc + (tok_accfree,)
                                tok_acc = p.op("dve", lambda e_, po=po, av=av: e_.tensor_scalar(
                                    av, ps_o[po][:].rearrange("p (g t) -> p g t", g=2),
                                    1.0, None, ALU.mult), wait=wacc, inc=sdve)
                            else:
                                tok_acc = p.op("dve", lambda e_, po=po, av=av: e_.scalar_tensor_tensor(
                                    av, ps_o[po][:].rearrange("p (g t) -> p g t", g=2),
                                    1.0, av, ALU.mult, ALU.add), wait=(tmm,), inc=sdve)
                            ps_o_free[po] = tok_acc
                act_free[ab] = tmm
            if ui + 1 < len(units):
                w01_free = last_up
                load_w01(ui + 1)
                w2_free = tmm
                load_w2(ui + 1)
        last = None
        for s in range(SL):
            t0g = h0 + s * SLAB
            t0 = s * SLAB
            b = s % 2
            aview = acc[:].rearrange("p (c t) -> p c t", c=NCH)[:, :, t0:t0 + SLAB]
            tok_ra = p.dma("pool", lambda e, av=aview, t0g=t0g: e.dma_start(
                out=av, in_=hbm_slabT(x_in, t0g, SLAB), accum_op=ALU.add),
                wait=(tok_acc,), inc=sxi[b])
            tok_acc = tok_ra
            last = p.dma("sp", lambda e, av=aview, t0g=t0g: e.dma_start(
                out=hbm_slabT(y_out, t0g, SLAB), in_=av), wait=(tok_ra,), inc=sst[b])
            tok_acc = last
        tok_st = last
        tok_seg_done = tok_st
    return [tok_st]



def _stage(inputs):
    f32 = lambda a: np.asarray(a, np.float32)
    wT = lambda a: np.ascontiguousarray(f32(a).T)
    pp = lambda v: np.ascontiguousarray(f32(v).reshape(8, 128).T)
    bpp = lambda b: np.ascontiguousarray(
        f32(b).reshape(8, 16, 128).transpose(2, 0, 1).reshape(128, 128))
    st = dict(make_const_arrays())
    st["w_qkv0"] = np.ascontiguousarray(np.concatenate(
        [wT(inputs["L0_attn_Wq"]), wT(inputs["L0_attn_Wk"]),
         wT(inputs["L0_attn_Wv"])], axis=1))
    st["w_o0"] = wT(inputs["L0_attn_Wo"])
    st["rtr0_w"] = wT(inputs["L0_ffn_router_w"])
    st["up0"] = np.ascontiguousarray(np.concatenate(
        [f32(inputs["L0_ffn_shared_up"]).transpose(0, 2, 1),
         f32(inputs["L0_ffn_routed_up"]).transpose(0, 2, 1)]))
    st["dn0"] = np.ascontiguousarray(np.concatenate(
        [f32(inputs["L0_ffn_shared_down"]).transpose(0, 2, 1),
         f32(inputs["L0_ffn_routed_down"]).transpose(0, 2, 1)]))
    st["w_qkv1"] = wT(inputs["L1_attn_w_qkv"])
    st["wg1"] = wT(inputs["L1_attn_w_g1"])
    st["wg2"] = wT(inputs["L1_attn_w_g2"])
    st["wout"] = wT(inputs["L1_attn_w_out"])
    st["bg1"] = np.ascontiguousarray(f32(inputs["L1_attn_b_g1"])[:, None])
    st["bg2"] = pp(inputs["L1_attn_b_g2"])
    st["gm"] = pp(inputs["L1_attn_gamma"])
    st["bt"] = pp(inputs["L1_attn_beta"])
    st["rtr1_w"] = wT(inputs["L1_ffn_gate_w"])
    st["rtr1_b"] = np.ascontiguousarray(f32(inputs["L1_ffn_gate_b"])[:, None])
    st["w0h"] = np.ascontiguousarray(f32(inputs["L1_ffn_w0"]).transpose(0, 2, 1))
    st["w1h"] = np.ascontiguousarray(f32(inputs["L1_ffn_w1"]).transpose(0, 2, 1))
    st["w2h"] = np.ascontiguousarray(f32(inputs["L1_ffn_w2"]).transpose(0, 2, 1))
    st["b0h"] = bpp(inputs["L1_ffn_b0"])
    st["b1h"] = bpp(inputs["L1_ffn_b1"])
    st["b2h"] = np.ascontiguousarray(f32(inputs["L1_ffn_b2"]))
    return st


def _run_op(build, st, per_core, out_names):
    p = Prog()
    build(p)
    nc = p.finalize()
    in_maps = []
    for i in range(8):
        m = {k: st[k] for k in st if k in p.ext_inputs}
        m.update(per_core[i])
        in_maps.append(m)
    res = run_bass_kernel_spmd(nc, in_maps, core_ids=list(range(8)))
    return {nm: [res.results[i][nm] for i in range(8)] for nm in out_names}


FUSED = True


def kernel(**inputs):
    if FUSED:
        return _kernel_fused(inputs)
    return _kernel_unfused(inputs)


def _kernel_unfused(inputs):
    st = _stage(inputs)
    x = np.asarray(inputs["L0_attn_x"], np.float32)
    xfm = [np.ascontiguousarray(x[i].T).reshape(NCH, 128, T) for i in range(8)]

    FMF = [NCH, 128, T]

    def ext(p, name, shape, dt):
        p.ext_inputs.add(name)
        return p.dram(name, shape, dt, kind="ExternalInput")

    def fin(p, toks):
        p.op("sp", lambda e: None, wait=tuple(toks))

    def b_ln(src_is, dst, gamma=None):
        def b(p):
            c = emit_consts(p)
            src = ext(p, "src", FMF, F32)
            d = p.dram(dst, FMF, BF16, kind="ExternalOutput")
            fin(p, emit_ln(p, c, src, d, [c["tok"], c["tok2"]]))
        return b

    def b_qkv(kind, wname):
        def b(p):
            c = emit_consts(p)
            xn = ext(p, "xn", FMF, BF16)
            w = ext(p, wname, [D, 3 * D], F32)
            qd = p.dram("qd", FMF, BF16, kind="ExternalOutput")
            kd = p.dram("kd", FMF, BF16, kind="ExternalOutput")
            vd = p.dram("vd", FMF, BF16, kind="ExternalOutput")
            fin(p, emit_qkv(p, c, xn, qd, kd, vd, w, kind, [c["tok"], c["tok2"]]))
        return b

    def b_attn(denom):
        def b(p):
            c = emit_consts(p)
            qd = ext(p, "qd", FMF, BF16)
            kd = ext(p, "kd", FMF, BF16)
            vd = ext(p, "vd", FMF, BF16)
            ad = p.dram("ad", FMF, BF16, kind="ExternalOutput")
            fin(p, emit_attn(p, c, qd, kd, vd, ad, denom, [c["tok"], c["tok2"]]))
        return b

    def b_wo(p):
        c = emit_consts(p)
        a = ext(p, "a", FMF, BF16)
        xi = ext(p, "xi", FMF, F32)
        w = ext(p, "w_o0", [D, D], F32)
        xo = p.dram("xo", FMF, F32, kind="ExternalOutput")
        fin(p, emit_proj_residual(p, c, a, xi, xo, w, [c["tok"], c["tok2"]]))

    def b_rtr(mode, wname, bias=None):
        E = 16 if mode == "top4x16" else 8
        def b(p):
            c = emit_consts(p)
            xn = ext(p, "xn", FMF, BF16)
            w = ext(p, wname, [D, E], F32)
            gd = p.dram("gd", [E, T], BF16, kind="ExternalOutput")
            deps = [c["tok"], c["tok2"]]
            bias_sb = None
            if bias is not None:
                bh = ext(p, bias, [E, 1], F32)
                bias_sb = p.sbuf("bias", [E, 1], F32)
                sbm = p.sem("bs")
                deps.append(p.dma("sp", lambda e: e.dma_start(out=bias_sb[:], in_=bh), inc=sbm))
            fin(p, emit_router(p, c, xn, gd, w, mode, deps, bias_sb=bias_sb))
        return b

    def b_moe0(p):
        c = emit_consts(p)
        xn = ext(p, "xn", FMF, BF16)
        gfm = ext(p, "gfm", [16, T], BF16)
        xi = ext(p, "xi", FMF, F32)
        up = ext(p, "up0", [18, D, 512], F32)
        dn = ext(p, "dn0", [18, 512, D], F32)
        xo = p.dram("xo", FMF, F32, kind="ExternalOutput")
        fin(p, emit_moe0(p, c, xn, gfm, xi, xo, up, dn, [c["tok"], c["tok2"]]))

    def b_pa1(p):
        c = emit_consts(p)
        at = ext(p, "at", FMF, BF16)
        xn = ext(p, "xn", FMF, BF16)
        xi = ext(p, "xi", FMF, F32)
        wg1 = ext(p, "wg1", [D, 64], F32)
        wg2 = ext(p, "wg2", [64, D], F32)
        wout = ext(p, "wout", [D, D], F32)
        bg1 = ext(p, "bg1", [64, 1], F32)
        bg2 = ext(p, "bg2", [128, 8], F32)
        gm = ext(p, "gm", [128, 8], F32)
        bt = ext(p, "bt", [128, 8], F32)
        xo = p.dram("xo", FMF, F32, kind="ExternalOutput")
        fin(p, emit_postattn1(p, c, at, xn, xi, xo, wg1, wg2, wout,
                              bg1, bg2, gm, bt, [c["tok"], c["tok2"]]))

    def b_moe1(p):
        c = emit_consts(p)
        xn = ext(p, "xn", FMF, BF16)
        wfm = ext(p, "wfm", [8, T], BF16)
        xi = ext(p, "xi", FMF, F32)
        w0 = ext(p, "w0h", [8, D, 2048], F32)
        w1 = ext(p, "w1h", [8, D, 2048], F32)
        w2 = ext(p, "w2h", [8, 2048, D], F32)
        b0 = ext(p, "b0h", [128, 128], F32)
        b1 = ext(p, "b1h", [128, 128], F32)
        b2 = ext(p, "b2h", [8, D], F32)
        yo = p.dram("yo", FMF, F32, kind="ExternalOutput")
        fin(p, emit_moe1(p, c, xn, wfm, xi, yo, w0, w1, w2, b0, b1, b2,
                         [c["tok"], c["tok2"]]))

    pc = lambda key, arrs: [{key: arrs[i]} for i in range(8)]

    def pc2(**kw):
        return [{k: v[i] for k, v in kw.items()} for i in range(8)]

    r = _run_op(b_ln("src", "dst"), st, pc("src", xfm), ["dst"])
    xn0 = r["dst"]
    r = _run_op(b_qkv("elu1", "w_qkv0"), st, pc("xn", xn0), ["qd", "kd", "vd"])
    r = _run_op(b_attn(True), st, pc2(qd=r["qd"], kd=r["kd"], vd=r["vd"]), ["ad"])
    a0 = r["ad"]
    r = _run_op(b_wo, st, pc2(a=a0, xi=xfm), ["xo"])
    x1 = r["xo"]
    r = _run_op(b_rtr("top4x16", "rtr0_w"), st, pc("xn", xn0), ["gd"])
    g0 = r["gd"]
    r = _run_op(b_moe0, st, pc2(xn=xn0, gfm=g0, xi=x1), ["xo"])
    x2 = r["xo"]
    r = _run_op(b_ln("src", "dst"), st, pc("src", x2), ["dst"])
    xn1 = r["dst"]
    r = _run_op(b_qkv("silu", "w_qkv1"), st, pc("xn", xn1), ["qd", "kd", "vd"])
    r = _run_op(b_attn(False), st, pc2(qd=r["qd"], kd=r["kd"], vd=r["vd"]), ["ad"])
    a1 = r["ad"]
    r = _run_op(b_rtr("top2x8", "rtr1_w", bias="rtr1_b"), st, pc("xn", xn1), ["gd"])
    w1g = r["gd"]
    r = _run_op(b_pa1, st, pc2(at=a1, xn=xn1, xi=x2), ["xo"])
    x3 = r["xo"]
    r = _run_op(b_moe1, st, pc2(xn=xn1, wfm=w1g, xi=x3), ["yo"])
    y = r["yo"]
    out = np.empty((8, T, D), np.float32)
    for i in range(8):
        out[i] = np.asarray(y[i], np.float32).reshape(D, T).T
    return out


def _kernel_fused(inputs):
    st = _stage(inputs)
    x = np.asarray(inputs["L0_attn_x"], np.float32)
    xfm = [np.ascontiguousarray(x[i].T).reshape(NCH, 128, T) for i in range(8)]
    FMF = [NCH, 128, T]
    p = Prog()

    def ext(name, shape, dt=F32):
        p.ext_inputs.add(name)
        return p.dram(name, shape, dt, kind="ExternalInput")

    c = emit_consts(p)
    x0 = ext("x0", FMF)
    w_qkv0 = ext("w_qkv0", [D, 3 * D])
    w_o0 = ext("w_o0", [D, D])
    rtr0_w = ext("rtr0_w", [D, 16])
    up0 = ext("up0", [18, D, 512])
    dn0 = ext("dn0", [18, 512, D])
    w_qkv1 = ext("w_qkv1", [D, 3 * D])
    wg1 = ext("wg1", [D, 64])
    wg2 = ext("wg2", [64, D])
    wout = ext("wout", [D, D])
    bg1 = ext("bg1", [64, 1])
    bg2 = ext("bg2", [128, 8])
    gm = ext("gm", [128, 8])
    bt = ext("bt", [128, 8])
    rtr1_w = ext("rtr1_w", [D, 8])
    rtr1_b = ext("rtr1_b", [8, 1])
    w0h = ext("w0h", [8, D, 2048])
    w1h = ext("w1h", [8, D, 2048])
    w2h = ext("w2h", [8, 2048, D])
    b0h = ext("b0h", [128, 128])
    b1h = ext("b1h", [128, 128])
    b2h = ext("b2h", [8, D])
    xn0 = p.dram("s_xn0", FMF, BF16)
    qd = p.dram("s_q", FMF, BF16)
    kd = p.dram("s_k", FMF, BF16)
    vd = p.dram("s_v", FMF, BF16)
    a0 = p.dram("s_a0", FMF, BF16)
    x1 = p.dram("s_x1", FMF, F32)
    g0 = p.dram("s_g0", [16, T], BF16)
    x2 = p.dram("s_x2", FMF, F32)
    xn1 = p.dram("s_xn1", FMF, BF16)
    qd1 = p.dram("s_q1", FMF, BF16)
    kd1 = p.dram("s_k1", FMF, BF16)
    vd1 = p.dram("s_v1", FMF, BF16)
    a1 = p.dram("s_a1", FMF, BF16)
    w1g = p.dram("s_w1g", [8, T], BF16)
    x3 = p.dram("s_x3", FMF, F32)
    y = p.dram("y", FMF, F32, kind="ExternalOutput")

    dep0 = [c["tok"], c["tok2"]]
    with p.scope():
        t_ln0 = emit_ln(p, c, x0, xn0, dep0, pfx="ln0")
    with p.scope():
        t_qkv0 = emit_qkv(p, c, xn0, qd, kd, vd, w_qkv0, "elu1", t_ln0, pfx="q0")
    with p.scope():
        t_at0 = emit_attn(p, c, qd, kd, vd, a0, True, t_qkv0, pfx="a0")
    with p.scope():
        t_wo0 = emit_proj_residual(p, c, a0, x0, x1, w_o0, t_at0, pfx="w0")
    with p.scope():
        t_rt0 = emit_router(p, c, xn0, g0, rtr0_w, "top4x16", t_wo0, pfx="r0")
    with p.scope():
        t_m0 = emit_moe0(p, c, xn0, g0, x1, x2, up0, dn0, t_rt0, pfx="m0")
    with p.scope():
        t_ln1 = emit_ln(p, c, x2, xn1, t_m0, pfx="ln1")
    with p.scope():
        t_qkv1 = emit_qkv(p, c, xn1, qd1, kd1, vd1, w_qkv1, "silu", t_ln1, pfx="q1")
    with p.scope():
        t_at1 = emit_attn(p, c, qd1, kd1, vd1, a1, False, t_qkv1, pfx="a1")
    with p.scope():
        t_rt1 = emit_router(p, c, xn1, w1g, rtr1_w, "top2x8", t_at1,
                            bias_sb=None, bias_hbm=rtr1_b, pfx="r1")
    with p.scope():
        t_pa1 = emit_postattn1(p, c, a1, xn1, x2, x3, wg1, wg2, wout,
                               bg1, bg2, gm, bt, t_rt1, pfx="pa")
    with p.scope():
        t_m1 = emit_moe1(p, c, xn1, w1g, x3, y, w0h, w1h, w2h, b0h, b1h, b2h,
                         t_pa1, pfx="m1")
    p.op("sp", lambda e: None, wait=tuple(t_m1))
    nc = p.finalize()
    in_maps = []
    for i in range(8):
        m = {k: st[k] for k in st if k in p.ext_inputs}
        m["x0"] = xfm[i]
        in_maps.append(m)
    res = run_bass_kernel_spmd(nc, in_maps, core_ids=list(range(8)))
    out = np.empty((8, T, D), np.float32)
    for i in range(8):
        out[i] = np.asarray(res.results[i]["y"], np.float32).reshape(D, T).T
    return out


# version 12
# speedup vs baseline: 1.1072x; 1.0578x over previous
import contextlib
from contextlib import contextmanager
import numpy as np
import concourse.bass as bass
import concourse.mybir as mybir
from concourse.bass_utils import run_bass_kernel_spmd

F32 = mybir.dt.float32
BF16 = mybir.dt.bfloat16
U8 = mybir.dt.uint8
AF = mybir.ActivationFunctionType
ALU = mybir.AluOpType
AX = mybir.AxisListType

T = 4096
D = 1024
NCH = 8
H = 16
HD = 64
CH = 128
NCHUNK = T // CH
SLAB = 512
NSLAB = T // SLAB
HALF = T // 2
EPS_LN = 1e-5
EPS_ATTN = 1e-12
NEG_BIG = -1e30
DBG_SKIP = set()


class Sem:
    __slots__ = ("h", "v")

    def __init__(self, h):
        self.h = h
        self.v = 0

    @property
    def tok(self):
        return (self, self.v)


class Prog:
    ENGINES = ("pe", "act", "dve", "pool", "sp")

    def __init__(self):
        self.nc = bass.Bass("TRN2", target_bir_lowering=False)
        self.stack = contextlib.ExitStack()
        self.sem_stack = contextlib.ExitStack()
        self.sched = {e: [] for e in self.ENGINES}
        self._names = {}
        self.all_sems = []
        self.ext_inputs = set()
        self.sem_free = []
        self._scope_sems = None

    @contextmanager
    def scope(self):
        old = self.stack
        self.stack = contextlib.ExitStack()
        outer_scope = self._scope_sems
        self._scope_sems = []
        try:
            yield
        finally:
            self.stack.close()
            self.stack = old
            self.sem_free.extend(self._scope_sems)
            self._scope_sems = outer_scope

    def name(self, base):
        n = self._names.get(base, 0)
        self._names[base] = n + 1
        return "%s_%d" % (base, n)

    def sem(self, base="s"):
        if self._scope_sems is not None and self.sem_free:
            sm = self.sem_free.pop()
        else:
            sm = Sem(self.sem_stack.enter_context(self.nc.semaphore(self.name(base))))
            self.all_sems.append(sm)
        if self._scope_sems is not None:
            self._scope_sems.append(sm)
        return sm

    def sbuf(self, base, shape, dt):
        return self.stack.enter_context(self.nc.sbuf_tensor(self.name(base), shape, dt))

    def psum(self, base, shape, dt):
        return self.stack.enter_context(self.nc.psum_tensor(self.name(base), shape, dt))

    def dram(self, name, shape, dt, kind=None):
        if kind is None:
            return self.nc.dram_tensor(name, shape, dt).ap()
        return self.nc.dram_tensor(name, shape, dt, kind=kind).ap()

    def _emit(self, eng, thunk, wait, inc, amt):
        tok = None
        if inc is not None:
            inc.v += amt
            tok = (inc, inc.v)
        self.sched[eng].append((tuple(wait), thunk, (inc, amt) if inc is not None else None))
        return tok

    def op(self, eng, thunk, wait=(), inc=None):
        return self._emit(eng, thunk, wait, inc, 1)

    def dma(self, eng, thunk, wait=(), inc=None):
        return self._emit(eng, thunk, wait, inc, 16)

    def finalize(self):
        tail = tuple((sm, sm.v) for sm in self.all_sems if sm.v > 0)
        self.sched["sp"].append((tail, lambda e: None, None))
        sched = self.sched

        def runner(eng_name):
            def go(eng):
                last = {}
                for waits, thunk, inc in sched[eng_name]:
                    for sem, val in waits:
                        if last.get(id(sem), -1) < val:
                            eng.wait_ge(sem.h, val)
                            last[id(sem)] = val
                    inst = thunk(eng)
                    if inc is not None:
                        sem, amt = inc
                        inst.then_inc(sem.h, amt)
            return go

        with self.nc.Block() as block:
            block.tensor(runner("pe"))
            block.scalar(runner("act"))
            block.vector(runner("dve"))
            block.gpsimd(runner("pool"))
            block.sync(runner("sp"))
        self.stack.close()
        self.sem_stack.close()
        return self.nc


def bcast_mid(ap2d, n):
    return bass.AP(ap2d.tensor, ap2d.offset,
                   [list(ap2d.ap[0]), [0, n], list(ap2d.ap[1])])


def fm3(sb_ap, width, nch=NCH):
    return sb_ap.rearrange("p (c t) -> p c t", c=nch)


def hbm_slabT(hbm, t0, w):
    return hbm[:, :, t0:t0 + w].transpose([1, 0, 2])



def make_const_arrays():
    c = {}
    c["c_ident"] = np.eye(128, dtype=np.float32)
    c["c_ones_col"] = np.ones((128, 1), dtype=np.float32)
    c["c_ones_row"] = np.ones((1, 128), dtype=np.float32)
    c["c_triu4"] = np.tile(np.triu(np.ones((128, 128), dtype=np.float32)), (1, 4))
    oh16 = np.zeros((16, 16 * 128), dtype=np.float32)
    for e in range(16):
        oh16[e, e * 128:(e + 1) * 128] = 1.0
    c["c_oh16"] = oh16
    oh8 = np.zeros((8, 8 * 128), dtype=np.float32)
    for e in range(8):
        oh8[e, e * 128:(e + 1) * 128] = 1.0
    c["c_oh8"] = oh8
    kb = np.zeros((128, 8), dtype=np.float32)
    kb[:, 0] = EPS_LN
    kb[:, 1] = EPS_ATTN
    c["c_kb"] = kb
    return c


def emit_consts(p):
    c = {}
    hb = {}
    for k, v in make_const_arrays().items():
        p.ext_inputs.add(k)
        hb[k] = p.dram(k, list(v.shape), F32, kind="ExternalInput")
    c["ident"] = p.sbuf("ident", [128, 128], BF16)
    c["ones_col_bf"] = p.sbuf("ones_col_bf", [128, 1], BF16)
    c["ones_col_f"] = p.sbuf("ones_col_f", [128, 1], F32)
    c["ones_row_f"] = p.sbuf("ones_row_f", [1, 128], F32)
    c["triu4"] = p.sbuf("triu4", [128, 512], F32)
    c["oh16"] = p.sbuf("oh16", [16, 2048], BF16)
    c["oh8"] = p.sbuf("oh8", [8, 1024], BF16)
    c["kb"] = p.sbuf("kb", [128, 8], F32)
    c["identf"] = p.sbuf("identf", [128, 128], F32)
    s1 = p.sem("csemp")
    s2 = p.sem("csems")
    p.dma("pool", lambda e: e.dma_start(out=c["ident"][:], in_=hb["c_ident"]), inc=s1)
    p.dma("pool", lambda e: e.dma_start(out=c["ones_col_bf"][:], in_=hb["c_ones_col"]), inc=s1)
    p.dma("pool", lambda e: e.dma_start(out=c["oh16"][:], in_=hb["c_oh16"]), inc=s1)
    p.dma("pool", lambda e: e.dma_start(out=c["oh8"][:], in_=hb["c_oh8"]), inc=s1)
    p.dma("sp", lambda e: e.dma_start(out=c["ones_col_f"][:], in_=hb["c_ones_col"]), inc=s2)
    p.dma("sp", lambda e: e.dma_start(out=c["ones_row_f"][:], in_=hb["c_ones_row"]), inc=s2)
    p.dma("sp", lambda e: e.dma_start(out=c["triu4"][:], in_=hb["c_triu4"]), inc=s2)
    p.dma("sp", lambda e: e.dma_start(out=c["kb"][:], in_=hb["c_kb"]), inc=s2)
    p.dma("sp", lambda e: e.dma_start(out=c["identf"][:], in_=hb["c_ident"]), inc=s2)
    c["tok"] = (s1, s1.v)
    c["tok2"] = (s2, s2.v)
    return c



def emit_ln_rows(p, toks, pfx, c, xs_chunk, nch, w, ps_mean, ps_sq, ps_A, ps_B,
                 rows, dep_sq_tok, dep_ld_tok, waitAB=(), wait_stats=(),
                 defer_bcast=False):
    spe, sact, sdve = toks["pe"], toks["act"], toks["dve"]
    t = None
    for cc in range(nch):
        w0 = ((dep_ld_tok,) if dep_ld_tok is not None else ()) + tuple(wait_stats)
        t = p.op("pe", lambda e, cc=cc: e.matmul(
            ps_mean[:], c["ones_col_f"][:], xs_chunk(cc),
            start=(cc == 0), stop=(cc == nch - 1)),
            wait=w0 if cc == 0 else (), inc=spe if cc == nch - 1 else None)
    tok_mean = t
    sq_chunk = toks["sq_chunk"]
    for cc in range(nch):
        t = p.op("pe", lambda e, cc=cc: e.matmul(
            ps_sq[:], c["ones_col_f"][:], sq_chunk(cc),
            start=(cc == 0), stop=(cc == nch - 1)),
            wait=(dep_sq_tok,) if cc == 0 else (), inc=spe if cc == nch - 1 else None)
    tok_sq = t
    mu, m2, musq, var, rstd, bb = (rows[0:1, i * w:(i + 1) * w] for i in range(6))
    tmu = p.op("act", lambda e: e.activation(out=mu, in_=ps_mean[:], func=AF.Copy, scale=1.0 / D),
         wait=(tok_mean,), inc=sact)
    p.op("act", lambda e: e.activation(out=m2, in_=ps_sq[:], func=AF.Copy, scale=1.0 / D),
         wait=(tok_sq,))
    tok_stats = p.op("act", lambda e: e.activation(out=musq, in_=mu, func=AF.Square),
                     wait=(tmu,), inc=sact)
    td = p.op("dve", lambda e: e.tensor_sub(var, m2, musq), wait=(tok_stats,), inc=sdve)
    ta = p.op("act", lambda e: e.activation(out=var, in_=var, func=AF.Ln, bias=c["kb"][0:1, 0:1]),
              wait=(td,), inc=sact)
    ta2 = p.op("act", lambda e: e.activation(out=rstd, in_=var, func=AF.Exp, scale=-0.5),
               wait=(ta,), inc=sact)
    tok_rows = p.op("dve", lambda e: e.scalar_tensor_tensor(
        bb, mu, -1.0, rstd, ALU.mult, ALU.mult), wait=(ta2,), inc=sdve)
    if toks.get("defer_bcast") or defer_bcast:
        return tok_rows, None, tok_stats
    p.op("pe", lambda e: e.matmul(ps_A[:], c["ones_row_f"][:], rstd, start=True, stop=True),
         wait=(tok_rows,) + tuple(waitAB))
    tok_bc = p.op("pe", lambda e: e.matmul(ps_B[:], c["ones_row_f"][:], bb, start=True, stop=True),
                  inc=spe)
    return tok_rows, tok_bc, tok_stats


def emit_ln(p, c, src, dst, deps, pfx="ln", gamma_sb=None, beta_sb=None):
    spe, sact, sdve = p.sem(pfx), p.sem(pfx), p.sem(pfx)
    sld = [p.sem(pfx) for _ in range(2)]
    sst = [p.sem(pfx) for _ in range(2)]
    xs = [p.sbuf(pfx + "x", [128, NCH * SLAB], F32) for _ in range(2)]
    xq = [p.sbuf(pfx + "q", [128, NCH * SLAB], F32) for _ in range(2)]
    xo = [p.sbuf(pfx + "o", [128, NCH * SLAB], BF16) for _ in range(2)]
    rows = [p.sbuf(pfx + "r", [1, 6 * SLAB], F32) for _ in range(2)]
    ps_mean = [p.psum(pfx + "pm", [1, SLAB], F32) for _ in range(2)]
    ps_sq = [p.psum(pfx + "ps", [1, SLAB], F32) for _ in range(2)]
    ps_A = [p.psum(pfx + "pa", [128, SLAB], F32) for _ in range(2)]
    ps_B = [p.psum(pfx + "pb", [128, SLAB], F32) for _ in range(2)]
    xs_free = [None, None]
    xo_free = [None, None]
    tok_x2 = [None, None]
    tok_stats = [None, None]
    tok_rows_s = [None, None]
    tok_st = None
    for s in range(NSLAB + 1):
        b = s % 2
        if s < NSLAB:
            t0 = s * SLAB
            w = tuple(deps) if s < 2 else ()
            if xs_free[b] is not None:
                w = w + (xs_free[b],)
            tok_ld = p.dma("sp", lambda e, b=b, t0=t0: e.dma_start(
                out=fm3(xs[b][:], SLAB), in_=hbm_slabT(src, t0, SLAB)), wait=w,
                inc=sld[b])
            tok_sq = p.op("act", lambda e, b=b: e.activation(
                out=xq[b][:], in_=xs[b][:], func=AF.Square), wait=(tok_ld,), inc=sact)
            toks = {"pe": spe, "act": sact, "dve": sdve,
                    "sq_chunk": lambda cc, b=b: xq[b][:, cc * SLAB:(cc + 1) * SLAB]}
            tok_rows, tok_bc_unused, tok_stats2 = emit_ln_rows(
                p, toks, pfx, c,
                lambda cc, b=b: xs[b][:, cc * SLAB:(cc + 1) * SLAB],
                NCH, SLAB, ps_mean[b], ps_sq[b], ps_A[b], ps_B[b], rows[b],
                tok_sq, tok_ld,
                waitAB=(tok_x2[b],) if tok_x2[b] is not None else (),
                wait_stats=(tok_stats[b],) if tok_stats[b] is not None else (),
                defer_bcast=True)
            tok_stats[b] = tok_stats2
            tok_rows_s[b] = tok_rows
        if s >= 1:
            sb_ = (s - 1) % 2
            t0p = (s - 1) * SLAB
            p.op("pe", lambda e, b=sb_: e.matmul(
                ps_A[b][:], c["ones_row_f"][:], rows[b][0:1, 4 * SLAB:5 * SLAB],
                start=True, stop=True),
                wait=(tok_rows_s[sb_],) + ((tok_x2[sb_],) if tok_x2[sb_] is not None else ()))
            tok_bc = p.op("pe", lambda e, b=sb_: e.matmul(
                ps_B[b][:], c["ones_row_f"][:], rows[b][0:1, 5 * SLAB:6 * SLAB],
                start=True, stop=True), inc=spe)
            wx = (tok_bc,)
            if xo_free[sb_] is not None:
                wx = wx + (xo_free[sb_],)
            xs3 = fm3(xs[sb_][:], SLAB)
            xq3 = fm3(xq[sb_][:], SLAB)
            xo3 = fm3(xo[sb_][:], SLAB)
            tmul = p.op("dve", lambda e, b=sb_, xs3=xs3, xq3=xq3: e.tensor_tensor(
                xq3, xs3, bcast_mid(ps_A[b][:], NCH), ALU.mult), wait=wx, inc=sdve)
            if gamma_sb is None:
                tok_x = p.op("dve", lambda e, b=sb_, xq3=xq3, xo3=xo3: e.tensor_tensor(
                    xo3, xq3, bcast_mid(ps_B[b][:], NCH), ALU.add),
                    wait=(tmul,), inc=sdve)
            else:
                tadd = p.op("dve", lambda e, b=sb_, xq3=xq3: e.tensor_tensor(
                    xq3, xq3, bcast_mid(ps_B[b][:], NCH), ALU.add),
                    wait=(tmul,), inc=sdve)
                tok_x = None
                for cc in range(NCH):
                    xq_c = xq[sb_][:, cc * SLAB:(cc + 1) * SLAB]
                    xo_c = xo[sb_][:, cc * SLAB:(cc + 1) * SLAB]
                    tok_x = p.op("dve", lambda e, a=xq_c, o=xo_c, cc=cc: e.tensor_scalar(
                        o, a, gamma_sb[:, cc:cc + 1], beta_sb[:, cc:cc + 1],
                        ALU.mult, ALU.add), wait=(tadd,) if cc == 0 else (),
                        inc=sdve if cc == NCH - 1 else None)
            xs_free[sb_] = tok_x
            tok_x2[sb_] = tok_x
            tok_st = p.dma("sp", lambda e, b=sb_, t0p=t0p: e.dma_start(
                out=hbm_slabT(dst, t0p, SLAB), in_=fm3(xo[b][:], SLAB)),
                wait=(tok_x,), inc=sst[sb_])
            xo_free[sb_] = tok_st
    return [tok_st]




def emit_qkv(p, c, xn, qd, kd, vd, w_hbm, kind, deps, pfx="qkv"):
    spe, sact, sdve, spool = p.sem(pfx), p.sem(pfx), p.sem(pfx), p.sem(pfx)
    sw, sx = p.sem(pfx), p.sem(pfx)
    sst = [p.sem(pfx) for _ in range(4)]
    w_sb = p.sbuf(pfx + "w", [128, NCH * 3072], BF16)
    tok_w = p.dma("pool", lambda e: e.dma_start(
        out=fm3(w_sb[:], 3072), in_=w_hbm.rearrange("(c p) m -> p c m", p=128)),
        wait=tuple(deps), inc=sw)
    xn_sb = p.sbuf(pfx + "x", [128, NCH * T], BF16)
    tok_x = p.dma("sp", lambda e: e.dma_start(
        out=fm3(xn_sb[:], T), in_=xn.transpose([1, 0, 2])), wait=tuple(deps), inc=sx)
    nps = 4
    pss = [p.psum(pfx + "p", [128, SLAB], F32) for _ in range(nps)]
    ps_free = [None] * nps
    sc1 = [p.sbuf(pfx + "s1", [128, SLAB], F32) for _ in range(2)]
    sc2 = [p.sbuf(pfx + "s2", [128, SLAB], F32) for _ in range(2)]
    ob = [p.sbuf(pfx + "o", [128, SLAB], BF16) for _ in range(4)]
    ob_free = [None] * 4
    sc_free = [None, None]
    tok_st = None
    i = 0
    for mc in range(24):
        dest = (qd, kd, vd)[mc // 8]
        cc_out = mc % 8
        act = kind if mc < 16 else "copy"
        for s in range(NSLAB):
            t0 = s * SLAB
            pb = i % nps
            w0 = (tok_w, tok_x) if i == 0 else ()
            if ps_free[pb] is not None:
                w0 = w0 + (ps_free[pb],)
            tmm = None
            for kc in range(NCH):
                lhs = w_sb[:, kc * 3072 + mc * 128: kc * 3072 + (mc + 1) * 128]
                rhs = xn_sb[:, kc * T + t0: kc * T + t0 + SLAB]
                tmm = p.op("pe", lambda e, l=lhs, r=rhs, pb=pb, kc=kc: e.matmul(
                    pss[pb][:], l, r, start=(kc == 0), stop=(kc == NCH - 1)),
                    wait=w0 if kc == 0 else (), inc=spe if kc == NCH - 1 else None)
            oi = i % 4
            wo = (tmm,)
            if ob_free[oi] is not None:
                wo = wo + (ob_free[oi],)
            if act == "silu":
                tok_ev = p.op("act", lambda e, pb=pb, oi=oi: e.activation(
                    out=ob[oi][:], in_=pss[pb][:], func=AF.Silu), wait=wo, inc=sact)
                ps_free[pb] = tok_ev
            elif act == "copy":
                tok_ev = p.op("act", lambda e, pb=pb, oi=oi: e.copy(
                    out=ob[oi][:], in_=pss[pb][:]), wait=wo, inc=sact)
                ps_free[pb] = tok_ev
            else:
                si = i % 2
                if sc_free[si] is not None:
                    wo = wo + (sc_free[si],)
                t1 = p.op("act", lambda e, pb=pb, si=si: e.activation(
                    out=sc1[si][:], in_=pss[pb][:], func=AF.Exp),
                    wait=wo, inc=sact)
                t2 = p.op("dve", lambda e, si=si: e.tensor_single_scalar(
                    sc1[si][:], sc1[si][:], 1.0, ALU.min), wait=(t1,), inc=sdve)
                t3 = p.op("dve", lambda e, pb=pb, si=si: e.tensor_single_scalar(
                    sc2[si][:], pss[pb][:], 0.0, ALU.max), wait=(t2,), inc=sdve)
                ps_free[pb] = t3
                tok_ev = p.op("pool", lambda e, si=si, oi=oi: e.tensor_add(
                    ob[oi][:], sc1[si][:], sc2[si][:]), wait=(t2, t3), inc=spool)
                sc_free[si] = tok_ev
            tok_st = p.dma("sp", lambda e, d=dest, cc=cc_out, t0=t0, oi=oi: e.dma_start(
                out=d[cc, :, t0:t0 + SLAB], in_=ob[oi][:]), wait=(tok_ev,), inc=sst[oi])
            ob_free[oi] = tok_st
            i += 1
    return [tok_st]



def emit_proj_residual(p, c, a_in, x_in, x_out, w_hbm, deps, pfx="wo"):
    spe, sact, sdve = p.sem(pfx), p.sem(pfx), p.sem(pfx)
    sw = p.sem(pfx)
    sa = [p.sem(pfx) for _ in range(2)]
    sxi = [p.sem(pfx) for _ in range(2)]
    sst = [p.sem(pfx) for _ in range(2)]
    w_sb = p.sbuf(pfx + "w", [128, NCH * D], BF16)
    tok_w = p.dma("pool", lambda e: e.dma_start(
        out=fm3(w_sb[:], D), in_=w_hbm.rearrange("(c p) m -> p c m", p=128)),
        wait=tuple(deps), inc=sw)
    ab = [p.sbuf(pfx + "a", [128, NCH * SLAB], BF16) for _ in range(2)]
    xb = [p.sbuf(pfx + "xi", [128, NCH * SLAB], F32) for _ in range(2)]
    ob = [p.sbuf(pfx + "xo", [128, NCH * SLAB], F32) for _ in range(2)]
    pss = [p.psum(pfx + "p", [128, SLAB], F32) for _ in range(4)]
    ps_free = [None] * 4
    in_free = [None, None]
    ob_free = [None, None]
    tok_st = None
    for s in range(NSLAB):
        b = s % 2
        t0 = s * SLAB
        w0 = tuple(deps) if s < 2 else ()
        if in_free[b] is not None:
            w0 = w0 + (in_free[b],)
        tok_a = p.dma("sp", lambda e, b=b, t0=t0: e.dma_start(
            out=fm3(ab[b][:], SLAB), in_=hbm_slabT(a_in, t0, SLAB)), wait=w0, inc=sa[b])
        tok_xi = p.dma("sp", lambda e, b=b, t0=t0: e.dma_start(
            out=fm3(xb[b][:], SLAB), in_=hbm_slabT(x_in, t0, SLAB)), inc=sxi[b])
        last_ev = None
        for mc in range(NCH):
            pb = mc % 4
            wmm = (tok_a,) if mc == 0 else ()
            wmm = wmm + ((tok_w,) if s == 0 and mc == 0 else ())
            if ps_free[pb] is not None:
                wmm = wmm + (ps_free[pb],)
            tmm = None
            for kc in range(NCH):
                lhs = w_sb[:, kc * D + mc * 128: kc * D + (mc + 1) * 128]
                rhs = ab[b][:, kc * SLAB: (kc + 1) * SLAB]
                tmm = p.op("pe", lambda e, l=lhs, r=rhs, pb=pb, kc=kc: e.matmul(
                    pss[pb][:], l, r, start=(kc == 0), stop=(kc == NCH - 1)),
                    wait=wmm if kc == 0 else (), inc=spe if kc == NCH - 1 else None)
            wev = (tmm, tok_xi)
            if mc == 0 and ob_free[b] is not None:
                wev = wev + (ob_free[b],)
            last_ev = p.op("dve", lambda e, b=b, mc=mc, pb=pb: e.tensor_add(
                ob[b][:, mc * SLAB:(mc + 1) * SLAB], pss[pb][:],
                xb[b][:, mc * SLAB:(mc + 1) * SLAB]), wait=wev, inc=sdve)
            ps_free[pb] = last_ev
        in_free[b] = last_ev
        tok_st = p.dma("sp", lambda e, b=b, t0=t0: e.dma_start(
            out=hbm_slabT(x_out, t0, SLAB), in_=fm3(ob[b][:], SLAB)),
            wait=(last_ev,), inc=sst[b])
        ob_free[b] = tok_st
    return [tok_st]



def emit_attn(p, c, qd, kd, vd, ad, denom, deps, pfx="at"):
    spe, sact, sdve = p.sem(pfx), p.sem(pfx), p.sem(pfx)
    sq = [p.sem(pfx) for _ in range(2)]
    sk = [p.sem(pfx) for _ in range(2)]
    sv = [p.sem(pfx) for _ in range(2)]
    sst = [p.sem(pfx) for _ in range(2)]
    qs = [p.sbuf(pfx + "q", [128, NCH * CH], BF16) for _ in range(2)]
    ks = [p.sbuf(pfx + "k", [128, NCH * CH], BF16) for _ in range(2)]
    vs = [p.sbuf(pfx + "v", [128, NCH * CH], BF16) for _ in range(2)]
    ktm = p.sbuf(pfx + "ktm", [128, NCH * CH], BF16)
    vtm = p.sbuf(pfx + "vtm", [128, NCH * CH], BF16)
    S_sb = p.sbuf(pfx + "S", [128, 512], BF16)
    z_sb = p.sbuf(pfx + "z", [128, 8], BF16)
    sc_sb = [p.sbuf(pfx + "sc", [128, 512], BF16) for _ in range(2)]
    nm_sb = [p.sbuf(pfx + "nm", [128, 256], F32) for _ in range(2)]
    rden = p.sbuf(pfx + "rd", [128, 256], F32)
    at_sb = [p.sbuf(pfx + "o", [128, NCH * CH], BF16) for _ in range(2)]
    ps_S = p.psum(pfx + "pS", [128, 512], F32)
    ps_sc = [p.psum(pfx + "psc", [128, 512], F32) for _ in range(2)]
    ps_nm = [p.psum(pfx + "pnm", [128, 512], F32) for _ in range(2)]
    ps_tr = p.psum(pfx + "ptr", [128, 1024], BF16)
    if denom:
        ps_z = p.psum(pfx + "pz", [128, 512], F32)

    def hslice(sb, h):
        pb = (h % 2) * 64
        return sb[pb:pb + 64, (h // 2) * CH:(h // 2) * CH + CH]

    SCCOL = [0, 256, 128, 384]

    t0s = tuple(deps)
    tok_S0 = p.op("dve", lambda e: e.memset(S_sb[:], 0.0), wait=t0s, inc=sdve)
    tok_psS0 = p.op("dve", lambda e: e.memset(ps_S[:], 0.0), inc=sdve)
    tok_psz0 = None
    if denom:
        tok_S0 = p.op("dve", lambda e: e.memset(z_sb[:], 0.0), inc=sdve)
        tok_psz0 = p.op("dve", lambda e: e.memset(ps_z[:, 0:8], 0.0), inc=sdve)
    qkv_free = [None, None]
    scfree = None
    scb_free = [None, None]
    nmfree = [None, None]
    at_free = [None, None]
    tok_trfree = None
    tok_Ssb = tok_S0
    tok_zsb = tok_S0
    tok_Supd_prev = None
    tok_zupd_prev = None
    tok_st = None
    for ci in range(NCHUNK):
        b = ci % 2
        c0 = ci * CH
        w0 = t0s if ci < 2 else ()
        if qkv_free[b] is not None:
            w0 = w0 + (qkv_free[b],)
        tok_q = p.dma("sp", lambda e, b=b, c0=c0: e.dma_start(
            out=fm3(qs[b][:], CH), in_=hbm_slabT(qd, c0, CH)), wait=w0, inc=sq[b])
        tok_k = p.dma("sp", lambda e, b=b, c0=c0: e.dma_start(
            out=fm3(ks[b][:], CH), in_=hbm_slabT(kd, c0, CH)), inc=sk[b])
        tok_v = p.dma("sp", lambda e, b=b, c0=c0: e.dma_start(
            out=fm3(vs[b][:], CH), in_=hbm_slabT(vd, c0, CH)), inc=sv[b])
        wtr = (tok_k,) + ((tok_trfree,) if tok_trfree is not None else ())
        tk = None
        if "tr" in DBG_SKIP:
            tok_ktm = p.op("dve", lambda e: e.memset(ktm[:], 0.001), wait=wtr, inc=sdve)
            tok_vtm = p.op("dve", lambda e: e.memset(vtm[:], 0.001), wait=(tok_v,), inc=sdve)
            tok_trfree = tok_vtm
            dbg_skip_tr = True
        else:
            dbg_skip_tr = False
        if not dbg_skip_tr:
            for cc in range(NCH):
                tk = p.op("pe", lambda e, b=b, cc=cc: e.transpose(
                    ps_tr[:, cc * CH:(cc + 1) * CH], ks[b][:, cc * CH:(cc + 1) * CH],
                    c["ident"][:]), wait=wtr if cc == 0 else (),
                    inc=spe if cc == NCH - 1 else None)
            tok_ktm = p.op("dve", lambda e: e.tensor_copy(out=ktm[:], in_=ps_tr[:]),
                           wait=(tk,), inc=sdve)
            tv = None
            for cc in range(NCH):
                tv = p.op("pe", lambda e, b=b, cc=cc: e.transpose(
                    ps_tr[:, cc * CH:(cc + 1) * CH], vs[b][:, cc * CH:(cc + 1) * CH],
                    c["ident"][:]), wait=(tok_v, tok_ktm) if cc == 0 else (),
                    inc=spe if cc == NCH - 1 else None)
            tok_vtm = p.op("dve", lambda e: e.tensor_copy(out=vtm[:], in_=ps_tr[:]),
                           wait=(tv,), inc=sdve)
            tok_trfree = tok_vtm
        ep_toks = []
        for g in range(4):
            b2 = g % 2
            heads = [4 * g + j for j in range(4)]
            wsc = (tok_q,)
            if scfree is not None:
                wsc = wsc + (scfree,)
            tsc = None
            if "sc" in DBG_SKIP:
                tsc = p.op("pe", lambda e: e.matmul(
                    ps_sc[0][:], c["ident"][:], qs[b][:, 0:512], start=True,
                    stop=True), wait=wsc, inc=spe)
            else:
              for j, h in enumerate(heads):
                pb = (h % 2) * 64
                tsc = p.op("pe", lambda e, b=b, j=j, h=h, pb=pb: e.matmul(
                    ps_sc[j % 2][:, (j // 2) * CH:(j // 2 + 1) * CH],
                    hslice(ks[b], h), hslice(qs[b], h), start=True, stop=True,
                    tile_position=(pb, 0)), wait=wsc if j == 0 else (),
                    inc=spe if j == 3 else None)
            wev = (tsc,)
            if scb_free[b2] is not None:
                wev = wev + (scb_free[b2],)
            p.op("dve", lambda e, b2=b2: e.tensor_mul(
                sc_sb[b2][:, 0:256], ps_sc[0][:, 0:256], c["triu4"][:, 0:256]),
                wait=wev)
            tok_scev = p.op("dve", lambda e, b2=b2: e.tensor_mul(
                sc_sb[b2][:, 256:512], ps_sc[1][:, 0:256], c["triu4"][:, 0:256]),
                inc=sdve)
            if denom:
                wd = (tok_scev, tok_zsb)
                td = None
                for j, h in enumerate(heads):
                    pb = (h % 2) * 64
                    cb = j // 2
                    dreg = ps_sc[b2][pb:pb + 64, 256 + cb * CH:256 + (cb + 1) * CH]
                    zsl = z_sb[pb:pb + 64, (h // 2):(h // 2) + 1]
                    zrep = bass.AP(zsl.tensor, zsl.offset,
                                   [list(zsl.ap[0]), [0, 64]])
                    ones64 = bass.AP(c["ones_col_bf"][:].tensor, 0,
                                     [[1, 128], [0, 64]])
                    td = p.op("pe", lambda e, b=b, o=dreg, z=zrep, h=h, pb=pb: e.matmul(
                        o, z, hslice(qs[b], h),
                        start=True, stop=False, tile_position=(pb, pb)),
                        wait=wd if j == 0 else (), inc=None)
                    td = p.op("pe", lambda e, o=dreg, n1=ones64, j=j, b2=b2, pb=pb: e.matmul(
                        o, n1, sc_sb[b2][:, SCCOL[j]:SCCOL[j] + CH],
                        start=False, stop=True, tile_position=(0, pb)),
                        inc=spe if j == 3 else None)
                tl = p.op("act", lambda e, b2=b2: e.activation(
                    out=rden[:], in_=ps_sc[b2][:, 256:512], func=AF.Ln),
                    wait=(td,), inc=sact)
                tok_rden = p.op("act", lambda e: e.activation(
                    out=rden[:], in_=rden[:], func=AF.Exp, scale=-1.0),
                    wait=(tl,), inc=sact)
            wnm = (tok_scev, tok_Ssb, tok_vtm)
            if nmfree[b2] is not None:
                wnm = wnm + (nmfree[b2],)
            tnm = None
            if "nm" in DBG_SKIP:
                tnm = p.op("pe", lambda e, b2=b2: e.matmul(
                    ps_nm[b2][:], c["ident"][:], qs[b][:, 0:512], start=True,
                    stop=True), wait=wnm, inc=spe)
            else:
              for j, h in enumerate(heads):
                pb = (h % 2) * 64
                cb = j // 2
                out_ap = ps_nm[b2][pb:pb + 64, cb * CH:(cb + 1) * CH]
                tnm = p.op("pe", lambda e, o=out_ap, h=h, pb=pb, b=b, j=j: e.matmul(
                    o, S_sb[pb:pb + 64, (h // 2) * 64:(h // 2) * 64 + 64],
                    hslice(qs[b], h), start=True, stop=False,
                    tile_position=(pb, pb)), wait=wnm if j == 0 else (), inc=None)
                tnm = p.op("pe", lambda e, o=out_ap, h=h, pb=pb, j=j, b2=b2: e.matmul(
                    o, vtm[:, h * 64:(h + 1) * 64],
                    sc_sb[b2][:, SCCOL[j]:SCCOL[j] + CH], start=False, stop=True,
                    tile_position=(0, pb)), inc=spe if j == 3 else None)
            if denom:
                tok_nmev = p.op("act", lambda e, b2=b2: e.copy(
                    out=nm_sb[b2][:], in_=ps_nm[b2][:, 0:256]), wait=(tnm,), inc=sact)
                nmfree[b2] = tok_nmev
                wep = (tok_nmev, tok_rden)
                if g < 2 and at_free[b] is not None:
                    wep = wep + (at_free[b],)
                tmul = p.op("dve", lambda e, b2=b2: e.tensor_mul(
                    nm_sb[b2][:], nm_sb[b2][:], rden[:]), wait=wep,
                    inc=sdve)
                tep = p.op("dve", lambda e, b=b, g=g, b2=b2: e.tensor_single_scalar(
                    at_sb[b][:, 2 * g * CH:(2 * g + 2) * CH], nm_sb[b2][:],
                    EPS_ATTN, ALU.add), wait=(tmul,), inc=sdve)
                scfree = tok_rden
                scb_free[b2] = tnm
            else:
                wep = (tnm,)
                if g < 2 and at_free[b] is not None:
                    wep = wep + (at_free[b],)
                tep = p.op("act", lambda e, b=b, g=g, b2=b2: e.copy(
                    out=at_sb[b][:, 2 * g * CH:(2 * g + 2) * CH],
                    in_=ps_nm[b2][:, 0:256]), wait=wep, inc=sact)
                nmfree[b2] = tep
                scfree = tok_scev
                scb_free[b2] = tnm
            ep_toks.append(tep)
        wS = (tok_ktm, tok_vtm)
        if ci == 0:
            wS = wS + (tok_psS0,)
        if tok_Ssb is not None:
            wS = wS + (tok_Ssb,)
        tS = None
        for h in range(H):
            pb = (h % 2) * 64
            tS = p.op("pe", lambda e, h=h, pb=pb, ci=ci: e.matmul(
                ps_S[pb:pb + 64, (h // 2) * 64:(h // 2) * 64 + 64],
                ktm[:, h * 64:(h + 1) * 64], vtm[:, h * 64:(h + 1) * 64],
                start=(ci == 0 and h == 0), stop=(ci == NCHUNK - 1 and h == H - 1),
                skip_group_check=True,
                tile_position=(0, pb)), wait=wS if h == 0 else (),
                inc=spe if h == H - 1 else None)
        tok_Supd = tS
        if denom:
            wz = (tok_zsb,)
            if ci == 0:
                wz = wz + (tok_psz0,)
            tz = None
            for h in range(H):
                pb = (h % 2) * 64
                tz = p.op("pe", lambda e, h=h, pb=pb, ci=ci: e.matmul(
                    ps_z[pb:pb + 64, (h // 2):(h // 2) + 1],
                    ktm[:, h * 64:(h + 1) * 64], c["ones_col_bf"][:],
                    start=(ci == 0 and h == 0),
                    stop=(ci == NCHUNK - 1 and h == H - 1),
                    skip_group_check=True,
                    tile_position=(0, pb)), wait=wz if h == 0 else (),
                    inc=spe if h == H - 1 else None)
            tok_zupd = tz
        if ci < NCHUNK - 1:
            tok_Ssb = p.op("act", lambda e: e.copy(out=S_sb[:], in_=ps_S[:]),
                           wait=(tok_Supd,), inc=sact)
            if denom:
                tok_zsb = p.op("act", lambda e: e.copy(
                    out=z_sb[:], in_=ps_z[:, 0:8]), wait=(tok_zupd,), inc=sact)
        qkv_free[b] = tok_Supd
        tok_st = p.dma("sp", lambda e, b=b, c0=c0: e.dma_start(
            out=hbm_slabT(ad, c0, CH), in_=fm3(at_sb[b][:], CH)),
            wait=(ep_toks[-1],), inc=sst[b])
        at_free[b] = tok_st
    return [tok_st]



def emit_router(p, c, xn, gd, w_hbm, mode, deps, bias_sb=None, bias_hbm=None,
                pfx="rt"):
    E = 16 if mode == "top4x16" else 8
    TOPK = 4 if mode == "top4x16" else 2
    spe, sact, sdve = p.sem(pfx), p.sem(pfx), p.sem(pfx)
    sx = [p.sem(pfx) for _ in range(2)]
    sw = p.sem(pfx)
    sst = [p.sem(pfx) for _ in range(2)]
    w_sb = p.sbuf(pfx + "w", [128, NCH * E], BF16)
    tok_w = p.dma("pool", lambda e: e.dma_start(
        out=fm3(w_sb[:], E), in_=w_hbm.rearrange("(c p) m -> p c m", p=128)),
        wait=tuple(deps), inc=sw)
    if bias_hbm is not None:
        bias_sb = p.sbuf(pfx + "bias", [E, 1], F32)
        sbm = p.sem(pfx)
        deps = list(deps) + [p.dma("sp", lambda e: e.dma_start(
            out=bias_sb[:], in_=bias_hbm), wait=tuple(deps), inc=sbm)]
    xs = [p.sbuf(pfx + "x", [128, NCH * SLAB], BF16) for _ in range(2)]
    lg = p.sbuf(pfx + "lg", [E, SLAB], F32)
    tm = p.sbuf(pfx + "tm", [128, 4 * E], F32)
    gt = p.sbuf(pfx + "gt", [128, 4 * E], F32)
    sc1 = p.sbuf(pfx + "s1", [128, 4 * E], F32)
    sc2 = p.sbuf(pfx + "s2", [128, 4 * E], F32)
    rowm = p.sbuf(pfx + "rm", [128, 8], F32)
    go = [p.sbuf(pfx + "go", [E, SLAB], BF16) for _ in range(2)]
    ps_l = p.psum(pfx + "pl", [128, 512], F32)
    ps_f = p.psum(pfx + "pf", [128, 512], F32)
    ps_b = p.psum(pfx + "pb", [128, 512], F32)
    xs_free = [None, None]
    go_free = [None, None]
    tok_lgread = None
    tok_tm = None
    tok_go = None
    tok_st = None
    for s in range(NSLAB):
        b = s % 2
        t0 = s * SLAB
        w0 = tuple(deps) if s < 2 else ()
        if xs_free[b] is not None:
            w0 = w0 + (xs_free[b],)
        tok_x = p.dma("sp", lambda e, b=b, t0=t0: e.dma_start(
            out=fm3(xs[b][:], SLAB), in_=hbm_slabT(xn, t0, SLAB)), wait=w0, inc=sx[b])
        wmm = (tok_x,) + ((tok_w,) if s == 0 else ())
        if tok_lgread is not None:
            wmm = wmm + (tok_lgread,)
        tmm = None
        for kc in range(NCH):
            tmm = p.op("pe", lambda e, kc=kc, b=b: e.matmul(
                ps_l[0:E, :], w_sb[:, kc * E:(kc + 1) * E],
                xs[b][:, kc * SLAB:(kc + 1) * SLAB],
                start=(kc == 0), stop=(kc == NCH - 1)),
                wait=wmm if kc == 0 else (), inc=spe if kc == NCH - 1 else None)
        xs_free[b] = tmm
        if bias_sb is None:
            tok_lg = p.op("dve", lambda e: e.tensor_copy(out=lg[:], in_=ps_l[0:E, :]),
                          wait=(tmm,), inc=sdve)
        else:
            tok_lg = p.op("dve", lambda e: e.tensor_scalar_add(
                lg[:], ps_l[0:E, :], bias_sb[0:E, 0:1]), wait=(tmm,), inc=sdve)
        tok_lgread = tok_lg
        wtr = (tok_lg,)
        if tok_tm is not None:
            wtr = wtr + (tok_tm,)
        ttr = None
        for j in range(4):
            ttr = p.op("pe", lambda e, j=j: e.transpose(
                ps_f[:, j * E:(j + 1) * E], lg[0:E, j * CH:(j + 1) * CH],
                c["identf"][0:E, 0:E]), wait=wtr if j == 0 else (),
                inc=spe if j == 3 else None)
        tok_tm = p.op("dve", lambda e: e.tensor_copy(out=tm[:], in_=ps_f[:, 0:4 * E]),
                      wait=(ttr,), inc=sdve)
        E4 = 4 * E
        tm3 = tm[:, 0:E4].rearrange("p (c t) -> p c t", c=4)
        gt3 = gt[:, 0:E4].rearrange("p (c t) -> p c t", c=4)
        s13 = sc1[:, 0:E4].rearrange("p (c t) -> p c t", c=4)
        s23 = sc2[:, 0:E4].rearrange("p (c t) -> p c t", c=4)
        m3 = rowm[:, 0:4].rearrange("p (c t) -> p c t", c=4)
        sm3 = rowm[:, 4:8].rearrange("p (c t) -> p c t", c=4)
        mb = bass.AP(rowm[:].tensor, rowm[:].offset,
                     [list(rowm[:].ap[0]), [1, 4], [0, E]])
        smb = bass.AP(rowm[:].tensor, rowm[:].offset + 4,
                      [list(rowm[:].ap[0]), [1, 4], [0, E]])
        tprev = [tok_tm]

        def step(thunk, eng="dve", extra=()):
            w = ((tprev[0],) if tprev[0] is not None else ()) + tuple(extra)
            tprev[0] = p.op(eng, thunk, wait=w, inc=(sdve if eng == "dve" else sact))
            return tprev[0]

        if mode == "top4x16":
            step(lambda e: e.reduce_max(m3, tm3, axis=AX.X))
            step(lambda e: e.tensor_tensor(s13, tm3, mb, ALU.subtract))
            step(lambda e: e.activation(out=sc1[:, 0:E4], in_=sc1[:, 0:E4],
                                        func=AF.Exp), eng="act")
            step(lambda e: e.reduce_sum(sm3, s13, axis=AX.X))
            step(lambda e: e.reciprocal(out=rowm[:, 4:8], in_=rowm[:, 4:8]))
            step(lambda e: e.tensor_tensor(tm3, s13, smb, ALU.mult))
            step(lambda e: e.tensor_copy(out=sc1[:, 0:E4], in_=tm[:, 0:E4]))
            for it in range(TOPK - 1):
                step(lambda e: e.reduce_max(m3, s13, axis=AX.X))
                step(lambda e: e.tensor_tensor(s23, s13, mb, ALU.is_ge))
                step(lambda e: e.scalar_tensor_tensor(
                    sc1[:, 0:E4], sc2[:, 0:E4], NEG_BIG, sc1[:, 0:E4],
                    ALU.mult, ALU.add))
            step(lambda e: e.reduce_max(m3, s13, axis=AX.X))
            step(lambda e: e.tensor_tensor(s23, tm3, mb, ALU.is_ge))
            tok_gt = step(lambda e: e.tensor_tensor(gt3, tm3, s23, ALU.mult))
        else:
            step(lambda e: e.reduce_max(m3, tm3, axis=AX.X))
            step(lambda e: e.tensor_tensor(s13, tm3, mb, ALU.subtract))
            step(lambda e: e.tensor_tensor(s23, tm3, mb, ALU.is_ge))
            step(lambda e: e.scalar_tensor_tensor(
                sc2[:, 0:E4], sc2[:, 0:E4], NEG_BIG, tm[:, 0:E4],
                ALU.mult, ALU.add))
            step(lambda e: e.activation(out=sc1[:, 0:E4], in_=sc1[:, 0:E4],
                                        func=AF.Exp), eng="act")
            step(lambda e: e.reduce_max(m3, s23, axis=AX.X))
            step(lambda e: e.tensor_tensor(s23, tm3, mb, ALU.is_ge))
            step(lambda e: e.tensor_tensor(s13, s13, s23, ALU.mult))
            step(lambda e: e.reduce_sum(sm3, s13, axis=AX.X))
            step(lambda e: e.reciprocal(out=rowm[:, 4:8], in_=rowm[:, 4:8]))
            tok_gt = step(lambda e: e.tensor_tensor(gt3, s13, smb, ALU.mult))
        wbk = (tok_gt,)
        if tok_go is not None:
            wbk = wbk + (tok_go,)
        tbk = None
        for j in range(4):
            tbk = p.op("pe", lambda e, j=j: e.transpose(
                ps_b[0:E, j * CH:(j + 1) * CH], gt[:, j * E:(j + 1) * E],
                c["identf"][:]), wait=wbk if j == 0 else (),
                inc=spe if j == 3 else None)
        wgo = (tbk,)
        if go_free[b] is not None:
            wgo = wgo + (go_free[b],)
        tok_go = p.op("dve", lambda e, b=b: e.tensor_copy(
            out=go[b][:], in_=ps_b[0:E, 0:SLAB]), wait=wgo, inc=sdve)
        tok_st = p.dma("sp", lambda e, b=b, t0=t0: e.dma_start(
            out=gd[0:E, t0:t0 + SLAB], in_=go[b][:]), wait=(tok_go,), inc=sst[b])
        go_free[b] = tok_st
    return [tok_st]



def emit_moe0(p, c, xn, gfm, x_in, x_out, up_hbm, dn_hbm, deps, pfx="m0"):
    NE = 18
    SEG = 2048
    NSEG = T // SEG
    SL = SEG // SLAB
    spe, sact, sdve = p.sem(pfx), p.sem(pfx), p.sem(pfx)
    swu = [p.sem(pfx) for _ in range(2)]
    swd = [p.sem(pfx) for _ in range(2)]
    sxh, sg = p.sem(pfx), p.sem(pfx)
    sxi = [p.sem(pfx) for _ in range(2)]
    sst = [p.sem(pfx) for _ in range(2)]
    acc = p.sbuf(pfx + "acc", [128, NCH * SEG], F32)
    xn_h = p.sbuf(pfx + "xn", [128, NCH * SEG], BF16)
    g_h = p.sbuf(pfx + "g", [16, SEG], BF16)
    wu = [p.sbuf(pfx + "wu", [128, NCH * 512], BF16) for _ in range(2)]
    wd = [p.sbuf(pfx + "wd", [128, 4 * D], BF16) for _ in range(2)]
    act_b = [p.sbuf(pfx + "act", [128, 4 * SLAB], BF16) for _ in range(2)]
    gl = p.sbuf(pfx + "gl", [128, SLAB], F32)
    xi_b = [p.sbuf(pfx + "xi", [128, NCH * SLAB], F32) for _ in range(2)]
    ob = xi_b
    ps_u = [p.psum(pfx + "pu", [128, SLAB], F32) for _ in range(3)]
    ps_d = [p.psum(pfx + "pd", [128, SLAB], F32) for _ in range(3)]
    ps_bc = p.psum(pfx + "pbc", [128, SLAB], F32)
    wu_free = [None, None]
    wd_free = [None, None]
    units = [(seg, e) for seg in range(NSEG) for e in range(NE)]
    tok_w = {}

    def load_unit(i, first=False):
        seg, e = units[i]
        par = i % 2
        w = tuple(deps) if i < 2 else ()
        if wu_free[par] is not None:
            w = w + (wu_free[par],)
        tok_wu = p.dma("pool", lambda e_, e=e: e_.dma_start(
            out=fm3(wu[par][:], 512),
            in_=up_hbm[e].rearrange("(c p) m -> p c m", p=128)), wait=w, inc=swu[par])
        w = ()
        if wd_free[par] is not None:
            w = (wd_free[par],)
        tok_wd = p.dma("pool", lambda e_, e=e: e_.dma_start(
            out=fm3(wd[par][:], D, nch=4),
            in_=dn_hbm[e].rearrange("(c p) m -> p c m", p=128)), wait=w, inc=swd[par])
        tok_w[i] = (tok_wu, tok_wd)

    load_unit(0, first=True)
    load_unit(1)
    tok_st = None
    ps_u_free = [None, None, None]
    ps_d_free = [None, None, None]
    bc_free = None
    act_free = [None, None]
    tok_acc = None
    for seg in range(NSEG):
        h0 = seg * SEG
        tok_xh = p.dma("sp", lambda e, h0=h0: e.dma_start(
            out=fm3(xn_h[:], SEG), in_=hbm_slabT(xn, h0, SEG)),
            wait=tuple(deps) + ((tok_acc,) if tok_acc is not None else ()), inc=sxh)
        tok_g = p.dma("sp", lambda e, h0=h0: e.dma_start(
            out=g_h[:], in_=gfm[0:16, h0:h0 + SEG]),
            wait=tuple(deps) if seg == 0 else (), inc=sg)
        tok_accfree = tok_acc
        pend = None

        def emit_down(e, sl, ab, tok_act, tok_wd, first_for_acc):
            nonlocal tok_acc
            t0 = sl * SLAB
            par_ = None
            tmm = None
            for mc in range(NCH):
                pd = mc % 3
                wmm = (tok_act,)
                if mc == 0 and sl == 0:
                    wmm = wmm + (tok_wd,)
                if ps_d_free[pd] is not None:
                    wmm = wmm + (ps_d_free[pd],)
                tmm = None
                for kc in range(4):
                    lhs = wd[e % 2 if False else _wpar[0]][:, kc * D + mc * 128: kc * D + (mc + 1) * 128]
                    rhs = act_b[ab][:, kc * SLAB:(kc + 1) * SLAB]
                    tmm = p.op("pe", lambda e_, l=lhs, r=rhs, pd=pd, kc=kc: e_.matmul(
                        ps_d[pd][:], l, r, start=(kc == 0), stop=(kc == 3)),
                        wait=wmm if kc == 0 else (), inc=spe if kc == 3 else None)
                if first_for_acc:
                    wacc = (tmm,)
                    if tok_accfree is not None:
                        wacc = wacc + (tok_accfree,)
                    tok_acc = p.op("dve", lambda e_, mc=mc, t0=t0, pd=pd: e_.tensor_scalar(
                        acc[:, mc * SEG + t0: mc * SEG + t0 + SLAB], ps_d[pd][:],
                        1.0, None, ALU.mult), wait=wacc, inc=sdve)
                else:
                    tok_acc = p.op("dve", lambda e_, mc=mc, t0=t0, pd=pd: e_.scalar_tensor_tensor(
                        acc[:, mc * SEG + t0: mc * SEG + t0 + SLAB], ps_d[pd][:], 1.0,
                        acc[:, mc * SEG + t0: mc * SEG + t0 + SLAB], ALU.mult, ALU.add),
                        wait=(tmm,), inc=sdve)
                ps_d_free[pd] = tok_acc
            return tmm

        _wpar = [0]
        for e in range(NE):
            ui = seg * NE + e
            par = ui % 2
            routed = e >= 2
            tok_wu, tok_wd = tok_w[ui]
            last_up_mm = None
            last_dn_mm = None
            for s in range(SL):
                t0 = s * SLAB
                ab = s % 2
                if routed:
                    wb = (tok_g,)
                    if bc_free is not None:
                        wb = wb + (bc_free,)
                    tok_bc = p.op("pe", lambda e_, e=e, t0=t0: e_.matmul(
                        ps_bc[:], c["oh16"][:, (e - 2) * 128:(e - 1) * 128],
                        g_h[0:16, t0:t0 + SLAB], start=True, stop=True), wait=wb,
                        inc=spe)
                tok_act = None
                for mc in range(4):
                    pu = mc % 3
                    wmm = ()
                    if mc == 0 and s == 0:
                        wmm = (tok_wu, tok_xh)
                    if ps_u_free[pu] is not None:
                        wmm = wmm + (ps_u_free[pu],)
                    if mc == 0 and act_free[ab] is not None:
                        wmm = wmm + (act_free[ab],)
                    tmm = None
                    for kc in range(NCH):
                        lhs = wu[par][:, kc * 512 + mc * 128: kc * 512 + (mc + 1) * 128]
                        rhs = xn_h[:, kc * SEG + t0: kc * SEG + t0 + SLAB]
                        tmm = p.op("pe", lambda e_, l=lhs, r=rhs, pu=pu, kc=kc: e_.matmul(
                            ps_u[pu][:], l, r, start=(kc == 0), stop=(kc == NCH - 1)),
                            wait=wmm if kc == 0 else (), inc=spe if kc == NCH - 1 else None)
                    last_up_mm = tmm
                    if routed:
                        tge = p.op("act", lambda e_, pu=pu: e_.activation(
                            out=gl[:], in_=ps_u[pu][:], func=AF.Gelu), wait=(tmm,), inc=sact)
                        ps_u_free[pu] = tge
                        tok_act = p.op("dve", lambda e_, mc=mc, ab=ab: e_.tensor_mul(
                            act_b[ab][:, mc * SLAB:(mc + 1) * SLAB], gl[:], ps_bc[:]),
                            wait=(tge, tok_bc), inc=sdve)
                    else:
                        tok_act = p.op("act", lambda e_, pu=pu, mc=mc, ab=ab: e_.activation(
                            out=act_b[ab][:, mc * SLAB:(mc + 1) * SLAB], in_=ps_u[pu][:],
                            func=AF.Gelu), wait=(tmm,), inc=sact)
                        ps_u_free[pu] = tok_act
                if routed:
                    bc_free = tok_act
                if pend is not None:
                    pe_, ps_, pab, pact, pwd, pfirst = pend
                    _wpar[0] = (seg * NE + pe_) % 2
                    last_dn_mm = emit_down(pe_, ps_, pab, pact, pwd, pfirst)
                    act_free[pab] = last_dn_mm
                    if ps_ == SL - 1:
                        wd_free[(seg * NE + pe_) % 2] = last_dn_mm
                        if seg * NE + pe_ + 2 < len(units):
                            load_unit(seg * NE + pe_ + 2)
                pend = (e, s, ab, tok_act, tok_wd, e == 0)
            wu_free[par] = last_up_mm
        if pend is not None:
            pe_, ps_, pab, pact, pwd, pfirst = pend
            _wpar[0] = (seg * NE + pe_) % 2
            last_dn_mm = emit_down(pe_, ps_, pab, pact, pwd, pfirst)
            act_free[pab] = last_dn_mm
            if ps_ == SL - 1:
                wd_free[(seg * NE + pe_) % 2] = last_dn_mm
                if seg * NE + pe_ + 2 < len(units):
                    load_unit(seg * NE + pe_ + 2)
            pend = None
        last = None
        for s in range(SL):
            t0g = h0 + s * SLAB
            t0 = s * SLAB
            b = s % 2
            w0 = (tok_acc,)
            if seg == 0 and s < 2:
                w0 = w0 + tuple(deps)
            if sst[b].v > 0:
                w0 = w0 + ((sst[b], sst[b].v),)
            tok_xi = p.dma("sp", lambda e, b=b, t0g=t0g: e.dma_start(
                out=fm3(xi_b[b][:], SLAB), in_=hbm_slabT(x_in, t0g, SLAB)),
                wait=w0, inc=sxi[b])
            wadd = (tok_xi, tok_acc)
            tok_add = p.op("dve", lambda e, b=b, t0=t0: e.tensor_add(
                xi_b[b][:].rearrange("p (c t) -> p c t", c=NCH),
                xi_b[b][:].rearrange("p (c t) -> p c t", c=NCH),
                acc[:].rearrange("p (c t) -> p c t", c=NCH)[:, :, t0:t0 + SLAB]),
                wait=wadd, inc=sdve)
            last = p.dma("sp", lambda e, b=b, t0g=t0g: e.dma_start(
                out=hbm_slabT(x_out, t0g, SLAB), in_=fm3(ob[b][:], SLAB)),
                wait=(tok_add,), inc=sst[b])
        tok_st = last
        tok_acc = tok_st
    return [tok_st]



def emit_postattn1(p, c, attn, xn2, x_in, x_out, wg1, wg2, wout,
                   bg1h, bg2h, gmh, bth, deps, pfx="pa"):
    spe, sact, sdve = p.sem(pfx), p.sem(pfx), p.sem(pfx)
    swp, sws = p.sem(pfx), p.sem(pfx)
    sa = [p.sem(pfx) for _ in range(2)]
    sx = [p.sem(pfx) for _ in range(2)]
    sxi = [p.sem(pfx) for _ in range(2)]
    sst = [p.sem(pfx) for _ in range(2)]
    wg1_sb = p.sbuf(pfx + "wg1", [128, NCH * 64], BF16)
    wg2_sb = p.sbuf(pfx + "wg2", [64, D], BF16)
    wo_sb = p.sbuf(pfx + "wo", [128, NCH * D], BF16)
    bg1_sb = p.sbuf(pfx + "bg1", [64, 1], F32)
    bg2_sb = p.sbuf(pfx + "bg2", [128, 8], F32)
    gm_sb = p.sbuf(pfx + "gm", [128, 8], F32)
    bt_sb = p.sbuf(pfx + "bt", [128, 8], F32)
    p.dma("pool", lambda e: e.dma_start(
        out=fm3(wg1_sb[:], 64), in_=wg1.rearrange("(c p) m -> p c m", p=128)),
        wait=tuple(deps), inc=swp)
    p.dma("pool", lambda e: e.dma_start(out=wg2_sb[:], in_=wg2), inc=swp)
    tok_wp = p.dma("pool", lambda e: e.dma_start(
        out=fm3(wo_sb[:], D), in_=wout.rearrange("(c p) m -> p c m", p=128)), inc=swp)
    p.dma("sp", lambda e: e.dma_start(out=bg1_sb[:], in_=bg1h),
          wait=tuple(deps), inc=sws)
    p.dma("sp", lambda e: e.dma_start(out=bg2_sb[:], in_=bg2h), inc=sws)
    p.dma("sp", lambda e: e.dma_start(out=gm_sb[:], in_=gmh), inc=sws)
    tok_ws = p.dma("sp", lambda e: e.dma_start(out=bt_sb[:], in_=bth), inc=sws)
    at_b = [p.sbuf(pfx + "at", [128, NCH * SLAB], BF16) for _ in range(2)]
    xn_b = [p.sbuf(pfx + "xn", [128, NCH * SLAB], BF16) for _ in range(2)]
    xi_b = [p.sbuf(pfx + "xi", [128, NCH * SLAB], F32) for _ in range(2)]
    g1_sb = p.sbuf(pfx + "g1", [64, SLAB], BF16)
    gate_sb = p.sbuf(pfx + "gate", [128, NCH * SLAB], BF16)
    ag = [p.sbuf(pfx + "ag", [128, NCH * SLAB], F32) for _ in range(2)]
    sq = [p.sbuf(pfx + "sq", [128, NCH * SLAB], F32) for _ in range(2)]
    ln_sb = p.sbuf(pfx + "ln", [128, NCH * SLAB], BF16)
    rows = [p.sbuf(pfx + "r", [1, 6 * SLAB], F32) for _ in range(2)]
    ps1 = p.psum(pfx + "p1", [128, SLAB], F32)
    ps2 = [p.psum(pfx + "p2", [128, SLAB], F32) for _ in range(2)]
    ps_mean = p.psum(pfx + "pm", [1, SLAB], F32)
    ps_sq = p.psum(pfx + "ps", [1, SLAB], F32)
    ps_A = p.psum(pfx + "pa", [128, SLAB], F32)
    ps_B = p.psum(pfx + "pb", [128, SLAB], F32)
    in_free = [None, None]
    st_free = [None, None]
    xn_free = [None, None]
    ps2_free = [None, None]
    tok_g1rd = None
    tok_gate_rd = None
    tok_stats = None
    tok_apply = [None, None]
    tok_rows_s = [None, None]
    tok_ag_s = [None, None]
    tok_st = None
    for s in range(NSLAB + 1):
        b = s % 2
        if s < NSLAB:
            t0 = s * SLAB
            w0 = tuple(deps) if s < 2 else ()
            if in_free[b] is not None:
                w0 = w0 + (in_free[b],)
            tok_at = p.dma("sp", lambda e, b=b, t0=t0: e.dma_start(
                out=fm3(at_b[b][:], SLAB), in_=hbm_slabT(attn, t0, SLAB)), wait=w0,
                inc=sa[b])
            tok_xn = p.dma("sp", lambda e, b=b, t0=t0: e.dma_start(
                out=fm3(xn_b[b][:], SLAB), in_=hbm_slabT(xn2, t0, SLAB)),
                wait=w0 + ((xn_free[b],) if xn_free[b] is not None else ()), inc=sx[b])
            tok_xi = p.dma("sp", lambda e, b=b, t0=t0: e.dma_start(
                out=fm3(xi_b[b][:], SLAB), in_=hbm_slabT(x_in, t0, SLAB)),
                wait=w0 + ((st_free[b],) if st_free[b] is not None else ()), inc=sxi[b])
            wmm = (tok_xn,)
            if s == 0:
                wmm = wmm + (tok_wp, tok_ws)
            if tok_g1rd is not None:
                wmm = wmm + (tok_g1rd,)
            tmm = None
            for kc in range(NCH):
                tmm = p.op("pe", lambda e, kc=kc, b=b: e.matmul(
                    ps1[0:64, :], wg1_sb[:, kc * 64:(kc + 1) * 64],
                    xn_b[b][:, kc * SLAB:(kc + 1) * SLAB],
                    start=(kc == 0), stop=(kc == NCH - 1)),
                    wait=wmm if kc == 0 else (), inc=spe if kc == NCH - 1 else None)
            tok_g1 = p.op("dve", lambda e: e.tensor_scalar_add(
                g1_sb[:], ps1[0:64, :], bg1_sb[:, 0:1]), wait=(tmm,), inc=sdve)
            tok_g1rd = tok_g1
            xn_free[b] = tmm
            last_gate = None
            for mc in range(NCH):
                pp = mc % 2
                wg = (tok_g1,)
                if ps2_free[pp] is not None:
                    wg = wg + (ps2_free[pp],)
                if mc == 0 and tok_gate_rd is not None:
                    wg = wg + (tok_gate_rd,)
                tg = p.op("pe", lambda e, mc=mc, pp=pp: e.matmul(
                    ps2[pp][:], wg2_sb[:, mc * 128:(mc + 1) * 128], g1_sb[:],
                    start=True, stop=True), wait=wg, inc=spe)
                last_gate = p.op("act", lambda e, mc=mc, pp=pp: e.activation(
                    out=gate_sb[:, mc * SLAB:(mc + 1) * SLAB], in_=ps2[pp][:],
                    func=AF.Sigmoid, bias=bg2_sb[:, mc:mc + 1]), wait=(tg,), inc=sact)
                ps2_free[pp] = last_gate
            tok_ag = p.op("dve", lambda e, b=b: e.tensor_mul(
                ag[b][:], at_b[b][:], gate_sb[:]), wait=(last_gate, tok_at), inc=sdve)
            tok_gate_rd = tok_ag
            tok_ag_s[b] = tok_ag
            in_free[b] = tok_ag
            tok_sq = p.op("act", lambda e, b=b: e.activation(
                out=sq[b][:], in_=ag[b][:], func=AF.Square), wait=(tok_ag,), inc=sact)
            toks = {"pe": spe, "act": sact, "dve": sdve,
                    "sq_chunk": lambda cc, b=b: sq[b][:, cc * SLAB:(cc + 1) * SLAB]}
            tok_rows, _unused, tok_stats2 = emit_ln_rows(
                p, toks, pfx, c,
                lambda cc, b=b: ag[b][:, cc * SLAB:(cc + 1) * SLAB],
                NCH, SLAB, ps_mean, ps_sq, ps_A, ps_B, rows[b],
                tok_sq, tok_ag,
                wait_stats=(tok_stats,) if tok_stats is not None else (),
                defer_bcast=True)
            tok_stats = tok_stats2
            tok_rows_s[b] = tok_rows
        if s >= 1:
            sb_ = (s - 1) % 2
            t0p = (s - 1) * SLAB
            wbc = (tok_rows_s[sb_],)
            if tok_apply[(s - 1 + 1) % 2] is not None:
                wbc = wbc + (tok_apply[(s - 1 + 1) % 2],)
            p.op("pe", lambda e, b=sb_: e.matmul(
                ps_A[:], c["ones_row_f"][:], rows[b][0:1, 4 * SLAB:5 * SLAB],
                start=True, stop=True), wait=wbc)
            tok_bc = p.op("pe", lambda e, b=sb_: e.matmul(
                ps_B[:], c["ones_row_f"][:], rows[b][0:1, 5 * SLAB:6 * SLAB],
                start=True, stop=True), inc=spe)
            ag3 = fm3(ag[sb_][:], SLAB)
            sq3 = fm3(sq[sb_][:], SLAB)
            tmul = p.op("dve", lambda e, ag3=ag3, sq3=sq3: e.tensor_tensor(
                sq3, ag3, bcast_mid(ps_A[:], NCH), ALU.mult), wait=(tok_bc,), inc=sdve)
            tadd = p.op("dve", lambda e, sq3=sq3: e.tensor_tensor(
                sq3, sq3, bcast_mid(ps_B[:], NCH), ALU.add), wait=(tmul,), inc=sdve)
            tok_x = None
            for cc in range(NCH):
                sq_c = sq[sb_][:, cc * SLAB:(cc + 1) * SLAB]
                ln_c = ln_sb[:, cc * SLAB:(cc + 1) * SLAB]
                tok_x = p.op("dve", lambda e, a=sq_c, o=ln_c, cc=cc: e.tensor_scalar(
                    o, a, gm_sb[:, cc:cc + 1], bt_sb[:, cc:cc + 1], ALU.mult, ALU.add),
                    wait=(tadd,) if cc == 0 else (), inc=sdve if cc == NCH - 1 else None)
            tok_apply[sb_] = tok_x
            last_ev = None
            for mc in range(NCH):
                pp = mc % 2
                wmm = (tok_x,) if mc == 0 else ()
                if ps2_free[pp] is not None:
                    wmm = wmm + (ps2_free[pp],)
                tmm = None
                for kc in range(NCH):
                    lhs = wo_sb[:, kc * D + mc * 128: kc * D + (mc + 1) * 128]
                    rhs = ln_sb[:, kc * SLAB:(kc + 1) * SLAB]
                    tmm = p.op("pe", lambda e, l=lhs, r=rhs, pp=pp, kc=kc: e.matmul(
                        ps2[pp][:], l, r, start=(kc == 0), stop=(kc == NCH - 1)),
                        wait=wmm if kc == 0 else (), inc=spe if kc == NCH - 1 else None)
                wev = (tmm,)
                if mc == 0:
                    wev = wev + ((sxi[sb_], sxi[sb_].v),)
                last_ev = p.op("dve", lambda e, b=sb_, mc=mc, pp=pp: e.tensor_add(
                    xi_b[b][:, mc * SLAB:(mc + 1) * SLAB], ps2[pp][:],
                    xi_b[b][:, mc * SLAB:(mc + 1) * SLAB]), wait=wev, inc=sdve)
                ps2_free[pp] = last_ev
            tok_st = p.dma("sp", lambda e, b=sb_, t0p=t0p: e.dma_start(
                out=hbm_slabT(x_out, t0p, SLAB), in_=fm3(xi_b[b][:], SLAB)),
                wait=(last_ev,), inc=sst[sb_])
            st_free[sb_] = tok_st
    return [tok_st]



def emit_moe1(p, c, xn2, wfm, x_in, y_out, w0h, w1h, w2h, b0h, b1h, b2h,
              deps, pfx="m1"):
    NE = 8
    HB = 2
    SEG = 1024
    NSEG = T // SEG
    SL = SEG // SLAB
    spe, sact, sdve, spool = p.sem(pfx), p.sem(pfx), p.sem(pfx), p.sem(pfx)
    sw0, sw1, sw2 = p.sem(pfx), p.sem(pfx), p.sem(pfx)
    swf, sb = p.sem(pfx), p.sem(pfx)
    sx = [p.sem(pfx) for _ in range(2)]
    sxi = [p.sem(pfx) for _ in range(2)]
    sst = [p.sem(pfx) for _ in range(2)]
    acc = p.sbuf(pfx + "acc", [128, NCH * SEG], F32)
    xn_b = [p.sbuf(pfx + "xn", [128, NCH * SLAB], BF16) for _ in range(2)]
    wf_h = p.sbuf(pfx + "wf", [8, SEG], BF16)
    w0_sb = p.sbuf(pfx + "w0", [128, NCH * 2048], BF16)
    w1_sb = p.sbuf(pfx + "w1", [128, NCH * 2048], BF16)
    w2_sb = p.sbuf(pfx + "w2", [128, 16 * D], BF16)
    b0_sb = p.sbuf(pfx + "b0", [128, 128], F32)
    b1_sb = p.sbuf(pfx + "b1", [128, 128], F32)
    b2_sb = p.sbuf(pfx + "b2", [8, D], BF16)
    act_sb = [p.sbuf(pfx + "act", [128, 16 * SLAB], BF16)]
    silu_sb = p.sbuf(pfx + "sl", [128, 2 * SLAB], F32)
    t_sb = p.sbuf(pfx + "t", [128, 2 * SLAB], F32)
    wbc_sb = p.sbuf(pfx + "wbc", [128, SEG], BF16)
    xi_b = [p.sbuf(pfx + "xi", [128, NCH * SLAB], F32)]
    ps_h = [p.psum(pfx + "ph", [128, SLAB], F32) for _ in range(4)]
    ps_o = [p.psum(pfx + "po", [128, SLAB], F32) for _ in range(4)]
    p.dma("sp", lambda e: e.dma_start(out=b0_sb[:], in_=b0h),
          wait=tuple(deps), inc=sb)
    p.dma("sp", lambda e: e.dma_start(out=b1_sb[:], in_=b1h), inc=sb)
    tok_bs = (sb, sb.v)
    sb2 = p.sem(pfx)
    tok_b2 = p.dma("pool", lambda e: e.dma_start(out=b2_sb[:], in_=b2h),
                   wait=tuple(deps), inc=sb2)

    w01_free = None
    w2_free = None
    tok_w01 = {}
    tok_w2 = {}
    units = [(seg, e) for seg in range(NSEG) for e in range(NE)]

    def load_w01(i, first=False):
        seg, e = units[i]
        w = tuple(deps) if i == 0 else ()
        if w01_free is not None:
            w = w + (w01_free,)
        ta_ = p.dma("pool", lambda e_, e=e: e_.dma_start(
            out=fm3(w0_sb[:], 2048),
            in_=w0h[e].rearrange("(c p) m -> p c m", p=128)), wait=w, inc=sw0)
        tb_ = p.dma("pool", lambda e_, e=e: e_.dma_start(
            out=fm3(w1_sb[:], 2048),
            in_=w1h[e].rearrange("(c p) m -> p c m", p=128)), inc=sw1)
        tok_w01[i] = (ta_, tb_)

    def load_w2(i):
        seg, e = units[i]
        w = tuple(deps) if i == 0 else ()
        if w2_free is not None:
            w = w + (w2_free,)
        tok_w2[i] = p.dma("pool", lambda e_, e=e: e_.dma_start(
            out=fm3(w2_sb[:], D, nch=16),
            in_=w2h[e].rearrange("(c p) m -> p c m", p=128)), wait=w, inc=sw2)

    load_w01(0, first=True)
    load_w2(0)
    ps_h_free = [None] * 4
    ps_o_free = [None] * 4
    wbc_w_free = None
    wbc_r_free = None
    act_free = [None]
    xn_free = [None, None]
    tok_acc = None
    tok_seg_done = None
    tok_st = None
    for seg in range(NSEG):
        h0 = seg * SEG
        wfw = () if tok_seg_done is None else (tok_seg_done,)
        tok_wf = p.dma("sp", lambda e, h0=h0: e.dma_start(
            out=wf_h[:], in_=wfm[0:8, h0:h0 + SEG]),
            wait=wfw + (tuple(deps) if seg == 0 else ()), inc=swf)
        tok_accfree = tok_acc
        for e in range(NE):
            ui = seg * NE + e
            tok_wbc = None
            for sl2 in range(SL):
                wb = (tok_wf,)
                if ps_h_free[0] is not None:
                    wb = wb + (ps_h_free[0],)
                if wbc_r_free is not None:
                    wb = wb + (wbc_r_free,)
                tbc = p.op("pe", lambda e_, e=e, sl2=sl2: e_.matmul(
                    ps_h[0][:], c["oh8"][:, e * 128:(e + 1) * 128],
                    wf_h[0:8, sl2 * SLAB:(sl2 + 1) * SLAB], start=True, stop=True),
                    wait=wb, inc=spe)
                tok_wbc = p.op("dve", lambda e_, sl2=sl2: e_.tensor_copy(
                    out=wbc_sb[:, sl2 * SLAB:(sl2 + 1) * SLAB], in_=ps_h[0][:]),
                    wait=(tbc,), inc=sdve)
                ps_h_free[0] = tok_wbc
            for s in range(SL):
                t0 = s * SLAB
                xb = s % 2
                ab = 0
                wxn = tuple(deps) if (seg == 0 and e == 0 and s < 2) else ()
                if xn_free[xb] is not None:
                    wxn = wxn + (xn_free[xb],)
                tok_xn = p.dma("sp", lambda e_, xb=xb, t0=t0, h0=h0: e_.dma_start(
                    out=fm3(xn_b[xb][:], SLAB), in_=hbm_slabT(xn2, h0 + t0, SLAB)),
                    wait=wxn, inc=sx[xb])
                tok_act = None
                prev_tt = [None, None]
                last_up = None
                for hb in range(HB):
                    for hc in range(8):
                        hq = hb * 8 + hc
                        pq = (hq % 2) * 2
                        wmm = ()
                        if hq == 0 and s == 0:
                            wmm = tok_w01[ui] + (tok_bs,)
                        if hq == 0:
                            wmm = wmm + (tok_xn, tok_wbc)
                            if act_free[ab] is not None:
                                wmm = wmm + (act_free[ab],)
                        t0mm = None
                        t1mm = None
                        for kc in range(NCH):
                            l0 = w0_sb[:, kc * 2048 + hq * 128: kc * 2048 + (hq + 1) * 128]
                            l1 = w1_sb[:, kc * 2048 + hq * 128: kc * 2048 + (hq + 1) * 128]
                            rhs = xn_b[xb][:, kc * SLAB:(kc + 1) * SLAB]
                            wk = wmm if kc == 0 else ()
                            if kc == 0 and ps_h_free[pq] is not None:
                                wk = wk + (ps_h_free[pq],)
                            t0mm = p.op("pe", lambda e_, l=l0, r=rhs, kc=kc, pq=pq: e_.matmul(
                                ps_h[pq][:], l, r, start=(kc == 0), stop=(kc == NCH - 1)),
                                wait=wk, inc=None)
                            wk2 = ()
                            if kc == 0 and ps_h_free[pq + 1] is not None:
                                wk2 = (ps_h_free[pq + 1],)
                            t1mm = p.op("pe", lambda e_, l=l1, r=rhs, kc=kc, pq=pq: e_.matmul(
                                ps_h[pq + 1][:], l, r, start=(kc == 0), stop=(kc == NCH - 1)),
                                wait=wk2, inc=spe if kc == NCH - 1 else None)
                        last_up = t1mm
                        bcol = e * 16 + hq
                        sq2 = hq % 2
                        wsl = (t1mm,)
                        if prev_tt[sq2] is not None:
                            wsl = wsl + (prev_tt[sq2],)
                        ts = p.op("act", lambda e_, bcol=bcol, pq=pq, sq2=sq2: e_.activation(
                            out=silu_sb[:, sq2 * SLAB:(sq2 + 1) * SLAB], in_=ps_h[pq + 1][:],
                            func=AF.Silu, bias=b1_sb[:, bcol:bcol + 1]), wait=wsl, inc=sact)
                        ps_h_free[pq + 1] = ts
                        tt = p.op("dve", lambda e_, bcol=bcol, pq=pq, sq2=sq2: e_.scalar_tensor_tensor(
                            t_sb[:, sq2 * SLAB:(sq2 + 1) * SLAB], ps_h[pq][:],
                            b0_sb[:, bcol:bcol + 1],
                            silu_sb[:, sq2 * SLAB:(sq2 + 1) * SLAB],
                            ALU.add, ALU.mult), wait=(ts,), inc=sdve)
                        prev_tt[sq2] = tt
                        ps_h_free[pq] = tt
                        tok_act = p.op("pool", lambda e_, hq=hq, ab=ab, sq2=sq2, t0=t0: e_.tensor_mul(
                            act_sb[ab][:, hq * SLAB:(hq + 1) * SLAB],
                            t_sb[:, sq2 * SLAB:(sq2 + 1) * SLAB],
                            wbc_sb[:, t0:t0 + SLAB]), wait=(tt,), inc=spool)
                wbc_r_free = tok_act
                for sub in range(2):
                    ts0 = t0 + sub * 256
                    for mc in range(NCH):
                        po = mc % 4
                        reg = ps_o[po][:, (mc // 4) * 256:(mc // 4) * 256 + 256]
                        wmm = (tok_act,)
                        if mc == 0 and s == 0 and sub == 0:
                            wmm = wmm + (tok_w2[ui],)
                        if ps_o_free[po] is not None:
                            wmm = wmm + (ps_o_free[po],)
                        tmm = None
                        for kq in range(16):
                            lhs = w2_sb[:, kq * D + mc * 128: kq * D + (mc + 1) * 128]
                            rhs = act_sb[ab][:, kq * SLAB + sub * 256: kq * SLAB + sub * 256 + 256]
                            tmm = p.op("pe", lambda e_, l=lhs, r=rhs, reg=reg, kq=kq, e2=e: e_.matmul(
                                reg, l, r, start=(kq == 0),
                                stop=(kq == 15 and e2 != 0)),
                                wait=wmm if kq == 0 else (),
                                inc=spe if (kq == 15 and e != 0) else None)
                        if e == 0:
                            tmm = p.op("pe", lambda e_, mc=mc, ts0=ts0, reg=reg: e_.matmul(
                                reg, b2_sb[0:8, mc * 128:(mc + 1) * 128],
                                wf_h[0:8, ts0:ts0 + 256], start=False, stop=True),
                                wait=(tok_b2,), inc=spe)
                        if mc >= 4:
                            av = acc[:].rearrange("p (g cq t) -> p g cq t", g=2, cq=4)[
                                :, :, mc - 4, t0 + sub * 256: t0 + sub * 256 + 256]
                            if e == 0:
                                wacc = (tmm,)
                                if tok_accfree is not None:
                                    wacc = wacc + (tok_accfree,)
                                tok_acc = p.op("dve", lambda e_, po=po, av=av: e_.tensor_scalar(
                                    av, ps_o[po][:].rearrange("p (g t) -> p g t", g=2),
                                    1.0, None, ALU.mult), wait=wacc, inc=sdve)
                            else:
                                tok_acc = p.op("dve", lambda e_, po=po, av=av: e_.scalar_tensor_tensor(
                                    av, ps_o[po][:].rearrange("p (g t) -> p g t", g=2),
                                    1.0, av, ALU.mult, ALU.add), wait=(tmm,), inc=sdve)
                            ps_o_free[po] = tok_acc
                act_free[ab] = tmm
                xn_free[xb] = last_up
            if ui + 1 < len(units):
                w01_free = last_up
                load_w01(ui + 1)
                w2_free = tmm
                load_w2(ui + 1)
        last = None
        for s in range(SL):
            t0g = h0 + s * SLAB
            t0 = s * SLAB
            b = 0
            w0_ = (tok_acc,)
            if sst[b].v > 0:
                w0_ = w0_ + ((sst[b], sst[b].v),)
            tok_xi = p.dma("sp", lambda e, b=b, t0g=t0g: e.dma_start(
                out=fm3(xi_b[b][:], SLAB), in_=hbm_slabT(x_in, t0g, SLAB)),
                wait=w0_, inc=sxi[b])
            tok_add = p.op("dve", lambda e, b=b, t0=t0: e.tensor_add(
                xi_b[b][:].rearrange("p (c t) -> p c t", c=NCH),
                xi_b[b][:].rearrange("p (c t) -> p c t", c=NCH),
                acc[:].rearrange("p (c t) -> p c t", c=NCH)[:, :, t0:t0 + SLAB]),
                wait=(tok_xi, tok_acc), inc=sdve)
            tok_acc = tok_add
            last = p.dma("sp", lambda e, b=b, t0g=t0g: e.dma_start(
                out=hbm_slabT(y_out, t0g, SLAB), in_=fm3(xi_b[b][:], SLAB)),
                wait=(tok_add,), inc=sst[b])
        tok_st = last
        tok_seg_done = tok_st
    return [tok_st]



def _stage(inputs):
    f32 = lambda a: np.asarray(a, np.float32)
    wT = lambda a: np.ascontiguousarray(f32(a).T)
    pp = lambda v: np.ascontiguousarray(f32(v).reshape(8, 128).T)
    bpp = lambda b: np.ascontiguousarray(
        f32(b).reshape(8, 16, 128).transpose(2, 0, 1).reshape(128, 128))
    st = dict(make_const_arrays())
    st["w_qkv0"] = np.ascontiguousarray(np.concatenate(
        [wT(inputs["L0_attn_Wq"]), wT(inputs["L0_attn_Wk"]),
         wT(inputs["L0_attn_Wv"])], axis=1))
    st["w_o0"] = wT(inputs["L0_attn_Wo"])
    st["rtr0_w"] = wT(inputs["L0_ffn_router_w"])
    st["up0"] = np.ascontiguousarray(np.concatenate(
        [f32(inputs["L0_ffn_shared_up"]).transpose(0, 2, 1),
         f32(inputs["L0_ffn_routed_up"]).transpose(0, 2, 1)]))
    st["dn0"] = np.ascontiguousarray(np.concatenate(
        [f32(inputs["L0_ffn_shared_down"]).transpose(0, 2, 1),
         f32(inputs["L0_ffn_routed_down"]).transpose(0, 2, 1)]))
    st["w_qkv1"] = wT(inputs["L1_attn_w_qkv"])
    st["wg1"] = wT(inputs["L1_attn_w_g1"])
    st["wg2"] = wT(inputs["L1_attn_w_g2"])
    st["wout"] = wT(inputs["L1_attn_w_out"])
    st["bg1"] = np.ascontiguousarray(f32(inputs["L1_attn_b_g1"])[:, None])
    st["bg2"] = pp(inputs["L1_attn_b_g2"])
    st["gm"] = pp(inputs["L1_attn_gamma"])
    st["bt"] = pp(inputs["L1_attn_beta"])
    st["rtr1_w"] = wT(inputs["L1_ffn_gate_w"])
    st["rtr1_b"] = np.ascontiguousarray(f32(inputs["L1_ffn_gate_b"])[:, None])
    st["w0h"] = np.ascontiguousarray(f32(inputs["L1_ffn_w0"]).transpose(0, 2, 1))
    st["w1h"] = np.ascontiguousarray(f32(inputs["L1_ffn_w1"]).transpose(0, 2, 1))
    st["w2h"] = np.ascontiguousarray(f32(inputs["L1_ffn_w2"]).transpose(0, 2, 1))
    st["b0h"] = bpp(inputs["L1_ffn_b0"])
    st["b1h"] = bpp(inputs["L1_ffn_b1"])
    st["b2h"] = np.ascontiguousarray(f32(inputs["L1_ffn_b2"]))
    return st


def _run_op(build, st, per_core, out_names):
    p = Prog()
    build(p)
    nc = p.finalize()
    in_maps = []
    for i in range(8):
        m = {k: st[k] for k in st if k in p.ext_inputs}
        m.update(per_core[i])
        in_maps.append(m)
    res = run_bass_kernel_spmd(nc, in_maps, core_ids=list(range(8)))
    return {nm: [res.results[i][nm] for i in range(8)] for nm in out_names}


FUSED = True


def kernel(**inputs):
    if FUSED:
        return _kernel_fused(inputs)
    return _kernel_unfused(inputs)


def _kernel_unfused(inputs):
    st = _stage(inputs)
    x = np.asarray(inputs["L0_attn_x"], np.float32)
    xfm = [np.ascontiguousarray(x[i].T).reshape(NCH, 128, T) for i in range(8)]

    FMF = [NCH, 128, T]

    def ext(p, name, shape, dt):
        p.ext_inputs.add(name)
        return p.dram(name, shape, dt, kind="ExternalInput")

    def fin(p, toks):
        p.op("sp", lambda e: None, wait=tuple(toks))

    def b_ln(src_is, dst, gamma=None):
        def b(p):
            c = emit_consts(p)
            src = ext(p, "src", FMF, F32)
            d = p.dram(dst, FMF, BF16, kind="ExternalOutput")
            fin(p, emit_ln(p, c, src, d, [c["tok"], c["tok2"]]))
        return b

    def b_qkv(kind, wname):
        def b(p):
            c = emit_consts(p)
            xn = ext(p, "xn", FMF, BF16)
            w = ext(p, wname, [D, 3 * D], F32)
            qd = p.dram("qd", FMF, BF16, kind="ExternalOutput")
            kd = p.dram("kd", FMF, BF16, kind="ExternalOutput")
            vd = p.dram("vd", FMF, BF16, kind="ExternalOutput")
            fin(p, emit_qkv(p, c, xn, qd, kd, vd, w, kind, [c["tok"], c["tok2"]]))
        return b

    def b_attn(denom):
        def b(p):
            c = emit_consts(p)
            qd = ext(p, "qd", FMF, BF16)
            kd = ext(p, "kd", FMF, BF16)
            vd = ext(p, "vd", FMF, BF16)
            ad = p.dram("ad", FMF, BF16, kind="ExternalOutput")
            fin(p, emit_attn(p, c, qd, kd, vd, ad, denom, [c["tok"], c["tok2"]]))
        return b

    def b_wo(p):
        c = emit_consts(p)
        a = ext(p, "a", FMF, BF16)
        xi = ext(p, "xi", FMF, F32)
        w = ext(p, "w_o0", [D, D], F32)
        xo = p.dram("xo", FMF, F32, kind="ExternalOutput")
        fin(p, emit_proj_residual(p, c, a, xi, xo, w, [c["tok"], c["tok2"]]))

    def b_rtr(mode, wname, bias=None):
        E = 16 if mode == "top4x16" else 8
        def b(p):
            c = emit_consts(p)
            xn = ext(p, "xn", FMF, BF16)
            w = ext(p, wname, [D, E], F32)
            gd = p.dram("gd", [E, T], BF16, kind="ExternalOutput")
            deps = [c["tok"], c["tok2"]]
            bias_sb = None
            if bias is not None:
                bh = ext(p, bias, [E, 1], F32)
                bias_sb = p.sbuf("bias", [E, 1], F32)
                sbm = p.sem("bs")
                deps.append(p.dma("sp", lambda e: e.dma_start(out=bias_sb[:], in_=bh), inc=sbm))
            fin(p, emit_router(p, c, xn, gd, w, mode, deps, bias_sb=bias_sb))
        return b

    def b_moe0(p):
        c = emit_consts(p)
        xn = ext(p, "xn", FMF, BF16)
        gfm = ext(p, "gfm", [16, T], BF16)
        xi = ext(p, "xi", FMF, F32)
        up = ext(p, "up0", [18, D, 512], F32)
        dn = ext(p, "dn0", [18, 512, D], F32)
        xo = p.dram("xo", FMF, F32, kind="ExternalOutput")
        fin(p, emit_moe0(p, c, xn, gfm, xi, xo, up, dn, [c["tok"], c["tok2"]]))

    def b_pa1(p):
        c = emit_consts(p)
        at = ext(p, "at", FMF, BF16)
        xn = ext(p, "xn", FMF, BF16)
        xi = ext(p, "xi", FMF, F32)
        wg1 = ext(p, "wg1", [D, 64], F32)
        wg2 = ext(p, "wg2", [64, D], F32)
        wout = ext(p, "wout", [D, D], F32)
        bg1 = ext(p, "bg1", [64, 1], F32)
        bg2 = ext(p, "bg2", [128, 8], F32)
        gm = ext(p, "gm", [128, 8], F32)
        bt = ext(p, "bt", [128, 8], F32)
        xo = p.dram("xo", FMF, F32, kind="ExternalOutput")
        fin(p, emit_postattn1(p, c, at, xn, xi, xo, wg1, wg2, wout,
                              bg1, bg2, gm, bt, [c["tok"], c["tok2"]]))

    def b_moe1(p):
        c = emit_consts(p)
        xn = ext(p, "xn", FMF, BF16)
        wfm = ext(p, "wfm", [8, T], BF16)
        xi = ext(p, "xi", FMF, F32)
        w0 = ext(p, "w0h", [8, D, 2048], F32)
        w1 = ext(p, "w1h", [8, D, 2048], F32)
        w2 = ext(p, "w2h", [8, 2048, D], F32)
        b0 = ext(p, "b0h", [128, 128], F32)
        b1 = ext(p, "b1h", [128, 128], F32)
        b2 = ext(p, "b2h", [8, D], F32)
        yo = p.dram("yo", FMF, F32, kind="ExternalOutput")
        fin(p, emit_moe1(p, c, xn, wfm, xi, yo, w0, w1, w2, b0, b1, b2,
                         [c["tok"], c["tok2"]]))

    pc = lambda key, arrs: [{key: arrs[i]} for i in range(8)]

    def pc2(**kw):
        return [{k: v[i] for k, v in kw.items()} for i in range(8)]

    r = _run_op(b_ln("src", "dst"), st, pc("src", xfm), ["dst"])
    xn0 = r["dst"]
    r = _run_op(b_qkv("elu1", "w_qkv0"), st, pc("xn", xn0), ["qd", "kd", "vd"])
    r = _run_op(b_attn(True), st, pc2(qd=r["qd"], kd=r["kd"], vd=r["vd"]), ["ad"])
    a0 = r["ad"]
    r = _run_op(b_wo, st, pc2(a=a0, xi=xfm), ["xo"])
    x1 = r["xo"]
    r = _run_op(b_rtr("top4x16", "rtr0_w"), st, pc("xn", xn0), ["gd"])
    g0 = r["gd"]
    r = _run_op(b_moe0, st, pc2(xn=xn0, gfm=g0, xi=x1), ["xo"])
    x2 = r["xo"]
    r = _run_op(b_ln("src", "dst"), st, pc("src", x2), ["dst"])
    xn1 = r["dst"]
    r = _run_op(b_qkv("silu", "w_qkv1"), st, pc("xn", xn1), ["qd", "kd", "vd"])
    r = _run_op(b_attn(False), st, pc2(qd=r["qd"], kd=r["kd"], vd=r["vd"]), ["ad"])
    a1 = r["ad"]
    r = _run_op(b_rtr("top2x8", "rtr1_w", bias="rtr1_b"), st, pc("xn", xn1), ["gd"])
    w1g = r["gd"]
    r = _run_op(b_pa1, st, pc2(at=a1, xn=xn1, xi=x2), ["xo"])
    x3 = r["xo"]
    r = _run_op(b_moe1, st, pc2(xn=xn1, wfm=w1g, xi=x3), ["yo"])
    y = r["yo"]
    out = np.empty((8, T, D), np.float32)
    for i in range(8):
        out[i] = np.asarray(y[i], np.float32).reshape(D, T).T
    return out


def _kernel_fused(inputs):
    st = _stage(inputs)
    x = np.asarray(inputs["L0_attn_x"], np.float32)
    xfm = [np.ascontiguousarray(x[i].T).reshape(NCH, 128, T) for i in range(8)]
    FMF = [NCH, 128, T]
    p = Prog()

    def ext(name, shape, dt=F32):
        p.ext_inputs.add(name)
        return p.dram(name, shape, dt, kind="ExternalInput")

    c = emit_consts(p)
    x0 = ext("x0", FMF)
    w_qkv0 = ext("w_qkv0", [D, 3 * D])
    w_o0 = ext("w_o0", [D, D])
    rtr0_w = ext("rtr0_w", [D, 16])
    up0 = ext("up0", [18, D, 512])
    dn0 = ext("dn0", [18, 512, D])
    w_qkv1 = ext("w_qkv1", [D, 3 * D])
    wg1 = ext("wg1", [D, 64])
    wg2 = ext("wg2", [64, D])
    wout = ext("wout", [D, D])
    bg1 = ext("bg1", [64, 1])
    bg2 = ext("bg2", [128, 8])
    gm = ext("gm", [128, 8])
    bt = ext("bt", [128, 8])
    rtr1_w = ext("rtr1_w", [D, 8])
    rtr1_b = ext("rtr1_b", [8, 1])
    w0h = ext("w0h", [8, D, 2048])
    w1h = ext("w1h", [8, D, 2048])
    w2h = ext("w2h", [8, 2048, D])
    b0h = ext("b0h", [128, 128])
    b1h = ext("b1h", [128, 128])
    b2h = ext("b2h", [8, D])
    xn0 = p.dram("s_xn0", FMF, BF16)
    qd = p.dram("s_q", FMF, BF16)
    kd = p.dram("s_k", FMF, BF16)
    vd = p.dram("s_v", FMF, BF16)
    a0 = p.dram("s_a0", FMF, BF16)
    x1 = p.dram("s_x1", FMF, F32)
    g0 = p.dram("s_g0", [16, T], BF16)
    x2 = p.dram("s_x2", FMF, F32)
    xn1 = p.dram("s_xn1", FMF, BF16)
    qd1 = p.dram("s_q1", FMF, BF16)
    kd1 = p.dram("s_k1", FMF, BF16)
    vd1 = p.dram("s_v1", FMF, BF16)
    a1 = p.dram("s_a1", FMF, BF16)
    w1g = p.dram("s_w1g", [8, T], BF16)
    x3 = p.dram("s_x3", FMF, F32)
    y = p.dram("y", FMF, F32, kind="ExternalOutput")

    dep0 = [c["tok"], c["tok2"]]
    with p.scope():
        t_ln0 = emit_ln(p, c, x0, xn0, dep0, pfx="ln0")
    with p.scope():
        t_qkv0 = emit_qkv(p, c, xn0, qd, kd, vd, w_qkv0, "elu1", t_ln0, pfx="q0")
    with p.scope():
        t_at0 = emit_attn(p, c, qd, kd, vd, a0, True, t_qkv0, pfx="a0")
    with p.scope():
        t_wo0 = emit_proj_residual(p, c, a0, x0, x1, w_o0, t_at0, pfx="w0")
    with p.scope():
        t_rt0 = emit_router(p, c, xn0, g0, rtr0_w, "top4x16", t_wo0, pfx="r0")
    with p.scope():
        t_m0 = emit_moe0(p, c, xn0, g0, x1, x2, up0, dn0, t_rt0, pfx="m0")
    with p.scope():
        t_ln1 = emit_ln(p, c, x2, xn1, t_m0, pfx="ln1")
    with p.scope():
        t_qkv1 = emit_qkv(p, c, xn1, qd1, kd1, vd1, w_qkv1, "silu", t_ln1, pfx="q1")
    with p.scope():
        t_at1 = emit_attn(p, c, qd1, kd1, vd1, a1, False, t_qkv1, pfx="a1")
    with p.scope():
        t_rt1 = emit_router(p, c, xn1, w1g, rtr1_w, "top2x8", t_at1,
                            bias_sb=None, bias_hbm=rtr1_b, pfx="r1")
    with p.scope():
        t_pa1 = emit_postattn1(p, c, a1, xn1, x2, x3, wg1, wg2, wout,
                               bg1, bg2, gm, bt, t_rt1, pfx="pa")
    with p.scope():
        t_m1 = emit_moe1(p, c, xn1, w1g, x3, y, w0h, w1h, w2h, b0h, b1h, b2h,
                         t_pa1, pfx="m1")
    p.op("sp", lambda e: None, wait=tuple(t_m1))
    nc = p.finalize()
    in_maps = []
    for i in range(8):
        m = {k: st[k] for k in st if k in p.ext_inputs}
        m["x0"] = xfm[i]
        in_maps.append(m)
    res = run_bass_kernel_spmd(nc, in_maps, core_ids=list(range(8)))
    out = np.empty((8, T, D), np.float32)
    for i in range(8):
        out[i] = np.asarray(res.results[i]["y"], np.float32).reshape(D, T).T
    return out


# version 14
# speedup vs baseline: 1.1456x; 1.0347x over previous
import contextlib
from contextlib import contextmanager
import numpy as np
import concourse.bass as bass
import concourse.mybir as mybir
from concourse.bass_utils import run_bass_kernel_spmd

F32 = mybir.dt.float32
BF16 = mybir.dt.bfloat16
U8 = mybir.dt.uint8
AF = mybir.ActivationFunctionType
ALU = mybir.AluOpType
AX = mybir.AxisListType

T = 4096
D = 1024
NCH = 8
H = 16
HD = 64
CH = 128
NCHUNK = T // CH
SLAB = 512
NSLAB = T // SLAB
HALF = T // 2
EPS_LN = 1e-5
EPS_ATTN = 1e-12
NEG_BIG = -1e30
DBG_SKIP = set()


class Sem:
    __slots__ = ("h", "v")

    def __init__(self, h):
        self.h = h
        self.v = 0

    @property
    def tok(self):
        return (self, self.v)


class Prog:
    ENGINES = ("pe", "act", "dve", "pool", "sp")

    def __init__(self):
        self.nc = bass.Bass("TRN2", target_bir_lowering=False)
        self.stack = contextlib.ExitStack()
        self.sem_stack = contextlib.ExitStack()
        self.sched = {e: [] for e in self.ENGINES}
        self._names = {}
        self.all_sems = []
        self.ext_inputs = set()
        self.sem_free = []
        self._scope_sems = None

    @contextmanager
    def scope(self):
        old = self.stack
        self.stack = contextlib.ExitStack()
        outer_scope = self._scope_sems
        self._scope_sems = []
        try:
            yield
        finally:
            self.stack.close()
            self.stack = old
            self.sem_free.extend(self._scope_sems)
            self._scope_sems = outer_scope

    def name(self, base):
        n = self._names.get(base, 0)
        self._names[base] = n + 1
        return "%s_%d" % (base, n)

    def sem(self, base="s"):
        if self._scope_sems is not None and self.sem_free:
            sm = self.sem_free.pop()
        else:
            sm = Sem(self.sem_stack.enter_context(self.nc.semaphore(self.name(base))))
            self.all_sems.append(sm)
        if self._scope_sems is not None:
            self._scope_sems.append(sm)
        return sm

    def sbuf(self, base, shape, dt):
        return self.stack.enter_context(self.nc.sbuf_tensor(self.name(base), shape, dt))

    def psum(self, base, shape, dt):
        return self.stack.enter_context(self.nc.psum_tensor(self.name(base), shape, dt))

    def dram(self, name, shape, dt, kind=None):
        if kind is None:
            return self.nc.dram_tensor(name, shape, dt).ap()
        return self.nc.dram_tensor(name, shape, dt, kind=kind).ap()

    def _emit(self, eng, thunk, wait, inc, amt):
        tok = None
        if inc is not None:
            inc.v += amt
            tok = (inc, inc.v)
        self.sched[eng].append((tuple(wait), thunk, (inc, amt) if inc is not None else None))
        return tok

    def op(self, eng, thunk, wait=(), inc=None):
        return self._emit(eng, thunk, wait, inc, 1)

    def dma(self, eng, thunk, wait=(), inc=None):
        return self._emit(eng, thunk, wait, inc, 16)

    def finalize(self):
        tail = tuple((sm, sm.v) for sm in self.all_sems if sm.v > 0)
        self.sched["sp"].append((tail, lambda e: None, None))
        sched = self.sched

        def runner(eng_name):
            def go(eng):
                last = {}
                for waits, thunk, inc in sched[eng_name]:
                    for sem, val in waits:
                        if last.get(id(sem), -1) < val:
                            eng.wait_ge(sem.h, val)
                            last[id(sem)] = val
                    inst = thunk(eng)
                    if inc is not None:
                        sem, amt = inc
                        inst.then_inc(sem.h, amt)
            return go

        with self.nc.Block() as block:
            block.tensor(runner("pe"))
            block.scalar(runner("act"))
            block.vector(runner("dve"))
            block.gpsimd(runner("pool"))
            block.sync(runner("sp"))
        self.stack.close()
        self.sem_stack.close()
        return self.nc


def bcast_mid(ap2d, n):
    return bass.AP(ap2d.tensor, ap2d.offset,
                   [list(ap2d.ap[0]), [0, n], list(ap2d.ap[1])])


def fm3(sb_ap, width, nch=NCH):
    return sb_ap.rearrange("p (c t) -> p c t", c=nch)


def hbm_slabT(hbm, t0, w):
    return hbm[:, :, t0:t0 + w].transpose([1, 0, 2])



def make_const_arrays():
    c = {}
    c["c_ident"] = np.eye(128, dtype=np.float32)
    c["c_ones_col"] = np.ones((128, 1), dtype=np.float32)
    c["c_ones_row"] = np.ones((1, 128), dtype=np.float32)
    c["c_triu4"] = np.tile(np.triu(np.ones((128, 128), dtype=np.float32)), (1, 4))
    oh16 = np.zeros((16, 16 * 128), dtype=np.float32)
    for e in range(16):
        oh16[e, e * 128:(e + 1) * 128] = 1.0
    c["c_oh16"] = oh16
    oh8 = np.zeros((8, 8 * 128), dtype=np.float32)
    for e in range(8):
        oh8[e, e * 128:(e + 1) * 128] = 1.0
    c["c_oh8"] = oh8
    kb = np.zeros((128, 8), dtype=np.float32)
    kb[:, 0] = EPS_LN
    kb[:, 1] = EPS_ATTN
    c["c_kb"] = kb
    return c


def emit_consts(p):
    c = {}
    hb = {}
    for k, v in make_const_arrays().items():
        p.ext_inputs.add(k)
        hb[k] = p.dram(k, list(v.shape), F32, kind="ExternalInput")
    c["ident"] = p.sbuf("ident", [128, 128], BF16)
    c["ones_col_bf"] = p.sbuf("ones_col_bf", [128, 1], BF16)
    c["ones_col_f"] = p.sbuf("ones_col_f", [128, 1], F32)
    c["ones_row_f"] = p.sbuf("ones_row_f", [1, 128], F32)
    c["triu4"] = p.sbuf("triu4", [128, 512], F32)
    c["oh16"] = p.sbuf("oh16", [16, 2048], BF16)
    c["oh8"] = p.sbuf("oh8", [8, 1024], BF16)
    c["kb"] = p.sbuf("kb", [128, 8], F32)
    c["identf"] = p.sbuf("identf", [128, 128], F32)
    s1 = p.sem("csemp")
    s2 = p.sem("csems")
    p.dma("pool", lambda e: e.dma_start(out=c["ident"][:], in_=hb["c_ident"]), inc=s1)
    p.dma("pool", lambda e: e.dma_start(out=c["ones_col_bf"][:], in_=hb["c_ones_col"]), inc=s1)
    p.dma("pool", lambda e: e.dma_start(out=c["oh16"][:], in_=hb["c_oh16"]), inc=s1)
    p.dma("pool", lambda e: e.dma_start(out=c["oh8"][:], in_=hb["c_oh8"]), inc=s1)
    p.dma("sp", lambda e: e.dma_start(out=c["ones_col_f"][:], in_=hb["c_ones_col"]), inc=s2)
    p.dma("sp", lambda e: e.dma_start(out=c["ones_row_f"][:], in_=hb["c_ones_row"]), inc=s2)
    p.dma("sp", lambda e: e.dma_start(out=c["triu4"][:], in_=hb["c_triu4"]), inc=s2)
    p.dma("sp", lambda e: e.dma_start(out=c["kb"][:], in_=hb["c_kb"]), inc=s2)
    p.dma("sp", lambda e: e.dma_start(out=c["identf"][:], in_=hb["c_ident"]), inc=s2)
    c["tok"] = (s1, s1.v)
    c["tok2"] = (s2, s2.v)
    return c



def emit_ln_rows(p, toks, pfx, c, xs_chunk, nch, w, ps_mean, ps_sq, ps_A, ps_B,
                 rows, dep_sq_tok, dep_ld_tok, waitAB=(), wait_stats=(),
                 defer_bcast=False):
    spe, sact, sdve = toks["pe"], toks["act"], toks["dve"]
    t = None
    for cc in range(nch):
        w0 = ((dep_ld_tok,) if dep_ld_tok is not None else ()) + tuple(wait_stats)
        t = p.op("pe", lambda e, cc=cc: e.matmul(
            ps_mean[:], c["ones_col_f"][:], xs_chunk(cc),
            start=(cc == 0), stop=(cc == nch - 1)),
            wait=w0 if cc == 0 else (), inc=spe if cc == nch - 1 else None)
    tok_mean = t
    sq_chunk = toks["sq_chunk"]
    for cc in range(nch):
        t = p.op("pe", lambda e, cc=cc: e.matmul(
            ps_sq[:], c["ones_col_f"][:], sq_chunk(cc),
            start=(cc == 0), stop=(cc == nch - 1)),
            wait=(dep_sq_tok,) if cc == 0 else (), inc=spe if cc == nch - 1 else None)
    tok_sq = t
    mu, m2, musq, var, rstd, bb = (rows[0:1, i * w:(i + 1) * w] for i in range(6))
    tmu = p.op("act", lambda e: e.activation(out=mu, in_=ps_mean[:], func=AF.Copy, scale=1.0 / D),
         wait=(tok_mean,), inc=sact)
    p.op("act", lambda e: e.activation(out=m2, in_=ps_sq[:], func=AF.Copy, scale=1.0 / D),
         wait=(tok_sq,))
    tok_stats = p.op("act", lambda e: e.activation(out=musq, in_=mu, func=AF.Square),
                     wait=(tmu,), inc=sact)
    td = p.op("dve", lambda e: e.tensor_sub(var, m2, musq), wait=(tok_stats,), inc=sdve)
    ta = p.op("act", lambda e: e.activation(out=var, in_=var, func=AF.Ln, bias=c["kb"][0:1, 0:1]),
              wait=(td,), inc=sact)
    ta2 = p.op("act", lambda e: e.activation(out=rstd, in_=var, func=AF.Exp, scale=-0.5),
               wait=(ta,), inc=sact)
    tok_rows = p.op("dve", lambda e: e.scalar_tensor_tensor(
        bb, mu, -1.0, rstd, ALU.mult, ALU.mult), wait=(ta2,), inc=sdve)
    if toks.get("defer_bcast") or defer_bcast:
        return tok_rows, None, tok_stats
    p.op("pe", lambda e: e.matmul(ps_A[:], c["ones_row_f"][:], rstd, start=True, stop=True),
         wait=(tok_rows,) + tuple(waitAB))
    tok_bc = p.op("pe", lambda e: e.matmul(ps_B[:], c["ones_row_f"][:], bb, start=True, stop=True),
                  inc=spe)
    return tok_rows, tok_bc, tok_stats


def emit_ln(p, c, src, dst, deps, pfx="ln", gamma_sb=None, beta_sb=None):
    spe, sact, sdve = p.sem(pfx), p.sem(pfx), p.sem(pfx)
    sld = [p.sem(pfx) for _ in range(2)]
    sst = [p.sem(pfx) for _ in range(2)]
    xs = [p.sbuf(pfx + "x", [128, NCH * SLAB], F32) for _ in range(2)]
    xq = [p.sbuf(pfx + "q", [128, NCH * SLAB], F32) for _ in range(2)]
    xo = [p.sbuf(pfx + "o", [128, NCH * SLAB], BF16) for _ in range(2)]
    rows = [p.sbuf(pfx + "r", [1, 6 * SLAB], F32) for _ in range(2)]
    ps_mean = [p.psum(pfx + "pm", [1, SLAB], F32) for _ in range(2)]
    ps_sq = [p.psum(pfx + "ps", [1, SLAB], F32) for _ in range(2)]
    ps_A = [p.psum(pfx + "pa", [128, SLAB], F32) for _ in range(2)]
    ps_B = [p.psum(pfx + "pb", [128, SLAB], F32) for _ in range(2)]
    xs_free = [None, None]
    xo_free = [None, None]
    tok_x2 = [None, None]
    tok_stats = [None, None]
    tok_rows_s = [None, None]
    tok_st = None
    for s in range(NSLAB + 1):
        b = s % 2
        if s < NSLAB:
            t0 = s * SLAB
            w = tuple(deps) if s < 2 else ()
            if xs_free[b] is not None:
                w = w + (xs_free[b],)
            tok_ld = p.dma("sp", lambda e, b=b, t0=t0: e.dma_start(
                out=fm3(xs[b][:], SLAB), in_=hbm_slabT(src, t0, SLAB)), wait=w,
                inc=sld[b])
            tok_sq = p.op("act", lambda e, b=b: e.activation(
                out=xq[b][:], in_=xs[b][:], func=AF.Square), wait=(tok_ld,), inc=sact)
            toks = {"pe": spe, "act": sact, "dve": sdve,
                    "sq_chunk": lambda cc, b=b: xq[b][:, cc * SLAB:(cc + 1) * SLAB]}
            tok_rows, tok_bc_unused, tok_stats2 = emit_ln_rows(
                p, toks, pfx, c,
                lambda cc, b=b: xs[b][:, cc * SLAB:(cc + 1) * SLAB],
                NCH, SLAB, ps_mean[b], ps_sq[b], ps_A[b], ps_B[b], rows[b],
                tok_sq, tok_ld,
                waitAB=(tok_x2[b],) if tok_x2[b] is not None else (),
                wait_stats=(tok_stats[b],) if tok_stats[b] is not None else (),
                defer_bcast=True)
            tok_stats[b] = tok_stats2
            tok_rows_s[b] = tok_rows
        if s >= 1:
            sb_ = (s - 1) % 2
            t0p = (s - 1) * SLAB
            p.op("pe", lambda e, b=sb_: e.matmul(
                ps_A[b][:], c["ones_row_f"][:], rows[b][0:1, 4 * SLAB:5 * SLAB],
                start=True, stop=True),
                wait=(tok_rows_s[sb_],) + ((tok_x2[sb_],) if tok_x2[sb_] is not None else ()))
            tok_bc = p.op("pe", lambda e, b=sb_: e.matmul(
                ps_B[b][:], c["ones_row_f"][:], rows[b][0:1, 5 * SLAB:6 * SLAB],
                start=True, stop=True), inc=spe)
            wx = (tok_bc,)
            if xo_free[sb_] is not None:
                wx = wx + (xo_free[sb_],)
            xs3 = fm3(xs[sb_][:], SLAB)
            xq3 = fm3(xq[sb_][:], SLAB)
            xo3 = fm3(xo[sb_][:], SLAB)
            tmul = p.op("dve", lambda e, b=sb_, xs3=xs3, xq3=xq3: e.tensor_tensor(
                xq3, xs3, bcast_mid(ps_A[b][:], NCH), ALU.mult), wait=wx, inc=sdve)
            if gamma_sb is None:
                tok_x = p.op("dve", lambda e, b=sb_, xq3=xq3, xo3=xo3: e.tensor_tensor(
                    xo3, xq3, bcast_mid(ps_B[b][:], NCH), ALU.add),
                    wait=(tmul,), inc=sdve)
            else:
                tadd = p.op("dve", lambda e, b=sb_, xq3=xq3: e.tensor_tensor(
                    xq3, xq3, bcast_mid(ps_B[b][:], NCH), ALU.add),
                    wait=(tmul,), inc=sdve)
                tok_x = None
                for cc in range(NCH):
                    xq_c = xq[sb_][:, cc * SLAB:(cc + 1) * SLAB]
                    xo_c = xo[sb_][:, cc * SLAB:(cc + 1) * SLAB]
                    tok_x = p.op("dve", lambda e, a=xq_c, o=xo_c, cc=cc: e.tensor_scalar(
                        o, a, gamma_sb[:, cc:cc + 1], beta_sb[:, cc:cc + 1],
                        ALU.mult, ALU.add), wait=(tadd,) if cc == 0 else (),
                        inc=sdve if cc == NCH - 1 else None)
            xs_free[sb_] = tok_x
            tok_x2[sb_] = tok_x
            tok_st = p.dma("sp", lambda e, b=sb_, t0p=t0p: e.dma_start(
                out=hbm_slabT(dst, t0p, SLAB), in_=fm3(xo[b][:], SLAB)),
                wait=(tok_x,), inc=sst[sb_])
            xo_free[sb_] = tok_st
    return [tok_st]




def emit_qkv(p, c, xn, qd, kd, vd, w_hbm, kind, deps, pfx="qkv"):
    spe, sact, sdve, spool = p.sem(pfx), p.sem(pfx), p.sem(pfx), p.sem(pfx)
    sw, sx = p.sem(pfx), p.sem(pfx)
    sst = [p.sem(pfx) for _ in range(4)]
    w_sb = p.sbuf(pfx + "w", [128, NCH * 3072], BF16)
    tok_w = p.dma("pool", lambda e: e.dma_start(
        out=fm3(w_sb[:], 3072), in_=w_hbm.rearrange("(c p) m -> p c m", p=128)),
        wait=tuple(deps), inc=sw)
    xn_sb = p.sbuf(pfx + "x", [128, NCH * T], BF16)
    tok_x = p.dma("sp", lambda e: e.dma_start(
        out=fm3(xn_sb[:], T), in_=xn.transpose([1, 0, 2])), wait=tuple(deps), inc=sx)
    nps = 4
    pss = [p.psum(pfx + "p", [128, SLAB], F32) for _ in range(nps)]
    ps_free = [None] * nps
    sc1 = [p.sbuf(pfx + "s1", [128, SLAB], F32) for _ in range(2)]
    sc2 = [p.sbuf(pfx + "s2", [128, SLAB], F32) for _ in range(2)]
    ob = [p.sbuf(pfx + "o", [128, SLAB], BF16) for _ in range(4)]
    ob_free = [None] * 4
    sc_free = [None, None]
    tok_st = None
    i = 0
    for mc in range(24):
        dest = (qd, kd, vd)[mc // 8]
        cc_out = mc % 8
        act = kind if mc < 16 else "copy"
        for s in range(NSLAB):
            t0 = s * SLAB
            pb = i % nps
            w0 = (tok_w, tok_x) if i == 0 else ()
            if ps_free[pb] is not None:
                w0 = w0 + (ps_free[pb],)
            tmm = None
            for kc in range(NCH):
                lhs = w_sb[:, kc * 3072 + mc * 128: kc * 3072 + (mc + 1) * 128]
                rhs = xn_sb[:, kc * T + t0: kc * T + t0 + SLAB]
                tmm = p.op("pe", lambda e, l=lhs, r=rhs, pb=pb, kc=kc: e.matmul(
                    pss[pb][:], l, r, start=(kc == 0), stop=(kc == NCH - 1)),
                    wait=w0 if kc == 0 else (), inc=spe if kc == NCH - 1 else None)
            oi = i % 4
            wo = (tmm,)
            if ob_free[oi] is not None:
                wo = wo + (ob_free[oi],)
            if act == "silu":
                tok_ev = p.op("act", lambda e, pb=pb, oi=oi: e.activation(
                    out=ob[oi][:], in_=pss[pb][:], func=AF.Silu), wait=wo, inc=sact)
                ps_free[pb] = tok_ev
            elif act == "copy":
                tok_ev = p.op("act", lambda e, pb=pb, oi=oi: e.copy(
                    out=ob[oi][:], in_=pss[pb][:]), wait=wo, inc=sact)
                ps_free[pb] = tok_ev
            else:
                si = i % 2
                if sc_free[si] is not None:
                    wo = wo + (sc_free[si],)
                t1 = p.op("act", lambda e, pb=pb, si=si: e.activation(
                    out=sc1[si][:], in_=pss[pb][:], func=AF.Exp),
                    wait=wo, inc=sact)
                t2 = p.op("dve", lambda e, si=si: e.tensor_single_scalar(
                    sc1[si][:], sc1[si][:], 1.0, ALU.min), wait=(t1,), inc=sdve)
                t3 = p.op("dve", lambda e, pb=pb, si=si: e.tensor_single_scalar(
                    sc2[si][:], pss[pb][:], 0.0, ALU.max), wait=(t2,), inc=sdve)
                ps_free[pb] = t3
                tok_ev = p.op("pool", lambda e, si=si, oi=oi: e.tensor_add(
                    ob[oi][:], sc1[si][:], sc2[si][:]), wait=(t2, t3), inc=spool)
                sc_free[si] = tok_ev
            tok_st = p.dma("sp", lambda e, d=dest, cc=cc_out, t0=t0, oi=oi: e.dma_start(
                out=d[cc, :, t0:t0 + SLAB], in_=ob[oi][:]), wait=(tok_ev,), inc=sst[oi])
            ob_free[oi] = tok_st
            i += 1
    return [tok_st]



def emit_proj_residual(p, c, a_in, x_in, x_out, w_hbm, deps, pfx="wo"):
    spe, sact, sdve = p.sem(pfx), p.sem(pfx), p.sem(pfx)
    sw = p.sem(pfx)
    sa = [p.sem(pfx) for _ in range(2)]
    sxi = [p.sem(pfx) for _ in range(2)]
    sst = [p.sem(pfx) for _ in range(2)]
    w_sb = p.sbuf(pfx + "w", [128, NCH * D], BF16)
    tok_w = p.dma("pool", lambda e: e.dma_start(
        out=fm3(w_sb[:], D), in_=w_hbm.rearrange("(c p) m -> p c m", p=128)),
        wait=tuple(deps), inc=sw)
    ab = [p.sbuf(pfx + "a", [128, NCH * SLAB], BF16) for _ in range(2)]
    xb = [p.sbuf(pfx + "xi", [128, NCH * SLAB], F32) for _ in range(2)]
    ob = [p.sbuf(pfx + "xo", [128, NCH * SLAB], F32) for _ in range(2)]
    pss = [p.psum(pfx + "p", [128, SLAB], F32) for _ in range(4)]
    ps_free = [None] * 4
    in_free = [None, None]
    ob_free = [None, None]
    tok_st = None
    for s in range(NSLAB):
        b = s % 2
        t0 = s * SLAB
        w0 = tuple(deps) if s < 2 else ()
        if in_free[b] is not None:
            w0 = w0 + (in_free[b],)
        tok_a = p.dma("sp", lambda e, b=b, t0=t0: e.dma_start(
            out=fm3(ab[b][:], SLAB), in_=hbm_slabT(a_in, t0, SLAB)), wait=w0, inc=sa[b])
        tok_xi = p.dma("sp", lambda e, b=b, t0=t0: e.dma_start(
            out=fm3(xb[b][:], SLAB), in_=hbm_slabT(x_in, t0, SLAB)), inc=sxi[b])
        last_ev = None
        for mc in range(NCH):
            pb = mc % 4
            wmm = (tok_a,) if mc == 0 else ()
            wmm = wmm + ((tok_w,) if s == 0 and mc == 0 else ())
            if ps_free[pb] is not None:
                wmm = wmm + (ps_free[pb],)
            tmm = None
            for kc in range(NCH):
                lhs = w_sb[:, kc * D + mc * 128: kc * D + (mc + 1) * 128]
                rhs = ab[b][:, kc * SLAB: (kc + 1) * SLAB]
                tmm = p.op("pe", lambda e, l=lhs, r=rhs, pb=pb, kc=kc: e.matmul(
                    pss[pb][:], l, r, start=(kc == 0), stop=(kc == NCH - 1)),
                    wait=wmm if kc == 0 else (), inc=spe if kc == NCH - 1 else None)
            wev = (tmm, tok_xi)
            if mc == 0 and ob_free[b] is not None:
                wev = wev + (ob_free[b],)
            last_ev = p.op("dve", lambda e, b=b, mc=mc, pb=pb: e.tensor_add(
                ob[b][:, mc * SLAB:(mc + 1) * SLAB], pss[pb][:],
                xb[b][:, mc * SLAB:(mc + 1) * SLAB]), wait=wev, inc=sdve)
            ps_free[pb] = last_ev
        in_free[b] = last_ev
        tok_st = p.dma("sp", lambda e, b=b, t0=t0: e.dma_start(
            out=hbm_slabT(x_out, t0, SLAB), in_=fm3(ob[b][:], SLAB)),
            wait=(last_ev,), inc=sst[b])
        ob_free[b] = tok_st
    return [tok_st]



def emit_attn(p, c, qd, kd, vd, ad, denom, deps, pfx="at"):
    spe, sact, sdve = p.sem(pfx), p.sem(pfx), p.sem(pfx)
    sq = [p.sem(pfx) for _ in range(2)]
    sk = [p.sem(pfx) for _ in range(2)]
    sv = [p.sem(pfx) for _ in range(2)]
    sst = [p.sem(pfx) for _ in range(2)]
    qs = [p.sbuf(pfx + "q", [128, NCH * CH], BF16) for _ in range(2)]
    ks = [p.sbuf(pfx + "k", [128, NCH * CH], BF16) for _ in range(2)]
    vs = [p.sbuf(pfx + "v", [128, NCH * CH], BF16) for _ in range(2)]
    ktm = p.sbuf(pfx + "ktm", [128, NCH * CH], BF16)
    vtm = p.sbuf(pfx + "vtm", [128, NCH * CH], BF16)
    S_sb = p.sbuf(pfx + "S", [128, 512], BF16)
    z_sb = p.sbuf(pfx + "z", [128, 8], BF16)
    sc_sb = [p.sbuf(pfx + "sc", [128, 512], BF16) for _ in range(2)]
    nm_sb = [p.sbuf(pfx + "nm", [128, 256], F32) for _ in range(2)]
    rden = p.sbuf(pfx + "rd", [128, 256], F32)
    at_sb = [p.sbuf(pfx + "o", [128, NCH * CH], BF16) for _ in range(2)]
    ps_S = p.psum(pfx + "pS", [128, 512], F32)
    ps_sc = [p.psum(pfx + "psc", [128, 512], F32) for _ in range(2)]
    ps_nm = [p.psum(pfx + "pnm", [128, 512], F32) for _ in range(2)]
    ps_tr = p.psum(pfx + "ptr", [128, 1024], BF16)
    if denom:
        ps_z = p.psum(pfx + "pz", [128, 512], F32)

    def hslice(sb, h):
        pb = (h % 2) * 64
        return sb[pb:pb + 64, (h // 2) * CH:(h // 2) * CH + CH]

    SCCOL = [0, 256, 128, 384]

    t0s = tuple(deps)
    tok_S0 = p.op("dve", lambda e: e.memset(S_sb[:], 0.0), wait=t0s, inc=sdve)
    tok_psS0 = p.op("dve", lambda e: e.memset(ps_S[:], 0.0), inc=sdve)
    tok_psz0 = None
    if denom:
        tok_S0 = p.op("dve", lambda e: e.memset(z_sb[:], 0.0), inc=sdve)
        tok_psz0 = p.op("dve", lambda e: e.memset(ps_z[:, 0:8], 0.0), inc=sdve)
    qkv_free = [None, None]
    scfree = None
    scb_free = [None, None]
    nmfree = [None, None]
    at_free = [None, None]
    tok_trfree = None
    tok_Ssb = tok_S0
    tok_zsb = tok_S0
    tok_Supd_prev = None
    tok_zupd_prev = None
    tok_st = None
    for ci in range(NCHUNK):
        b = ci % 2
        c0 = ci * CH
        w0 = t0s if ci < 2 else ()
        if qkv_free[b] is not None:
            w0 = w0 + (qkv_free[b],)
        tok_q = p.dma("sp", lambda e, b=b, c0=c0: e.dma_start(
            out=fm3(qs[b][:], CH), in_=hbm_slabT(qd, c0, CH)), wait=w0, inc=sq[b])
        tok_k = p.dma("sp", lambda e, b=b, c0=c0: e.dma_start(
            out=fm3(ks[b][:], CH), in_=hbm_slabT(kd, c0, CH)), inc=sk[b])
        tok_v = p.dma("sp", lambda e, b=b, c0=c0: e.dma_start(
            out=fm3(vs[b][:], CH), in_=hbm_slabT(vd, c0, CH)), inc=sv[b])
        wtr = (tok_k,) + ((tok_trfree,) if tok_trfree is not None else ())
        tk = None
        if "tr" in DBG_SKIP:
            tok_ktm = p.op("dve", lambda e: e.memset(ktm[:], 0.001), wait=wtr, inc=sdve)
            tok_vtm = p.op("dve", lambda e: e.memset(vtm[:], 0.001), wait=(tok_v,), inc=sdve)
            tok_trfree = tok_vtm
            dbg_skip_tr = True
        else:
            dbg_skip_tr = False
        if not dbg_skip_tr:
            for cc in range(NCH):
                tk = p.op("pe", lambda e, b=b, cc=cc: e.transpose(
                    ps_tr[:, cc * CH:(cc + 1) * CH], ks[b][:, cc * CH:(cc + 1) * CH],
                    c["ident"][:]), wait=wtr if cc == 0 else (),
                    inc=spe if cc == NCH - 1 else None)
            tok_ktm = p.op("dve", lambda e: e.tensor_copy(out=ktm[:], in_=ps_tr[:]),
                           wait=(tk,), inc=sdve)
            tv = None
            for cc in range(NCH):
                tv = p.op("pe", lambda e, b=b, cc=cc: e.transpose(
                    ps_tr[:, cc * CH:(cc + 1) * CH], vs[b][:, cc * CH:(cc + 1) * CH],
                    c["ident"][:]), wait=(tok_v, tok_ktm) if cc == 0 else (),
                    inc=spe if cc == NCH - 1 else None)
            tok_vtm = p.op("dve", lambda e: e.tensor_copy(out=vtm[:], in_=ps_tr[:]),
                           wait=(tv,), inc=sdve)
            tok_trfree = tok_vtm
        ep_toks = []
        for g in range(4):
            b2 = g % 2
            heads = [4 * g + j for j in range(4)]
            wsc = (tok_q,)
            if scfree is not None:
                wsc = wsc + (scfree,)
            tsc = None
            if "sc" in DBG_SKIP:
                tsc = p.op("pe", lambda e: e.matmul(
                    ps_sc[0][:], c["ident"][:], qs[b][:, 0:512], start=True,
                    stop=True), wait=wsc, inc=spe)
            else:
              for j, h in enumerate(heads):
                pb = (h % 2) * 64
                tsc = p.op("pe", lambda e, b=b, j=j, h=h, pb=pb: e.matmul(
                    ps_sc[j % 2][:, (j // 2) * CH:(j // 2 + 1) * CH],
                    hslice(ks[b], h), hslice(qs[b], h), start=True, stop=True,
                    tile_position=(pb, 0)), wait=wsc if j == 0 else (),
                    inc=spe if j == 3 else None)
            wev = (tsc,)
            if scb_free[b2] is not None:
                wev = wev + (scb_free[b2],)
            p.op("dve", lambda e, b2=b2: e.tensor_mul(
                sc_sb[b2][:, 0:256], ps_sc[0][:, 0:256], c["triu4"][:, 0:256]),
                wait=wev)
            tok_scev = p.op("dve", lambda e, b2=b2: e.tensor_mul(
                sc_sb[b2][:, 256:512], ps_sc[1][:, 0:256], c["triu4"][:, 0:256]),
                inc=sdve)
            if denom:
                wd = (tok_scev, tok_zsb)
                td = None
                for j, h in enumerate(heads):
                    pb = (h % 2) * 64
                    cb = j // 2
                    dreg = ps_sc[b2][pb:pb + 64, 256 + cb * CH:256 + (cb + 1) * CH]
                    zsl = z_sb[pb:pb + 64, (h // 2):(h // 2) + 1]
                    zrep = bass.AP(zsl.tensor, zsl.offset,
                                   [list(zsl.ap[0]), [0, 64]])
                    ones64 = bass.AP(c["ones_col_bf"][:].tensor, 0,
                                     [[1, 128], [0, 64]])
                    td = p.op("pe", lambda e, b=b, o=dreg, z=zrep, h=h, pb=pb: e.matmul(
                        o, z, hslice(qs[b], h),
                        start=True, stop=False, tile_position=(pb, pb)),
                        wait=wd if j == 0 else (), inc=None)
                    td = p.op("pe", lambda e, o=dreg, n1=ones64, j=j, b2=b2, pb=pb: e.matmul(
                        o, n1, sc_sb[b2][:, SCCOL[j]:SCCOL[j] + CH],
                        start=False, stop=True, tile_position=(0, pb)),
                        inc=spe if j == 3 else None)
                tl = p.op("act", lambda e, b2=b2: e.activation(
                    out=rden[:], in_=ps_sc[b2][:, 256:512], func=AF.Ln),
                    wait=(td,), inc=sact)
                tok_rden = p.op("act", lambda e: e.activation(
                    out=rden[:], in_=rden[:], func=AF.Exp, scale=-1.0),
                    wait=(tl,), inc=sact)
            wnm = (tok_scev, tok_Ssb, tok_vtm)
            if nmfree[b2] is not None:
                wnm = wnm + (nmfree[b2],)
            tnm = None
            if "nm" in DBG_SKIP:
                tnm = p.op("pe", lambda e, b2=b2: e.matmul(
                    ps_nm[b2][:], c["ident"][:], qs[b][:, 0:512], start=True,
                    stop=True), wait=wnm, inc=spe)
            else:
              for j, h in enumerate(heads):
                pb = (h % 2) * 64
                cb = j // 2
                out_ap = ps_nm[b2][pb:pb + 64, cb * CH:(cb + 1) * CH]
                tnm = p.op("pe", lambda e, o=out_ap, h=h, pb=pb, b=b, j=j: e.matmul(
                    o, S_sb[pb:pb + 64, (h // 2) * 64:(h // 2) * 64 + 64],
                    hslice(qs[b], h), start=True, stop=False,
                    tile_position=(pb, pb)), wait=wnm if j == 0 else (), inc=None)
                tnm = p.op("pe", lambda e, o=out_ap, h=h, pb=pb, j=j, b2=b2: e.matmul(
                    o, vtm[:, h * 64:(h + 1) * 64],
                    sc_sb[b2][:, SCCOL[j]:SCCOL[j] + CH], start=False, stop=True,
                    tile_position=(0, pb)), inc=spe if j == 3 else None)
            if denom:
                tok_nmev = p.op("act", lambda e, b2=b2: e.copy(
                    out=nm_sb[b2][:], in_=ps_nm[b2][:, 0:256]), wait=(tnm,), inc=sact)
                nmfree[b2] = tok_nmev
                wep = (tok_nmev, tok_rden)
                if g < 2 and at_free[b] is not None:
                    wep = wep + (at_free[b],)
                tmul = p.op("dve", lambda e, b2=b2: e.tensor_mul(
                    nm_sb[b2][:], nm_sb[b2][:], rden[:]), wait=wep,
                    inc=sdve)
                tep = p.op("dve", lambda e, b=b, g=g, b2=b2: e.tensor_single_scalar(
                    at_sb[b][:, 2 * g * CH:(2 * g + 2) * CH], nm_sb[b2][:],
                    EPS_ATTN, ALU.add), wait=(tmul,), inc=sdve)
                scfree = tok_rden
                scb_free[b2] = tnm
            else:
                wep = (tnm,)
                if g < 2 and at_free[b] is not None:
                    wep = wep + (at_free[b],)
                tep = p.op("act", lambda e, b=b, g=g, b2=b2: e.copy(
                    out=at_sb[b][:, 2 * g * CH:(2 * g + 2) * CH],
                    in_=ps_nm[b2][:, 0:256]), wait=wep, inc=sact)
                nmfree[b2] = tep
                scfree = tok_scev
                scb_free[b2] = tnm
            ep_toks.append(tep)
        wS = (tok_ktm, tok_vtm)
        if ci == 0:
            wS = wS + (tok_psS0,)
        if tok_Ssb is not None:
            wS = wS + (tok_Ssb,)
        tS = None
        for h in range(H):
            pb = (h % 2) * 64
            tS = p.op("pe", lambda e, h=h, pb=pb, ci=ci: e.matmul(
                ps_S[pb:pb + 64, (h // 2) * 64:(h // 2) * 64 + 64],
                ktm[:, h * 64:(h + 1) * 64], vtm[:, h * 64:(h + 1) * 64],
                start=(ci == 0 and h == 0), stop=(ci == NCHUNK - 1 and h == H - 1),
                skip_group_check=True,
                tile_position=(0, pb)), wait=wS if h == 0 else (),
                inc=spe if h == H - 1 else None)
        tok_Supd = tS
        if denom:
            wz = (tok_zsb,)
            if ci == 0:
                wz = wz + (tok_psz0,)
            tz = None
            for h in range(H):
                pb = (h % 2) * 64
                tz = p.op("pe", lambda e, h=h, pb=pb, ci=ci: e.matmul(
                    ps_z[pb:pb + 64, (h // 2):(h // 2) + 1],
                    ktm[:, h * 64:(h + 1) * 64], c["ones_col_bf"][:],
                    start=(ci == 0 and h == 0),
                    stop=(ci == NCHUNK - 1 and h == H - 1),
                    skip_group_check=True,
                    tile_position=(0, pb)), wait=wz if h == 0 else (),
                    inc=spe if h == H - 1 else None)
            tok_zupd = tz
        if ci < NCHUNK - 1:
            tok_Ssb = p.op("act", lambda e: e.copy(out=S_sb[:], in_=ps_S[:]),
                           wait=(tok_Supd,), inc=sact)
            if denom:
                tok_zsb = p.op("act", lambda e: e.copy(
                    out=z_sb[:], in_=ps_z[:, 0:8]), wait=(tok_zupd,), inc=sact)
        qkv_free[b] = tok_Supd
        tok_st = p.dma("sp", lambda e, b=b, c0=c0: e.dma_start(
            out=hbm_slabT(ad, c0, CH), in_=fm3(at_sb[b][:], CH)),
            wait=(ep_toks[-1],), inc=sst[b])
        at_free[b] = tok_st
    return [tok_st]



def emit_router(p, c, xn, gd, w_hbm, mode, deps, bias_sb=None, bias_hbm=None,
                pfx="rt"):
    E = 16 if mode == "top4x16" else 8
    TOPK = 4 if mode == "top4x16" else 2
    spe, sact, sdve = p.sem(pfx), p.sem(pfx), p.sem(pfx)
    sx = [p.sem(pfx) for _ in range(2)]
    sw = p.sem(pfx)
    sst = [p.sem(pfx) for _ in range(2)]
    w_sb = p.sbuf(pfx + "w", [128, NCH * E], BF16)
    tok_w = p.dma("pool", lambda e: e.dma_start(
        out=fm3(w_sb[:], E), in_=w_hbm.rearrange("(c p) m -> p c m", p=128)),
        wait=tuple(deps), inc=sw)
    if bias_hbm is not None:
        bias_sb = p.sbuf(pfx + "bias", [E, 1], F32)
        sbm = p.sem(pfx)
        deps = list(deps) + [p.dma("sp", lambda e: e.dma_start(
            out=bias_sb[:], in_=bias_hbm), wait=tuple(deps), inc=sbm)]
    xs = [p.sbuf(pfx + "x", [128, NCH * SLAB], BF16) for _ in range(2)]
    lg = p.sbuf(pfx + "lg", [E, SLAB], F32)
    tm = p.sbuf(pfx + "tm", [128, 4 * E], F32)
    gt = p.sbuf(pfx + "gt", [128, 4 * E], F32)
    sc1 = p.sbuf(pfx + "s1", [128, 4 * E], F32)
    sc2 = p.sbuf(pfx + "s2", [128, 4 * E], F32)
    rowm = p.sbuf(pfx + "rm", [128, 8], F32)
    go = [p.sbuf(pfx + "go", [E, SLAB], BF16) for _ in range(2)]
    ps_l = p.psum(pfx + "pl", [128, 512], F32)
    ps_f = p.psum(pfx + "pf", [128, 512], F32)
    ps_b = p.psum(pfx + "pb", [128, 512], F32)
    xs_free = [None, None]
    go_free = [None, None]
    tok_lgread = None
    tok_tm = None
    tok_go = None
    tok_st = None
    for s in range(NSLAB):
        b = s % 2
        t0 = s * SLAB
        w0 = tuple(deps) if s < 2 else ()
        if xs_free[b] is not None:
            w0 = w0 + (xs_free[b],)
        tok_x = p.dma("sp", lambda e, b=b, t0=t0: e.dma_start(
            out=fm3(xs[b][:], SLAB), in_=hbm_slabT(xn, t0, SLAB)), wait=w0, inc=sx[b])
        wmm = (tok_x,) + ((tok_w,) if s == 0 else ())
        if tok_lgread is not None:
            wmm = wmm + (tok_lgread,)
        tmm = None
        for kc in range(NCH):
            tmm = p.op("pe", lambda e, kc=kc, b=b: e.matmul(
                ps_l[0:E, :], w_sb[:, kc * E:(kc + 1) * E],
                xs[b][:, kc * SLAB:(kc + 1) * SLAB],
                start=(kc == 0), stop=(kc == NCH - 1)),
                wait=wmm if kc == 0 else (), inc=spe if kc == NCH - 1 else None)
        xs_free[b] = tmm
        if bias_sb is None:
            tok_lg = p.op("dve", lambda e: e.tensor_copy(out=lg[:], in_=ps_l[0:E, :]),
                          wait=(tmm,), inc=sdve)
        else:
            tok_lg = p.op("dve", lambda e: e.tensor_scalar_add(
                lg[:], ps_l[0:E, :], bias_sb[0:E, 0:1]), wait=(tmm,), inc=sdve)
        tok_lgread = tok_lg
        wtr = (tok_lg,)
        if tok_tm is not None:
            wtr = wtr + (tok_tm,)
        ttr = None
        for j in range(4):
            ttr = p.op("pe", lambda e, j=j: e.transpose(
                ps_f[:, j * E:(j + 1) * E], lg[0:E, j * CH:(j + 1) * CH],
                c["identf"][0:E, 0:E]), wait=wtr if j == 0 else (),
                inc=spe if j == 3 else None)
        tok_tm = p.op("dve", lambda e: e.tensor_copy(out=tm[:], in_=ps_f[:, 0:4 * E]),
                      wait=(ttr,), inc=sdve)
        E4 = 4 * E
        tm3 = tm[:, 0:E4].rearrange("p (c t) -> p c t", c=4)
        gt3 = gt[:, 0:E4].rearrange("p (c t) -> p c t", c=4)
        s13 = sc1[:, 0:E4].rearrange("p (c t) -> p c t", c=4)
        s23 = sc2[:, 0:E4].rearrange("p (c t) -> p c t", c=4)
        m3 = rowm[:, 0:4].rearrange("p (c t) -> p c t", c=4)
        sm3 = rowm[:, 4:8].rearrange("p (c t) -> p c t", c=4)
        mb = bass.AP(rowm[:].tensor, rowm[:].offset,
                     [list(rowm[:].ap[0]), [1, 4], [0, E]])
        smb = bass.AP(rowm[:].tensor, rowm[:].offset + 4,
                      [list(rowm[:].ap[0]), [1, 4], [0, E]])
        tprev = [tok_tm]

        def step(thunk, eng="dve", extra=()):
            w = ((tprev[0],) if tprev[0] is not None else ()) + tuple(extra)
            tprev[0] = p.op(eng, thunk, wait=w, inc=(sdve if eng == "dve" else sact))
            return tprev[0]

        if mode == "top4x16":
            step(lambda e: e.reduce_max(m3, tm3, axis=AX.X))
            step(lambda e: e.tensor_tensor(s13, tm3, mb, ALU.subtract))
            step(lambda e: e.activation(out=sc1[:, 0:E4], in_=sc1[:, 0:E4],
                                        func=AF.Exp), eng="act")
            step(lambda e: e.reduce_sum(sm3, s13, axis=AX.X))
            step(lambda e: e.reciprocal(out=rowm[:, 4:8], in_=rowm[:, 4:8]))
            step(lambda e: e.tensor_tensor(tm3, s13, smb, ALU.mult))
            step(lambda e: e.tensor_copy(out=sc1[:, 0:E4], in_=tm[:, 0:E4]))
            for it in range(TOPK - 1):
                step(lambda e: e.reduce_max(m3, s13, axis=AX.X))
                step(lambda e: e.tensor_tensor(s23, s13, mb, ALU.is_ge))
                step(lambda e: e.scalar_tensor_tensor(
                    sc1[:, 0:E4], sc2[:, 0:E4], NEG_BIG, sc1[:, 0:E4],
                    ALU.mult, ALU.add))
            step(lambda e: e.reduce_max(m3, s13, axis=AX.X))
            step(lambda e: e.tensor_tensor(s23, tm3, mb, ALU.is_ge))
            tok_gt = step(lambda e: e.tensor_tensor(gt3, tm3, s23, ALU.mult))
        else:
            step(lambda e: e.reduce_max(m3, tm3, axis=AX.X))
            step(lambda e: e.tensor_tensor(s13, tm3, mb, ALU.subtract))
            step(lambda e: e.tensor_tensor(s23, tm3, mb, ALU.is_ge))
            step(lambda e: e.scalar_tensor_tensor(
                sc2[:, 0:E4], sc2[:, 0:E4], NEG_BIG, tm[:, 0:E4],
                ALU.mult, ALU.add))
            step(lambda e: e.activation(out=sc1[:, 0:E4], in_=sc1[:, 0:E4],
                                        func=AF.Exp), eng="act")
            step(lambda e: e.reduce_max(m3, s23, axis=AX.X))
            step(lambda e: e.tensor_tensor(s23, tm3, mb, ALU.is_ge))
            step(lambda e: e.tensor_tensor(s13, s13, s23, ALU.mult))
            step(lambda e: e.reduce_sum(sm3, s13, axis=AX.X))
            step(lambda e: e.reciprocal(out=rowm[:, 4:8], in_=rowm[:, 4:8]))
            tok_gt = step(lambda e: e.tensor_tensor(gt3, s13, smb, ALU.mult))
        wbk = (tok_gt,)
        if tok_go is not None:
            wbk = wbk + (tok_go,)
        tbk = None
        for j in range(4):
            tbk = p.op("pe", lambda e, j=j: e.transpose(
                ps_b[0:E, j * CH:(j + 1) * CH], gt[:, j * E:(j + 1) * E],
                c["identf"][:]), wait=wbk if j == 0 else (),
                inc=spe if j == 3 else None)
        wgo = (tbk,)
        if go_free[b] is not None:
            wgo = wgo + (go_free[b],)
        tok_go = p.op("dve", lambda e, b=b: e.tensor_copy(
            out=go[b][:], in_=ps_b[0:E, 0:SLAB]), wait=wgo, inc=sdve)
        tok_st = p.dma("sp", lambda e, b=b, t0=t0: e.dma_start(
            out=gd[0:E, t0:t0 + SLAB], in_=go[b][:]), wait=(tok_go,), inc=sst[b])
        go_free[b] = tok_st
    return [tok_st]



def emit_moe0(p, c, xn, gfm, x_in, x_out, up_hbm, dn_hbm, deps, pfx="m0"):
    NE = 18
    SEG = 2048
    NSEG = T // SEG
    SL = SEG // SLAB
    spe, sact, sdve = p.sem(pfx), p.sem(pfx), p.sem(pfx)
    swu = [p.sem(pfx) for _ in range(2)]
    swd = [p.sem(pfx) for _ in range(2)]
    sxh, sg = p.sem(pfx), p.sem(pfx)
    sxi = [p.sem(pfx) for _ in range(2)]
    sst = [p.sem(pfx) for _ in range(2)]
    acc = p.sbuf(pfx + "acc", [128, NCH * SEG], F32)
    xn_h = p.sbuf(pfx + "xn", [128, NCH * SEG], BF16)
    g_h = p.sbuf(pfx + "g", [16, SEG], BF16)
    wu = [p.sbuf(pfx + "wu", [128, NCH * 512], BF16) for _ in range(2)]
    wd = [p.sbuf(pfx + "wd", [128, 4 * D], BF16) for _ in range(2)]
    act_b = [p.sbuf(pfx + "act", [128, 4 * SLAB], BF16) for _ in range(2)]
    gl = p.sbuf(pfx + "gl", [128, SLAB], F32)
    xi_b = [p.sbuf(pfx + "xi", [128, NCH * SLAB], F32) for _ in range(2)]
    ob = xi_b
    ps_u = [p.psum(pfx + "pu", [128, SLAB], F32) for _ in range(3)]
    ps_d = [p.psum(pfx + "pd", [128, SLAB], F32) for _ in range(3)]
    ps_bc = p.psum(pfx + "pbc", [128, SLAB], F32)
    wu_free = [None, None]
    wd_free = [None, None]
    units = [(seg, e) for seg in range(NSEG) for e in range(NE)]
    tok_w = {}

    def load_unit(i, first=False):
        seg, e = units[i]
        par = i % 2
        w = tuple(deps) if i < 2 else ()
        if wu_free[par] is not None:
            w = w + (wu_free[par],)
        tok_wu = p.dma("pool", lambda e_, e=e: e_.dma_start(
            out=fm3(wu[par][:], 512),
            in_=up_hbm[e].rearrange("(c p) m -> p c m", p=128)), wait=w, inc=swu[par])
        w = ()
        if wd_free[par] is not None:
            w = (wd_free[par],)
        tok_wd = p.dma("pool", lambda e_, e=e: e_.dma_start(
            out=fm3(wd[par][:], D, nch=4),
            in_=dn_hbm[e].rearrange("(c p) m -> p c m", p=128)), wait=w, inc=swd[par])
        tok_w[i] = (tok_wu, tok_wd)

    load_unit(0, first=True)
    load_unit(1)
    tok_st = None
    ps_u_free = [None, None, None]
    ps_d_free = [None, None, None]
    bc_free = None
    act_free = [None, None]
    tok_acc = None
    for seg in range(NSEG):
        h0 = seg * SEG
        tok_xh = p.dma("sp", lambda e, h0=h0: e.dma_start(
            out=fm3(xn_h[:], SEG), in_=hbm_slabT(xn, h0, SEG)),
            wait=tuple(deps) + ((tok_acc,) if tok_acc is not None else ()), inc=sxh)
        tok_g = p.dma("sp", lambda e, h0=h0: e.dma_start(
            out=g_h[:], in_=gfm[0:16, h0:h0 + SEG]),
            wait=tuple(deps) if seg == 0 else (), inc=sg)
        tok_accfree = tok_acc
        pend = None

        def emit_down(e, sl, ab, tok_act, tok_wd, first_for_acc):
            nonlocal tok_acc
            t0 = sl * SLAB
            par_ = None
            tmm = None
            for mc in range(NCH):
                pd = mc % 3
                wmm = (tok_act,)
                if mc == 0 and sl == 0:
                    wmm = wmm + (tok_wd,)
                if ps_d_free[pd] is not None:
                    wmm = wmm + (ps_d_free[pd],)
                tmm = None
                for kc in range(4):
                    lhs = wd[e % 2 if False else _wpar[0]][:, kc * D + mc * 128: kc * D + (mc + 1) * 128]
                    rhs = act_b[ab][:, kc * SLAB:(kc + 1) * SLAB]
                    tmm = p.op("pe", lambda e_, l=lhs, r=rhs, pd=pd, kc=kc: e_.matmul(
                        ps_d[pd][:], l, r, start=(kc == 0), stop=(kc == 3)),
                        wait=wmm if kc == 0 else (), inc=spe if kc == 3 else None)
                if first_for_acc:
                    wacc = (tmm,)
                    if tok_accfree is not None:
                        wacc = wacc + (tok_accfree,)
                    tok_acc = p.op("dve", lambda e_, mc=mc, t0=t0, pd=pd: e_.tensor_scalar(
                        acc[:, mc * SEG + t0: mc * SEG + t0 + SLAB], ps_d[pd][:],
                        1.0, None, ALU.mult), wait=wacc, inc=sdve)
                else:
                    tok_acc = p.op("dve", lambda e_, mc=mc, t0=t0, pd=pd: e_.scalar_tensor_tensor(
                        acc[:, mc * SEG + t0: mc * SEG + t0 + SLAB], ps_d[pd][:], 1.0,
                        acc[:, mc * SEG + t0: mc * SEG + t0 + SLAB], ALU.mult, ALU.add),
                        wait=(tmm,), inc=sdve)
                ps_d_free[pd] = tok_acc
            return tmm

        _wpar = [0]
        for e in range(NE):
            ui = seg * NE + e
            par = ui % 2
            routed = e >= 2
            tok_wu, tok_wd = tok_w[ui]
            last_up_mm = None
            last_dn_mm = None
            for s in range(SL):
                t0 = s * SLAB
                ab = s % 2
                if routed:
                    wb = (tok_g,)
                    if bc_free is not None:
                        wb = wb + (bc_free,)
                    tok_bc = p.op("pe", lambda e_, e=e, t0=t0: e_.matmul(
                        ps_bc[:], c["oh16"][:, (e - 2) * 128:(e - 1) * 128],
                        g_h[0:16, t0:t0 + SLAB], start=True, stop=True), wait=wb,
                        inc=spe)
                tok_act = None
                for mc in range(4):
                    pu = mc % 3
                    wmm = ()
                    if mc == 0 and s == 0:
                        wmm = (tok_wu, tok_xh)
                    if ps_u_free[pu] is not None:
                        wmm = wmm + (ps_u_free[pu],)
                    if mc == 0 and act_free[ab] is not None:
                        wmm = wmm + (act_free[ab],)
                    tmm = None
                    for kc in range(NCH):
                        lhs = wu[par][:, kc * 512 + mc * 128: kc * 512 + (mc + 1) * 128]
                        rhs = xn_h[:, kc * SEG + t0: kc * SEG + t0 + SLAB]
                        tmm = p.op("pe", lambda e_, l=lhs, r=rhs, pu=pu, kc=kc: e_.matmul(
                            ps_u[pu][:], l, r, start=(kc == 0), stop=(kc == NCH - 1)),
                            wait=wmm if kc == 0 else (), inc=spe if kc == NCH - 1 else None)
                    last_up_mm = tmm
                    if routed:
                        tge = p.op("act", lambda e_, pu=pu: e_.activation(
                            out=gl[:], in_=ps_u[pu][:], func=AF.Gelu), wait=(tmm,), inc=sact)
                        ps_u_free[pu] = tge
                        tok_act = p.op("dve", lambda e_, mc=mc, ab=ab: e_.tensor_mul(
                            act_b[ab][:, mc * SLAB:(mc + 1) * SLAB], gl[:], ps_bc[:]),
                            wait=(tge, tok_bc), inc=sdve)
                    else:
                        tok_act = p.op("act", lambda e_, pu=pu, mc=mc, ab=ab: e_.activation(
                            out=act_b[ab][:, mc * SLAB:(mc + 1) * SLAB], in_=ps_u[pu][:],
                            func=AF.Gelu), wait=(tmm,), inc=sact)
                        ps_u_free[pu] = tok_act
                if routed:
                    bc_free = tok_act
                if pend is not None:
                    pe_, ps_, pab, pact, pwd, pfirst = pend
                    _wpar[0] = (seg * NE + pe_) % 2
                    last_dn_mm = emit_down(pe_, ps_, pab, pact, pwd, pfirst)
                    act_free[pab] = last_dn_mm
                    if ps_ == SL - 1:
                        wd_free[(seg * NE + pe_) % 2] = last_dn_mm
                        if seg * NE + pe_ + 2 < len(units):
                            load_unit(seg * NE + pe_ + 2)
                pend = (e, s, ab, tok_act, tok_wd, e == 0)
            wu_free[par] = last_up_mm
        if pend is not None:
            pe_, ps_, pab, pact, pwd, pfirst = pend
            _wpar[0] = (seg * NE + pe_) % 2
            last_dn_mm = emit_down(pe_, ps_, pab, pact, pwd, pfirst)
            act_free[pab] = last_dn_mm
            if ps_ == SL - 1:
                wd_free[(seg * NE + pe_) % 2] = last_dn_mm
                if seg * NE + pe_ + 2 < len(units):
                    load_unit(seg * NE + pe_ + 2)
            pend = None
        last = None
        for s in range(SL):
            t0g = h0 + s * SLAB
            t0 = s * SLAB
            b = s % 2
            w0 = (tok_acc,)
            if seg == 0 and s < 2:
                w0 = w0 + tuple(deps)
            if sst[b].v > 0:
                w0 = w0 + ((sst[b], sst[b].v),)
            tok_xi = p.dma("sp", lambda e, b=b, t0g=t0g: e.dma_start(
                out=fm3(xi_b[b][:], SLAB), in_=hbm_slabT(x_in, t0g, SLAB)),
                wait=w0, inc=sxi[b])
            wadd = (tok_xi, tok_acc)
            tok_add = p.op("dve", lambda e, b=b, t0=t0: e.tensor_add(
                xi_b[b][:].rearrange("p (c t) -> p c t", c=NCH),
                xi_b[b][:].rearrange("p (c t) -> p c t", c=NCH),
                acc[:].rearrange("p (c t) -> p c t", c=NCH)[:, :, t0:t0 + SLAB]),
                wait=wadd, inc=sdve)
            last = p.dma("sp", lambda e, b=b, t0g=t0g: e.dma_start(
                out=hbm_slabT(x_out, t0g, SLAB), in_=fm3(ob[b][:], SLAB)),
                wait=(tok_add,), inc=sst[b])
        tok_st = last
        tok_acc = tok_st
    return [tok_st]



def emit_postattn1(p, c, attn, xn2, x_in, x_out, wg1, wg2, wout,
                   bg1h, bg2h, gmh, bth, deps, pfx="pa"):
    spe, sact, sdve = p.sem(pfx), p.sem(pfx), p.sem(pfx)
    swp, sws = p.sem(pfx), p.sem(pfx)
    sa = [p.sem(pfx) for _ in range(2)]
    sx = [p.sem(pfx) for _ in range(2)]
    sxi = [p.sem(pfx) for _ in range(2)]
    sst = [p.sem(pfx) for _ in range(2)]
    wg1_sb = p.sbuf(pfx + "wg1", [128, NCH * 64], BF16)
    wg2_sb = p.sbuf(pfx + "wg2", [64, D], BF16)
    wo_sb = p.sbuf(pfx + "wo", [128, NCH * D], BF16)
    bg1_sb = p.sbuf(pfx + "bg1", [64, 1], F32)
    bg2_sb = p.sbuf(pfx + "bg2", [128, 8], F32)
    gm_sb = p.sbuf(pfx + "gm", [128, 8], F32)
    bt_sb = p.sbuf(pfx + "bt", [128, 8], F32)
    p.dma("pool", lambda e: e.dma_start(
        out=fm3(wg1_sb[:], 64), in_=wg1.rearrange("(c p) m -> p c m", p=128)),
        wait=tuple(deps), inc=swp)
    p.dma("pool", lambda e: e.dma_start(out=wg2_sb[:], in_=wg2), inc=swp)
    tok_wp = p.dma("pool", lambda e: e.dma_start(
        out=fm3(wo_sb[:], D), in_=wout.rearrange("(c p) m -> p c m", p=128)), inc=swp)
    p.dma("sp", lambda e: e.dma_start(out=bg1_sb[:], in_=bg1h),
          wait=tuple(deps), inc=sws)
    p.dma("sp", lambda e: e.dma_start(out=bg2_sb[:], in_=bg2h), inc=sws)
    p.dma("sp", lambda e: e.dma_start(out=gm_sb[:], in_=gmh), inc=sws)
    tok_ws = p.dma("sp", lambda e: e.dma_start(out=bt_sb[:], in_=bth), inc=sws)
    at_b = [p.sbuf(pfx + "at", [128, NCH * SLAB], BF16) for _ in range(2)]
    xn_b = [p.sbuf(pfx + "xn", [128, NCH * SLAB], BF16) for _ in range(2)]
    xi_b = [p.sbuf(pfx + "xi", [128, NCH * SLAB], F32) for _ in range(2)]
    g1_sb = p.sbuf(pfx + "g1", [64, SLAB], BF16)
    gate_sb = p.sbuf(pfx + "gate", [128, NCH * SLAB], BF16)
    ag = [p.sbuf(pfx + "ag", [128, NCH * SLAB], F32) for _ in range(2)]
    sq = [p.sbuf(pfx + "sq", [128, NCH * SLAB], F32) for _ in range(2)]
    ln_sb = p.sbuf(pfx + "ln", [128, NCH * SLAB], BF16)
    rows = [p.sbuf(pfx + "r", [1, 6 * SLAB], F32) for _ in range(2)]
    ps1 = p.psum(pfx + "p1", [128, SLAB], F32)
    ps2 = [p.psum(pfx + "p2", [128, SLAB], F32) for _ in range(2)]
    ps_mean = p.psum(pfx + "pm", [1, SLAB], F32)
    ps_sq = p.psum(pfx + "ps", [1, SLAB], F32)
    ps_A = p.psum(pfx + "pa", [128, SLAB], F32)
    ps_B = p.psum(pfx + "pb", [128, SLAB], F32)
    in_free = [None, None]
    st_free = [None, None]
    xn_free = [None, None]
    ps2_free = [None, None]
    tok_g1rd = None
    tok_gate_rd = None
    tok_stats = None
    tok_apply = [None, None]
    tok_rows_s = [None, None]
    tok_ag_s = [None, None]
    tok_st = None
    for s in range(NSLAB + 1):
        b = s % 2
        if s < NSLAB:
            t0 = s * SLAB
            w0 = tuple(deps) if s < 2 else ()
            if in_free[b] is not None:
                w0 = w0 + (in_free[b],)
            tok_at = p.dma("sp", lambda e, b=b, t0=t0: e.dma_start(
                out=fm3(at_b[b][:], SLAB), in_=hbm_slabT(attn, t0, SLAB)), wait=w0,
                inc=sa[b])
            tok_xn = p.dma("sp", lambda e, b=b, t0=t0: e.dma_start(
                out=fm3(xn_b[b][:], SLAB), in_=hbm_slabT(xn2, t0, SLAB)),
                wait=w0 + ((xn_free[b],) if xn_free[b] is not None else ()), inc=sx[b])
            tok_xi = p.dma("sp", lambda e, b=b, t0=t0: e.dma_start(
                out=fm3(xi_b[b][:], SLAB), in_=hbm_slabT(x_in, t0, SLAB)),
                wait=w0 + ((st_free[b],) if st_free[b] is not None else ()), inc=sxi[b])
            wmm = (tok_xn,)
            if s == 0:
                wmm = wmm + (tok_wp, tok_ws)
            if tok_g1rd is not None:
                wmm = wmm + (tok_g1rd,)
            tmm = None
            for kc in range(NCH):
                tmm = p.op("pe", lambda e, kc=kc, b=b: e.matmul(
                    ps1[0:64, :], wg1_sb[:, kc * 64:(kc + 1) * 64],
                    xn_b[b][:, kc * SLAB:(kc + 1) * SLAB],
                    start=(kc == 0), stop=(kc == NCH - 1)),
                    wait=wmm if kc == 0 else (), inc=spe if kc == NCH - 1 else None)
            tok_g1 = p.op("dve", lambda e: e.tensor_scalar_add(
                g1_sb[:], ps1[0:64, :], bg1_sb[:, 0:1]), wait=(tmm,), inc=sdve)
            tok_g1rd = tok_g1
            xn_free[b] = tmm
            last_gate = None
            for mc in range(NCH):
                pp = mc % 2
                wg = (tok_g1,)
                if ps2_free[pp] is not None:
                    wg = wg + (ps2_free[pp],)
                if mc == 0 and tok_gate_rd is not None:
                    wg = wg + (tok_gate_rd,)
                tg = p.op("pe", lambda e, mc=mc, pp=pp: e.matmul(
                    ps2[pp][:], wg2_sb[:, mc * 128:(mc + 1) * 128], g1_sb[:],
                    start=True, stop=True), wait=wg, inc=spe)
                last_gate = p.op("act", lambda e, mc=mc, pp=pp: e.activation(
                    out=gate_sb[:, mc * SLAB:(mc + 1) * SLAB], in_=ps2[pp][:],
                    func=AF.Sigmoid, bias=bg2_sb[:, mc:mc + 1]), wait=(tg,), inc=sact)
                ps2_free[pp] = last_gate
            tok_ag = p.op("dve", lambda e, b=b: e.tensor_mul(
                ag[b][:], at_b[b][:], gate_sb[:]), wait=(last_gate, tok_at), inc=sdve)
            tok_gate_rd = tok_ag
            tok_ag_s[b] = tok_ag
            in_free[b] = tok_ag
            tok_sq = p.op("act", lambda e, b=b: e.activation(
                out=sq[b][:], in_=ag[b][:], func=AF.Square), wait=(tok_ag,), inc=sact)
            toks = {"pe": spe, "act": sact, "dve": sdve,
                    "sq_chunk": lambda cc, b=b: sq[b][:, cc * SLAB:(cc + 1) * SLAB]}
            tok_rows, _unused, tok_stats2 = emit_ln_rows(
                p, toks, pfx, c,
                lambda cc, b=b: ag[b][:, cc * SLAB:(cc + 1) * SLAB],
                NCH, SLAB, ps_mean, ps_sq, ps_A, ps_B, rows[b],
                tok_sq, tok_ag,
                wait_stats=(tok_stats,) if tok_stats is not None else (),
                defer_bcast=True)
            tok_stats = tok_stats2
            tok_rows_s[b] = tok_rows
        if s >= 1:
            sb_ = (s - 1) % 2
            t0p = (s - 1) * SLAB
            wbc = (tok_rows_s[sb_],)
            if tok_apply[(s - 1 + 1) % 2] is not None:
                wbc = wbc + (tok_apply[(s - 1 + 1) % 2],)
            p.op("pe", lambda e, b=sb_: e.matmul(
                ps_A[:], c["ones_row_f"][:], rows[b][0:1, 4 * SLAB:5 * SLAB],
                start=True, stop=True), wait=wbc)
            tok_bc = p.op("pe", lambda e, b=sb_: e.matmul(
                ps_B[:], c["ones_row_f"][:], rows[b][0:1, 5 * SLAB:6 * SLAB],
                start=True, stop=True), inc=spe)
            ag3 = fm3(ag[sb_][:], SLAB)
            sq3 = fm3(sq[sb_][:], SLAB)
            tmul = p.op("dve", lambda e, ag3=ag3, sq3=sq3: e.tensor_tensor(
                sq3, ag3, bcast_mid(ps_A[:], NCH), ALU.mult), wait=(tok_bc,), inc=sdve)
            tadd = p.op("dve", lambda e, sq3=sq3: e.tensor_tensor(
                sq3, sq3, bcast_mid(ps_B[:], NCH), ALU.add), wait=(tmul,), inc=sdve)
            tok_x = None
            for cc in range(NCH):
                sq_c = sq[sb_][:, cc * SLAB:(cc + 1) * SLAB]
                ln_c = ln_sb[:, cc * SLAB:(cc + 1) * SLAB]
                tok_x = p.op("dve", lambda e, a=sq_c, o=ln_c, cc=cc: e.tensor_scalar(
                    o, a, gm_sb[:, cc:cc + 1], bt_sb[:, cc:cc + 1], ALU.mult, ALU.add),
                    wait=(tadd,) if cc == 0 else (), inc=sdve if cc == NCH - 1 else None)
            tok_apply[sb_] = tok_x
            last_ev = None
            for mc in range(NCH):
                pp = mc % 2
                wmm = (tok_x,) if mc == 0 else ()
                if ps2_free[pp] is not None:
                    wmm = wmm + (ps2_free[pp],)
                tmm = None
                for kc in range(NCH):
                    lhs = wo_sb[:, kc * D + mc * 128: kc * D + (mc + 1) * 128]
                    rhs = ln_sb[:, kc * SLAB:(kc + 1) * SLAB]
                    tmm = p.op("pe", lambda e, l=lhs, r=rhs, pp=pp, kc=kc: e.matmul(
                        ps2[pp][:], l, r, start=(kc == 0), stop=(kc == NCH - 1)),
                        wait=wmm if kc == 0 else (), inc=spe if kc == NCH - 1 else None)
                wev = (tmm,)
                if mc == 0:
                    wev = wev + ((sxi[sb_], sxi[sb_].v),)
                last_ev = p.op("dve", lambda e, b=sb_, mc=mc, pp=pp: e.tensor_add(
                    xi_b[b][:, mc * SLAB:(mc + 1) * SLAB], ps2[pp][:],
                    xi_b[b][:, mc * SLAB:(mc + 1) * SLAB]), wait=wev, inc=sdve)
                ps2_free[pp] = last_ev
            tok_st = p.dma("sp", lambda e, b=sb_, t0p=t0p: e.dma_start(
                out=hbm_slabT(x_out, t0p, SLAB), in_=fm3(xi_b[b][:], SLAB)),
                wait=(last_ev,), inc=sst[sb_])
            st_free[sb_] = tok_st
    return [tok_st]



def emit_moe1(p, c, xn2, wfm, x_in, y_out, w0h, w1h, w2h, b0h, b1h, b2h,
              deps, pfx="m1"):
    NE = 8
    HB = 2
    SEG = 1024
    NSEG = T // SEG
    SL = SEG // SLAB
    spe, sact, sdve, spool = p.sem(pfx), p.sem(pfx), p.sem(pfx), p.sem(pfx)
    sw0, sw1, sw2 = p.sem(pfx), p.sem(pfx), p.sem(pfx)
    swf, sb = p.sem(pfx), p.sem(pfx)
    sx = [p.sem(pfx) for _ in range(2)]
    sxi = [p.sem(pfx) for _ in range(2)]
    sst = [p.sem(pfx) for _ in range(2)]
    acc = p.sbuf(pfx + "acc", [128, NCH * SEG], F32)
    xn_b = [p.sbuf(pfx + "xn", [128, NCH * SLAB], BF16) for _ in range(2)]
    wf_h = p.sbuf(pfx + "wf", [8, SEG], BF16)
    w0_sb = p.sbuf(pfx + "w0", [128, NCH * 2048], BF16)
    w1_sb = p.sbuf(pfx + "w1", [128, NCH * 2048], BF16)
    w2_sb = p.sbuf(pfx + "w2", [128, 16 * D], BF16)
    b0_sb = p.sbuf(pfx + "b0", [128, 128], F32)
    b1_sb = p.sbuf(pfx + "b1", [128, 128], F32)
    b2_sb = p.sbuf(pfx + "b2", [8, D], BF16)
    act_sb = [p.sbuf(pfx + "act", [128, 16 * SLAB], BF16)]
    silu_sb = p.sbuf(pfx + "sl", [128, 2 * SLAB], F32)
    t_sb = p.sbuf(pfx + "t", [128, 2 * SLAB], F32)
    wbc_sb = p.sbuf(pfx + "wbc", [128, SEG], BF16)
    xi_b = [p.sbuf(pfx + "xi", [128, NCH * SLAB], F32)]
    ps_h = [p.psum(pfx + "ph", [128, SLAB], F32) for _ in range(4)]
    ps_o = [p.psum(pfx + "po", [128, SLAB], F32) for _ in range(4)]
    p.dma("sp", lambda e: e.dma_start(out=b0_sb[:], in_=b0h),
          wait=tuple(deps), inc=sb)
    p.dma("sp", lambda e: e.dma_start(out=b1_sb[:], in_=b1h), inc=sb)
    tok_bs = (sb, sb.v)
    sb2 = p.sem(pfx)
    tok_b2 = p.dma("pool", lambda e: e.dma_start(out=b2_sb[:], in_=b2h),
                   wait=tuple(deps), inc=sb2)

    w01_free = None
    w2_free = None
    tok_w01 = {}
    tok_w2 = {}
    units = [(seg, e) for seg in range(NSEG) for e in range(NE)]

    def load_w01(i, first=False):
        seg, e = units[i]
        w = tuple(deps) if i == 0 else ()
        if w01_free is not None:
            w = w + (w01_free,)
        ta_ = p.dma("pool", lambda e_, e=e: e_.dma_start(
            out=fm3(w0_sb[:], 2048),
            in_=w0h[e].rearrange("(c p) m -> p c m", p=128)), wait=w, inc=sw0)
        tb_ = p.dma("pool", lambda e_, e=e: e_.dma_start(
            out=fm3(w1_sb[:], 2048),
            in_=w1h[e].rearrange("(c p) m -> p c m", p=128)), inc=sw1)
        tok_w01[i] = (ta_, tb_)

    def load_w2(i):
        seg, e = units[i]
        w = tuple(deps) if i == 0 else ()
        if w2_free is not None:
            w = w + (w2_free,)
        tok_w2[i] = p.dma("pool", lambda e_, e=e: e_.dma_start(
            out=fm3(w2_sb[:], D, nch=16),
            in_=w2h[e].rearrange("(c p) m -> p c m", p=128)), wait=w, inc=sw2)

    load_w01(0, first=True)
    load_w2(0)
    ps_h_free = [None] * 4
    ps_o_free = [None] * 4
    wbc_w_free = None
    wbc_r_free = None
    act_free = [None]
    xn_free = [None, None]
    tok_acc = None
    tok_seg_done = None
    tok_st = None
    for seg in range(NSEG):
        h0 = seg * SEG
        wfw = () if tok_seg_done is None else (tok_seg_done,)
        tok_wf = p.dma("sp", lambda e, h0=h0: e.dma_start(
            out=wf_h[:], in_=wfm[0:8, h0:h0 + SEG]),
            wait=wfw + (tuple(deps) if seg == 0 else ()), inc=swf)
        tok_accfree = tok_acc
        for e in range(NE):
            ui = seg * NE + e
            tok_wbc = None
            for sl2 in range(SL):
                wb = (tok_wf,)
                if ps_h_free[0] is not None:
                    wb = wb + (ps_h_free[0],)
                if wbc_r_free is not None:
                    wb = wb + (wbc_r_free,)
                tbc = p.op("pe", lambda e_, e=e, sl2=sl2: e_.matmul(
                    ps_h[0][:], c["oh8"][:, e * 128:(e + 1) * 128],
                    wf_h[0:8, sl2 * SLAB:(sl2 + 1) * SLAB], start=True, stop=True),
                    wait=wb, inc=spe)
                tok_wbc = p.op("dve", lambda e_, sl2=sl2: e_.tensor_copy(
                    out=wbc_sb[:, sl2 * SLAB:(sl2 + 1) * SLAB], in_=ps_h[0][:]),
                    wait=(tbc,), inc=sdve)
                ps_h_free[0] = tok_wbc
            for s in range(SL):
                t0 = s * SLAB
                xb = s % 2
                ab = 0
                wxn = tuple(deps) if (seg == 0 and e == 0 and s < 2) else ()
                if xn_free[xb] is not None:
                    wxn = wxn + (xn_free[xb],)
                tok_xn = p.dma("sp", lambda e_, xb=xb, t0=t0, h0=h0: e_.dma_start(
                    out=fm3(xn_b[xb][:], SLAB), in_=hbm_slabT(xn2, h0 + t0, SLAB)),
                    wait=wxn, inc=sx[xb])
                tok_act = None
                prev_tt = [None, None]
                last_up = None
                for hb in range(HB):
                    for hc in range(8):
                        hq = hb * 8 + hc
                        pq = (hq % 2) * 2
                        wmm = ()
                        if hq == 0 and s == 0:
                            wmm = tok_w01[ui] + (tok_bs,)
                        if hq == 0:
                            wmm = wmm + (tok_xn, tok_wbc)
                            if act_free[ab] is not None:
                                wmm = wmm + (act_free[ab],)
                        t0mm = None
                        t1mm = None
                        for kc in range(NCH):
                            l0 = w0_sb[:, kc * 2048 + hq * 128: kc * 2048 + (hq + 1) * 128]
                            l1 = w1_sb[:, kc * 2048 + hq * 128: kc * 2048 + (hq + 1) * 128]
                            rhs = xn_b[xb][:, kc * SLAB:(kc + 1) * SLAB]
                            wk = wmm if kc == 0 else ()
                            if kc == 0 and ps_h_free[pq] is not None:
                                wk = wk + (ps_h_free[pq],)
                            t0mm = p.op("pe", lambda e_, l=l0, r=rhs, kc=kc, pq=pq: e_.matmul(
                                ps_h[pq][:], l, r, start=(kc == 0), stop=(kc == NCH - 1)),
                                wait=wk, inc=None)
                            wk2 = ()
                            if kc == 0 and ps_h_free[pq + 1] is not None:
                                wk2 = (ps_h_free[pq + 1],)
                            t1mm = p.op("pe", lambda e_, l=l1, r=rhs, kc=kc, pq=pq: e_.matmul(
                                ps_h[pq + 1][:], l, r, start=(kc == 0), stop=(kc == NCH - 1)),
                                wait=wk2, inc=spe if kc == NCH - 1 else None)
                        last_up = t1mm
                        bcol = e * 16 + hq
                        sq2 = hq % 2
                        wsl = (t1mm,)
                        if prev_tt[sq2] is not None:
                            wsl = wsl + (prev_tt[sq2],)
                        ts = p.op("act", lambda e_, bcol=bcol, pq=pq, sq2=sq2: e_.activation(
                            out=silu_sb[:, sq2 * SLAB:(sq2 + 1) * SLAB], in_=ps_h[pq + 1][:],
                            func=AF.Silu, bias=b1_sb[:, bcol:bcol + 1]), wait=wsl, inc=sact)
                        ps_h_free[pq + 1] = ts
                        tt = p.op("dve", lambda e_, bcol=bcol, pq=pq, sq2=sq2: e_.scalar_tensor_tensor(
                            t_sb[:, sq2 * SLAB:(sq2 + 1) * SLAB], ps_h[pq][:],
                            b0_sb[:, bcol:bcol + 1],
                            silu_sb[:, sq2 * SLAB:(sq2 + 1) * SLAB],
                            ALU.add, ALU.mult), wait=(ts,), inc=sdve)
                        prev_tt[sq2] = tt
                        ps_h_free[pq] = tt
                        tok_act = p.op("pool", lambda e_, hq=hq, ab=ab, sq2=sq2, t0=t0: e_.tensor_mul(
                            act_sb[ab][:, hq * SLAB:(hq + 1) * SLAB],
                            t_sb[:, sq2 * SLAB:(sq2 + 1) * SLAB],
                            wbc_sb[:, t0:t0 + SLAB]), wait=(tt,), inc=spool)
                wbc_r_free = tok_act
                for sub in range(2):
                    ts0 = t0 + sub * 256
                    for mc in range(NCH):
                        po = mc % 4
                        reg = ps_o[po][:, (mc // 4) * 256:(mc // 4) * 256 + 256]
                        wmm = (tok_act,)
                        if mc == 0 and s == 0 and sub == 0:
                            wmm = wmm + (tok_w2[ui],)
                        if ps_o_free[po] is not None:
                            wmm = wmm + (ps_o_free[po],)
                        tmm = None
                        for kq in range(16):
                            lhs = w2_sb[:, kq * D + mc * 128: kq * D + (mc + 1) * 128]
                            rhs = act_sb[ab][:, kq * SLAB + sub * 256: kq * SLAB + sub * 256 + 256]
                            tmm = p.op("pe", lambda e_, l=lhs, r=rhs, reg=reg, kq=kq, e2=e: e_.matmul(
                                reg, l, r, start=(kq == 0),
                                stop=(kq == 15 and e2 != 0)),
                                wait=wmm if kq == 0 else (),
                                inc=spe if (kq == 15 and e != 0) else None)
                        if e == 0:
                            tmm = p.op("pe", lambda e_, mc=mc, ts0=ts0, reg=reg: e_.matmul(
                                reg, b2_sb[0:8, mc * 128:(mc + 1) * 128],
                                wf_h[0:8, ts0:ts0 + 256], start=False, stop=True),
                                wait=(tok_b2,), inc=spe)
                        if mc >= 4:
                            av = acc[:].rearrange("p (g cq t) -> p g cq t", g=2, cq=4)[
                                :, :, mc - 4, t0 + sub * 256: t0 + sub * 256 + 256]
                            if e == 0:
                                wacc = (tmm,)
                                if tok_accfree is not None:
                                    wacc = wacc + (tok_accfree,)
                                tok_acc = p.op("dve", lambda e_, po=po, av=av: e_.tensor_scalar(
                                    av, ps_o[po][:].rearrange("p (g t) -> p g t", g=2),
                                    1.0, None, ALU.mult), wait=wacc, inc=sdve)
                            else:
                                tok_acc = p.op("dve", lambda e_, po=po, av=av: e_.scalar_tensor_tensor(
                                    av, ps_o[po][:].rearrange("p (g t) -> p g t", g=2),
                                    1.0, av, ALU.mult, ALU.add), wait=(tmm,), inc=sdve)
                            ps_o_free[po] = tok_acc
                act_free[ab] = tmm
                xn_free[xb] = last_up
                if e == NE - 1:
                    t0g = h0 + s * SLAB
                    b = 0
                    w0_ = (tok_acc,)
                    if sst[b].v > 0:
                        w0_ = w0_ + ((sst[b], sst[b].v),)
                    tok_xi = p.dma("sp", lambda e_, b=b, t0g=t0g: e_.dma_start(
                        out=fm3(xi_b[b][:], SLAB), in_=hbm_slabT(x_in, t0g, SLAB)),
                        wait=w0_, inc=sxi[b])
                    tok_add = p.op("dve", lambda e_, b=b, t0=t0: e_.tensor_add(
                        xi_b[b][:].rearrange("p (c t) -> p c t", c=NCH),
                        xi_b[b][:].rearrange("p (c t) -> p c t", c=NCH),
                        acc[:].rearrange("p (c t) -> p c t", c=NCH)[:, :, t0:t0 + SLAB]),
                        wait=(tok_xi, tok_acc), inc=sdve)
                    tok_acc = tok_add
                    tok_st = p.dma("sp", lambda e_, b=b, t0g=t0g: e_.dma_start(
                        out=hbm_slabT(y_out, t0g, SLAB), in_=fm3(xi_b[b][:], SLAB)),
                        wait=(tok_add,), inc=sst[b])
            if ui + 1 < len(units):
                w01_free = last_up
                load_w01(ui + 1)
                w2_free = tmm
                load_w2(ui + 1)
        tok_seg_done = tok_st
    return [tok_st]



def _stage(inputs):
    f32 = lambda a: np.asarray(a, np.float32)
    wT = lambda a: np.ascontiguousarray(f32(a).T)
    pp = lambda v: np.ascontiguousarray(f32(v).reshape(8, 128).T)
    bpp = lambda b: np.ascontiguousarray(
        f32(b).reshape(8, 16, 128).transpose(2, 0, 1).reshape(128, 128))
    st = dict(make_const_arrays())
    st["w_qkv0"] = np.ascontiguousarray(np.concatenate(
        [wT(inputs["L0_attn_Wq"]), wT(inputs["L0_attn_Wk"]),
         wT(inputs["L0_attn_Wv"])], axis=1))
    st["w_o0"] = wT(inputs["L0_attn_Wo"])
    st["rtr0_w"] = wT(inputs["L0_ffn_router_w"])
    st["up0"] = np.ascontiguousarray(np.concatenate(
        [f32(inputs["L0_ffn_shared_up"]).transpose(0, 2, 1),
         f32(inputs["L0_ffn_routed_up"]).transpose(0, 2, 1)]))
    st["dn0"] = np.ascontiguousarray(np.concatenate(
        [f32(inputs["L0_ffn_shared_down"]).transpose(0, 2, 1),
         f32(inputs["L0_ffn_routed_down"]).transpose(0, 2, 1)]))
    st["w_qkv1"] = wT(inputs["L1_attn_w_qkv"])
    st["wg1"] = wT(inputs["L1_attn_w_g1"])
    st["wg2"] = wT(inputs["L1_attn_w_g2"])
    st["wout"] = wT(inputs["L1_attn_w_out"])
    st["bg1"] = np.ascontiguousarray(f32(inputs["L1_attn_b_g1"])[:, None])
    st["bg2"] = pp(inputs["L1_attn_b_g2"])
    st["gm"] = pp(inputs["L1_attn_gamma"])
    st["bt"] = pp(inputs["L1_attn_beta"])
    st["rtr1_w"] = wT(inputs["L1_ffn_gate_w"])
    st["rtr1_b"] = np.ascontiguousarray(f32(inputs["L1_ffn_gate_b"])[:, None])
    st["w0h"] = np.ascontiguousarray(f32(inputs["L1_ffn_w0"]).transpose(0, 2, 1))
    st["w1h"] = np.ascontiguousarray(f32(inputs["L1_ffn_w1"]).transpose(0, 2, 1))
    st["w2h"] = np.ascontiguousarray(f32(inputs["L1_ffn_w2"]).transpose(0, 2, 1))
    st["b0h"] = bpp(inputs["L1_ffn_b0"])
    st["b1h"] = bpp(inputs["L1_ffn_b1"])
    st["b2h"] = np.ascontiguousarray(f32(inputs["L1_ffn_b2"]))
    return st


def _run_op(build, st, per_core, out_names):
    p = Prog()
    build(p)
    nc = p.finalize()
    in_maps = []
    for i in range(8):
        m = {k: st[k] for k in st if k in p.ext_inputs}
        m.update(per_core[i])
        in_maps.append(m)
    res = run_bass_kernel_spmd(nc, in_maps, core_ids=list(range(8)))
    return {nm: [res.results[i][nm] for i in range(8)] for nm in out_names}


FUSED = True


def kernel(**inputs):
    if FUSED:
        return _kernel_fused(inputs)
    return _kernel_unfused(inputs)


def _kernel_unfused(inputs):
    st = _stage(inputs)
    x = np.asarray(inputs["L0_attn_x"], np.float32)
    xfm = [np.ascontiguousarray(x[i].T).reshape(NCH, 128, T) for i in range(8)]

    FMF = [NCH, 128, T]

    def ext(p, name, shape, dt):
        p.ext_inputs.add(name)
        return p.dram(name, shape, dt, kind="ExternalInput")

    def fin(p, toks):
        p.op("sp", lambda e: None, wait=tuple(toks))

    def b_ln(src_is, dst, gamma=None):
        def b(p):
            c = emit_consts(p)
            src = ext(p, "src", FMF, F32)
            d = p.dram(dst, FMF, BF16, kind="ExternalOutput")
            fin(p, emit_ln(p, c, src, d, [c["tok"], c["tok2"]]))
        return b

    def b_qkv(kind, wname):
        def b(p):
            c = emit_consts(p)
            xn = ext(p, "xn", FMF, BF16)
            w = ext(p, wname, [D, 3 * D], F32)
            qd = p.dram("qd", FMF, BF16, kind="ExternalOutput")
            kd = p.dram("kd", FMF, BF16, kind="ExternalOutput")
            vd = p.dram("vd", FMF, BF16, kind="ExternalOutput")
            fin(p, emit_qkv(p, c, xn, qd, kd, vd, w, kind, [c["tok"], c["tok2"]]))
        return b

    def b_attn(denom):
        def b(p):
            c = emit_consts(p)
            qd = ext(p, "qd", FMF, BF16)
            kd = ext(p, "kd", FMF, BF16)
            vd = ext(p, "vd", FMF, BF16)
            ad = p.dram("ad", FMF, BF16, kind="ExternalOutput")
            fin(p, emit_attn(p, c, qd, kd, vd, ad, denom, [c["tok"], c["tok2"]]))
        return b

    def b_wo(p):
        c = emit_consts(p)
        a = ext(p, "a", FMF, BF16)
        xi = ext(p, "xi", FMF, F32)
        w = ext(p, "w_o0", [D, D], F32)
        xo = p.dram("xo", FMF, F32, kind="ExternalOutput")
        fin(p, emit_proj_residual(p, c, a, xi, xo, w, [c["tok"], c["tok2"]]))

    def b_rtr(mode, wname, bias=None):
        E = 16 if mode == "top4x16" else 8
        def b(p):
            c = emit_consts(p)
            xn = ext(p, "xn", FMF, BF16)
            w = ext(p, wname, [D, E], F32)
            gd = p.dram("gd", [E, T], BF16, kind="ExternalOutput")
            deps = [c["tok"], c["tok2"]]
            bias_sb = None
            if bias is not None:
                bh = ext(p, bias, [E, 1], F32)
                bias_sb = p.sbuf("bias", [E, 1], F32)
                sbm = p.sem("bs")
                deps.append(p.dma("sp", lambda e: e.dma_start(out=bias_sb[:], in_=bh), inc=sbm))
            fin(p, emit_router(p, c, xn, gd, w, mode, deps, bias_sb=bias_sb))
        return b

    def b_moe0(p):
        c = emit_consts(p)
        xn = ext(p, "xn", FMF, BF16)
        gfm = ext(p, "gfm", [16, T], BF16)
        xi = ext(p, "xi", FMF, F32)
        up = ext(p, "up0", [18, D, 512], F32)
        dn = ext(p, "dn0", [18, 512, D], F32)
        xo = p.dram("xo", FMF, F32, kind="ExternalOutput")
        fin(p, emit_moe0(p, c, xn, gfm, xi, xo, up, dn, [c["tok"], c["tok2"]]))

    def b_pa1(p):
        c = emit_consts(p)
        at = ext(p, "at", FMF, BF16)
        xn = ext(p, "xn", FMF, BF16)
        xi = ext(p, "xi", FMF, F32)
        wg1 = ext(p, "wg1", [D, 64], F32)
        wg2 = ext(p, "wg2", [64, D], F32)
        wout = ext(p, "wout", [D, D], F32)
        bg1 = ext(p, "bg1", [64, 1], F32)
        bg2 = ext(p, "bg2", [128, 8], F32)
        gm = ext(p, "gm", [128, 8], F32)
        bt = ext(p, "bt", [128, 8], F32)
        xo = p.dram("xo", FMF, F32, kind="ExternalOutput")
        fin(p, emit_postattn1(p, c, at, xn, xi, xo, wg1, wg2, wout,
                              bg1, bg2, gm, bt, [c["tok"], c["tok2"]]))

    def b_moe1(p):
        c = emit_consts(p)
        xn = ext(p, "xn", FMF, BF16)
        wfm = ext(p, "wfm", [8, T], BF16)
        xi = ext(p, "xi", FMF, F32)
        w0 = ext(p, "w0h", [8, D, 2048], F32)
        w1 = ext(p, "w1h", [8, D, 2048], F32)
        w2 = ext(p, "w2h", [8, 2048, D], F32)
        b0 = ext(p, "b0h", [128, 128], F32)
        b1 = ext(p, "b1h", [128, 128], F32)
        b2 = ext(p, "b2h", [8, D], F32)
        yo = p.dram("yo", FMF, F32, kind="ExternalOutput")
        fin(p, emit_moe1(p, c, xn, wfm, xi, yo, w0, w1, w2, b0, b1, b2,
                         [c["tok"], c["tok2"]]))

    pc = lambda key, arrs: [{key: arrs[i]} for i in range(8)]

    def pc2(**kw):
        return [{k: v[i] for k, v in kw.items()} for i in range(8)]

    r = _run_op(b_ln("src", "dst"), st, pc("src", xfm), ["dst"])
    xn0 = r["dst"]
    r = _run_op(b_qkv("elu1", "w_qkv0"), st, pc("xn", xn0), ["qd", "kd", "vd"])
    r = _run_op(b_attn(True), st, pc2(qd=r["qd"], kd=r["kd"], vd=r["vd"]), ["ad"])
    a0 = r["ad"]
    r = _run_op(b_wo, st, pc2(a=a0, xi=xfm), ["xo"])
    x1 = r["xo"]
    r = _run_op(b_rtr("top4x16", "rtr0_w"), st, pc("xn", xn0), ["gd"])
    g0 = r["gd"]
    r = _run_op(b_moe0, st, pc2(xn=xn0, gfm=g0, xi=x1), ["xo"])
    x2 = r["xo"]
    r = _run_op(b_ln("src", "dst"), st, pc("src", x2), ["dst"])
    xn1 = r["dst"]
    r = _run_op(b_qkv("silu", "w_qkv1"), st, pc("xn", xn1), ["qd", "kd", "vd"])
    r = _run_op(b_attn(False), st, pc2(qd=r["qd"], kd=r["kd"], vd=r["vd"]), ["ad"])
    a1 = r["ad"]
    r = _run_op(b_rtr("top2x8", "rtr1_w", bias="rtr1_b"), st, pc("xn", xn1), ["gd"])
    w1g = r["gd"]
    r = _run_op(b_pa1, st, pc2(at=a1, xn=xn1, xi=x2), ["xo"])
    x3 = r["xo"]
    r = _run_op(b_moe1, st, pc2(xn=xn1, wfm=w1g, xi=x3), ["yo"])
    y = r["yo"]
    out = np.empty((8, T, D), np.float32)
    for i in range(8):
        out[i] = np.asarray(y[i], np.float32).reshape(D, T).T
    return out


def _kernel_fused(inputs):
    st = _stage(inputs)
    x = np.asarray(inputs["L0_attn_x"], np.float32)
    xfm = [np.ascontiguousarray(x[i].T).reshape(NCH, 128, T) for i in range(8)]
    FMF = [NCH, 128, T]
    p = Prog()

    def ext(name, shape, dt=F32):
        p.ext_inputs.add(name)
        return p.dram(name, shape, dt, kind="ExternalInput")

    c = emit_consts(p)
    x0 = ext("x0", FMF)
    w_qkv0 = ext("w_qkv0", [D, 3 * D])
    w_o0 = ext("w_o0", [D, D])
    rtr0_w = ext("rtr0_w", [D, 16])
    up0 = ext("up0", [18, D, 512])
    dn0 = ext("dn0", [18, 512, D])
    w_qkv1 = ext("w_qkv1", [D, 3 * D])
    wg1 = ext("wg1", [D, 64])
    wg2 = ext("wg2", [64, D])
    wout = ext("wout", [D, D])
    bg1 = ext("bg1", [64, 1])
    bg2 = ext("bg2", [128, 8])
    gm = ext("gm", [128, 8])
    bt = ext("bt", [128, 8])
    rtr1_w = ext("rtr1_w", [D, 8])
    rtr1_b = ext("rtr1_b", [8, 1])
    w0h = ext("w0h", [8, D, 2048])
    w1h = ext("w1h", [8, D, 2048])
    w2h = ext("w2h", [8, 2048, D])
    b0h = ext("b0h", [128, 128])
    b1h = ext("b1h", [128, 128])
    b2h = ext("b2h", [8, D])
    xn0 = p.dram("s_xn0", FMF, BF16)
    qd = p.dram("s_q", FMF, BF16)
    kd = p.dram("s_k", FMF, BF16)
    vd = p.dram("s_v", FMF, BF16)
    a0 = p.dram("s_a0", FMF, BF16)
    x1 = p.dram("s_x1", FMF, F32)
    g0 = p.dram("s_g0", [16, T], BF16)
    x2 = p.dram("s_x2", FMF, F32)
    xn1 = p.dram("s_xn1", FMF, BF16)
    qd1 = p.dram("s_q1", FMF, BF16)
    kd1 = p.dram("s_k1", FMF, BF16)
    vd1 = p.dram("s_v1", FMF, BF16)
    a1 = p.dram("s_a1", FMF, BF16)
    w1g = p.dram("s_w1g", [8, T], BF16)
    x3 = p.dram("s_x3", FMF, F32)
    y = p.dram("y", FMF, F32, kind="ExternalOutput")

    dep0 = [c["tok"], c["tok2"]]
    with p.scope():
        t_ln0 = emit_ln(p, c, x0, xn0, dep0, pfx="ln0")
    with p.scope():
        t_qkv0 = emit_qkv(p, c, xn0, qd, kd, vd, w_qkv0, "elu1", t_ln0, pfx="q0")
    with p.scope():
        t_at0 = emit_attn(p, c, qd, kd, vd, a0, True, t_qkv0, pfx="a0")
    with p.scope():
        t_wo0 = emit_proj_residual(p, c, a0, x0, x1, w_o0, t_at0, pfx="w0")
    with p.scope():
        t_rt0 = emit_router(p, c, xn0, g0, rtr0_w, "top4x16", t_wo0, pfx="r0")
    with p.scope():
        t_m0 = emit_moe0(p, c, xn0, g0, x1, x2, up0, dn0, t_rt0, pfx="m0")
    with p.scope():
        t_ln1 = emit_ln(p, c, x2, xn1, t_m0, pfx="ln1")
    with p.scope():
        t_qkv1 = emit_qkv(p, c, xn1, qd1, kd1, vd1, w_qkv1, "silu", t_ln1, pfx="q1")
    with p.scope():
        t_at1 = emit_attn(p, c, qd1, kd1, vd1, a1, False, t_qkv1, pfx="a1")
    with p.scope():
        t_rt1 = emit_router(p, c, xn1, w1g, rtr1_w, "top2x8", t_at1,
                            bias_sb=None, bias_hbm=rtr1_b, pfx="r1")
    with p.scope():
        t_pa1 = emit_postattn1(p, c, a1, xn1, x2, x3, wg1, wg2, wout,
                               bg1, bg2, gm, bt, t_rt1, pfx="pa")
    with p.scope():
        t_m1 = emit_moe1(p, c, xn1, w1g, x3, y, w0h, w1h, w2h, b0h, b1h, b2h,
                         t_pa1, pfx="m1")
    p.op("sp", lambda e: None, wait=tuple(t_m1))
    nc = p.finalize()
    in_maps = []
    for i in range(8):
        m = {k: st[k] for k in st if k in p.ext_inputs}
        m["x0"] = xfm[i]
        in_maps.append(m)
    res = run_bass_kernel_spmd(nc, in_maps, core_ids=list(range(8)))
    out = np.empty((8, T, D), np.float32)
    for i in range(8):
        out[i] = np.asarray(res.results[i]["y"], np.float32).reshape(D, T).T
    return out


# version 15 (final)
# speedup vs baseline: 1.1479x; 1.0020x over previous
import contextlib
from contextlib import contextmanager
import numpy as np
import concourse.bass as bass
import concourse.mybir as mybir
from concourse.bass_utils import run_bass_kernel_spmd

F32 = mybir.dt.float32
BF16 = mybir.dt.bfloat16
U8 = mybir.dt.uint8
AF = mybir.ActivationFunctionType
ALU = mybir.AluOpType
AX = mybir.AxisListType

T = 4096
D = 1024
NCH = 8
H = 16
HD = 64
CH = 128
NCHUNK = T // CH
SLAB = 512
NSLAB = T // SLAB
HALF = T // 2
EPS_LN = 1e-5
EPS_ATTN = 1e-12
NEG_BIG = -1e30
DBG_SKIP = set()


class Sem:
    __slots__ = ("h", "v")

    def __init__(self, h):
        self.h = h
        self.v = 0

    @property
    def tok(self):
        return (self, self.v)


class Prog:
    ENGINES = ("pe", "act", "dve", "pool", "sp")

    def __init__(self):
        self.nc = bass.Bass("TRN2", target_bir_lowering=False)
        self.stack = contextlib.ExitStack()
        self.sem_stack = contextlib.ExitStack()
        self.sched = {e: [] for e in self.ENGINES}
        self._names = {}
        self.all_sems = []
        self.ext_inputs = set()
        self.sem_free = []
        self._scope_sems = None

    @contextmanager
    def scope(self):
        old = self.stack
        self.stack = contextlib.ExitStack()
        outer_scope = self._scope_sems
        self._scope_sems = []
        try:
            yield
        finally:
            self.stack.close()
            self.stack = old
            self.sem_free.extend(self._scope_sems)
            self._scope_sems = outer_scope

    def name(self, base):
        n = self._names.get(base, 0)
        self._names[base] = n + 1
        return "%s_%d" % (base, n)

    def sem(self, base="s"):
        if self._scope_sems is not None and self.sem_free:
            sm = self.sem_free.pop()
        else:
            sm = Sem(self.sem_stack.enter_context(self.nc.semaphore(self.name(base))))
            self.all_sems.append(sm)
        if self._scope_sems is not None:
            self._scope_sems.append(sm)
        return sm

    def sbuf(self, base, shape, dt):
        return self.stack.enter_context(self.nc.sbuf_tensor(self.name(base), shape, dt))

    def psum(self, base, shape, dt):
        return self.stack.enter_context(self.nc.psum_tensor(self.name(base), shape, dt))

    def dram(self, name, shape, dt, kind=None):
        if kind is None:
            return self.nc.dram_tensor(name, shape, dt).ap()
        return self.nc.dram_tensor(name, shape, dt, kind=kind).ap()

    def _emit(self, eng, thunk, wait, inc, amt):
        tok = None
        if inc is not None:
            inc.v += amt
            tok = (inc, inc.v)
        self.sched[eng].append((tuple(wait), thunk, (inc, amt) if inc is not None else None))
        return tok

    def op(self, eng, thunk, wait=(), inc=None):
        return self._emit(eng, thunk, wait, inc, 1)

    def dma(self, eng, thunk, wait=(), inc=None):
        return self._emit(eng, thunk, wait, inc, 16)

    def finalize(self):
        tail = tuple((sm, sm.v) for sm in self.all_sems if sm.v > 0)
        self.sched["sp"].append((tail, lambda e: None, None))
        sched = self.sched

        def runner(eng_name):
            def go(eng):
                last = {}
                for waits, thunk, inc in sched[eng_name]:
                    for sem, val in waits:
                        if last.get(id(sem), -1) < val:
                            eng.wait_ge(sem.h, val)
                            last[id(sem)] = val
                    inst = thunk(eng)
                    if inc is not None:
                        sem, amt = inc
                        inst.then_inc(sem.h, amt)
            return go

        with self.nc.Block() as block:
            block.tensor(runner("pe"))
            block.scalar(runner("act"))
            block.vector(runner("dve"))
            block.gpsimd(runner("pool"))
            block.sync(runner("sp"))
        self.stack.close()
        self.sem_stack.close()
        return self.nc


def bcast_mid(ap2d, n):
    return bass.AP(ap2d.tensor, ap2d.offset,
                   [list(ap2d.ap[0]), [0, n], list(ap2d.ap[1])])


def fm3(sb_ap, width, nch=NCH):
    return sb_ap.rearrange("p (c t) -> p c t", c=nch)


def hbm_slabT(hbm, t0, w):
    return hbm[:, :, t0:t0 + w].transpose([1, 0, 2])



def make_const_arrays():
    c = {}
    c["c_ident"] = np.eye(128, dtype=np.float32)
    c["c_ones_col"] = np.ones((128, 1), dtype=np.float32)
    c["c_ones_row"] = np.ones((1, 128), dtype=np.float32)
    c["c_triu4"] = np.tile(np.triu(np.ones((128, 128), dtype=np.float32)), (1, 4))
    oh16 = np.zeros((16, 16 * 128), dtype=np.float32)
    for e in range(16):
        oh16[e, e * 128:(e + 1) * 128] = 1.0
    c["c_oh16"] = oh16
    oh8 = np.zeros((8, 8 * 128), dtype=np.float32)
    for e in range(8):
        oh8[e, e * 128:(e + 1) * 128] = 1.0
    c["c_oh8"] = oh8
    kb = np.zeros((128, 8), dtype=np.float32)
    kb[:, 0] = EPS_LN
    kb[:, 1] = EPS_ATTN
    c["c_kb"] = kb
    return c


def emit_consts(p):
    c = {}
    hb = {}
    for k, v in make_const_arrays().items():
        p.ext_inputs.add(k)
        hb[k] = p.dram(k, list(v.shape), F32, kind="ExternalInput")
    c["ident"] = p.sbuf("ident", [128, 128], BF16)
    c["ones_col_bf"] = p.sbuf("ones_col_bf", [128, 1], BF16)
    c["ones_col_f"] = p.sbuf("ones_col_f", [128, 1], F32)
    c["ones_row_f"] = p.sbuf("ones_row_f", [1, 128], F32)
    c["triu4"] = p.sbuf("triu4", [128, 512], F32)
    c["oh16"] = p.sbuf("oh16", [16, 2048], BF16)
    c["oh8"] = p.sbuf("oh8", [8, 1024], BF16)
    c["kb"] = p.sbuf("kb", [128, 8], F32)
    c["identf"] = p.sbuf("identf", [128, 128], F32)
    s1 = p.sem("csemp")
    s2 = p.sem("csems")
    p.dma("pool", lambda e: e.dma_start(out=c["ident"][:], in_=hb["c_ident"]), inc=s1)
    p.dma("pool", lambda e: e.dma_start(out=c["ones_col_bf"][:], in_=hb["c_ones_col"]), inc=s1)
    p.dma("pool", lambda e: e.dma_start(out=c["oh16"][:], in_=hb["c_oh16"]), inc=s1)
    p.dma("pool", lambda e: e.dma_start(out=c["oh8"][:], in_=hb["c_oh8"]), inc=s1)
    p.dma("sp", lambda e: e.dma_start(out=c["ones_col_f"][:], in_=hb["c_ones_col"]), inc=s2)
    p.dma("sp", lambda e: e.dma_start(out=c["ones_row_f"][:], in_=hb["c_ones_row"]), inc=s2)
    p.dma("sp", lambda e: e.dma_start(out=c["triu4"][:], in_=hb["c_triu4"]), inc=s2)
    p.dma("sp", lambda e: e.dma_start(out=c["kb"][:], in_=hb["c_kb"]), inc=s2)
    p.dma("sp", lambda e: e.dma_start(out=c["identf"][:], in_=hb["c_ident"]), inc=s2)
    c["tok"] = (s1, s1.v)
    c["tok2"] = (s2, s2.v)
    return c



def emit_ln_rows(p, toks, pfx, c, xs_chunk, nch, w, ps_mean, ps_sq, ps_A, ps_B,
                 rows, dep_sq_tok, dep_ld_tok, waitAB=(), wait_stats=(),
                 defer_bcast=False):
    spe, sact, sdve = toks["pe"], toks["act"], toks["dve"]
    t = None
    for cc in range(nch):
        w0 = ((dep_ld_tok,) if dep_ld_tok is not None else ()) + tuple(wait_stats)
        t = p.op("pe", lambda e, cc=cc: e.matmul(
            ps_mean[:], c["ones_col_f"][:], xs_chunk(cc),
            start=(cc == 0), stop=(cc == nch - 1)),
            wait=w0 if cc == 0 else (), inc=spe if cc == nch - 1 else None)
    tok_mean = t
    sq_chunk = toks["sq_chunk"]
    for cc in range(nch):
        t = p.op("pe", lambda e, cc=cc: e.matmul(
            ps_sq[:], c["ones_col_f"][:], sq_chunk(cc),
            start=(cc == 0), stop=(cc == nch - 1)),
            wait=(dep_sq_tok,) if cc == 0 else (), inc=spe if cc == nch - 1 else None)
    tok_sq = t
    mu, m2, musq, var, rstd, bb = (rows[0:1, i * w:(i + 1) * w] for i in range(6))
    tmu = p.op("act", lambda e: e.activation(out=mu, in_=ps_mean[:], func=AF.Copy, scale=1.0 / D),
         wait=(tok_mean,), inc=sact)
    p.op("act", lambda e: e.activation(out=m2, in_=ps_sq[:], func=AF.Copy, scale=1.0 / D),
         wait=(tok_sq,))
    tok_stats = p.op("act", lambda e: e.activation(out=musq, in_=mu, func=AF.Square),
                     wait=(tmu,), inc=sact)
    td = p.op("dve", lambda e: e.tensor_sub(var, m2, musq), wait=(tok_stats,), inc=sdve)
    ta = p.op("act", lambda e: e.activation(out=var, in_=var, func=AF.Ln, bias=c["kb"][0:1, 0:1]),
              wait=(td,), inc=sact)
    ta2 = p.op("act", lambda e: e.activation(out=rstd, in_=var, func=AF.Exp, scale=-0.5),
               wait=(ta,), inc=sact)
    tok_rows = p.op("dve", lambda e: e.scalar_tensor_tensor(
        bb, mu, -1.0, rstd, ALU.mult, ALU.mult), wait=(ta2,), inc=sdve)
    if toks.get("defer_bcast") or defer_bcast:
        return tok_rows, None, tok_stats
    p.op("pe", lambda e: e.matmul(ps_A[:], c["ones_row_f"][:], rstd, start=True, stop=True),
         wait=(tok_rows,) + tuple(waitAB))
    tok_bc = p.op("pe", lambda e: e.matmul(ps_B[:], c["ones_row_f"][:], bb, start=True, stop=True),
                  inc=spe)
    return tok_rows, tok_bc, tok_stats


def emit_ln(p, c, src, dst, deps, pfx="ln", gamma_sb=None, beta_sb=None):
    spe, sact, sdve = p.sem(pfx), p.sem(pfx), p.sem(pfx)
    sld = [p.sem(pfx) for _ in range(2)]
    sst = [p.sem(pfx) for _ in range(2)]
    xs = [p.sbuf(pfx + "x", [128, NCH * SLAB], F32) for _ in range(2)]
    xq = [p.sbuf(pfx + "q", [128, NCH * SLAB], F32) for _ in range(2)]
    xo = [p.sbuf(pfx + "o", [128, NCH * SLAB], BF16) for _ in range(2)]
    rows = [p.sbuf(pfx + "r", [1, 6 * SLAB], F32) for _ in range(2)]
    ps_mean = [p.psum(pfx + "pm", [1, SLAB], F32) for _ in range(2)]
    ps_sq = [p.psum(pfx + "ps", [1, SLAB], F32) for _ in range(2)]
    ps_A = [p.psum(pfx + "pa", [128, SLAB], F32) for _ in range(2)]
    ps_B = [p.psum(pfx + "pb", [128, SLAB], F32) for _ in range(2)]
    xs_free = [None, None]
    xo_free = [None, None]
    tok_x2 = [None, None]
    tok_stats = [None, None]
    tok_rows_s = [None, None]
    tok_st = None
    for s in range(NSLAB + 1):
        b = s % 2
        if s < NSLAB:
            t0 = s * SLAB
            w = tuple(deps) if s < 2 else ()
            if xs_free[b] is not None:
                w = w + (xs_free[b],)
            tok_ld = p.dma("sp", lambda e, b=b, t0=t0: e.dma_start(
                out=fm3(xs[b][:], SLAB), in_=hbm_slabT(src, t0, SLAB)), wait=w,
                inc=sld[b])
            tok_sq = p.op("act", lambda e, b=b: e.activation(
                out=xq[b][:], in_=xs[b][:], func=AF.Square), wait=(tok_ld,), inc=sact)
            toks = {"pe": spe, "act": sact, "dve": sdve,
                    "sq_chunk": lambda cc, b=b: xq[b][:, cc * SLAB:(cc + 1) * SLAB]}
            tok_rows, tok_bc_unused, tok_stats2 = emit_ln_rows(
                p, toks, pfx, c,
                lambda cc, b=b: xs[b][:, cc * SLAB:(cc + 1) * SLAB],
                NCH, SLAB, ps_mean[b], ps_sq[b], ps_A[b], ps_B[b], rows[b],
                tok_sq, tok_ld,
                waitAB=(tok_x2[b],) if tok_x2[b] is not None else (),
                wait_stats=(tok_stats[b],) if tok_stats[b] is not None else (),
                defer_bcast=True)
            tok_stats[b] = tok_stats2
            tok_rows_s[b] = tok_rows
        if s >= 1:
            sb_ = (s - 1) % 2
            t0p = (s - 1) * SLAB
            p.op("pe", lambda e, b=sb_: e.matmul(
                ps_A[b][:], c["ones_row_f"][:], rows[b][0:1, 4 * SLAB:5 * SLAB],
                start=True, stop=True),
                wait=(tok_rows_s[sb_],) + ((tok_x2[sb_],) if tok_x2[sb_] is not None else ()))
            tok_bc = p.op("pe", lambda e, b=sb_: e.matmul(
                ps_B[b][:], c["ones_row_f"][:], rows[b][0:1, 5 * SLAB:6 * SLAB],
                start=True, stop=True), inc=spe)
            wx = (tok_bc,)
            if xo_free[sb_] is not None:
                wx = wx + (xo_free[sb_],)
            xs3 = fm3(xs[sb_][:], SLAB)
            xq3 = fm3(xq[sb_][:], SLAB)
            xo3 = fm3(xo[sb_][:], SLAB)
            tmul = p.op("dve", lambda e, b=sb_, xs3=xs3, xq3=xq3: e.tensor_tensor(
                xq3, xs3, bcast_mid(ps_A[b][:], NCH), ALU.mult), wait=wx, inc=sdve)
            if gamma_sb is None:
                tok_x = p.op("dve", lambda e, b=sb_, xq3=xq3, xo3=xo3: e.tensor_tensor(
                    xo3, xq3, bcast_mid(ps_B[b][:], NCH), ALU.add),
                    wait=(tmul,), inc=sdve)
            else:
                tadd = p.op("dve", lambda e, b=sb_, xq3=xq3: e.tensor_tensor(
                    xq3, xq3, bcast_mid(ps_B[b][:], NCH), ALU.add),
                    wait=(tmul,), inc=sdve)
                tok_x = None
                for cc in range(NCH):
                    xq_c = xq[sb_][:, cc * SLAB:(cc + 1) * SLAB]
                    xo_c = xo[sb_][:, cc * SLAB:(cc + 1) * SLAB]
                    tok_x = p.op("dve", lambda e, a=xq_c, o=xo_c, cc=cc: e.tensor_scalar(
                        o, a, gamma_sb[:, cc:cc + 1], beta_sb[:, cc:cc + 1],
                        ALU.mult, ALU.add), wait=(tadd,) if cc == 0 else (),
                        inc=sdve if cc == NCH - 1 else None)
            xs_free[sb_] = tok_x
            tok_x2[sb_] = tok_x
            tok_st = p.dma("sp", lambda e, b=sb_, t0p=t0p: e.dma_start(
                out=hbm_slabT(dst, t0p, SLAB), in_=fm3(xo[b][:], SLAB)),
                wait=(tok_x,), inc=sst[sb_])
            xo_free[sb_] = tok_st
    return [tok_st]




def emit_qkv(p, c, xn, qd, kd, vd, w_hbm, kind, deps, pfx="qkv"):
    spe, sact, sdve, spool = p.sem(pfx), p.sem(pfx), p.sem(pfx), p.sem(pfx)
    sw, sx = p.sem(pfx), p.sem(pfx)
    sst = [p.sem(pfx) for _ in range(4)]
    w_sb = p.sbuf(pfx + "w", [128, NCH * 3072], BF16)
    tok_w = p.dma("pool", lambda e: e.dma_start(
        out=fm3(w_sb[:], 3072), in_=w_hbm.rearrange("(c p) m -> p c m", p=128)),
        wait=tuple(deps), inc=sw)
    xn_sb = p.sbuf(pfx + "x", [128, NCH * T], BF16)
    tok_x = p.dma("sp", lambda e: e.dma_start(
        out=fm3(xn_sb[:], T), in_=xn.transpose([1, 0, 2])), wait=tuple(deps), inc=sx)
    nps = 4
    pss = [p.psum(pfx + "p", [128, SLAB], F32) for _ in range(nps)]
    ps_free = [None] * nps
    sc1 = [p.sbuf(pfx + "s1", [128, SLAB], F32) for _ in range(2)]
    sc2 = [p.sbuf(pfx + "s2", [128, SLAB], F32) for _ in range(2)]
    ob = [p.sbuf(pfx + "o", [128, SLAB], BF16) for _ in range(4)]
    ob_free = [None] * 4
    sc_free = [None, None]
    tok_st = None
    i = 0
    for mc in range(24):
        dest = (qd, kd, vd)[mc // 8]
        cc_out = mc % 8
        act = kind if mc < 16 else "copy"
        for s in range(NSLAB):
            t0 = s * SLAB
            pb = i % nps
            w0 = (tok_w, tok_x) if i == 0 else ()
            if ps_free[pb] is not None:
                w0 = w0 + (ps_free[pb],)
            tmm = None
            for kc in range(NCH):
                lhs = w_sb[:, kc * 3072 + mc * 128: kc * 3072 + (mc + 1) * 128]
                rhs = xn_sb[:, kc * T + t0: kc * T + t0 + SLAB]
                tmm = p.op("pe", lambda e, l=lhs, r=rhs, pb=pb, kc=kc: e.matmul(
                    pss[pb][:], l, r, start=(kc == 0), stop=(kc == NCH - 1)),
                    wait=w0 if kc == 0 else (), inc=spe if kc == NCH - 1 else None)
            oi = i % 4
            wo = (tmm,)
            if ob_free[oi] is not None:
                wo = wo + (ob_free[oi],)
            if act == "silu":
                tok_ev = p.op("act", lambda e, pb=pb, oi=oi: e.activation(
                    out=ob[oi][:], in_=pss[pb][:], func=AF.Silu), wait=wo, inc=sact)
                ps_free[pb] = tok_ev
            elif act == "copy":
                tok_ev = p.op("act", lambda e, pb=pb, oi=oi: e.copy(
                    out=ob[oi][:], in_=pss[pb][:]), wait=wo, inc=sact)
                ps_free[pb] = tok_ev
            else:
                si = i % 2
                if sc_free[si] is not None:
                    wo = wo + (sc_free[si],)
                t1 = p.op("act", lambda e, pb=pb, si=si: e.activation(
                    out=sc1[si][:], in_=pss[pb][:], func=AF.Exp),
                    wait=wo, inc=sact)
                t2 = p.op("dve", lambda e, si=si: e.tensor_single_scalar(
                    sc1[si][:], sc1[si][:], 1.0, ALU.min), wait=(t1,), inc=sdve)
                t3 = p.op("dve", lambda e, pb=pb, si=si: e.tensor_single_scalar(
                    sc2[si][:], pss[pb][:], 0.0, ALU.max), wait=(t2,), inc=sdve)
                ps_free[pb] = t3
                tok_ev = p.op("pool", lambda e, si=si, oi=oi: e.tensor_add(
                    ob[oi][:], sc1[si][:], sc2[si][:]), wait=(t2, t3), inc=spool)
                sc_free[si] = tok_ev
            tok_st = p.dma("sp", lambda e, d=dest, cc=cc_out, t0=t0, oi=oi: e.dma_start(
                out=d[cc, :, t0:t0 + SLAB], in_=ob[oi][:]), wait=(tok_ev,), inc=sst[oi])
            ob_free[oi] = tok_st
            i += 1
    return [tok_st]



def emit_proj_residual(p, c, a_in, x_in, x_out, w_hbm, deps, pfx="wo"):
    spe, sact, sdve = p.sem(pfx), p.sem(pfx), p.sem(pfx)
    sw = p.sem(pfx)
    sa = [p.sem(pfx) for _ in range(2)]
    sxi = [p.sem(pfx) for _ in range(2)]
    sst = [p.sem(pfx) for _ in range(2)]
    w_sb = p.sbuf(pfx + "w", [128, NCH * D], BF16)
    tok_w = p.dma("pool", lambda e: e.dma_start(
        out=fm3(w_sb[:], D), in_=w_hbm.rearrange("(c p) m -> p c m", p=128)),
        wait=tuple(deps), inc=sw)
    ab = [p.sbuf(pfx + "a", [128, NCH * SLAB], BF16) for _ in range(2)]
    xb = [p.sbuf(pfx + "xi", [128, NCH * SLAB], F32) for _ in range(2)]
    ob = [p.sbuf(pfx + "xo", [128, NCH * SLAB], F32) for _ in range(2)]
    pss = [p.psum(pfx + "p", [128, SLAB], F32) for _ in range(4)]
    ps_free = [None] * 4
    in_free = [None, None]
    ob_free = [None, None]
    tok_st = None
    for s in range(NSLAB):
        b = s % 2
        t0 = s * SLAB
        w0 = tuple(deps) if s < 2 else ()
        if in_free[b] is not None:
            w0 = w0 + (in_free[b],)
        tok_a = p.dma("sp", lambda e, b=b, t0=t0: e.dma_start(
            out=fm3(ab[b][:], SLAB), in_=hbm_slabT(a_in, t0, SLAB)), wait=w0, inc=sa[b])
        tok_xi = p.dma("sp", lambda e, b=b, t0=t0: e.dma_start(
            out=fm3(xb[b][:], SLAB), in_=hbm_slabT(x_in, t0, SLAB)), inc=sxi[b])
        last_ev = None
        for mc in range(NCH):
            pb = mc % 4
            wmm = (tok_a,) if mc == 0 else ()
            wmm = wmm + ((tok_w,) if s == 0 and mc == 0 else ())
            if ps_free[pb] is not None:
                wmm = wmm + (ps_free[pb],)
            tmm = None
            for kc in range(NCH):
                lhs = w_sb[:, kc * D + mc * 128: kc * D + (mc + 1) * 128]
                rhs = ab[b][:, kc * SLAB: (kc + 1) * SLAB]
                tmm = p.op("pe", lambda e, l=lhs, r=rhs, pb=pb, kc=kc: e.matmul(
                    pss[pb][:], l, r, start=(kc == 0), stop=(kc == NCH - 1)),
                    wait=wmm if kc == 0 else (), inc=spe if kc == NCH - 1 else None)
            wev = (tmm, tok_xi)
            if mc == 0 and ob_free[b] is not None:
                wev = wev + (ob_free[b],)
            last_ev = p.op("dve", lambda e, b=b, mc=mc, pb=pb: e.tensor_add(
                ob[b][:, mc * SLAB:(mc + 1) * SLAB], pss[pb][:],
                xb[b][:, mc * SLAB:(mc + 1) * SLAB]), wait=wev, inc=sdve)
            ps_free[pb] = last_ev
        in_free[b] = last_ev
        tok_st = p.dma("sp", lambda e, b=b, t0=t0: e.dma_start(
            out=hbm_slabT(x_out, t0, SLAB), in_=fm3(ob[b][:], SLAB)),
            wait=(last_ev,), inc=sst[b])
        ob_free[b] = tok_st
    return [tok_st]



def emit_attn(p, c, qd, kd, vd, ad, denom, deps, pfx="at"):
    spe, sact, sdve = p.sem(pfx), p.sem(pfx), p.sem(pfx)
    sq = [p.sem(pfx) for _ in range(2)]
    sk = [p.sem(pfx) for _ in range(2)]
    sv = [p.sem(pfx) for _ in range(2)]
    sst = [p.sem(pfx) for _ in range(2)]
    qs = [p.sbuf(pfx + "q", [128, NCH * CH], BF16) for _ in range(2)]
    ks = [p.sbuf(pfx + "k", [128, NCH * CH], BF16) for _ in range(2)]
    vs = [p.sbuf(pfx + "v", [128, NCH * CH], BF16) for _ in range(2)]
    ktm = p.sbuf(pfx + "ktm", [128, NCH * CH], BF16)
    vtm = p.sbuf(pfx + "vtm", [128, NCH * CH], BF16)
    S_sb = p.sbuf(pfx + "S", [128, 512], BF16)
    z_sb = p.sbuf(pfx + "z", [128, 8], BF16)
    sc_sb = [p.sbuf(pfx + "sc", [128, 512], BF16) for _ in range(2)]
    nm_sb = [p.sbuf(pfx + "nm", [128, 256], F32) for _ in range(2)]
    rden = p.sbuf(pfx + "rd", [128, 256], F32)
    at_sb = [p.sbuf(pfx + "o", [128, NCH * CH], BF16) for _ in range(2)]
    ps_S = p.psum(pfx + "pS", [128, 512], F32)
    ps_sc2 = p.psum(pfx + "psc", [128, 1024], F32)
    ps_sc = [ps_sc2[:, 0:512], ps_sc2[:, 512:1024]]
    ps_nm = [p.psum(pfx + "pnm", [128, 512], F32) for _ in range(2)]
    ps_tr = p.psum(pfx + "ptr", [128, 1024], BF16)
    if denom:
        ps_z = p.psum(pfx + "pz", [128, 512], F32)

    def hslice(sb, h):
        pb = (h % 2) * 64
        return sb[pb:pb + 64, (h // 2) * CH:(h // 2) * CH + CH]

    SCCOL = [0, 256, 128, 384]

    t0s = tuple(deps)
    tok_S0 = p.op("dve", lambda e: e.memset(S_sb[:], 0.0), wait=t0s, inc=sdve)
    tok_psS0 = p.op("dve", lambda e: e.memset(ps_S[:], 0.0), inc=sdve)
    tok_psz0 = None
    if denom:
        tok_S0 = p.op("dve", lambda e: e.memset(z_sb[:], 0.0), inc=sdve)
        tok_psz0 = p.op("dve", lambda e: e.memset(ps_z[:, 0:8], 0.0), inc=sdve)
    qkv_free = [None, None]
    scfree = None
    scb_free = [None, None]
    nmfree = [None, None]
    at_free = [None, None]
    tok_trfree = None
    tok_Ssb = tok_S0
    tok_zsb = tok_S0
    tok_Supd_prev = None
    tok_zupd_prev = None
    tok_st = None
    for ci in range(NCHUNK):
        b = ci % 2
        c0 = ci * CH
        w0 = t0s if ci < 2 else ()
        if qkv_free[b] is not None:
            w0 = w0 + (qkv_free[b],)
        tok_q = p.dma("sp", lambda e, b=b, c0=c0: e.dma_start(
            out=fm3(qs[b][:], CH), in_=hbm_slabT(qd, c0, CH)), wait=w0, inc=sq[b])
        tok_k = p.dma("sp", lambda e, b=b, c0=c0: e.dma_start(
            out=fm3(ks[b][:], CH), in_=hbm_slabT(kd, c0, CH)), inc=sk[b])
        tok_v = p.dma("sp", lambda e, b=b, c0=c0: e.dma_start(
            out=fm3(vs[b][:], CH), in_=hbm_slabT(vd, c0, CH)), inc=sv[b])
        wtr = (tok_k,) + ((tok_trfree,) if tok_trfree is not None else ())
        tk = None
        if "tr" in DBG_SKIP:
            tok_ktm = p.op("dve", lambda e: e.memset(ktm[:], 0.001), wait=wtr, inc=sdve)
            tok_vtm = p.op("dve", lambda e: e.memset(vtm[:], 0.001), wait=(tok_v,), inc=sdve)
            tok_trfree = tok_vtm
            dbg_skip_tr = True
        else:
            dbg_skip_tr = False
        if not dbg_skip_tr:
            for cc in range(NCH):
                tk = p.op("pe", lambda e, b=b, cc=cc: e.transpose(
                    ps_tr[:, cc * CH:(cc + 1) * CH], ks[b][:, cc * CH:(cc + 1) * CH],
                    c["ident"][:]), wait=wtr if cc == 0 else (),
                    inc=spe if cc == NCH - 1 else None)
            tok_ktm = p.op("dve", lambda e: e.tensor_copy(out=ktm[:], in_=ps_tr[:]),
                           wait=(tk,), inc=sdve)
            tv = None
            for cc in range(NCH):
                tv = p.op("pe", lambda e, b=b, cc=cc: e.transpose(
                    ps_tr[:, cc * CH:(cc + 1) * CH], vs[b][:, cc * CH:(cc + 1) * CH],
                    c["ident"][:]), wait=(tok_v, tok_ktm) if cc == 0 else (),
                    inc=spe if cc == NCH - 1 else None)
            tok_vtm = p.op("dve", lambda e: e.tensor_copy(out=vtm[:], in_=ps_tr[:]),
                           wait=(tv,), inc=sdve)
            tok_trfree = tok_vtm
        ep_toks = []
        for g in range(4):
            b2 = g % 2
            heads = [4 * g + j for j in range(4)]
            wsc = (tok_q,)
            if scfree is not None:
                wsc = wsc + (scfree,)
            tsc = None
            if "sc" in DBG_SKIP:
                tsc = p.op("pe", lambda e: e.matmul(
                    ps_sc[0][:], c["ident"][:], qs[b][:, 0:512], start=True,
                    stop=True), wait=wsc, inc=spe)
            else:
              for j, h in enumerate(heads):
                pb = (h % 2) * 64
                tsc = p.op("pe", lambda e, b=b, j=j, h=h, pb=pb: e.matmul(
                    ps_sc[j % 2][:, (j // 2) * CH:(j // 2 + 1) * CH],
                    hslice(ks[b], h), hslice(qs[b], h), start=True, stop=True,
                    tile_position=(pb, 0)), wait=wsc if j == 0 else (),
                    inc=spe if j == 3 else None)
            wev = (tsc,)
            if scb_free[b2] is not None:
                wev = wev + (scb_free[b2],)
            ps_both = bass.AP(ps_sc2[:].tensor, ps_sc2[:].offset,
                              [list(ps_sc2[:].ap[0]), [512, 2], [1, 256]])
            msk2 = bass.AP(c["triu4"][:].tensor, c["triu4"][:].offset,
                           [list(c["triu4"][:].ap[0]), [0, 2], [1, 256]])
            tok_scev = p.op("dve", lambda e, b2=b2, ps_both=ps_both, msk2=msk2: e.tensor_tensor(
                sc_sb[b2][:].rearrange("p (g t) -> p g t", g=2), ps_both, msk2,
                ALU.mult), wait=wev, inc=sdve)
            if denom:
                wd = (tok_scev, tok_zsb)
                td = None
                for j, h in enumerate(heads):
                    pb = (h % 2) * 64
                    cb = j // 2
                    dreg = ps_sc[b2][pb:pb + 64, 256 + cb * CH:256 + (cb + 1) * CH]
                    zsl = z_sb[pb:pb + 64, (h // 2):(h // 2) + 1]
                    zrep = bass.AP(zsl.tensor, zsl.offset,
                                   [list(zsl.ap[0]), [0, 64]])
                    ones64 = bass.AP(c["ones_col_bf"][:].tensor, 0,
                                     [[1, 128], [0, 64]])
                    td = p.op("pe", lambda e, b=b, o=dreg, z=zrep, h=h, pb=pb: e.matmul(
                        o, z, hslice(qs[b], h),
                        start=True, stop=False, tile_position=(pb, pb)),
                        wait=wd if j == 0 else (), inc=None)
                    td = p.op("pe", lambda e, o=dreg, n1=ones64, j=j, b2=b2, pb=pb: e.matmul(
                        o, n1, sc_sb[b2][:, SCCOL[j]:SCCOL[j] + CH],
                        start=False, stop=True, tile_position=(0, pb)),
                        inc=spe if j == 3 else None)
                tl = p.op("act", lambda e, b2=b2: e.activation(
                    out=rden[:], in_=ps_sc[b2][:, 256:512], func=AF.Ln),
                    wait=(td,), inc=sact)
                tok_rden = p.op("act", lambda e: e.activation(
                    out=rden[:], in_=rden[:], func=AF.Exp, scale=-1.0),
                    wait=(tl,), inc=sact)
            wnm = (tok_scev, tok_Ssb, tok_vtm)
            if nmfree[b2] is not None:
                wnm = wnm + (nmfree[b2],)
            tnm = None
            if "nm" in DBG_SKIP:
                tnm = p.op("pe", lambda e, b2=b2: e.matmul(
                    ps_nm[b2][:], c["ident"][:], qs[b][:, 0:512], start=True,
                    stop=True), wait=wnm, inc=spe)
            else:
              for j, h in enumerate(heads):
                pb = (h % 2) * 64
                cb = j // 2
                out_ap = ps_nm[b2][pb:pb + 64, cb * CH:(cb + 1) * CH]
                tnm = p.op("pe", lambda e, o=out_ap, h=h, pb=pb, b=b, j=j: e.matmul(
                    o, S_sb[pb:pb + 64, (h // 2) * 64:(h // 2) * 64 + 64],
                    hslice(qs[b], h), start=True, stop=False,
                    tile_position=(pb, pb)), wait=wnm if j == 0 else (), inc=None)
                tnm = p.op("pe", lambda e, o=out_ap, h=h, pb=pb, j=j, b2=b2: e.matmul(
                    o, vtm[:, h * 64:(h + 1) * 64],
                    sc_sb[b2][:, SCCOL[j]:SCCOL[j] + CH], start=False, stop=True,
                    tile_position=(0, pb)), inc=spe if j == 3 else None)
            if denom:
                tok_nmev = p.op("act", lambda e, b2=b2: e.copy(
                    out=nm_sb[b2][:], in_=ps_nm[b2][:, 0:256]), wait=(tnm,), inc=sact)
                nmfree[b2] = tok_nmev
                wep = (tok_nmev, tok_rden)
                if g < 2 and at_free[b] is not None:
                    wep = wep + (at_free[b],)
                tmul = p.op("dve", lambda e, b2=b2: e.tensor_mul(
                    nm_sb[b2][:], nm_sb[b2][:], rden[:]), wait=wep,
                    inc=sdve)
                tep = p.op("dve", lambda e, b=b, g=g, b2=b2: e.tensor_single_scalar(
                    at_sb[b][:, 2 * g * CH:(2 * g + 2) * CH], nm_sb[b2][:],
                    EPS_ATTN, ALU.add), wait=(tmul,), inc=sdve)
                scfree = tok_rden
                scb_free[b2] = tnm
            else:
                wep = (tnm,)
                if g < 2 and at_free[b] is not None:
                    wep = wep + (at_free[b],)
                tep = p.op("act", lambda e, b=b, g=g, b2=b2: e.copy(
                    out=at_sb[b][:, 2 * g * CH:(2 * g + 2) * CH],
                    in_=ps_nm[b2][:, 0:256]), wait=wep, inc=sact)
                nmfree[b2] = tep
                scfree = tok_scev
                scb_free[b2] = tnm
            ep_toks.append(tep)
        wS = (tok_ktm, tok_vtm)
        if ci == 0:
            wS = wS + (tok_psS0,)
        if tok_Ssb is not None:
            wS = wS + (tok_Ssb,)
        tS = None
        for h in range(H):
            pb = (h % 2) * 64
            tS = p.op("pe", lambda e, h=h, pb=pb, ci=ci: e.matmul(
                ps_S[pb:pb + 64, (h // 2) * 64:(h // 2) * 64 + 64],
                ktm[:, h * 64:(h + 1) * 64], vtm[:, h * 64:(h + 1) * 64],
                start=(ci == 0 and h == 0), stop=(ci == NCHUNK - 1 and h == H - 1),
                skip_group_check=True,
                tile_position=(0, pb)), wait=wS if h == 0 else (),
                inc=spe if h == H - 1 else None)
        tok_Supd = tS
        if denom:
            wz = (tok_zsb,)
            if ci == 0:
                wz = wz + (tok_psz0,)
            tz = None
            for h in range(H):
                pb = (h % 2) * 64
                tz = p.op("pe", lambda e, h=h, pb=pb, ci=ci: e.matmul(
                    ps_z[pb:pb + 64, (h // 2):(h // 2) + 1],
                    ktm[:, h * 64:(h + 1) * 64], c["ones_col_bf"][:],
                    start=(ci == 0 and h == 0),
                    stop=(ci == NCHUNK - 1 and h == H - 1),
                    skip_group_check=True,
                    tile_position=(0, pb)), wait=wz if h == 0 else (),
                    inc=spe if h == H - 1 else None)
            tok_zupd = tz
        if ci < NCHUNK - 1:
            tok_Ssb = p.op("act", lambda e: e.copy(out=S_sb[:], in_=ps_S[:]),
                           wait=(tok_Supd,), inc=sact)
            if denom:
                tok_zsb = p.op("act", lambda e: e.copy(
                    out=z_sb[:], in_=ps_z[:, 0:8]), wait=(tok_zupd,), inc=sact)
        qkv_free[b] = tok_Supd
        tok_st = p.dma("sp", lambda e, b=b, c0=c0: e.dma_start(
            out=hbm_slabT(ad, c0, CH), in_=fm3(at_sb[b][:], CH)),
            wait=(ep_toks[-1],), inc=sst[b])
        at_free[b] = tok_st
    return [tok_st]



def emit_router(p, c, xn, gd, w_hbm, mode, deps, bias_sb=None, bias_hbm=None,
                pfx="rt"):
    E = 16 if mode == "top4x16" else 8
    TOPK = 4 if mode == "top4x16" else 2
    spe, sact, sdve = p.sem(pfx), p.sem(pfx), p.sem(pfx)
    sx = [p.sem(pfx) for _ in range(2)]
    sw = p.sem(pfx)
    sst = [p.sem(pfx) for _ in range(2)]
    w_sb = p.sbuf(pfx + "w", [128, NCH * E], BF16)
    tok_w = p.dma("pool", lambda e: e.dma_start(
        out=fm3(w_sb[:], E), in_=w_hbm.rearrange("(c p) m -> p c m", p=128)),
        wait=tuple(deps), inc=sw)
    if bias_hbm is not None:
        bias_sb = p.sbuf(pfx + "bias", [E, 1], F32)
        sbm = p.sem(pfx)
        deps = list(deps) + [p.dma("sp", lambda e: e.dma_start(
            out=bias_sb[:], in_=bias_hbm), wait=tuple(deps), inc=sbm)]
    xs = [p.sbuf(pfx + "x", [128, NCH * SLAB], BF16) for _ in range(2)]
    lg = p.sbuf(pfx + "lg", [E, SLAB], F32)
    tm = p.sbuf(pfx + "tm", [128, 4 * E], F32)
    gt = p.sbuf(pfx + "gt", [128, 4 * E], F32)
    sc1 = p.sbuf(pfx + "s1", [128, 4 * E], F32)
    sc2 = p.sbuf(pfx + "s2", [128, 4 * E], F32)
    rowm = p.sbuf(pfx + "rm", [128, 8], F32)
    go = [p.sbuf(pfx + "go", [E, SLAB], BF16) for _ in range(2)]
    ps_l = p.psum(pfx + "pl", [128, 512], F32)
    ps_f = p.psum(pfx + "pf", [128, 512], F32)
    ps_b = p.psum(pfx + "pb", [128, 512], F32)
    xs_free = [None, None]
    go_free = [None, None]
    tok_lgread = None
    tok_tm = None
    tok_go = None
    tok_st = None
    for s in range(NSLAB):
        b = s % 2
        t0 = s * SLAB
        w0 = tuple(deps) if s < 2 else ()
        if xs_free[b] is not None:
            w0 = w0 + (xs_free[b],)
        tok_x = p.dma("sp", lambda e, b=b, t0=t0: e.dma_start(
            out=fm3(xs[b][:], SLAB), in_=hbm_slabT(xn, t0, SLAB)), wait=w0, inc=sx[b])
        wmm = (tok_x,) + ((tok_w,) if s == 0 else ())
        if tok_lgread is not None:
            wmm = wmm + (tok_lgread,)
        tmm = None
        for kc in range(NCH):
            tmm = p.op("pe", lambda e, kc=kc, b=b: e.matmul(
                ps_l[0:E, :], w_sb[:, kc * E:(kc + 1) * E],
                xs[b][:, kc * SLAB:(kc + 1) * SLAB],
                start=(kc == 0), stop=(kc == NCH - 1)),
                wait=wmm if kc == 0 else (), inc=spe if kc == NCH - 1 else None)
        xs_free[b] = tmm
        if bias_sb is None:
            tok_lg = p.op("dve", lambda e: e.tensor_copy(out=lg[:], in_=ps_l[0:E, :]),
                          wait=(tmm,), inc=sdve)
        else:
            tok_lg = p.op("dve", lambda e: e.tensor_scalar_add(
                lg[:], ps_l[0:E, :], bias_sb[0:E, 0:1]), wait=(tmm,), inc=sdve)
        tok_lgread = tok_lg
        wtr = (tok_lg,)
        if tok_tm is not None:
            wtr = wtr + (tok_tm,)
        ttr = None
        for j in range(4):
            ttr = p.op("pe", lambda e, j=j: e.transpose(
                ps_f[:, j * E:(j + 1) * E], lg[0:E, j * CH:(j + 1) * CH],
                c["identf"][0:E, 0:E]), wait=wtr if j == 0 else (),
                inc=spe if j == 3 else None)
        tok_tm = p.op("dve", lambda e: e.tensor_copy(out=tm[:], in_=ps_f[:, 0:4 * E]),
                      wait=(ttr,), inc=sdve)
        E4 = 4 * E
        tm3 = tm[:, 0:E4].rearrange("p (c t) -> p c t", c=4)
        gt3 = gt[:, 0:E4].rearrange("p (c t) -> p c t", c=4)
        s13 = sc1[:, 0:E4].rearrange("p (c t) -> p c t", c=4)
        s23 = sc2[:, 0:E4].rearrange("p (c t) -> p c t", c=4)
        m3 = rowm[:, 0:4].rearrange("p (c t) -> p c t", c=4)
        sm3 = rowm[:, 4:8].rearrange("p (c t) -> p c t", c=4)
        mb = bass.AP(rowm[:].tensor, rowm[:].offset,
                     [list(rowm[:].ap[0]), [1, 4], [0, E]])
        smb = bass.AP(rowm[:].tensor, rowm[:].offset + 4,
                      [list(rowm[:].ap[0]), [1, 4], [0, E]])
        tprev = [tok_tm]

        def step(thunk, eng="dve", extra=()):
            w = ((tprev[0],) if tprev[0] is not None else ()) + tuple(extra)
            tprev[0] = p.op(eng, thunk, wait=w, inc=(sdve if eng == "dve" else sact))
            return tprev[0]

        if mode == "top4x16":
            step(lambda e: e.reduce_max(m3, tm3, axis=AX.X))
            step(lambda e: e.tensor_tensor(s13, tm3, mb, ALU.subtract))
            step(lambda e: e.activation(out=sc1[:, 0:E4], in_=sc1[:, 0:E4],
                                        func=AF.Exp), eng="act")
            step(lambda e: e.reduce_sum(sm3, s13, axis=AX.X))
            step(lambda e: e.reciprocal(out=rowm[:, 4:8], in_=rowm[:, 4:8]))
            step(lambda e: e.tensor_tensor(tm3, s13, smb, ALU.mult))
            step(lambda e: e.tensor_copy(out=sc1[:, 0:E4], in_=tm[:, 0:E4]))
            for it in range(TOPK - 1):
                step(lambda e: e.reduce_max(m3, s13, axis=AX.X))
                step(lambda e: e.tensor_tensor(s23, s13, mb, ALU.is_ge))
                step(lambda e: e.scalar_tensor_tensor(
                    sc1[:, 0:E4], sc2[:, 0:E4], NEG_BIG, sc1[:, 0:E4],
                    ALU.mult, ALU.add))
            step(lambda e: e.reduce_max(m3, s13, axis=AX.X))
            step(lambda e: e.tensor_tensor(s23, tm3, mb, ALU.is_ge))
            tok_gt = step(lambda e: e.tensor_tensor(gt3, tm3, s23, ALU.mult))
        else:
            step(lambda e: e.reduce_max(m3, tm3, axis=AX.X))
            step(lambda e: e.tensor_tensor(s13, tm3, mb, ALU.subtract))
            step(lambda e: e.tensor_tensor(s23, tm3, mb, ALU.is_ge))
            step(lambda e: e.scalar_tensor_tensor(
                sc2[:, 0:E4], sc2[:, 0:E4], NEG_BIG, tm[:, 0:E4],
                ALU.mult, ALU.add))
            step(lambda e: e.activation(out=sc1[:, 0:E4], in_=sc1[:, 0:E4],
                                        func=AF.Exp), eng="act")
            step(lambda e: e.reduce_max(m3, s23, axis=AX.X))
            step(lambda e: e.tensor_tensor(s23, tm3, mb, ALU.is_ge))
            step(lambda e: e.tensor_tensor(s13, s13, s23, ALU.mult))
            step(lambda e: e.reduce_sum(sm3, s13, axis=AX.X))
            step(lambda e: e.reciprocal(out=rowm[:, 4:8], in_=rowm[:, 4:8]))
            tok_gt = step(lambda e: e.tensor_tensor(gt3, s13, smb, ALU.mult))
        wbk = (tok_gt,)
        if tok_go is not None:
            wbk = wbk + (tok_go,)
        tbk = None
        for j in range(4):
            tbk = p.op("pe", lambda e, j=j: e.transpose(
                ps_b[0:E, j * CH:(j + 1) * CH], gt[:, j * E:(j + 1) * E],
                c["identf"][:]), wait=wbk if j == 0 else (),
                inc=spe if j == 3 else None)
        wgo = (tbk,)
        if go_free[b] is not None:
            wgo = wgo + (go_free[b],)
        tok_go = p.op("dve", lambda e, b=b: e.tensor_copy(
            out=go[b][:], in_=ps_b[0:E, 0:SLAB]), wait=wgo, inc=sdve)
        tok_st = p.dma("sp", lambda e, b=b, t0=t0: e.dma_start(
            out=gd[0:E, t0:t0 + SLAB], in_=go[b][:]), wait=(tok_go,), inc=sst[b])
        go_free[b] = tok_st
    return [tok_st]



def emit_moe0(p, c, xn, gfm, x_in, x_out, up_hbm, dn_hbm, deps, pfx="m0"):
    NE = 18
    SEG = 2048
    NSEG = T // SEG
    SL = SEG // SLAB
    spe, sact, sdve = p.sem(pfx), p.sem(pfx), p.sem(pfx)
    swu = [p.sem(pfx) for _ in range(2)]
    swd = [p.sem(pfx) for _ in range(2)]
    sxh, sg = p.sem(pfx), p.sem(pfx)
    sxi = [p.sem(pfx) for _ in range(2)]
    sst = [p.sem(pfx) for _ in range(2)]
    acc = p.sbuf(pfx + "acc", [128, NCH * SEG], F32)
    xn_h = p.sbuf(pfx + "xn", [128, NCH * SEG], BF16)
    g_h = p.sbuf(pfx + "g", [16, SEG], BF16)
    wu = [p.sbuf(pfx + "wu", [128, NCH * 512], BF16) for _ in range(2)]
    wd = [p.sbuf(pfx + "wd", [128, 4 * D], BF16) for _ in range(2)]
    act_b = [p.sbuf(pfx + "act", [128, 4 * SLAB], BF16) for _ in range(2)]
    gl = p.sbuf(pfx + "gl", [128, SLAB], F32)
    xi_b = [p.sbuf(pfx + "xi", [128, NCH * SLAB], F32) for _ in range(2)]
    ob = xi_b
    ps_u = [p.psum(pfx + "pu", [128, SLAB], F32) for _ in range(3)]
    ps_d = [p.psum(pfx + "pd", [128, SLAB], F32) for _ in range(3)]
    ps_bc = p.psum(pfx + "pbc", [128, SLAB], F32)
    wu_free = [None, None]
    wd_free = [None, None]
    units = [(seg, e) for seg in range(NSEG) for e in range(NE)]
    tok_w = {}

    def load_unit(i, first=False):
        seg, e = units[i]
        par = i % 2
        w = tuple(deps) if i < 2 else ()
        if wu_free[par] is not None:
            w = w + (wu_free[par],)
        tok_wu = p.dma("pool", lambda e_, e=e: e_.dma_start(
            out=fm3(wu[par][:], 512),
            in_=up_hbm[e].rearrange("(c p) m -> p c m", p=128)), wait=w, inc=swu[par])
        w = ()
        if wd_free[par] is not None:
            w = (wd_free[par],)
        tok_wd = p.dma("pool", lambda e_, e=e: e_.dma_start(
            out=fm3(wd[par][:], D, nch=4),
            in_=dn_hbm[e].rearrange("(c p) m -> p c m", p=128)), wait=w, inc=swd[par])
        tok_w[i] = (tok_wu, tok_wd)

    load_unit(0, first=True)
    load_unit(1)
    tok_st = None
    ps_u_free = [None, None, None]
    ps_d_free = [None, None, None]
    bc_free = None
    act_free = [None, None]
    tok_acc = None
    for seg in range(NSEG):
        h0 = seg * SEG
        tok_xh = p.dma("sp", lambda e, h0=h0: e.dma_start(
            out=fm3(xn_h[:], SEG), in_=hbm_slabT(xn, h0, SEG)),
            wait=tuple(deps) + ((tok_acc,) if tok_acc is not None else ()), inc=sxh)
        tok_g = p.dma("sp", lambda e, h0=h0: e.dma_start(
            out=g_h[:], in_=gfm[0:16, h0:h0 + SEG]),
            wait=tuple(deps) if seg == 0 else (), inc=sg)
        tok_accfree = tok_acc
        pend = None

        def emit_down(e, sl, ab, tok_act, tok_wd, first_for_acc):
            nonlocal tok_acc
            t0 = sl * SLAB
            par_ = None
            tmm = None
            for mc in range(NCH):
                pd = mc % 3
                wmm = (tok_act,)
                if mc == 0 and sl == 0:
                    wmm = wmm + (tok_wd,)
                if ps_d_free[pd] is not None:
                    wmm = wmm + (ps_d_free[pd],)
                tmm = None
                for kc in range(4):
                    lhs = wd[e % 2 if False else _wpar[0]][:, kc * D + mc * 128: kc * D + (mc + 1) * 128]
                    rhs = act_b[ab][:, kc * SLAB:(kc + 1) * SLAB]
                    tmm = p.op("pe", lambda e_, l=lhs, r=rhs, pd=pd, kc=kc: e_.matmul(
                        ps_d[pd][:], l, r, start=(kc == 0), stop=(kc == 3)),
                        wait=wmm if kc == 0 else (), inc=spe if kc == 3 else None)
                if first_for_acc:
                    wacc = (tmm,)
                    if tok_accfree is not None:
                        wacc = wacc + (tok_accfree,)
                    tok_acc = p.op("dve", lambda e_, mc=mc, t0=t0, pd=pd: e_.tensor_scalar(
                        acc[:, mc * SEG + t0: mc * SEG + t0 + SLAB], ps_d[pd][:],
                        1.0, None, ALU.mult), wait=wacc, inc=sdve)
                else:
                    tok_acc = p.op("dve", lambda e_, mc=mc, t0=t0, pd=pd: e_.scalar_tensor_tensor(
                        acc[:, mc * SEG + t0: mc * SEG + t0 + SLAB], ps_d[pd][:], 1.0,
                        acc[:, mc * SEG + t0: mc * SEG + t0 + SLAB], ALU.mult, ALU.add),
                        wait=(tmm,), inc=sdve)
                ps_d_free[pd] = tok_acc
            return tmm

        _wpar = [0]
        for e in range(NE):
            ui = seg * NE + e
            par = ui % 2
            routed = e >= 2
            tok_wu, tok_wd = tok_w[ui]
            last_up_mm = None
            last_dn_mm = None
            for s in range(SL):
                t0 = s * SLAB
                ab = s % 2
                if routed:
                    wb = (tok_g,)
                    if bc_free is not None:
                        wb = wb + (bc_free,)
                    tok_bc = p.op("pe", lambda e_, e=e, t0=t0: e_.matmul(
                        ps_bc[:], c["oh16"][:, (e - 2) * 128:(e - 1) * 128],
                        g_h[0:16, t0:t0 + SLAB], start=True, stop=True), wait=wb,
                        inc=spe)
                tok_act = None
                for mc in range(4):
                    pu = mc % 3
                    wmm = ()
                    if mc == 0 and s == 0:
                        wmm = (tok_wu, tok_xh)
                    if ps_u_free[pu] is not None:
                        wmm = wmm + (ps_u_free[pu],)
                    if mc == 0 and act_free[ab] is not None:
                        wmm = wmm + (act_free[ab],)
                    tmm = None
                    for kc in range(NCH):
                        lhs = wu[par][:, kc * 512 + mc * 128: kc * 512 + (mc + 1) * 128]
                        rhs = xn_h[:, kc * SEG + t0: kc * SEG + t0 + SLAB]
                        tmm = p.op("pe", lambda e_, l=lhs, r=rhs, pu=pu, kc=kc: e_.matmul(
                            ps_u[pu][:], l, r, start=(kc == 0), stop=(kc == NCH - 1)),
                            wait=wmm if kc == 0 else (), inc=spe if kc == NCH - 1 else None)
                    last_up_mm = tmm
                    if routed:
                        tge = p.op("act", lambda e_, pu=pu: e_.activation(
                            out=gl[:], in_=ps_u[pu][:], func=AF.Gelu), wait=(tmm,), inc=sact)
                        ps_u_free[pu] = tge
                        tok_act = p.op("dve", lambda e_, mc=mc, ab=ab: e_.tensor_mul(
                            act_b[ab][:, mc * SLAB:(mc + 1) * SLAB], gl[:], ps_bc[:]),
                            wait=(tge, tok_bc), inc=sdve)
                    else:
                        tok_act = p.op("act", lambda e_, pu=pu, mc=mc, ab=ab: e_.activation(
                            out=act_b[ab][:, mc * SLAB:(mc + 1) * SLAB], in_=ps_u[pu][:],
                            func=AF.Gelu), wait=(tmm,), inc=sact)
                        ps_u_free[pu] = tok_act
                if routed:
                    bc_free = tok_act
                if pend is not None:
                    pe_, ps_, pab, pact, pwd, pfirst = pend
                    _wpar[0] = (seg * NE + pe_) % 2
                    last_dn_mm = emit_down(pe_, ps_, pab, pact, pwd, pfirst)
                    act_free[pab] = last_dn_mm
                    if ps_ == SL - 1:
                        wd_free[(seg * NE + pe_) % 2] = last_dn_mm
                        if seg * NE + pe_ + 2 < len(units):
                            load_unit(seg * NE + pe_ + 2)
                pend = (e, s, ab, tok_act, tok_wd, e == 0)
            wu_free[par] = last_up_mm
        if pend is not None:
            pe_, ps_, pab, pact, pwd, pfirst = pend
            _wpar[0] = (seg * NE + pe_) % 2
            last_dn_mm = emit_down(pe_, ps_, pab, pact, pwd, pfirst)
            act_free[pab] = last_dn_mm
            if ps_ == SL - 1:
                wd_free[(seg * NE + pe_) % 2] = last_dn_mm
                if seg * NE + pe_ + 2 < len(units):
                    load_unit(seg * NE + pe_ + 2)
            pend = None
        last = None
        for s in range(SL):
            t0g = h0 + s * SLAB
            t0 = s * SLAB
            b = s % 2
            w0 = (tok_acc,)
            if seg == 0 and s < 2:
                w0 = w0 + tuple(deps)
            if sst[b].v > 0:
                w0 = w0 + ((sst[b], sst[b].v),)
            tok_xi = p.dma("sp", lambda e, b=b, t0g=t0g: e.dma_start(
                out=fm3(xi_b[b][:], SLAB), in_=hbm_slabT(x_in, t0g, SLAB)),
                wait=w0, inc=sxi[b])
            wadd = (tok_xi, tok_acc)
            tok_add = p.op("dve", lambda e, b=b, t0=t0: e.tensor_add(
                xi_b[b][:].rearrange("p (c t) -> p c t", c=NCH),
                xi_b[b][:].rearrange("p (c t) -> p c t", c=NCH),
                acc[:].rearrange("p (c t) -> p c t", c=NCH)[:, :, t0:t0 + SLAB]),
                wait=wadd, inc=sdve)
            last = p.dma("sp", lambda e, b=b, t0g=t0g: e.dma_start(
                out=hbm_slabT(x_out, t0g, SLAB), in_=fm3(ob[b][:], SLAB)),
                wait=(tok_add,), inc=sst[b])
        tok_st = last
        tok_acc = tok_st
    return [tok_st]



def emit_postattn1(p, c, attn, xn2, x_in, x_out, wg1, wg2, wout,
                   bg1h, bg2h, gmh, bth, deps, pfx="pa"):
    spe, sact, sdve = p.sem(pfx), p.sem(pfx), p.sem(pfx)
    swp, sws = p.sem(pfx), p.sem(pfx)
    sa = [p.sem(pfx) for _ in range(2)]
    sx = [p.sem(pfx) for _ in range(2)]
    sxi = [p.sem(pfx) for _ in range(2)]
    sst = [p.sem(pfx) for _ in range(2)]
    wg1_sb = p.sbuf(pfx + "wg1", [128, NCH * 64], BF16)
    wg2_sb = p.sbuf(pfx + "wg2", [64, D], BF16)
    wo_sb = p.sbuf(pfx + "wo", [128, NCH * D], BF16)
    bg1_sb = p.sbuf(pfx + "bg1", [64, 1], F32)
    bg2_sb = p.sbuf(pfx + "bg2", [128, 8], F32)
    gm_sb = p.sbuf(pfx + "gm", [128, 8], F32)
    bt_sb = p.sbuf(pfx + "bt", [128, 8], F32)
    p.dma("pool", lambda e: e.dma_start(
        out=fm3(wg1_sb[:], 64), in_=wg1.rearrange("(c p) m -> p c m", p=128)),
        wait=tuple(deps), inc=swp)
    p.dma("pool", lambda e: e.dma_start(out=wg2_sb[:], in_=wg2), inc=swp)
    tok_wp = p.dma("pool", lambda e: e.dma_start(
        out=fm3(wo_sb[:], D), in_=wout.rearrange("(c p) m -> p c m", p=128)), inc=swp)
    p.dma("sp", lambda e: e.dma_start(out=bg1_sb[:], in_=bg1h),
          wait=tuple(deps), inc=sws)
    p.dma("sp", lambda e: e.dma_start(out=bg2_sb[:], in_=bg2h), inc=sws)
    p.dma("sp", lambda e: e.dma_start(out=gm_sb[:], in_=gmh), inc=sws)
    tok_ws = p.dma("sp", lambda e: e.dma_start(out=bt_sb[:], in_=bth), inc=sws)
    at_b = [p.sbuf(pfx + "at", [128, NCH * SLAB], BF16) for _ in range(2)]
    xn_b = [p.sbuf(pfx + "xn", [128, NCH * SLAB], BF16) for _ in range(2)]
    xi_b = [p.sbuf(pfx + "xi", [128, NCH * SLAB], F32) for _ in range(2)]
    g1_sb = p.sbuf(pfx + "g1", [64, SLAB], BF16)
    gate_sb = p.sbuf(pfx + "gate", [128, NCH * SLAB], BF16)
    ag = [p.sbuf(pfx + "ag", [128, NCH * SLAB], F32) for _ in range(2)]
    sq = [p.sbuf(pfx + "sq", [128, NCH * SLAB], F32) for _ in range(2)]
    ln_sb = p.sbuf(pfx + "ln", [128, NCH * SLAB], BF16)
    rows = [p.sbuf(pfx + "r", [1, 6 * SLAB], F32) for _ in range(2)]
    ps1 = p.psum(pfx + "p1", [128, SLAB], F32)
    ps2 = [p.psum(pfx + "p2", [128, SLAB], F32) for _ in range(2)]
    ps_mean = p.psum(pfx + "pm", [1, SLAB], F32)
    ps_sq = p.psum(pfx + "ps", [1, SLAB], F32)
    ps_A = p.psum(pfx + "pa", [128, SLAB], F32)
    ps_B = p.psum(pfx + "pb", [128, SLAB], F32)
    in_free = [None, None]
    st_free = [None, None]
    xn_free = [None, None]
    ps2_free = [None, None]
    tok_g1rd = None
    tok_gate_rd = None
    tok_stats = None
    tok_apply = [None, None]
    tok_rows_s = [None, None]
    tok_ag_s = [None, None]
    tok_st = None
    for s in range(NSLAB + 1):
        b = s % 2
        if s < NSLAB:
            t0 = s * SLAB
            w0 = tuple(deps) if s < 2 else ()
            if in_free[b] is not None:
                w0 = w0 + (in_free[b],)
            tok_at = p.dma("sp", lambda e, b=b, t0=t0: e.dma_start(
                out=fm3(at_b[b][:], SLAB), in_=hbm_slabT(attn, t0, SLAB)), wait=w0,
                inc=sa[b])
            tok_xn = p.dma("sp", lambda e, b=b, t0=t0: e.dma_start(
                out=fm3(xn_b[b][:], SLAB), in_=hbm_slabT(xn2, t0, SLAB)),
                wait=w0 + ((xn_free[b],) if xn_free[b] is not None else ()), inc=sx[b])
            tok_xi = p.dma("sp", lambda e, b=b, t0=t0: e.dma_start(
                out=fm3(xi_b[b][:], SLAB), in_=hbm_slabT(x_in, t0, SLAB)),
                wait=w0 + ((st_free[b],) if st_free[b] is not None else ()), inc=sxi[b])
            wmm = (tok_xn,)
            if s == 0:
                wmm = wmm + (tok_wp, tok_ws)
            if tok_g1rd is not None:
                wmm = wmm + (tok_g1rd,)
            tmm = None
            for kc in range(NCH):
                tmm = p.op("pe", lambda e, kc=kc, b=b: e.matmul(
                    ps1[0:64, :], wg1_sb[:, kc * 64:(kc + 1) * 64],
                    xn_b[b][:, kc * SLAB:(kc + 1) * SLAB],
                    start=(kc == 0), stop=(kc == NCH - 1)),
                    wait=wmm if kc == 0 else (), inc=spe if kc == NCH - 1 else None)
            tok_g1 = p.op("dve", lambda e: e.tensor_scalar_add(
                g1_sb[:], ps1[0:64, :], bg1_sb[:, 0:1]), wait=(tmm,), inc=sdve)
            tok_g1rd = tok_g1
            xn_free[b] = tmm
            last_gate = None
            for mc in range(NCH):
                pp = mc % 2
                wg = (tok_g1,)
                if ps2_free[pp] is not None:
                    wg = wg + (ps2_free[pp],)
                if mc == 0 and tok_gate_rd is not None:
                    wg = wg + (tok_gate_rd,)
                tg = p.op("pe", lambda e, mc=mc, pp=pp: e.matmul(
                    ps2[pp][:], wg2_sb[:, mc * 128:(mc + 1) * 128], g1_sb[:],
                    start=True, stop=True), wait=wg, inc=spe)
                last_gate = p.op("act", lambda e, mc=mc, pp=pp: e.activation(
                    out=gate_sb[:, mc * SLAB:(mc + 1) * SLAB], in_=ps2[pp][:],
                    func=AF.Sigmoid, bias=bg2_sb[:, mc:mc + 1]), wait=(tg,), inc=sact)
                ps2_free[pp] = last_gate
            tok_ag = p.op("dve", lambda e, b=b: e.tensor_mul(
                ag[b][:], at_b[b][:], gate_sb[:]), wait=(last_gate, tok_at), inc=sdve)
            tok_gate_rd = tok_ag
            tok_ag_s[b] = tok_ag
            in_free[b] = tok_ag
            tok_sq = p.op("act", lambda e, b=b: e.activation(
                out=sq[b][:], in_=ag[b][:], func=AF.Square), wait=(tok_ag,), inc=sact)
            toks = {"pe": spe, "act": sact, "dve": sdve,
                    "sq_chunk": lambda cc, b=b: sq[b][:, cc * SLAB:(cc + 1) * SLAB]}
            tok_rows, _unused, tok_stats2 = emit_ln_rows(
                p, toks, pfx, c,
                lambda cc, b=b: ag[b][:, cc * SLAB:(cc + 1) * SLAB],
                NCH, SLAB, ps_mean, ps_sq, ps_A, ps_B, rows[b],
                tok_sq, tok_ag,
                wait_stats=(tok_stats,) if tok_stats is not None else (),
                defer_bcast=True)
            tok_stats = tok_stats2
            tok_rows_s[b] = tok_rows
        if s >= 1:
            sb_ = (s - 1) % 2
            t0p = (s - 1) * SLAB
            wbc = (tok_rows_s[sb_],)
            if tok_apply[(s - 1 + 1) % 2] is not None:
                wbc = wbc + (tok_apply[(s - 1 + 1) % 2],)
            p.op("pe", lambda e, b=sb_: e.matmul(
                ps_A[:], c["ones_row_f"][:], rows[b][0:1, 4 * SLAB:5 * SLAB],
                start=True, stop=True), wait=wbc)
            tok_bc = p.op("pe", lambda e, b=sb_: e.matmul(
                ps_B[:], c["ones_row_f"][:], rows[b][0:1, 5 * SLAB:6 * SLAB],
                start=True, stop=True), inc=spe)
            ag3 = fm3(ag[sb_][:], SLAB)
            sq3 = fm3(sq[sb_][:], SLAB)
            tmul = p.op("dve", lambda e, ag3=ag3, sq3=sq3: e.tensor_tensor(
                sq3, ag3, bcast_mid(ps_A[:], NCH), ALU.mult), wait=(tok_bc,), inc=sdve)
            tadd = p.op("dve", lambda e, sq3=sq3: e.tensor_tensor(
                sq3, sq3, bcast_mid(ps_B[:], NCH), ALU.add), wait=(tmul,), inc=sdve)
            tok_x = None
            for cc in range(NCH):
                sq_c = sq[sb_][:, cc * SLAB:(cc + 1) * SLAB]
                ln_c = ln_sb[:, cc * SLAB:(cc + 1) * SLAB]
                tok_x = p.op("dve", lambda e, a=sq_c, o=ln_c, cc=cc: e.tensor_scalar(
                    o, a, gm_sb[:, cc:cc + 1], bt_sb[:, cc:cc + 1], ALU.mult, ALU.add),
                    wait=(tadd,) if cc == 0 else (), inc=sdve if cc == NCH - 1 else None)
            tok_apply[sb_] = tok_x
            last_ev = None
            for mc in range(NCH):
                pp = mc % 2
                wmm = (tok_x,) if mc == 0 else ()
                if ps2_free[pp] is not None:
                    wmm = wmm + (ps2_free[pp],)
                tmm = None
                for kc in range(NCH):
                    lhs = wo_sb[:, kc * D + mc * 128: kc * D + (mc + 1) * 128]
                    rhs = ln_sb[:, kc * SLAB:(kc + 1) * SLAB]
                    tmm = p.op("pe", lambda e, l=lhs, r=rhs, pp=pp, kc=kc: e.matmul(
                        ps2[pp][:], l, r, start=(kc == 0), stop=(kc == NCH - 1)),
                        wait=wmm if kc == 0 else (), inc=spe if kc == NCH - 1 else None)
                wev = (tmm,)
                if mc == 0:
                    wev = wev + ((sxi[sb_], sxi[sb_].v),)
                last_ev = p.op("dve", lambda e, b=sb_, mc=mc, pp=pp: e.tensor_add(
                    xi_b[b][:, mc * SLAB:(mc + 1) * SLAB], ps2[pp][:],
                    xi_b[b][:, mc * SLAB:(mc + 1) * SLAB]), wait=wev, inc=sdve)
                ps2_free[pp] = last_ev
            tok_st = p.dma("sp", lambda e, b=sb_, t0p=t0p: e.dma_start(
                out=hbm_slabT(x_out, t0p, SLAB), in_=fm3(xi_b[b][:], SLAB)),
                wait=(last_ev,), inc=sst[sb_])
            st_free[sb_] = tok_st
    return [tok_st]



def emit_moe1(p, c, xn2, wfm, x_in, y_out, w0h, w1h, w2h, b0h, b1h, b2h,
              deps, pfx="m1"):
    NE = 8
    HB = 2
    SEG = 1024
    NSEG = T // SEG
    SL = SEG // SLAB
    spe, sact, sdve, spool = p.sem(pfx), p.sem(pfx), p.sem(pfx), p.sem(pfx)
    sw0, sw1, sw2 = p.sem(pfx), p.sem(pfx), p.sem(pfx)
    swf, sb = p.sem(pfx), p.sem(pfx)
    sx = [p.sem(pfx) for _ in range(2)]
    sxi = [p.sem(pfx) for _ in range(2)]
    sst = [p.sem(pfx) for _ in range(2)]
    acc = p.sbuf(pfx + "acc", [128, NCH * SEG], F32)
    xn_b = [p.sbuf(pfx + "xn", [128, NCH * SLAB], BF16) for _ in range(2)]
    wf_h = p.sbuf(pfx + "wf", [8, SEG], BF16)
    w0_sb = p.sbuf(pfx + "w0", [128, NCH * 2048], BF16)
    w1_sb = p.sbuf(pfx + "w1", [128, NCH * 2048], BF16)
    w2_sb = p.sbuf(pfx + "w2", [128, 16 * D], BF16)
    b0_sb = p.sbuf(pfx + "b0", [128, 128], F32)
    b1_sb = p.sbuf(pfx + "b1", [128, 128], F32)
    b2_sb = p.sbuf(pfx + "b2", [8, D], BF16)
    act_sb = [p.sbuf(pfx + "act", [128, 16 * SLAB], BF16)]
    silu_sb = p.sbuf(pfx + "sl", [128, 2 * SLAB], F32)
    t_sb = p.sbuf(pfx + "t", [128, 2 * SLAB], F32)
    wbc_sb = p.sbuf(pfx + "wbc", [128, SEG], BF16)
    xi_b = [p.sbuf(pfx + "xi", [128, NCH * SLAB], F32)]
    ps_h = [p.psum(pfx + "ph", [128, SLAB], F32) for _ in range(4)]
    ps_o = [p.psum(pfx + "po", [128, SLAB], F32) for _ in range(4)]
    p.dma("sp", lambda e: e.dma_start(out=b0_sb[:], in_=b0h),
          wait=tuple(deps), inc=sb)
    p.dma("sp", lambda e: e.dma_start(out=b1_sb[:], in_=b1h), inc=sb)
    tok_bs = (sb, sb.v)
    sb2 = p.sem(pfx)
    tok_b2 = p.dma("pool", lambda e: e.dma_start(out=b2_sb[:], in_=b2h),
                   wait=tuple(deps), inc=sb2)

    w01_free = None
    w2_free = None
    tok_w01 = {}
    tok_w2 = {}
    units = [(seg, e) for seg in range(NSEG) for e in range(NE)]

    def load_w01(i, first=False):
        seg, e = units[i]
        w = tuple(deps) if i == 0 else ()
        if w01_free is not None:
            w = w + (w01_free,)
        ta_ = p.dma("pool", lambda e_, e=e: e_.dma_start(
            out=fm3(w0_sb[:], 2048),
            in_=w0h[e].rearrange("(c p) m -> p c m", p=128)), wait=w, inc=sw0)
        tb_ = p.dma("pool", lambda e_, e=e: e_.dma_start(
            out=fm3(w1_sb[:], 2048),
            in_=w1h[e].rearrange("(c p) m -> p c m", p=128)), inc=sw1)
        tok_w01[i] = (ta_, tb_)

    def load_w2(i):
        seg, e = units[i]
        w = tuple(deps) if i == 0 else ()
        if w2_free is not None:
            w = w + (w2_free,)
        tok_w2[i] = p.dma("pool", lambda e_, e=e: e_.dma_start(
            out=fm3(w2_sb[:], D, nch=16),
            in_=w2h[e].rearrange("(c p) m -> p c m", p=128)), wait=w, inc=sw2)

    load_w01(0, first=True)
    load_w2(0)
    ps_h_free = [None] * 4
    ps_o_free = [None] * 4
    wbc_w_free = None
    wbc_r_free = None
    act_free = [None]
    xn_free = [None, None]
    tok_acc = None
    tok_seg_done = None
    tok_st = None
    for seg in range(NSEG):
        h0 = seg * SEG
        wfw = () if tok_seg_done is None else (tok_seg_done,)
        tok_wf = p.dma("sp", lambda e, h0=h0: e.dma_start(
            out=wf_h[:], in_=wfm[0:8, h0:h0 + SEG]),
            wait=wfw + (tuple(deps) if seg == 0 else ()), inc=swf)
        tok_accfree = tok_acc
        for e in range(NE):
            ui = seg * NE + e
            tok_wbc = None
            for sl2 in range(SL):
                wb = (tok_wf,)
                if ps_h_free[0] is not None:
                    wb = wb + (ps_h_free[0],)
                if wbc_r_free is not None:
                    wb = wb + (wbc_r_free,)
                tbc = p.op("pe", lambda e_, e=e, sl2=sl2: e_.matmul(
                    ps_h[0][:], c["oh8"][:, e * 128:(e + 1) * 128],
                    wf_h[0:8, sl2 * SLAB:(sl2 + 1) * SLAB], start=True, stop=True),
                    wait=wb, inc=spe)
                tok_wbc = p.op("dve", lambda e_, sl2=sl2: e_.tensor_copy(
                    out=wbc_sb[:, sl2 * SLAB:(sl2 + 1) * SLAB], in_=ps_h[0][:]),
                    wait=(tbc,), inc=sdve)
                ps_h_free[0] = tok_wbc
            for s in range(SL):
                t0 = s * SLAB
                xb = s % 2
                ab = 0
                wxn = tuple(deps) if (seg == 0 and e == 0 and s < 2) else ()
                if xn_free[xb] is not None:
                    wxn = wxn + (xn_free[xb],)
                tok_xn = p.dma("sp", lambda e_, xb=xb, t0=t0, h0=h0: e_.dma_start(
                    out=fm3(xn_b[xb][:], SLAB), in_=hbm_slabT(xn2, h0 + t0, SLAB)),
                    wait=wxn, inc=sx[xb])
                tok_act = None
                prev_tt = [None, None]
                last_up = None
                for hb in range(HB):
                    for hc in range(8):
                        hq = hb * 8 + hc
                        pq = (hq % 2) * 2
                        wmm = ()
                        if hq == 0 and s == 0:
                            wmm = tok_w01[ui] + (tok_bs,)
                        if hq == 0:
                            wmm = wmm + (tok_xn, tok_wbc)
                            if act_free[ab] is not None:
                                wmm = wmm + (act_free[ab],)
                        t0mm = None
                        t1mm = None
                        for kc in range(NCH):
                            l0 = w0_sb[:, kc * 2048 + hq * 128: kc * 2048 + (hq + 1) * 128]
                            l1 = w1_sb[:, kc * 2048 + hq * 128: kc * 2048 + (hq + 1) * 128]
                            rhs = xn_b[xb][:, kc * SLAB:(kc + 1) * SLAB]
                            wk = wmm if kc == 0 else ()
                            if kc == 0 and ps_h_free[pq] is not None:
                                wk = wk + (ps_h_free[pq],)
                            t0mm = p.op("pe", lambda e_, l=l0, r=rhs, kc=kc, pq=pq: e_.matmul(
                                ps_h[pq][:], l, r, start=(kc == 0), stop=(kc == NCH - 1)),
                                wait=wk, inc=None)
                            wk2 = ()
                            if kc == 0 and ps_h_free[pq + 1] is not None:
                                wk2 = (ps_h_free[pq + 1],)
                            t1mm = p.op("pe", lambda e_, l=l1, r=rhs, kc=kc, pq=pq: e_.matmul(
                                ps_h[pq + 1][:], l, r, start=(kc == 0), stop=(kc == NCH - 1)),
                                wait=wk2, inc=spe if kc == NCH - 1 else None)
                        last_up = t1mm
                        bcol = e * 16 + hq
                        sq2 = hq % 2
                        wsl = (t1mm,)
                        if prev_tt[sq2] is not None:
                            wsl = wsl + (prev_tt[sq2],)
                        ts = p.op("act", lambda e_, bcol=bcol, pq=pq, sq2=sq2: e_.activation(
                            out=silu_sb[:, sq2 * SLAB:(sq2 + 1) * SLAB], in_=ps_h[pq + 1][:],
                            func=AF.Silu, bias=b1_sb[:, bcol:bcol + 1]), wait=wsl, inc=sact)
                        ps_h_free[pq + 1] = ts
                        tt = p.op("dve", lambda e_, bcol=bcol, pq=pq, sq2=sq2: e_.scalar_tensor_tensor(
                            t_sb[:, sq2 * SLAB:(sq2 + 1) * SLAB], ps_h[pq][:],
                            b0_sb[:, bcol:bcol + 1],
                            silu_sb[:, sq2 * SLAB:(sq2 + 1) * SLAB],
                            ALU.add, ALU.mult), wait=(ts,), inc=sdve)
                        prev_tt[sq2] = tt
                        ps_h_free[pq] = tt
                        tok_act = p.op("pool", lambda e_, hq=hq, ab=ab, sq2=sq2, t0=t0: e_.tensor_mul(
                            act_sb[ab][:, hq * SLAB:(hq + 1) * SLAB],
                            t_sb[:, sq2 * SLAB:(sq2 + 1) * SLAB],
                            wbc_sb[:, t0:t0 + SLAB]), wait=(tt,), inc=spool)
                wbc_r_free = tok_act
                for sub in range(2):
                    ts0 = t0 + sub * 256
                    for mc in range(NCH):
                        po = mc % 4
                        reg = ps_o[po][:, (mc // 4) * 256:(mc // 4) * 256 + 256]
                        wmm = (tok_act,)
                        if mc == 0 and s == 0 and sub == 0:
                            wmm = wmm + (tok_w2[ui],)
                        if ps_o_free[po] is not None:
                            wmm = wmm + (ps_o_free[po],)
                        tmm = None
                        for kq in range(16):
                            lhs = w2_sb[:, kq * D + mc * 128: kq * D + (mc + 1) * 128]
                            rhs = act_sb[ab][:, kq * SLAB + sub * 256: kq * SLAB + sub * 256 + 256]
                            tmm = p.op("pe", lambda e_, l=lhs, r=rhs, reg=reg, kq=kq, e2=e: e_.matmul(
                                reg, l, r, start=(kq == 0),
                                stop=(kq == 15 and e2 != 0)),
                                wait=wmm if kq == 0 else (),
                                inc=spe if (kq == 15 and e != 0) else None)
                        if e == 0:
                            tmm = p.op("pe", lambda e_, mc=mc, ts0=ts0, reg=reg: e_.matmul(
                                reg, b2_sb[0:8, mc * 128:(mc + 1) * 128],
                                wf_h[0:8, ts0:ts0 + 256], start=False, stop=True),
                                wait=(tok_b2,), inc=spe)
                        if mc >= 4:
                            av = acc[:].rearrange("p (g cq t) -> p g cq t", g=2, cq=4)[
                                :, :, mc - 4, t0 + sub * 256: t0 + sub * 256 + 256]
                            if e == 0:
                                wacc = (tmm,)
                                if tok_accfree is not None:
                                    wacc = wacc + (tok_accfree,)
                                tok_acc = p.op("dve", lambda e_, po=po, av=av: e_.tensor_scalar(
                                    av, ps_o[po][:].rearrange("p (g t) -> p g t", g=2),
                                    1.0, None, ALU.mult), wait=wacc, inc=sdve)
                            else:
                                tok_acc = p.op("dve", lambda e_, po=po, av=av: e_.scalar_tensor_tensor(
                                    av, ps_o[po][:].rearrange("p (g t) -> p g t", g=2),
                                    1.0, av, ALU.mult, ALU.add), wait=(tmm,), inc=sdve)
                            ps_o_free[po] = tok_acc
                act_free[ab] = tmm
                xn_free[xb] = last_up
                if e == NE - 1:
                    t0g = h0 + s * SLAB
                    b = 0
                    w0_ = (tok_acc,)
                    if sst[b].v > 0:
                        w0_ = w0_ + ((sst[b], sst[b].v),)
                    tok_xi = p.dma("sp", lambda e_, b=b, t0g=t0g: e_.dma_start(
                        out=fm3(xi_b[b][:], SLAB), in_=hbm_slabT(x_in, t0g, SLAB)),
                        wait=w0_, inc=sxi[b])
                    tok_add = p.op("dve", lambda e_, b=b, t0=t0: e_.tensor_add(
                        xi_b[b][:].rearrange("p (c t) -> p c t", c=NCH),
                        xi_b[b][:].rearrange("p (c t) -> p c t", c=NCH),
                        acc[:].rearrange("p (c t) -> p c t", c=NCH)[:, :, t0:t0 + SLAB]),
                        wait=(tok_xi, tok_acc), inc=sdve)
                    tok_acc = tok_add
                    tok_st = p.dma("sp", lambda e_, b=b, t0g=t0g: e_.dma_start(
                        out=hbm_slabT(y_out, t0g, SLAB), in_=fm3(xi_b[b][:], SLAB)),
                        wait=(tok_add,), inc=sst[b])
            if ui + 1 < len(units):
                w01_free = last_up
                load_w01(ui + 1)
                w2_free = tmm
                load_w2(ui + 1)
        tok_seg_done = tok_st
    return [tok_st]



def _stage(inputs):
    f32 = lambda a: np.asarray(a, np.float32)
    wT = lambda a: np.ascontiguousarray(f32(a).T)
    pp = lambda v: np.ascontiguousarray(f32(v).reshape(8, 128).T)
    bpp = lambda b: np.ascontiguousarray(
        f32(b).reshape(8, 16, 128).transpose(2, 0, 1).reshape(128, 128))
    st = dict(make_const_arrays())
    st["w_qkv0"] = np.ascontiguousarray(np.concatenate(
        [wT(inputs["L0_attn_Wq"]), wT(inputs["L0_attn_Wk"]),
         wT(inputs["L0_attn_Wv"])], axis=1))
    st["w_o0"] = wT(inputs["L0_attn_Wo"])
    st["rtr0_w"] = wT(inputs["L0_ffn_router_w"])
    st["up0"] = np.ascontiguousarray(np.concatenate(
        [f32(inputs["L0_ffn_shared_up"]).transpose(0, 2, 1),
         f32(inputs["L0_ffn_routed_up"]).transpose(0, 2, 1)]))
    st["dn0"] = np.ascontiguousarray(np.concatenate(
        [f32(inputs["L0_ffn_shared_down"]).transpose(0, 2, 1),
         f32(inputs["L0_ffn_routed_down"]).transpose(0, 2, 1)]))
    st["w_qkv1"] = wT(inputs["L1_attn_w_qkv"])
    st["wg1"] = wT(inputs["L1_attn_w_g1"])
    st["wg2"] = wT(inputs["L1_attn_w_g2"])
    st["wout"] = wT(inputs["L1_attn_w_out"])
    st["bg1"] = np.ascontiguousarray(f32(inputs["L1_attn_b_g1"])[:, None])
    st["bg2"] = pp(inputs["L1_attn_b_g2"])
    st["gm"] = pp(inputs["L1_attn_gamma"])
    st["bt"] = pp(inputs["L1_attn_beta"])
    st["rtr1_w"] = wT(inputs["L1_ffn_gate_w"])
    st["rtr1_b"] = np.ascontiguousarray(f32(inputs["L1_ffn_gate_b"])[:, None])
    st["w0h"] = np.ascontiguousarray(f32(inputs["L1_ffn_w0"]).transpose(0, 2, 1))
    st["w1h"] = np.ascontiguousarray(f32(inputs["L1_ffn_w1"]).transpose(0, 2, 1))
    st["w2h"] = np.ascontiguousarray(f32(inputs["L1_ffn_w2"]).transpose(0, 2, 1))
    st["b0h"] = bpp(inputs["L1_ffn_b0"])
    st["b1h"] = bpp(inputs["L1_ffn_b1"])
    st["b2h"] = np.ascontiguousarray(f32(inputs["L1_ffn_b2"]))
    return st


def _run_op(build, st, per_core, out_names):
    p = Prog()
    build(p)
    nc = p.finalize()
    in_maps = []
    for i in range(8):
        m = {k: st[k] for k in st if k in p.ext_inputs}
        m.update(per_core[i])
        in_maps.append(m)
    res = run_bass_kernel_spmd(nc, in_maps, core_ids=list(range(8)))
    return {nm: [res.results[i][nm] for i in range(8)] for nm in out_names}


FUSED = True


def kernel(**inputs):
    if FUSED:
        return _kernel_fused(inputs)
    return _kernel_unfused(inputs)


def _kernel_unfused(inputs):
    st = _stage(inputs)
    x = np.asarray(inputs["L0_attn_x"], np.float32)
    xfm = [np.ascontiguousarray(x[i].T).reshape(NCH, 128, T) for i in range(8)]

    FMF = [NCH, 128, T]

    def ext(p, name, shape, dt):
        p.ext_inputs.add(name)
        return p.dram(name, shape, dt, kind="ExternalInput")

    def fin(p, toks):
        p.op("sp", lambda e: None, wait=tuple(toks))

    def b_ln(src_is, dst, gamma=None):
        def b(p):
            c = emit_consts(p)
            src = ext(p, "src", FMF, F32)
            d = p.dram(dst, FMF, BF16, kind="ExternalOutput")
            fin(p, emit_ln(p, c, src, d, [c["tok"], c["tok2"]]))
        return b

    def b_qkv(kind, wname):
        def b(p):
            c = emit_consts(p)
            xn = ext(p, "xn", FMF, BF16)
            w = ext(p, wname, [D, 3 * D], F32)
            qd = p.dram("qd", FMF, BF16, kind="ExternalOutput")
            kd = p.dram("kd", FMF, BF16, kind="ExternalOutput")
            vd = p.dram("vd", FMF, BF16, kind="ExternalOutput")
            fin(p, emit_qkv(p, c, xn, qd, kd, vd, w, kind, [c["tok"], c["tok2"]]))
        return b

    def b_attn(denom):
        def b(p):
            c = emit_consts(p)
            qd = ext(p, "qd", FMF, BF16)
            kd = ext(p, "kd", FMF, BF16)
            vd = ext(p, "vd", FMF, BF16)
            ad = p.dram("ad", FMF, BF16, kind="ExternalOutput")
            fin(p, emit_attn(p, c, qd, kd, vd, ad, denom, [c["tok"], c["tok2"]]))
        return b

    def b_wo(p):
        c = emit_consts(p)
        a = ext(p, "a", FMF, BF16)
        xi = ext(p, "xi", FMF, F32)
        w = ext(p, "w_o0", [D, D], F32)
        xo = p.dram("xo", FMF, F32, kind="ExternalOutput")
        fin(p, emit_proj_residual(p, c, a, xi, xo, w, [c["tok"], c["tok2"]]))

    def b_rtr(mode, wname, bias=None):
        E = 16 if mode == "top4x16" else 8
        def b(p):
            c = emit_consts(p)
            xn = ext(p, "xn", FMF, BF16)
            w = ext(p, wname, [D, E], F32)
            gd = p.dram("gd", [E, T], BF16, kind="ExternalOutput")
            deps = [c["tok"], c["tok2"]]
            bias_sb = None
            if bias is not None:
                bh = ext(p, bias, [E, 1], F32)
                bias_sb = p.sbuf("bias", [E, 1], F32)
                sbm = p.sem("bs")
                deps.append(p.dma("sp", lambda e: e.dma_start(out=bias_sb[:], in_=bh), inc=sbm))
            fin(p, emit_router(p, c, xn, gd, w, mode, deps, bias_sb=bias_sb))
        return b

    def b_moe0(p):
        c = emit_consts(p)
        xn = ext(p, "xn", FMF, BF16)
        gfm = ext(p, "gfm", [16, T], BF16)
        xi = ext(p, "xi", FMF, F32)
        up = ext(p, "up0", [18, D, 512], F32)
        dn = ext(p, "dn0", [18, 512, D], F32)
        xo = p.dram("xo", FMF, F32, kind="ExternalOutput")
        fin(p, emit_moe0(p, c, xn, gfm, xi, xo, up, dn, [c["tok"], c["tok2"]]))

    def b_pa1(p):
        c = emit_consts(p)
        at = ext(p, "at", FMF, BF16)
        xn = ext(p, "xn", FMF, BF16)
        xi = ext(p, "xi", FMF, F32)
        wg1 = ext(p, "wg1", [D, 64], F32)
        wg2 = ext(p, "wg2", [64, D], F32)
        wout = ext(p, "wout", [D, D], F32)
        bg1 = ext(p, "bg1", [64, 1], F32)
        bg2 = ext(p, "bg2", [128, 8], F32)
        gm = ext(p, "gm", [128, 8], F32)
        bt = ext(p, "bt", [128, 8], F32)
        xo = p.dram("xo", FMF, F32, kind="ExternalOutput")
        fin(p, emit_postattn1(p, c, at, xn, xi, xo, wg1, wg2, wout,
                              bg1, bg2, gm, bt, [c["tok"], c["tok2"]]))

    def b_moe1(p):
        c = emit_consts(p)
        xn = ext(p, "xn", FMF, BF16)
        wfm = ext(p, "wfm", [8, T], BF16)
        xi = ext(p, "xi", FMF, F32)
        w0 = ext(p, "w0h", [8, D, 2048], F32)
        w1 = ext(p, "w1h", [8, D, 2048], F32)
        w2 = ext(p, "w2h", [8, 2048, D], F32)
        b0 = ext(p, "b0h", [128, 128], F32)
        b1 = ext(p, "b1h", [128, 128], F32)
        b2 = ext(p, "b2h", [8, D], F32)
        yo = p.dram("yo", FMF, F32, kind="ExternalOutput")
        fin(p, emit_moe1(p, c, xn, wfm, xi, yo, w0, w1, w2, b0, b1, b2,
                         [c["tok"], c["tok2"]]))

    pc = lambda key, arrs: [{key: arrs[i]} for i in range(8)]

    def pc2(**kw):
        return [{k: v[i] for k, v in kw.items()} for i in range(8)]

    r = _run_op(b_ln("src", "dst"), st, pc("src", xfm), ["dst"])
    xn0 = r["dst"]
    r = _run_op(b_qkv("elu1", "w_qkv0"), st, pc("xn", xn0), ["qd", "kd", "vd"])
    r = _run_op(b_attn(True), st, pc2(qd=r["qd"], kd=r["kd"], vd=r["vd"]), ["ad"])
    a0 = r["ad"]
    r = _run_op(b_wo, st, pc2(a=a0, xi=xfm), ["xo"])
    x1 = r["xo"]
    r = _run_op(b_rtr("top4x16", "rtr0_w"), st, pc("xn", xn0), ["gd"])
    g0 = r["gd"]
    r = _run_op(b_moe0, st, pc2(xn=xn0, gfm=g0, xi=x1), ["xo"])
    x2 = r["xo"]
    r = _run_op(b_ln("src", "dst"), st, pc("src", x2), ["dst"])
    xn1 = r["dst"]
    r = _run_op(b_qkv("silu", "w_qkv1"), st, pc("xn", xn1), ["qd", "kd", "vd"])
    r = _run_op(b_attn(False), st, pc2(qd=r["qd"], kd=r["kd"], vd=r["vd"]), ["ad"])
    a1 = r["ad"]
    r = _run_op(b_rtr("top2x8", "rtr1_w", bias="rtr1_b"), st, pc("xn", xn1), ["gd"])
    w1g = r["gd"]
    r = _run_op(b_pa1, st, pc2(at=a1, xn=xn1, xi=x2), ["xo"])
    x3 = r["xo"]
    r = _run_op(b_moe1, st, pc2(xn=xn1, wfm=w1g, xi=x3), ["yo"])
    y = r["yo"]
    out = np.empty((8, T, D), np.float32)
    for i in range(8):
        out[i] = np.asarray(y[i], np.float32).reshape(D, T).T
    return out


def _kernel_fused(inputs):
    st = _stage(inputs)
    x = np.asarray(inputs["L0_attn_x"], np.float32)
    xfm = [np.ascontiguousarray(x[i].T).reshape(NCH, 128, T) for i in range(8)]
    FMF = [NCH, 128, T]
    p = Prog()

    def ext(name, shape, dt=F32):
        p.ext_inputs.add(name)
        return p.dram(name, shape, dt, kind="ExternalInput")

    c = emit_consts(p)
    x0 = ext("x0", FMF)
    w_qkv0 = ext("w_qkv0", [D, 3 * D])
    w_o0 = ext("w_o0", [D, D])
    rtr0_w = ext("rtr0_w", [D, 16])
    up0 = ext("up0", [18, D, 512])
    dn0 = ext("dn0", [18, 512, D])
    w_qkv1 = ext("w_qkv1", [D, 3 * D])
    wg1 = ext("wg1", [D, 64])
    wg2 = ext("wg2", [64, D])
    wout = ext("wout", [D, D])
    bg1 = ext("bg1", [64, 1])
    bg2 = ext("bg2", [128, 8])
    gm = ext("gm", [128, 8])
    bt = ext("bt", [128, 8])
    rtr1_w = ext("rtr1_w", [D, 8])
    rtr1_b = ext("rtr1_b", [8, 1])
    w0h = ext("w0h", [8, D, 2048])
    w1h = ext("w1h", [8, D, 2048])
    w2h = ext("w2h", [8, 2048, D])
    b0h = ext("b0h", [128, 128])
    b1h = ext("b1h", [128, 128])
    b2h = ext("b2h", [8, D])
    xn0 = p.dram("s_xn0", FMF, BF16)
    qd = p.dram("s_q", FMF, BF16)
    kd = p.dram("s_k", FMF, BF16)
    vd = p.dram("s_v", FMF, BF16)
    a0 = p.dram("s_a0", FMF, BF16)
    x1 = p.dram("s_x1", FMF, F32)
    g0 = p.dram("s_g0", [16, T], BF16)
    x2 = p.dram("s_x2", FMF, F32)
    xn1 = p.dram("s_xn1", FMF, BF16)
    qd1 = p.dram("s_q1", FMF, BF16)
    kd1 = p.dram("s_k1", FMF, BF16)
    vd1 = p.dram("s_v1", FMF, BF16)
    a1 = p.dram("s_a1", FMF, BF16)
    w1g = p.dram("s_w1g", [8, T], BF16)
    x3 = p.dram("s_x3", FMF, F32)
    y = p.dram("y", FMF, F32, kind="ExternalOutput")

    dep0 = [c["tok"], c["tok2"]]
    with p.scope():
        t_ln0 = emit_ln(p, c, x0, xn0, dep0, pfx="ln0")
    with p.scope():
        t_qkv0 = emit_qkv(p, c, xn0, qd, kd, vd, w_qkv0, "elu1", t_ln0, pfx="q0")
    with p.scope():
        t_at0 = emit_attn(p, c, qd, kd, vd, a0, True, t_qkv0, pfx="a0")
    with p.scope():
        t_wo0 = emit_proj_residual(p, c, a0, x0, x1, w_o0, t_at0, pfx="w0")
    with p.scope():
        t_rt0 = emit_router(p, c, xn0, g0, rtr0_w, "top4x16", t_wo0, pfx="r0")
    with p.scope():
        t_m0 = emit_moe0(p, c, xn0, g0, x1, x2, up0, dn0, t_rt0, pfx="m0")
    with p.scope():
        t_ln1 = emit_ln(p, c, x2, xn1, t_m0, pfx="ln1")
    with p.scope():
        t_qkv1 = emit_qkv(p, c, xn1, qd1, kd1, vd1, w_qkv1, "silu", t_ln1, pfx="q1")
    with p.scope():
        t_at1 = emit_attn(p, c, qd1, kd1, vd1, a1, False, t_qkv1, pfx="a1")
    with p.scope():
        t_rt1 = emit_router(p, c, xn1, w1g, rtr1_w, "top2x8", t_at1,
                            bias_sb=None, bias_hbm=rtr1_b, pfx="r1")
    with p.scope():
        t_pa1 = emit_postattn1(p, c, a1, xn1, x2, x3, wg1, wg2, wout,
                               bg1, bg2, gm, bt, t_rt1, pfx="pa")
    with p.scope():
        t_m1 = emit_moe1(p, c, xn1, w1g, x3, y, w0h, w1h, w2h, b0h, b1h, b2h,
                         t_pa1, pfx="m1")
    p.op("sp", lambda e: None, wait=tuple(t_m1))
    nc = p.finalize()
    in_maps = []
    for i in range(8):
        m = {k: st[k] for k in st if k in p.ext_inputs}
        m["x0"] = xfm[i]
        in_maps.append(m)
    res = run_bass_kernel_spmd(nc, in_maps, core_ids=list(range(8)))
    out = np.empty((8, T, D), np.float32)
    for i in range(8):
        out[i] = np.asarray(res.results[i]["y"], np.float32).reshape(D, T).T
    return out
